# version 08
# baseline (speedup 1.0000x reference)
import contextlib

import numpy as np
import ml_dtypes
import concourse.bass as bass
import concourse.mybir as mybir
from concourse.bass_utils import run_bass_kernel_spmd

F32 = mybir.dt.float32
BF16 = mybir.dt.bfloat16
AF = mybir.ActivationFunctionType
ALU = mybir.AluOpType
AX = mybir.AxisListType

NCORES = 8
T = 4096
D = 1024
DI = 2048
DSTATE = 64
NH = 16
PD = 128
DCONV = 4
CHUNK = 256
NCH = T // CHUNK
CONV_DIM = DI + 2 * DSTATE
D_IN_PROJ = 2 * DI + 2 * DSTATE + NH
DIP_PAD = 4352
DH = 4096

BF = ml_dtypes.bfloat16
EPS = 1e-5
NEG = -1.0e30



class Sem:
    __slots__ = ("name", "n", "h")

    def __init__(self, name):
        self.name = name
        self.n = 0
        self.h = None

    def inc(self, k=1):
        self.n += k
        return self.n


class Ctx:
    ENGINES = ("sync", "tensor", "vector", "scalar", "gpsimd")

    def __init__(self, nc):
        self.nc = nc
        self.ops = {e: [] for e in self.ENGINES}
        self.sems = {}
        self._last_wait = {}

    def sem(self, name):
        if name not in self.sems:
            self.sems[name] = Sem(name)
        return self.sems[name]

    def begin_phase(self, tag):
        pass

    def barrier(self):
        bs = {e: self.sem(f"bar_{e}") for e in self.ENGINES}
        for e in self.ENGINES:
            s = bs[e]
            tgt = s.n + 1
            self.ops[e].append(lambda eng, _s=s, _t=tgt: eng.sem_inc(_s.h, 1))
            s.n = tgt
        for e in self.ENGINES:
            for e2 in self.ENGINES:
                self.wait(e, bs[e2], bs[e2].n)

    def do(self, engine, opname, *args, waits=(), incs=(), **kw):
        wlist = []
        for s, v in waits:
            key = (engine, s.name)
            if self._last_wait.get(key, -1) >= v:
                continue
            self._last_wait[key] = v
            wlist.append((s, v))
        ilist = list(incs)
        assert len(ilist) <= 1, "one sem update per instruction"
        for s, k in ilist:
            s.inc(k)

        def run(eng, _wl=wlist, _il=ilist, _op=opname, _a=args, _kw=kw):
            for s, v in _wl:
                eng.wait_ge(s.h, v)
            inst = getattr(eng, _op)(*_a, **_kw)
            for s, k in _il:
                inst.then_inc(s.h, k)

        self.ops[engine].append(run)

    def wait(self, engine, s, v):
        key = (engine, s.name)
        if self._last_wait.get(key, -1) >= v:
            return
        self._last_wait[key] = v
        self.ops[engine].append(lambda eng, _s=s, _v=v: eng.wait_ge(_s.h, _v))

    def build_block(self, es):
        for s in self.sems.values():
            s.h = es.enter_context(self.nc.semaphore(s.name))
        block = es.enter_context(self.nc.Block())
        for e in self.ENGINES:
            fns = self.ops[e]
            if not fns:
                continue
            getattr(block, e)(lambda eng, _fns=fns: [f(eng) for f in _fns])


def ceil(a, b):
    return -(-a // b)



WCAST = [
    ("wip", "L0_attn_in_proj_w", D_IN_PROJ, D, DIP_PAD),
    ("wop", "L0_attn_out_proj_w", D, DI, None),
    ("wr", "L0_ffn_W_r", D, D, None),
    ("wk", "L0_ffn_W_k", D, D, None),
    ("wv", "L0_ffn_W_v", D, D, None),
    ("qw", "L1_attn_Qw", D, D, None),
    ("kw", "L1_attn_Kw", D, D, None),
    ("vw", "L1_attn_Vw", D, D, None),
    ("w1", "L1_ffn_w1_w", DH, D, None),
    ("w2", "L1_ffn_w2_w", DH, D, None),
    ("h1", "L1_ffn_h1_w", DH, DH, None),
    ("h2", "L1_ffn_h2_w", DH, DH, None),
    ("w3", "L1_ffn_w3_w", D, DH, None),
]

SCRATCH = {}
for _n, _src, _r, _c, _p in WCAST:
    SCRATCH[_n + "_bf"] = ([_p if _p else _r, _c], BF16)
SCRATCH.update({
    "xn0": ([T, D], BF16),
    "xn0f": ([D, T], BF16),
    "xn1f": ([D, T], BF16),
    "zs": ([DI, T], F32),
    "xsbc": ([CONV_DIM, T], BF16),
    "dtv": ([16, T], F32),
    "lndt": ([16, T], F32),
    "macc": ([T, D], F32),
    "x1": ([T, D], F32),
    "xn1": ([T, D], BF16),
    "qf": ([D, T], BF16),
    "kf": ([D, T], BF16),
    "vt": ([T, D], BF16),
    "attn": ([T, D], F32),
    "a5": ([DH, T], BF16),
    "b5": ([DH, T], BF16),
    "c5": ([DH, T], BF16),
})

MODULE_IN = {
    "u": ("L0_attn_u", [T, D], F32),
}

VEC_IN = {
    "L0_attn_conv_w": ([CONV_DIM, DCONV], F32),
    "L0_attn_conv_b": ([CONV_DIM], F32),
    "L0_attn_dt_bias": ([16], F32),
    "L0_attn_A_log": ([16], F32),
    "L0_attn_Dp": ([16], F32),
    "L0_attn_norm_w": ([DI], F32),
    "L0_ffn_mu_r": ([D], F32),
    "L0_ffn_mu_k": ([D], F32),
    "L1_ffn_w1_s": ([DH], F32),
    "L1_ffn_w2_s": ([DH], F32),
    "L1_ffn_h1_s": ([DH], F32),
    "L1_ffn_h2_s": ([DH], F32),
    "L1_ffn_w3_s": ([D], F32),
}

WEIGHT_IN = {src: ([r, c], F32) for _, src, r, c, _ in WCAST}



PT_GROUP = {"wip": "ptg1", "wop": "ptg2", "wr": "ptg2", "wk": "ptg2",
            "wv": "ptg2", "qw": "ptg3", "kw": "ptg3", "vw": "ptg3",
            "w1": "ptg3", "w2": "ptg3", "h1": "ptg3", "h2": "ptg3",
            "w3": "ptg3"}
PT_ORDER = ["wip", "wop", "wr", "wk", "wv", "qw", "kw", "vw", "w1", "w2",
            "h1", "h2", "w3"]


def emit_pt(ctx, dram, es):
    if "zpad" not in dram:
        dram["zpad"] = ctx.nc.inline_tensor(
            np.zeros((DIP_PAD - D_IN_PROJ, D), dtype=BF), "zpad").ap()
    wc = {n: e for (n, *e_) in [(w[0], w) for w in WCAST] for e in [e_]}
    wmap = {w[0]: w for w in WCAST}
    for name in PT_ORDER:
        _, src, rows, cols, pad = wmap[name]
        g = ctx.sem(PT_GROUP[name])
        dst = dram[name + "_bf"]
        step = max(512, min(rows, (1 << 25) // (cols * 4)))
        r = 0
        while r < rows:
            rr = min(step, rows - r)
            ctx.do("gpsimd", "dma_start", out=dst[r:r + rr, :],
                   in_=dram[src][r:r + rr, :], incs=[(g, 16)])
            r += rr
        if pad:
            ctx.do("gpsimd", "dma_start", out=dst[rows:pad, :],
                   in_=dram["zpad"][0:pad - rows, :], incs=[(g, 16)])
    for gn in ["ptg1", "ptg2", "ptg3"]:
        g = ctx.sem(gn)
        ctx.wait("gpsimd", g, g.n)


def emit_ln(ctx, dram, es, src_name, dst_name, fm_name, tag, semtag="ln"):
    nc = ctx.nc
    NTT = T // 128
    xt = [es.enter_context(nc.sbuf_tensor(f"{tag}x{i}", [128, D], F32)) for i in range(2)]
    xq = es.enter_context(nc.sbuf_tensor(f"{tag}xq", [128, D], F32))
    on = [es.enter_context(nc.sbuf_tensor(f"{tag}o{i}", [128, D], BF16)) for i in range(2)]
    fmS = [es.enter_context(nc.sbuf_tensor(f"{tag}f{i}", [128, 8 * 512], BF16)) for i in range(2)]
    idb = es.enter_context(nc.sbuf_tensor(f"{tag}ib", [128, 128], BF16))
    cs = es.enter_context(nc.sbuf_tensor(f"{tag}cs", [128, 1], F32))
    cq = es.enter_context(nc.sbuf_tensor(f"{tag}cq", [128, 1], F32))
    nm = es.enter_context(nc.sbuf_tensor(f"{tag}nm", [128, 1], F32))
    cv = es.enter_context(nc.sbuf_tensor(f"{tag}cv", [128, 1], F32))
    cr = es.enter_context(nc.sbuf_tensor(f"{tag}cr", [128, 1], F32))
    cte = es.enter_context(nc.sbuf_tensor(f"{tag}ct", [128, 1], F32))
    pT = [es.enter_context(nc.psum_tensor(f"{tag}pT{i}", [128, 128], BF16)) for i in range(2)]

    ld = [ctx.sem(f"{semtag}ld0"), ctx.sem(f"{semtag}ld1")]
    cn = ctx.sem(f"{semtag}cn")
    sq = ctx.sem(f"{semtag}sq")
    vv = ctx.sem(f"{semtag}vv")
    ts = ctx.sem(f"{semtag}ts")
    fe = ctx.sem(f"{semtag}fe")
    st = [ctx.sem(f"{semtag}st0"), ctx.sem(f"{semtag}st1")]
    fs = [ctx.sem(f"{semtag}fs0"), ctx.sem(f"{semtag}fs1")]

    identd = nc.inline_tensor(np.eye(128, dtype=BF), f"ib{tag}_d").ap()
    ctx.do("sync", "dma_start", out=idb[:], in_=identd[:], incs=[(ld[0], 16)])
    ctx.do("vector", "memset", cte[:], float(EPS), incs=[(cn, 1)])
    vv0, sq0, ts0 = vv.n, sq.n, ts.n
    ptfree = [fe.n, fe.n]
    src, dst, fmd = dram[src_name], dram[dst_name], dram[fm_name]
    for i in range(NTT):
        b = i % 2
        g = i % 8 // 4
        w = [(vv, vv0 + (i - 2) * 2 + 2)] if i >= 2 else []
        ctx.do("sync", "dma_start", out=xt[b][:], in_=src[i * 128:(i + 1) * 128, :],
               waits=w, incs=[(ld[b], 16)])
        ldv = (ld[b], ld[b].n)
        ctx.do("scalar", "activation", xq[:], xt[b][:], AF.Square,
               accum_out=cq[:, 0:1], waits=[ldv], incs=[(sq, 1)])
        tc = ctx.sem(f"{semtag}tc")
        ctx.do("vector", "reduce_sum", cs[:, 0:1], xt[b][:], AX.X,
               waits=[ldv], incs=[(tc, 1)])
        ctx.do("vector", "tensor_scalar", nm[:, 0:1], cs[:, 0:1], -1.0 / D, None, ALU.mult,
               waits=[(tc, tc.n)], incs=[(tc, 1)])
        ctx.do("vector", "tensor_scalar", cv[:, 0:1], cq[:, 0:1], 1.0 / D, None, ALU.mult,
               waits=[(sq, sq0 + 2 * i + 1)], incs=[(tc, 1)])
        ctx.do("vector", "scalar_tensor_tensor", cv[:, 0:1], nm[:, 0:1], nm[:, 0:1],
               cv[:, 0:1], ALU.mult, ALU.subtract, waits=[(tc, tc.n)], incs=[(vv, 1)])
        ctx.do("scalar", "activation", cr[:, 0:1], cv[:, 0:1], AF.Ln,
               bias=cte[:, 0:1], scale=-1.0, waits=[(vv, vv0 + i * 2 + 1), (cn, cn.n)])
        ctx.do("scalar", "activation", cr[:, 0:1], cr[:, 0:1], AF.Exp,
               scale=-0.5, incs=[(sq, 1)])
        w = [(sq, sq.n), (st[b], st[b].n)]
        if i >= 2:
            w.append((ts, ts0 + (i - 2) * 8 + 8))
        ctx.do("vector", "tensor_scalar", on[b][:], xt[b][:], nm[:, 0:1], cr[:, 0:1],
               ALU.add, op1=ALU.mult, waits=w, incs=[(vv, 1)])
        xnv = vv.n
        ctx.do("scalar", "dma_start", out=dst[i * 128:(i + 1) * 128, :], in_=on[b][:],
               waits=[(vv, xnv)], incs=[(st[b], 16)])
        for dc in range(8):
            pb = pT[dc % 2]
            ctx.do("tensor", "transpose", pb[:], on[b][:, dc * 128:(dc + 1) * 128],
                   idb[:], waits=[(vv, xnv), (ld[0], 16), (fe, ptfree[dc % 2])],
                   incs=[(ts, 1)])
            ctx.do("scalar", "activation",
                   fmS[g][:, dc * 512 + (i % 4) * 128: dc * 512 + (i % 4) * 128 + 128],
                   pb[:], AF.Identity,
                   waits=[(ts, ts.n), (fs[g], fs[g].n)], incs=[(fe, 1)])
            ptfree[dc % 2] = fe.n
        if i % 4 == 3:
            g0 = (i // 4) * 512
            for dc in range(8):
                ctx.do("scalar", "dma_start",
                       out=fmd[dc * 128:(dc + 1) * 128, g0:g0 + 512],
                       in_=fmS[g][:, dc * 512:(dc + 1) * 512],
                       waits=[(fe, fe.n)], incs=[(fs[g], 16)])
    ctx.wait("scalar", st[0], st[0].n)
    ctx.wait("scalar", st[1], st[1].n)
    ctx.wait("scalar", fs[0], fs[0].n)
    ctx.wait("scalar", fs[1], fs[1].n)


def emit_p1a(ctx, dram, es):
    nc = ctx.nc
    NT = 8
    TB = 512
    CS = 516

    ipT = es.enter_context(nc.sbuf_tensor("ipT", [128, 8 * DIP_PAD], BF16))
    xnb = [es.enter_context(nc.sbuf_tensor(f"xnb{i}", [128, 8 * TB], BF16))
           for i in range(3)]
    xbc = es.enter_context(nc.sbuf_tensor("xbc", [128, 17 * CS], F32))
    acc = [es.enter_context(nc.sbuf_tensor(f"acc{i}", [128, TB], F32)) for i in range(2)]
    zb = [es.enter_context(nc.sbuf_tensor(f"zb{i}", [128, TB], F32)) for i in range(2)]
    ob = [es.enter_context(nc.sbuf_tensor(f"ob{i}", [128, TB], BF16)) for i in range(2)]
    dtraw = es.enter_context(nc.sbuf_tensor("dtraw", [16, T], F32))
    dps = [es.enter_context(nc.sbuf_tensor(f"dps{i}", [16, T], F32)) for i in range(2)]
    cw = es.enter_context(nc.sbuf_tensor("cw", [128, 17 * 4], F32))
    cbp = es.enter_context(nc.sbuf_tensor("cbp", [17, 128], F32))
    cb = es.enter_context(nc.sbuf_tensor("cb", [128, 17], F32))
    dtb = es.enter_context(nc.sbuf_tensor("dtb", [16, 1], F32))
    ident = es.enter_context(nc.sbuf_tensor("identA", [128, 128], F32))
    c1 = es.enter_context(nc.sbuf_tensor("c1A", [16, 1], F32))
    ps = [es.enter_context(nc.psum_tensor(f"pA{i}", [128, TB], F32)) for i in range(4)]
    pst = es.enter_context(nc.psum_tensor("pAt", [128, 17], F32))

    ld = ctx.sem("Ald")
    xs_ = [ctx.sem("Axn0"), ctx.sem("Axn1"), ctx.sem("Axn2")]
    zs_ = [ctx.sem("Az0"), ctx.sem("Az1")]
    xo_ = [ctx.sem("Ax0"), ctx.sem("Ax1")]
    gl = ctx.sem("Agl")
    ts = ctx.sem("Ats")
    ev = ctx.sem("Aev")
    vc = ctx.sem("Avc")
    st = ctx.sem("Ast")

    identd = nc.inline_tensor(np.eye(128, dtype=np.float32), "identA_d").ap()
    identbd = nc.inline_tensor(np.eye(128, dtype=BF), "identAb_d").ap()
    identb_a = es.enter_context(nc.sbuf_tensor("Aidb", [128, 128], BF16))

    _g1 = ctx.sem("ptg1")
    ctx.do("sync", "dma_start", out=identb_a[:], in_=identbd[:], incs=[(ld, 16)])
    ws_ = [ctx.sem("Wst0"), ctx.sem("Wst1")]
    aie = ctx.sem("Tie")
    wstage = [es.enter_context(nc.sbuf_tensor(f"Awst{i}", [128, 1024], BF16))
              for i in range(2)]
    pTb = [es.enter_context(nc.psum_tensor(f"ApT{i}", [128, 128], BF16))
           for i in range(2)]
    ptfree = [0, 0]
    wsfree = [0, 0]
    for m in range(34):
        s = ws_[m % 2]
        ctx.do("sync", "dma_start", out=wstage[m % 2][:],
               in_=dram["wip_bf"][m * 128:(m + 1) * 128, :],
               waits=[(_g1, _g1.n), (ts, wsfree[m % 2])], incs=[(s, 16)])
        for dc in range(8):
            ctx.do("tensor", "transpose", pTb[dc % 2][:],
                   wstage[m % 2][:, dc * 128:(dc + 1) * 128], identb_a[:],
                   waits=[(s, s.n), (aie, ptfree[dc % 2]), (ld, 16)],
                   incs=[(ts, 1)])
            ctx.do("scalar", "activation",
                   ipT[:, dc * DIP_PAD + m * 128: dc * DIP_PAD + m * 128 + 128],
                   pTb[dc % 2][:], AF.Identity, waits=[(ts, ts.n)], incs=[(aie, 1)])
            ptfree[dc % 2] = aie.n
        wsfree[m % 2] = ts.n
    ipv = aie.n

    xnld = {}

    def load_xn_block(tb, waits=()):
        s = xs_[tb % 3]
        ctx.do("sync", "dma_start",
               out=xnb[tb % 3][:].rearrange("p (dc t) -> p dc t", t=TB),
               in_=dram["xn0f"][0:D, tb * TB:(tb + 1) * TB]
               .rearrange("(dc p) t -> p dc t", p=128),
               waits=waits, incs=[(s, 16)])
        xnld[tb] = (s, s.n)

    load_xn_block(0)
    load_xn_block(1)
    ctx.do("gpsimd", "dma_start",
           out=cw[:].rearrange("p (c w) -> p c w", w=4),
           in_=dram["L0_attn_conv_w"].rearrange("(c p) w -> p c w", p=128),
           incs=[(gl, 16)])
    ctx.do("gpsimd", "dma_start", out=cbp[:],
           in_=dram["L0_attn_conv_b"].rearrange("(c p) -> c p", p=128),
           incs=[(gl, 16)])
    ctx.do("gpsimd", "dma_start", out=dtb[:],
           in_=dram["L0_attn_dt_bias"].rearrange("(a b) -> a b", b=1),
           incs=[(gl, 16)])
    ctx.do("sync", "dma_start", out=ident[:], in_=identd[:], incs=[(ld, 16)])
    ctx.do("tensor", "transpose", pst[:, 0:17], cbp[:], ident[0:17, 0:17],
           waits=[(gl, 32), (ld, ld.n)], incs=[(ts, 1)])
    ctx.do("scalar", "activation", cb[:], pst[:, 0:17], AF.Identity,
           waits=[(ts, ts.n)], incs=[(ev, 1)])
    ctx.do("vector", "memset", xbc[:], 0.0, incs=[(vc, 1)])
    ctx.do("vector", "memset", c1[:], 1.0, incs=[(vc, 1)])

    gl_all = gl.n
    bank_free = [(ev, 0)] * 4
    nb = 0
    tailv = [0] * 17
    evacv = [[0] * 34 for _ in range(NT)]
    zst = [0, 0]
    ost = [0, 0]
    siluv = [0, 0]

    lastmm = [0] * NT
    for tb in range(NT):
        if tb + 2 < NT:
            load_xn_block(tb + 2, waits=[(ts, lastmm[tb - 1])] if tb >= 1 else ())
        for m in range(34):
            bk = ps[nb]
            w = [(aie, ipv), xnld[tb], bank_free[nb]]
            for k in range(8):
                ctx.do("tensor", "matmul",
                       bk[:, :] if m != 33 else bk[0:16, :],
                       ipT[:, k * DIP_PAD + m * 128: k * DIP_PAD + m * 128 + (128 if m != 33 else 16)],
                       xnb[tb % 3][:, k * TB: k * TB + TB],
                       start=(k == 0), stop=(k == 7),
                       waits=w if k == 0 else (),
                       incs=[(ts, 1)] if k == 7 else ())
            lastmm[tb] = ts.n
            mmv = ts.n
            if m < 16:
                zi = (tb * 16 + m) % 2
                ctx.do("scalar", "activation", zb[zi][:], bk[:], AF.Silu,
                       waits=[(ts, mmv), (zs_[zi], zs_[zi].n)], incs=[(ev, 1)])
                evc = ev.n
                ctx.do("scalar", "dma_start",
                       out=dram["zs"][m * 128:(m + 1) * 128, tb * TB:(tb + 1) * TB],
                       in_=zb[zi][:], waits=[(ev, evc)], incs=[(zs_[zi], 16)])
            elif m < 33:
                cc = m - 16
                r0 = cc * CS
                w2 = [(ts, mmv), (vc, 1)]
                if tb > 0:
                    w2.append((vc, tailv[cc]))
                ctx.do("scalar", "activation", xbc[:, r0 + 3:r0 + 3 + TB], bk[:],
                       AF.Identity, waits=w2, incs=[(ev, 1)])
            else:
                ctx.do("scalar", "activation", dtraw[0:16, tb * TB:(tb + 1) * TB],
                       bk[0:16, :], AF.Identity, waits=[(ts, mmv)], incs=[(ev, 1)])
            evacv[tb][m] = ev.n
            bank_free[nb] = (ev, ev.n)
            nb = (nb + 1) % 4
        for cc in range(17):
            r0 = cc * CS
            ai = cc % 2
            aw = [(ev, max(evacv[tb][16 + cc], siluv[ai])), (gl, gl_all)]
            ctx.do("vector", "tensor_scalar", acc[ai][:], xbc[:, r0:r0 + TB],
                   cw[:, cc * 4:cc * 4 + 1], cb[:, cc:cc + 1], ALU.mult, op1=ALU.add,
                   waits=aw + [(vc, vc.n)], incs=[(vc, 1)])
            for wtap in range(1, 4):
                ctx.do("vector", "scalar_tensor_tensor", acc[ai][:],
                       xbc[:, r0 + wtap:r0 + wtap + TB], cw[:, cc * 4 + wtap:cc * 4 + wtap + 1],
                       acc[ai][:], ALU.mult, ALU.add,
                       waits=[(vc, vc.n)], incs=[(vc, 1)])
            convv = vc.n
            oi = cc % 2
            ctx.do("scalar", "activation", ob[oi][:], acc[ai][:], AF.Silu,
                   waits=[(vc, convv), (xo_[oi], xo_[oi].n)], incs=[(ev, 1)])
            siluv[ai] = ev.n
            ctx.do("scalar", "dma_start",
                   out=dram["xsbc"][cc * 128:(cc + 1) * 128, tb * TB:(tb + 1) * TB],
                   in_=ob[oi][:], waits=[(ev, ev.n)], incs=[(xo_[oi], 16)])
            ctx.do("vector", "tensor_copy", xbc[:, r0:r0 + 3],
                   xbc[:, r0 + TB:r0 + TB + 3], waits=[(vc, convv)], incs=[(vc, 1)])
            tailv[cc] = vc.n
    dn = dtraw[0:16, :]
    d1, d2, d3 = dtraw[0:16, :], dps[0][0:16, :], dps[1][0:16, :]
    ctx.do("vector", "tensor_scalar", d1, dn, dtb[0:16, 0:1], None, ALU.add,
           waits=[(ev, evacv[NT - 1][33]), (gl, gl_all), (vc, vc.n)], incs=[(vc, 1)])
    ctx.do("scalar", "activation", d2, d1, AF.Abs,
           waits=[(vc, vc.n)], incs=[(ev, 1)])
    ctx.do("scalar", "activation", d3, d2, AF.Exp, scale=-1.0,
           waits=[(ev, ev.n)], incs=[(ev, 1)])
    expv = ev.n
    ctx.do("scalar", "activation", d3, d3, AF.Ln, bias=c1[0:16, 0:1],
           waits=[(ev, expv), (vc, 2)], incs=[(ev, 1)])
    ctx.do("vector", "tensor_scalar", d2, d1, 0.0, None, ALU.max,
           waits=[(vc, vc.n), (ev, expv)], incs=[(vc, 1)])
    ctx.do("vector", "tensor_tensor", d1, d2, d3, ALU.add,
           waits=[(vc, vc.n), (ev, ev.n)], incs=[(vc, 1)])
    ctx.do("sync", "dma_start", out=dram["dtv"][:], in_=d1,
           waits=[(vc, vc.n)], incs=[(st, 16)])
    ctx.do("scalar", "activation", d2, d1, AF.Ln, waits=[(vc, vc.n)], incs=[(ev, 1)])
    ctx.do("sync", "dma_start", out=dram["lndt"][:], in_=d2,
           waits=[(ev, ev.n)], incs=[(st, 16)])
    ctx.wait("sync", st, st.n)
    for s in (zs_ + xo_):
        ctx.wait("scalar", s, s.n)


def emit_p1b(ctx, dram, es):
    nc = ctx.nc
    CH = CHUNK

    WoT = es.enter_context(nc.sbuf_tensor("WoT", [128, 16 * 1024], BF16))
    dtv = es.enter_context(nc.sbuf_tensor("dtvS", [16, T], F32))
    lnd = es.enter_context(nc.sbuf_tensor("lndS", [16, T], F32))
    af = es.enter_context(nc.sbuf_tensor("afS", [16, T], F32))
    prev = es.enter_context(nc.sbuf_tensor("prevS", [64, NH * 128], F32))
    prevb = es.enter_context(nc.sbuf_tensor("prevbS", [64, NH * 128], BF16))
    ident = es.enter_context(nc.sbuf_tensor("identB", [128, 128], F32))
    sel = es.enter_context(nc.sbuf_tensor("selB", [16, NH * 128], F32))
    mk = es.enter_context(nc.sbuf_tensor("mkB", [128, 2 * CH], F32))
    onesr = es.enter_context(nc.sbuf_tensor("onesrB", [1, 128], F32))
    onesh = es.enter_context(nc.sbuf_tensor("oneshB", [16, 128], F32))
    onesc = es.enter_context(nc.sbuf_tensor("onescB", [128, 1], BF16))
    Acol = es.enter_context(nc.sbuf_tensor("AcolB", [16, 1], F32))
    Dpb = es.enter_context(nc.sbuf_tensor("DpbB", [128, 16], F32))
    nw = es.enter_context(nc.sbuf_tensor("nwB", [128, 16], F32))
    nwp = es.enter_context(nc.sbuf_tensor("nwpB", [16, 128], F32))
    epsc = es.enter_context(nc.sbuf_tensor("epscB", [1, 1], F32))
    vtmp = es.enter_context(nc.sbuf_tensor("vtmpB", [16, 1], F32))
    xsf = es.enter_context(nc.sbuf_tensor("xsfB", [128, NH * CH], BF16))
    xst = [es.enter_context(nc.sbuf_tensor(f"xstB{i}", [128, DI], BF16)) for i in range(2)]
    Bcf = es.enter_context(nc.sbuf_tensor("BcfB", [64, CH], BF16))
    Ccf = es.enter_context(nc.sbuf_tensor("CcfB", [64, CH], BF16))
    Bct = [es.enter_context(nc.sbuf_tensor(f"BctB{i}", [128, 64], BF16)) for i in range(2)]
    Bcw = [es.enter_context(nc.sbuf_tensor(f"BcwB{i}", [128, 64], BF16)) for i in range(2)]
    acum = es.enter_context(nc.sbuf_tensor("acumB", [16, CH], F32))
    eac = es.enter_context(nc.sbuf_tensor("eacB", [16, CH], F32))
    acT = [es.enter_context(nc.sbuf_tensor(f"acTB{i}", [128, 16], F32)) for i in range(2)]
    colk = [es.enter_context(nc.sbuf_tensor(f"colkB{i}", [128, 16], F32)) for i in range(2)]
    dtds = [es.enter_context(nc.sbuf_tensor(f"dtdsB{i}", [128, 16], F32)) for i in range(2)]
    dtT = [es.enter_context(nc.sbuf_tensor(f"dtTB{i}", [128, 16], F32)) for i in range(2)]
    alb = es.enter_context(nc.sbuf_tensor("albB", [128, 16], F32))
    alc = es.enter_context(nc.sbuf_tensor("alcB", [16, 1], F32))
    alr = es.enter_context(nc.sbuf_tensor("alrB", [1, 16], F32))
    cdb = es.enter_context(nc.sbuf_tensor("cdbB", [64, 16], F32))
    scT = [es.enter_context(nc.sbuf_tensor(f"scTB{i}", [128, CH], F32)) for i in range(2)]
    arg = [es.enter_context(nc.sbuf_tensor(f"argB{i}", [128, CH], F32)) for i in range(2)]
    eaa = [es.enter_context(nc.sbuf_tensor(f"eaaB{i}", [128, CH], F32)) for i in range(2)]
    MT = [es.enter_context(nc.sbuf_tensor(f"MTB{i}", [128, CH], BF16)) for i in range(2)]
    Ccs = es.enter_context(nc.sbuf_tensor("CcsB", [64, CH], BF16))
    Yf = es.enter_context(nc.sbuf_tensor("YfB", [128, NH * CH], F32))
    zsc = es.enter_context(nc.sbuf_tensor("zscB", [128, NH * CH], F32))
    ysq = [es.enter_context(nc.sbuf_tensor(f"ysqB{i}", [128, CH], BF16)) for i in range(2)]
    r1 = es.enter_context(nc.sbuf_tensor("r1B", [1, CH], F32))
    ynb = es.enter_context(nc.sbuf_tensor("ynbB", [128, NH * CH], BF16))
    osb = [es.enter_context(nc.sbuf_tensor(f"osbB{i}", [128, 512], F32)) for i in range(2)]
    pk = [es.enter_context(nc.psum_tensor(f"pB{i}", [128, 512], F32)) if i != 1
          else None for i in range(8)]
    pkT = es.enter_context(nc.psum_tensor("pBT", [128, 512], BF16))

    ld = ctx.sem("Bld"); gl = ctx.sem("Bgl"); tsn = ctx.sem("Bts")
    ev = ctx.sem("Bev"); vc = ctx.sem("Bvc"); gp = ctx.sem("Bgp")
    gs = ctx.sem("Bgs")
    so_ = [ctx.sem("Bso0"), ctx.sem("Bso1")]

    identd = nc.inline_tensor(np.eye(128, dtype=np.float32), "identB_d").ap()
    seln = np.zeros((16, NH * 128), dtype=np.float32)
    for h in range(NH):
        seln[h, h * 128:(h + 1) * 128] = 1.0
    seld = nc.inline_tensor(seln, "selB_d").ap()
    mkn = np.zeros((128, 2 * CH), dtype=np.float32)
    for kt in range(2):
        for kk in range(128):
            gk = kt * 128 + kk
            mkn[kk, kt * CH + gk:(kt + 1) * CH] = 1.0
    mkd = nc.inline_tensor(mkn, "mkB_d").ap()

    _g2 = ctx.sem("ptg2")
    bws = [ctx.sem("Wst0"), ctx.sem("Wst1")]
    bie = ctx.sem("Tie")
    wstage = [es.enter_context(nc.sbuf_tensor(f"Bwst{i}", [128, DI], BF16))
              for i in range(2)]
    identbd = nc.inline_tensor(np.eye(128, dtype=BF), "identBb_d").ap()
    identb_b = es.enter_context(nc.sbuf_tensor("Bidb", [128, 128], BF16))
    ctx.do("sync", "dma_start", out=identb_b[:], in_=identbd[:], incs=[(ld, 16)])
    _ptfree = [0, 0]
    _wsfree = [0, 0]
    for m in range(8):
        s = bws[m % 2]
        ctx.do("sync", "dma_start", out=wstage[m % 2][:],
               in_=dram["wop_bf"][m * 128:(m + 1) * 128, :],
               waits=[(_g2, _g2.n), (tsn, _wsfree[m % 2])], incs=[(s, 16)])
        for ec in range(16):
            ctx.do("tensor", "transpose", pkT[0:128, 0:128],
                   wstage[m % 2][:, ec * 128:(ec + 1) * 128], identb_b[:],
                   waits=[(s, s.n), (bie, _ptfree[0]), (ld, 16)],
                   incs=[(tsn, 1)])
            ctx.do("scalar", "activation",
                   WoT[:, ec * 1024 + m * 128: ec * 1024 + m * 128 + 128],
                   pkT[0:128, 0:128], AF.Identity, waits=[(tsn, tsn.n)],
                   incs=[(bie, 1)])
            _ptfree[0] = bie.n
        _wsfree[m % 2] = tsn.n
    wov = bie.n
    ctx.do("sync", "dma_start", out=dtv[:], in_=dram["dtv"][:], incs=[(ld, 16)])
    ctx.do("sync", "dma_start", out=lnd[:], in_=dram["lndt"][:], incs=[(ld, 16)])
    ctx.do("sync", "dma_start", out=ident[:], in_=identd[:], incs=[(ld, 16)])
    ctx.do("sync", "dma_start", out=sel[:], in_=seld[:], incs=[(ld, 16)])
    ctx.do("sync", "dma_start", out=mk[:], in_=mkd[:], incs=[(ld, 16)])
    ctx.do("gpsimd", "dma_start", out=vtmp[:],
           in_=dram["L0_attn_A_log"].rearrange("(a b) -> a b", b=1), incs=[(gs, 16)])
    ctx.do("gpsimd", "dma_start", out=Acol[:],
           in_=dram["L0_attn_Dp"].rearrange("(a b) -> a b", b=1), incs=[(gs, 16)])
    ctx.do("gpsimd", "dma_start", out=nwp[:],
           in_=dram["L0_attn_norm_w"].rearrange("(c p) -> c p", p=128), incs=[(gs, 16)])
    ld0, gl0 = ld.n, gs.n

    for t_, v_ in [(onesr, 1.0), (onesh, 1.0), (onesc, 1.0), (epsc, float(EPS))]:
        ctx.do("vector", "memset", t_[:], v_, incs=[(vc, 1)])
    ctx.do("vector", "memset", prev[:], 0.0, incs=[(vc, 1)])
    msv = vc.n
    ctx.do("tensor", "transpose", pk[2][0:1, 0:16], Acol[:], ident[0:16, 0:16],
           waits=[(gs, gl0), (ld, ld0)], incs=[(tsn, 1)])
    ctx.do("scalar", "activation", alr[:], pk[2][0:1, 0:16], AF.Identity,
           waits=[(tsn, tsn.n)], incs=[(ev, 1)])
    ctx.do("tensor", "matmul", pk[2][:, 0:16], onesr[:], alr[:], start=True,
           stop=True, waits=[(ev, ev.n), (vc, msv)], incs=[(tsn, 1)])
    ctx.do("scalar", "activation", Dpb[:], pk[2][:, 0:16], AF.Identity,
           waits=[(tsn, tsn.n)], incs=[(ev, 1)])
    dpv = ev.n
    ctx.do("tensor", "transpose", pk[2][:, 0:16], nwp[:], ident[0:16, 0:16],
           waits=[(ev, dpv)], incs=[(tsn, 1)])
    ctx.do("scalar", "activation", nw[:], pk[2][:, 0:16], AF.Identity,
           waits=[(tsn, tsn.n)], incs=[(ev, 1)])
    nwv = ev.n
    ctx.do("scalar", "activation", Acol[:], vtmp[:], AF.Exp,
           waits=[(gs, gl0), (ev, dpv)], incs=[(ev, 1)])
    ctx.do("vector", "tensor_scalar", Acol[:], Acol[:], -1.0, None, ALU.mult,
           waits=[(ev, ev.n)], incs=[(vc, 1)])
    ctx.do("vector", "tensor_scalar", af[:], dtv[:], Acol[0:16, 0:1], None, ALU.mult,
           waits=[(vc, vc.n), (ld, ld0)], incs=[(vc, 1)])
    afv = vc.n

    bank_free = {i: (tsn, 0) for i in range(8)}
    G = {k: (tsn, 0) for k in
         ["xsf", "xst", "Bcf", "Ccf", "Bct", "zsc", "scT", "acum", "eac",
          "alb", "alr", "cdb", "acT", "dtT", "dtds", "MT0", "MT1", "Ccs",
          "Bcw", "prevdone", "Yf", "ynb", "arg0", "arg1", "eaa0", "eaa1"]}
    ysqv = [0] * 16
    gavp = [0] * 16
    osbst = [0, 0]
    ldch = {}

    def load_chunk(c):
        t0 = c * CH
        ctx.do("sync", "dma_start",
               out=xsf[:].rearrange("p (ec t) -> p ec t", t=CH),
               in_=dram["xsbc"][0:DI, t0:t0 + CH].rearrange("(ec p) t -> p ec t", p=128),
               waits=[G["xsf"]], incs=[(ld, 16)])
        ctx.do("sync", "dma_start", out=Bcf[:], in_=dram["xsbc"][DI:DI + 64, t0:t0 + CH],
               waits=[G["Bcf"]], incs=[(ld, 16)])
        ctx.do("sync", "dma_start", out=Ccf[:],
               in_=dram["xsbc"][DI + 64:DI + 128, t0:t0 + CH],
               waits=[G["Ccf"]], incs=[(ld, 16)])
        ctx.do("gpsimd", "dma_start",
               out=zsc[:].rearrange("p (ec t) -> p ec t", t=CH),
               in_=dram["zs"][0:DI, t0:t0 + CH].rearrange("(ec p) t -> p ec t", p=128),
               waits=[G["zsc"]], incs=[(gl, 16)])
        ldch[c] = (ld.n, gl.n)

    load_chunk(0)

    xtie = ctx.sem("Bxe")
    _xptf = [0, 0]
    for c in range(NCH):
        t0 = c * CH
        lv, gv = ldch[c]
        acs = af[0:16, t0:t0 + CH]
        for tt in range(2):
            for ec in range(16):
                ctx.do("tensor", "transpose", pkT[0:128, 0:128],
                       xsf[:, ec * CH + tt * 128:ec * CH + (tt + 1) * 128],
                       identb_b[:],
                       waits=[(ld, lv), (xtie, _xptf[0]), G["xst"]],
                       incs=[(tsn, 1)])
                ctx.do("scalar", "activation",
                       xst[tt][:, ec * 128:(ec + 1) * 128], pkT[0:128, 0:128],
                       AF.Identity, waits=[(tsn, tsn.n)], incs=[(xtie, 1)])
                _xptf[0] = xtie.n
            ctx.do("tensor", "transpose", pkT[0:128, 0:64],
                   Bcf[0:64, tt * 128:(tt + 1) * 128], identb_b[0:64, 0:64],
                   waits=[(ld, lv), (xtie, _xptf[0])], incs=[(tsn, 1)])
            ctx.do("scalar", "activation", Bct[tt][:, 0:64],
                   pkT[0:128, 0:64], AF.Identity,
                   waits=[(tsn, tsn.n)], incs=[(xtie, 1)])
            _xptf[0] = xtie.n
        xstv = xtie.n
        ctx.do("vector", "tensor_tensor_scan", acum[0:16, 0:128], onesh[:],
               acs[:, 0:128], 0.0, ALU.mult, ALU.add,
               waits=[(vc, afv), G["acum"]], incs=[(vc, 1)])
        ctx.do("vector", "tensor_copy", vtmp[:], acum[0:16, 127:128],
               waits=[(vc, vc.n)], incs=[(vc, 1)])
        ctx.do("vector", "tensor_tensor_scan", acum[0:16, 128:256], onesh[:],
               acs[:, 128:256], vtmp[0:16, 0:1], ALU.mult, ALU.add,
               waits=[(vc, vc.n)], incs=[(vc, 1)])
        acv = vc.n
        ctx.do("vector", "reduce_sum", alc[0:16, 0:1], acs, AX.X,
               waits=[(vc, afv)], incs=[(vc, 1)])
        alv = vc.n
        ctx.do("scalar", "activation", eac[:], acum[0:16, :], AF.Exp,
               waits=[(vc, acv), G["eac"]], incs=[(ev, 1)])
        eacv = ev.n

        def small_tr(dst, src_ap, k, wt):
            ctx.do("tensor", "transpose", pk[2][:, 0:k], src_ap, ident[0:k, 0:k],
                   waits=wt + [bank_free[2]], incs=[(tsn, 1)])
            ctx.do("scalar", "activation", dst, pk[2][:, 0:dst.shape[1]], AF.Identity,
                   waits=[(tsn, tsn.n)], incs=[(ev, 1)])
            bank_free[2] = (ev, ev.n)
            return ev.n

        for tt in range(2):
            small_tr(acT[tt][:], acum[0:16, tt * 128:(tt + 1) * 128], 16,
                     [(vc, acv), G["acT"]])
            small_tr(dtT[tt][:], dtv[0:16, t0 + tt * 128:t0 + (tt + 1) * 128], 16,
                     [(ld, ld0), G["dtT"]])
            ctx.do("tensor", "transpose", pk[2][:, 0:16],
                   lnd[0:16, t0 + tt * 128:t0 + (tt + 1) * 128], ident[0:16, 0:16],
                   waits=[(ld, ld0), bank_free[2]], incs=[(tsn, 1)])
            ctx.do("vector", "tensor_tensor", colk[tt][:], pk[2][:, 0:16], acT[tt][:],
                   ALU.subtract, waits=[(tsn, tsn.n), (ev, ev.n)], incs=[(vc, 1)])
            bank_free[2] = (vc, vc.n)
        ctx.do("tensor", "transpose", pk[2][0:1, 0:16], alc[:], ident[0:16, 0:16],
               waits=[(vc, alv), bank_free[2], G["alr"]], incs=[(tsn, 1)])
        ctx.do("scalar", "activation", alr[:], pk[2][0:1, 0:16], AF.Identity,
               waits=[(tsn, tsn.n)], incs=[(ev, 1)])
        alrv = ev.n
        bank_free[2] = (ev, alrv)
        ctx.do("tensor", "matmul", pk[2][:, 0:16], onesr[:], alr[:], start=True,
               stop=True, waits=[(ev, alrv), bank_free[2]], incs=[(tsn, 1)])
        ctx.do("scalar", "activation", alb[:], pk[2][:, 0:16], AF.Identity,
               waits=[(tsn, tsn.n), G["alb"]], incs=[(ev, 1)])
        albv = ev.n
        bank_free[2] = (ev, albv)
        ctx.do("scalar", "activation", alr[:], alr[:], AF.Exp,
               waits=[(ev, albv)], incs=[(ev, 1)])
        cdrv = ev.n
        ctx.do("tensor", "matmul", pk[2][0:64, 0:16], onesr[0:1, 0:64], alr[:],
               start=True, stop=True, waits=[(ev, cdrv), bank_free[2]], incs=[(tsn, 1)])
        ctx.do("scalar", "activation", cdb[:], pk[2][0:64, 0:16], AF.Identity,
               waits=[(tsn, tsn.n), G["cdb"]], incs=[(ev, 1)])
        cdbv = ev.n
        bank_free[2] = (ev, cdbv)
        dtdv = []
        for tt in range(2):
            ctx.do("vector", "tensor_tensor", dtds[tt][:], alb[:], acT[tt][:],
                   ALU.subtract, waits=[(ev, albv), (vc, vc.n), G["dtds"]],
                   incs=[(vc, 1)])
            ctx.do("scalar", "activation", dtds[tt][:], dtds[tt][:], AF.Exp,
                   waits=[(vc, vc.n)], incs=[(ev, 1)])
            ctx.do("vector", "tensor_tensor", dtds[tt][:], dtds[tt][:], dtT[tt][:],
                   ALU.mult, waits=[(ev, ev.n)], incs=[(vc, 1)])
            dtdv.append(vc.n)
        scv = []
        for kt in range(2):
            ctx.do("tensor", "matmul", pk[0][:, 0:CH], Bcf[:, kt * 128:(kt + 1) * 128],
                   Ccf[:], start=True, stop=True,
                   waits=[(ld, lv), bank_free[0]], incs=[(tsn, 1)])
            ctx.do("vector", "tensor_tensor", scT[kt][:], pk[0][:, 0:CH],
                   mk[:, kt * CH:(kt + 1) * CH], ALU.mult,
                   waits=[(tsn, tsn.n), (ld, ld0), G["scT"]], incs=[(vc, 1)])
            bank_free[0] = (vc, vc.n)
            scv.append(vc.n)
        ctx.do("scalar", "activation", prevb[:], prev[:], AF.Identity,
               waits=[(vc, msv if c == 0 else G["prevdone"][1])], incs=[(ev, 1)])
        pbv = ev.n
        epv = []
        for h in range(NH):
            yb = 5 + (h % 2)
            ctx.do("tensor", "matmul", pk[3][:, 0:CH], sel[:, h * 128:(h + 1) * 128],
                   acum[0:16, :], start=True, stop=True,
                   waits=[(vc, acv), bank_free[3]], incs=[(tsn, 1)])
            bcv = tsn.n
            ctx.do("tensor", "matmul", pk[4][0:64, 0:CH],
                   sel[:, h * 128:h * 128 + 64], eac[:], start=True, stop=True,
                   waits=[(ev, eacv), bank_free[4]], incs=[(tsn, 1)])
            ebv = tsn.n
            ctx.do("vector", "tensor_tensor", Ccs[:], pk[4][0:64, 0:CH], Ccf[:],
                   ALU.mult, waits=[(tsn, ebv), (ld, lv), G["Ccs"]], incs=[(vc, 1)])
            bank_free[4] = (vc, vc.n)
            ccsv = vc.n
            mtv = []
            for kt in range(2):
                ctx.do("vector", "tensor_scalar", arg[kt][:], pk[3][:, 0:CH],
                       colk[kt][:, h:h + 1], 30.0, ALU.add, op1=ALU.min,
                       waits=[(tsn, bcv), (vc, vc.n), G[f"arg{kt}"]], incs=[(vc, 1)])
                ctx.do("scalar", "activation", eaa[kt][:], arg[kt][:], AF.Exp,
                       waits=[(vc, vc.n), G[f"eaa{kt}"]], incs=[(ev, 1)])
                ctx.do("gpsimd", "tensor_tensor", MT[kt][:], eaa[kt][:], scT[kt][:],
                       ALU.mult, waits=[(ev, ev.n), (vc, scv[kt]), G[f"MT{kt}"]],
                       incs=[(gp, 1)])
                G[f"arg{kt}"] = (ev, ev.n)
                G[f"eaa{kt}"] = (gp, gp.n)
                mtv.append(gp.n)
            bank_free[3] = (vc, vc.n)
            for kt in range(2):
                ctx.do("tensor", "matmul", pk[yb][:, 0:CH],
                       xst[kt][:, h * 128:(h + 1) * 128], MT[kt][:],
                       start=(kt == 0), stop=False,
                       waits=[(gp, mtv[kt]), (xtie, xstv), bank_free[yb]],
                       incs=[(tsn, 1)])
                G[f"MT{kt}"] = (tsn, tsn.n)
            ctx.do("tensor", "matmul", pk[yb][:, 0:CH],
                   prevb[0:64, h * 128:(h + 1) * 128], Ccs[0:64, :],
                   start=False, stop=True, waits=[(ev, pbv), (vc, ccsv)],
                   incs=[(tsn, 1)])
            yv = tsn.n
            G["Ccs"] = (tsn, yv)
            ctx.do("vector", "scalar_tensor_tensor", Yf[:, h * CH:(h + 1) * CH],
                   xsf[:, h * CH:(h + 1) * CH], Dpb[:, h:h + 1], pk[yb][:, 0:CH],
                   ALU.mult, ALU.add,
                   waits=[(tsn, yv), (ev, dpv), (ld, lv),
                          (ev, ysqv[h]), (gp, gavp[h])],
                   incs=[(vc, 1)])
            bank_free[yb] = (vc, vc.n)
            epv.append(vc.n)
            bcwv = []
            for tt in range(2):
                ctx.do("gpsimd", "tensor_scalar", Bcw[tt][:], Bct[tt][:],
                       dtds[tt][:, h:h + 1], None, ALU.mult,
                       waits=[(xtie, xstv), (vc, dtdv[tt]), G["Bcw"]], incs=[(gp, 1)])
                bcwv.append(gp.n)
            for tt in range(2):
                ctx.do("tensor", "matmul", pk[7][0:64, 0:128], Bcw[tt][:],
                       xst[tt][:, h * 128:(h + 1) * 128],
                       start=(tt == 0), stop=(tt == 1),
                       waits=[(gp, bcwv[1])] + ([bank_free[7]] if tt == 0 else []),
                       incs=[(tsn, 1)])
            spv = tsn.n
            G["Bcw"] = (tsn, spv)
            ctx.do("vector", "scalar_tensor_tensor", prev[:, h * 128:(h + 1) * 128],
                   prev[:, h * 128:(h + 1) * 128], cdb[0:64, h:h + 1],
                   pk[7][0:64, 0:128], ALU.mult, ALU.add,
                   waits=[(tsn, spv), (ev, max(pbv, cdbv)), (vc, vc.n)], incs=[(vc, 1)])
            bank_free[7] = (vc, vc.n)
        G["prevdone"] = (vc, vc.n)
        G["acum"] = (tsn, bcv)
        G["eac"] = (tsn, ebv)
        G["scT"] = (gp, max(mtv))
        G["acT"] = (vc, max(dtdv))
        G["dtT"] = (vc, max(dtdv))
        G["alb"] = (vc, max(dtdv))
        G["alr"] = (tsn, tsn.n)
        G["cdb"] = (vc, vc.n)
        G["dtds"] = (gp, gp.n)
        G["Bct"] = (gp, gp.n)
        gav = []
        for ec in range(16):
            ctx.do("gpsimd", "tensor_tensor", Yf[:, ec * CH:(ec + 1) * CH],
                   Yf[:, ec * CH:(ec + 1) * CH], zsc[:, ec * CH:(ec + 1) * CH],
                   ALU.mult, waits=[(vc, epv[ec]), (gl, gv)], incs=[(gp, 1)])
            gav.append(gp.n)
        G["zsc"] = (gp, gp.n)
        gavp = list(gav)
        rmm = []
        for ec in range(16):
            qi = ec % 2
            ctx.do("scalar", "activation", ysq[qi][:], Yf[:, ec * CH:(ec + 1) * CH],
                   AF.Square, waits=[(gp, gav[ec]),
                                     (tsn, rmm[ec - 2]) if ec >= 2 else (tsn, 0)],
                   incs=[(ev, 1)])
            ysqv[ec] = ev.n
            ctx.do("tensor", "matmul", pk[0][0:1, 0:CH], onesc[:], ysq[qi][:],
                   start=(ec == 0), stop=(ec == 15),
                   waits=[(ev, ev.n)] + ([bank_free[0]] if ec == 0 else []),
                   incs=[(tsn, 1)])
            rmm.append(tsn.n)
        ctx.do("scalar", "activation", r1[:], pk[0][0:1, 0:CH], AF.Ln,
               scale=1.0 / DI, bias=epsc[0:1, 0:1], waits=[(tsn, rmm[15])],
               incs=[(ev, 1)])
        bank_free[0] = (ev, ev.n)
        ctx.do("scalar", "activation", r1[:], r1[:], AF.Exp, scale=-0.5,
               waits=[(ev, ev.n)], incs=[(ev, 1)])
        rsv = ev.n
        ctx.do("tensor", "matmul", pk[2][:, 0:CH], onesr[:], r1[:], start=True,
               stop=True, waits=[(ev, rsv), bank_free[2]], incs=[(tsn, 1)])
        rbv = tsn.n
        ynv = []
        for ec in range(16):
            ctx.do("vector", "scalar_tensor_tensor", ynb[:, ec * CH:(ec + 1) * CH],
                   Yf[:, ec * CH:(ec + 1) * CH], nw[:, ec:ec + 1], pk[2][:, 0:CH],
                   ALU.mult, ALU.mult,
                   waits=[(tsn, rbv), (ev, nwv), (gp, gav[ec]), G["ynb"]],
                   incs=[(vc, 1)])
            ynv.append(vc.n)
        bank_free[2] = (vc, vc.n)
        G["Yf"] = (vc, vc.n)
        opv = 0
        for idx, (tt, db) in enumerate([(a, b) for a in range(2) for b in range(2)]):
            ob_ = pk[3] if idx % 2 == 0 else pk[4]
            obi = 3 if idx % 2 == 0 else 4
            for ec in range(16):
                ctx.do("tensor", "matmul", ob_[:, 0:512],
                       ynb[:, ec * CH + tt * 128:ec * CH + (tt + 1) * 128],
                       WoT[:, ec * 1024 + db * 512:ec * 1024 + (db + 1) * 512],
                       start=(ec == 0), stop=(ec == 15),
                       waits=[(vc, ynv[15]), (bie, wov), bank_free[obi]]
                       if ec == 0 else (), incs=[(tsn, 1)] if ec == 15 else ())
            opv = tsn.n
            oi = idx % 2
            ctx.do("scalar", "activation", osb[oi][:], ob_[:, 0:512], AF.Identity,
                   waits=[(tsn, opv), (so_[oi], so_[oi].n)], incs=[(ev, 1)])
            bank_free[obi] = (ev, ev.n)
            ctx.do("scalar", "dma_start",
                   out=dram["macc"][t0 + tt * 128:t0 + (tt + 1) * 128,
                                    db * 512:(db + 1) * 512],
                   in_=osb[oi][:], waits=[(ev, ev.n)], incs=[(so_[oi], 16)])
        G["ynb"] = (tsn, opv)
        G["xsf"] = (vc, epv[15])
        G["xst"] = (tsn, spv)
        G["Bcf"] = (tsn, tsn.n)
        G["Ccf"] = (vc, vc.n)
        if c + 1 < NCH:
            load_chunk(c + 1)
    ctx.wait("scalar", so_[0], so_[0].n)
    ctx.wait("scalar", so_[1], so_[1].n)


def emit_p2(ctx, dram, es):
    nc = ctx.nc
    NT = 8
    TB = 512

    xnf = es.enter_context(nc.sbuf_tensor("Cxnf", [128, 8 * T], BF16))
    WrT = es.enter_context(nc.sbuf_tensor("CWrT", [128, 8 * 1024], BF16))
    WkT = es.enter_context(nc.sbuf_tensor("CWkT", [128, 8 * 1024], BF16))
    WvT = es.enter_context(nc.sbuf_tensor("CWvT", [128, 8 * 1024], BF16))
    mur = es.enter_context(nc.sbuf_tensor("Cmur", [128, 8], F32))
    muk = es.enter_context(nc.sbuf_tensor("Cmuk", [128, 8], F32))
    omr = es.enter_context(nc.sbuf_tensor("Comr", [128, 8], F32))
    omk = es.enter_context(nc.sbuf_tensor("Comk", [128, 8], F32))
    mup = es.enter_context(nc.sbuf_tensor("Cmup", [8, 128], F32))
    ident = es.enter_context(nc.sbuf_tensor("Cident", [128, 128], F32))
    t1 = [es.enter_context(nc.sbuf_tensor(f"Ct1{i}", [128, TB], F32)) for i in range(2)]
    xrb = es.enter_context(nc.sbuf_tensor("Cxrb", [128, 8 * TB], BF16))
    xkb = es.enter_context(nc.sbuf_tensor("Cxkb", [128, 8 * TB], BF16))
    kab = es.enter_context(nc.sbuf_tensor("Ckab", [128, 8 * TB], BF16))
    ktmp = [es.enter_context(nc.sbuf_tensor(f"Ckt{i}", [128, TB], F32)) for i in range(2)]
    rbuf = es.enter_context(nc.sbuf_tensor("Crbuf", [128, 8 * 512], F32))
    mtile = es.enter_context(nc.sbuf_tensor("Cmt", [128, 4 * 1024], F32))
    utile = es.enter_context(nc.sbuf_tensor("Cut", [128, 4 * 1024], F32))
    t2 = [es.enter_context(nc.sbuf_tensor(f"Ct2{i}", [128, 512], F32)) for i in range(2)]
    x1b = [es.enter_context(nc.sbuf_tensor(f"Cx1{i}", [128, 512], F32)) for i in range(2)]
    pk = [es.enter_context(nc.psum_tensor(f"pC{i}", [128, 512], F32)) for i in range(7)]

    ld = ctx.sem("Cld"); gl = ctx.sem("Cgl"); tsn = ctx.sem("Cts")
    ev = ctx.sem("Cev"); vc = ctx.sem("Cvc"); gp = ctx.sem("Cgp")
    mu_ld = ctx.sem("Cmu")
    xst_ = [ctx.sem("Cxo0"), ctx.sem("Cxo1")]

    identd = nc.inline_tensor(np.eye(128, dtype=np.float32), "identC_d").ap()

    ctx.do("sync", "dma_start",
           out=xnf[:].rearrange("p (dc t) -> p dc t", t=T),
           in_=dram["xn0f"].rearrange("(dc p) t -> p dc t", p=128), incs=[(ld, 16)])
    ctx.do("sync", "dma_start", out=ident[:], in_=identd[:], incs=[(ld, 16)])
    identbd = nc.inline_tensor(np.eye(128, dtype=BF), "identCb_d").ap()
    identb_c = es.enter_context(nc.sbuf_tensor("Cidb", [128, 128], BF16))
    ctx.do("sync", "dma_start", out=identb_c[:], in_=identbd[:], incs=[(ld, 16)])
    _g2 = ctx.sem("ptg2")
    cws = [ctx.sem("Wst0"), ctx.sem("Wst1")]
    cie = ctx.sem("Tie")
    wstage = [es.enter_context(nc.sbuf_tensor(f"Cwst{i}", [128, 1024], BF16))
              for i in range(2)]
    pTb = es.enter_context(nc.psum_tensor("CpT", [128, 128], BF16))
    ptfree = [0, 0]
    wsfree = [0, 0]
    pi = 0
    for Wt, nm in [(WrT, "wr_bf"), (WkT, "wk_bf"), (WvT, "wv_bf")]:
        for m in range(8):
            s = cws[pi % 2]
            ctx.do("sync", "dma_start", out=wstage[pi % 2][:],
                   in_=dram[nm][m * 128:(m + 1) * 128, :],
                   waits=[(_g2, _g2.n), (tsn, wsfree[pi % 2])], incs=[(s, 16)])
            for dc in range(8):
                ctx.do("tensor", "transpose", pTb[:],
                       wstage[pi % 2][:, dc * 128:(dc + 1) * 128], identb_c[:],
                       waits=[(s, s.n), (cie, ptfree[0]), (ld, 48)],
                       incs=[(tsn, 1)])
                ctx.do("scalar", "activation",
                       Wt[:, dc * 1024 + m * 128: dc * 1024 + m * 128 + 128],
                       pTb[:], AF.Identity, waits=[(tsn, tsn.n)],
                       incs=[(cie, 1)])
                ptfree[0] = cie.n
            wsfree[pi % 2] = tsn.n
            pi += 1
    wiv = cie.n
    ld0 = ld.n
    muv = {}
    for i, nm in enumerate(["L0_ffn_mu_r", "L0_ffn_mu_k"]):
        ctx.do("gpsimd", "dma_start", out=mup[:],
               in_=dram[nm].rearrange("(c p) -> c p", p=128),
               waits=[(tsn, muv.get("tr", 0))], incs=[(gl, 16)])
        ctx.do("tensor", "transpose", pk[0][:, 0:8], mup[:], ident[0:8, 0:8],
               waits=[(gl, gl.n), (ld, ld0)] + ([(ev, muv["ev"])] if i else []),
               incs=[(tsn, 1)])
        muv["tr"] = tsn.n
        ctx.do("scalar", "activation", (mur if i == 0 else muk)[:], pk[0][:, 0:8],
               AF.Identity, waits=[(tsn, tsn.n)], incs=[(ev, 1)])
        muv["ev"] = ev.n
    ctx.do("vector", "tensor_scalar", omr[:], mur[:], -1.0, 1.0, ALU.mult,
           op1=ALU.add, waits=[(ev, muv["ev"])], incs=[(vc, 1)])
    ctx.do("vector", "tensor_scalar", omk[:], muk[:], -1.0, 1.0, ALU.mult,
           op1=ALU.add, incs=[(vc, 1)])
    omv = vc.n
    bank_free = {i: (tsn, 0) for i in range(7)}
    bank_free[0] = (ev, muv["ev"])
    xr_free = (tsn, 0)
    ka_free = (tsn, 0)
    t1f = [(vc, 0), (vc, 0)]
    t2f = [(vc, 0), (vc, 0)]
    x1f = [(xst_[0], 0), (xst_[1], 0)]
    ktf = [(ev, 0), (ev, 0)]
    rb_free = (vc, 0)
    mload = (vc, 0)

    def build_xrxk(tb, guard):
        for i, (xb, mu_, om_) in enumerate([(xrb, mur, omr), (xkb, muk, omk)]):
            for dc in range(8):
                bi = dc % 2
                c0 = dc * T + tb * TB
                if tb == 0:
                    ctx.do("gpsimd", "memset", t1[bi][:, 0:1], 0.0,
                           waits=[t1f[bi]], incs=[(gp, 1)])
                    ctx.do("gpsimd", "tensor_scalar", t1[bi][:, 1:TB],
                           xnf[:, c0:c0 + TB - 1], om_[:, dc:dc + 1], None, ALU.mult,
                           waits=[(ld, ld0), (vc, omv)], incs=[(gp, 1)])
                else:
                    ctx.do("gpsimd", "tensor_scalar", t1[bi][:],
                           xnf[:, c0 - 1:c0 + TB - 1], om_[:, dc:dc + 1], None,
                           ALU.mult, waits=[(ld, ld0), (vc, omv), t1f[bi]],
                           incs=[(gp, 1)])
                ctx.do("vector", "scalar_tensor_tensor", xb[:, dc * TB:(dc + 1) * TB],
                       xnf[:, c0:c0 + TB], mu_[:, dc:dc + 1], t1[bi][:],
                       ALU.mult, ALU.add,
                       waits=[(gp, gp.n), guard], incs=[(vc, 1)])
                t1f[bi] = (vc, vc.n)
        return vc.n

    xbv_next = build_xrxk(0, (tsn, 0))
    for tb in range(NT):
        ctx.do("sync", "dma_start",
               out=mtile[:].rearrange("p (tt d) -> p tt d", d=1024),
               in_=dram["macc"][tb * TB:(tb + 1) * TB, :]
               .rearrange("(tt p) d -> p tt d", p=128),
               waits=[mload], incs=[(mu_ld, 16)])
        ctx.do("sync", "dma_start",
               out=utile[:].rearrange("p (tt d) -> p tt d", d=1024),
               in_=dram["u"][tb * TB:(tb + 1) * TB, :]
               .rearrange("(tt p) d -> p tt d", p=128), incs=[(mu_ld, 16)])
        mlv = mu_ld.n
        xbv = xbv_next
        rev = []
        for tt in range(4):
            for db in range(2):
                bi = (tt * 2 + db) % 3
                for dc in range(8):
                    ctx.do("tensor", "matmul", pk[bi][:, 0:512],
                           xrb[:, dc * TB + tt * 128:dc * TB + (tt + 1) * 128],
                           WrT[:, dc * 1024 + db * 512:dc * 1024 + (db + 1) * 512],
                           start=(dc == 0), stop=(dc == 7),
                           waits=[(vc, xbv), (cie, wiv), bank_free[bi]]
                           if dc == 0 else (), incs=[(tsn, 1)] if dc == 7 else ())
                ctx.do("scalar", "activation", rbuf[:, (tt * 2 + db) * 512:
                                                    (tt * 2 + db + 1) * 512],
                       pk[bi][:, 0:512], AF.Sigmoid,
                       waits=[(tsn, tsn.n), rb_free], incs=[(ev, 1)])
                bank_free[bi] = (ev, ev.n)
                rev.append(ev.n)
        kav = []
        for m in range(8):
            bi = 3 + m % 2
            for dc in range(8):
                ctx.do("tensor", "matmul", pk[bi][:, 0:512],
                       WkT[:, dc * 1024 + m * 128:dc * 1024 + m * 128 + 128],
                       xkb[:, dc * TB:(dc + 1) * TB],
                       start=(dc == 0), stop=(dc == 7),
                       waits=[(vc, xbv), (cie, wiv), bank_free[bi], ka_free]
                       if dc == 0 else (), incs=[(tsn, 1)] if dc == 7 else ())
            mmv = tsn.n
            qi = m % 2
            ctx.do("vector", "tensor_scalar", ktmp[qi][:], pk[bi][:, 0:512], 0.0,
                   None, ALU.max, waits=[(tsn, mmv), ktf[qi]], incs=[(vc, 1)])
            bank_free[bi] = (vc, vc.n)
            ctx.do("scalar", "activation", kab[:, m * TB:(m + 1) * TB], ktmp[qi][:],
                   AF.Square, waits=[(vc, vc.n)], incs=[(ev, 1)])
            ktf[qi] = (ev, ev.n)
            kav.append(ev.n)
        if tb + 1 < NT:
            xbv_next = build_xrxk(tb + 1, (tsn, tsn.n))
        for tt in range(4):
            for db in range(2):
                bi = 5 + db
                for m in range(8):
                    ctx.do("tensor", "matmul", pk[bi][:, 0:512],
                           kab[:, m * TB + tt * 128:m * TB + (tt + 1) * 128],
                           WvT[:, m * 1024 + db * 512:m * 1024 + (db + 1) * 512],
                           start=(m == 0), stop=(m == 7),
                           waits=[(ev, kav[7]), (cie, wiv), bank_free[bi]]
                           if m == 0 else (), incs=[(tsn, 1)] if m == 7 else ())
                mmv = tsn.n
                oi = db
                ctx.do("vector", "tensor_tensor", t2[oi][:], pk[bi][:, 0:512],
                       rbuf[:, (tt * 2 + db) * 512:(tt * 2 + db + 1) * 512],
                       ALU.mult, waits=[(tsn, mmv), (ev, rev[tt * 2 + db]),
                                        t2f[oi]], incs=[(vc, 1)])
                bank_free[bi] = (vc, vc.n)
                ctx.do("gpsimd", "tensor_tensor", t2[oi][:], t2[oi][:],
                       mtile[:, tt * 1024 + db * 512:tt * 1024 + (db + 1) * 512],
                       ALU.add, waits=[(vc, vc.n), (mu_ld, mlv)], incs=[(gp, 1)])
                ctx.do("vector", "tensor_tensor", x1b[oi][:], t2[oi][:],
                       utile[:, tt * 1024 + db * 512:tt * 1024 + (db + 1) * 512],
                       ALU.add, waits=[(gp, gp.n), x1f[oi]], incs=[(vc, 1)])
                t2f[oi] = (vc, vc.n)
                ctx.do("scalar", "dma_start",
                       out=dram["x1"][tb * TB + tt * 128:tb * TB + (tt + 1) * 128,
                                      db * 512:(db + 1) * 512],
                       in_=x1b[oi][:], waits=[(vc, vc.n)], incs=[(xst_[oi], 16)])
                x1f[oi] = (xst_[oi], xst_[oi].n)
        ka_free = (tsn, tsn.n)
        xr_free = (tsn, tsn.n)
        rb_free = (vc, vc.n)
        mload = (vc, vc.n)
    ctx.wait("scalar", xst_[0], xst_[0].n)
    ctx.wait("scalar", xst_[1], xst_[1].n)


def emit_p4a(ctx, dram, es):
    nc = ctx.nc
    xnf = es.enter_context(nc.sbuf_tensor("Dxnf", [128, 8 * T], BF16))
    qwS = es.enter_context(nc.sbuf_tensor("DqwS", [128, 8 * 1024], BF16))
    kwS = es.enter_context(nc.sbuf_tensor("DkwS", [128, 8 * 1024], BF16))
    vwS = es.enter_context(nc.sbuf_tensor("DvwS", [128, 8 * 1024], BF16))
    ob = [es.enter_context(nc.sbuf_tensor(f"Dob{i}", [128, 512], BF16)) for i in range(4)]
    pk = [es.enter_context(nc.psum_tensor(f"pD{i}", [128, 512], F32)) for i in range(4)]

    ld = ctx.sem("Dld"); tsn = ctx.sem("Dts"); ev = ctx.sem("Dev")
    do_ = [ctx.sem(f"Do{i}") for i in range(4)]

    ctx.do("sync", "dma_start",
           out=xnf[:].rearrange("p (dc t) -> p dc t", t=T),
           in_=dram["xn1f"].rearrange("(dc p) t -> p dc t", p=128), incs=[(ld, 16)])
    _g3 = ctx.sem("ptg3")
    for Wt, nm in [(qwS, "qw_bf"), (kwS, "kw_bf"), (vwS, "vw_bf")]:
        ctx.do("sync", "dma_start",
               out=Wt[:].rearrange("p (dc j) -> p dc j", j=1024),
               in_=dram[nm].rearrange("(dc p) j -> p dc j", p=128),
               waits=[(_g3, _g3.n)], incs=[(ld, 16)])
    ld0 = ld.n
    bank_free = {i: (ev, 0) for i in range(4)}
    obst = [0] * 4
    nb = 0
    for kind in ["k", "q", "v"]:
        if kind in ("k", "q"):
            wS = kwS if kind == "k" else qwS
            for tb in range(8):
                for m in range(8):
                    for dc in range(8):
                        ctx.do("tensor", "matmul", pk[nb][:, 0:512],
                               wS[:, dc * 1024 + m * 128:dc * 1024 + (m + 1) * 128],
                               xnf[:, dc * T + tb * 512:dc * T + (tb + 1) * 512],
                               start=(dc == 0), stop=(dc == 7),
                               waits=[(ld, ld0), bank_free[nb]] if dc == 0 else (),
                               incs=[(tsn, 1)] if dc == 7 else ())
                    ctx.do("scalar", "activation", ob[nb][:], pk[nb][:, 0:512],
                           AF.Identity, scale=(1.0 / 32.0 if kind == "q" else 1.0),
                           waits=[(tsn, tsn.n), (do_[nb], do_[nb].n)], incs=[(ev, 1)])
                    bank_free[nb] = (ev, ev.n)
                    ctx.do("scalar", "dma_start",
                           out=dram["kf" if kind == "k" else "qf"]
                           [m * 128:(m + 1) * 128, tb * 512:(tb + 1) * 512],
                           in_=ob[nb][:], waits=[(ev, ev.n)], incs=[(do_[nb], 16)])
                    nb = (nb + 1) % 4
        else:
            for tt in range(32):
                for db in range(2):
                    for dc in range(8):
                        ctx.do("tensor", "matmul", pk[nb][:, 0:512],
                               xnf[:, dc * T + tt * 128:dc * T + (tt + 1) * 128],
                               vwS[:, dc * 1024 + db * 512:dc * 1024 + (db + 1) * 512],
                               start=(dc == 0), stop=(dc == 7),
                               waits=[(ld, ld0), bank_free[nb]] if dc == 0 else (),
                               incs=[(tsn, 1)] if dc == 7 else ())
                    ctx.do("scalar", "activation", ob[nb][:], pk[nb][:, 0:512],
                           AF.Identity, waits=[(tsn, tsn.n), (do_[nb], do_[nb].n)],
                           incs=[(ev, 1)])
                    bank_free[nb] = (ev, ev.n)
                    ctx.do("scalar", "dma_start",
                           out=dram["vt"][tt * 128:(tt + 1) * 128,
                                          db * 512:(db + 1) * 512],
                           in_=ob[nb][:], waits=[(ev, ev.n)], incs=[(do_[nb], 16)])
                    nb = (nb + 1) % 4
    for s in do_:
        ctx.wait("scalar", s, s.n)


def emit_p4b(ctx, dram, es):
    nc = ctx.nc
    Kf = es.enter_context(nc.sbuf_tensor("EKf", [128, 8 * T], BF16))
    Vt = es.enter_context(nc.sbuf_tensor("EVt", [128, 32 * 1024], BF16))
    qt_ = [es.enter_context(nc.sbuf_tensor(f"Eqt{i}", [128, 8 * 128], BF16)) for i in range(2)]
    mkt = es.enter_context(nc.sbuf_tensor("Emkt", [128, 4 * 512], F32))
    identb = es.enter_context(nc.sbuf_tensor("Eib", [128, 128], BF16))
    sms = es.enter_context(nc.sbuf_tensor("Esms", [128, 512], F32))
    pb = es.enter_context(nc.sbuf_tensor("Epb", [128, 512], BF16))
    pTs = es.enter_context(nc.sbuf_tensor("EpTs", [128, 512], BF16))
    acc = es.enter_context(nc.sbuf_tensor("Eacc", [128, 1024], F32))
    mcol = es.enter_context(nc.sbuf_tensor("Emc", [128, 1], F32))
    mbc = es.enter_context(nc.sbuf_tensor("Emb", [128, 1], F32))
    mnc = es.enter_context(nc.sbuf_tensor("Emn", [128, 1], F32))
    ngm = es.enter_context(nc.sbuf_tensor("Engm", [128, 1], F32))
    alp = es.enter_context(nc.sbuf_tensor("Ealp", [128, 1], F32))
    lcol = es.enter_context(nc.sbuf_tensor("Elc", [128, 1], F32))
    lb = es.enter_context(nc.sbuf_tensor("Elb", [128, 1], F32))
    rec = es.enter_context(nc.sbuf_tensor("Erec", [128, 1], F32))
    oatt = [es.enter_context(nc.sbuf_tensor(f"Eo{i}", [128, 1024], F32)) for i in range(2)]
    pS = [es.enter_context(nc.psum_tensor(f"pE{i}", [128, 512], F32)) for i in range(2)]
    pV = [es.enter_context(nc.psum_tensor(f"pEv{i}", [128, 512], F32)) for i in range(2)]
    pT = [es.enter_context(nc.psum_tensor(f"pEt{i}", [128, 128], BF16)) for i in range(4)]

    ld = ctx.sem("Eld"); tsn = ctx.sem("Ets"); ev = ctx.sem("Eev")
    vc = ctx.sem("Evc")
    eq = [ctx.sem("Eq0"), ctx.sem("Eq1")]
    eo = [ctx.sem("Eo0"), ctx.sem("Eo1")]

    identd = nc.inline_tensor(np.eye(128, dtype=BF), "identE_d").ap()
    mkn = np.zeros((128, 4 * 512), dtype=np.float32)
    for v in range(4):
        for r_ in range(128):
            for cb in range(4):
                pass
    for v in range(4):
        cols = np.arange(512)
        rows = np.arange(128)[:, None]
        mkn[:, v * 512:(v + 1) * 512] = np.where(v * 128 + rows >= cols, 0.0, NEG)
    mkd = nc.inline_tensor(mkn, "mkE_d").ap()

    ctx.do("sync", "dma_start", out=Kf[:].rearrange("p (m k) -> p m k", k=T),
           in_=dram["kf"].rearrange("(m p) k -> p m k", p=128), incs=[(ld, 16)])
    ctx.do("sync", "dma_start", out=Vt[:].rearrange("p (tt j) -> p tt j", j=1024),
           in_=dram["vt"].rearrange("(tt p) j -> p tt j", p=128), incs=[(ld, 16)])
    ctx.do("sync", "dma_start", out=mkt[:], in_=mkd[:], incs=[(ld, 16)])
    ctx.do("sync", "dma_start", out=identb[:], in_=identd[:], incs=[(ld, 16)])
    ld0 = ld.n
    ctx.do("vector", "memset", acc[:], 0.0, incs=[(vc, 1)])
    ctx.do("vector", "memset", lcol[:], 1.0, incs=[(vc, 1)])
    msv = vc.n

    qld = {}

    def load_q(qt, w=()):
        s = eq[qt % 2]
        ctx.do("sync", "dma_start",
               out=qt_[qt % 2][:].rearrange("p (jc q) -> p jc q", q=128),
               in_=dram["qf"][0:D, qt * 128:(qt + 1) * 128]
               .rearrange("(jc p) q -> p jc q", p=128),
               waits=w, incs=[(s, 16)])
        qld[qt] = (s, s.n)

    load_q(0)
    bank_free = {k: (vc, 0) for k in ["s0", "s1", "v0", "v1"]}
    pt_free = [(ev, 0)] * 4
    sms_free = (ev, 0)
    pb_free = (tsn, 0)
    pts_free = (tsn, 0)
    ost = [0, 0]
    lastq = 0

    for qt in range(32):
        if qt + 1 < 32:
            load_q(qt + 1, w=[(tsn, lastq)])
        nkb = qt // 4 + 1
        ctx.do("vector", "memset", mcol[:], -3.0e38, waits=[(vc, vc.n)], incs=[(vc, 1)])
        for kb in range(nkb):
            sb = kb % 2
            pSb = pS[sb]
            diag = (kb == qt // 4)
            for jc in range(8):
                ctx.do("tensor", "matmul", pSb[:, 0:512],
                       qt_[qt % 2][:, jc * 128:(jc + 1) * 128],
                       Kf[:, jc * T + kb * 512:jc * T + (kb + 1) * 512],
                       start=(jc == 0), stop=(jc == 7),
                       waits=[qld[qt], (ld, ld0), bank_free[f"s{sb}"]] if jc == 0 else (),
                       incs=[(tsn, 1)] if jc == 7 else ())
            smv = tsn.n
            if diag:
                ctx.do("vector", "tensor_tensor", sms[:], pSb[:, 0:512],
                       mkt[:, (qt % 4) * 512:(qt % 4 + 1) * 512], ALU.add,
                       waits=[(tsn, smv), (ld, ld0), sms_free], incs=[(vc, 1)])
                ctx.do("vector", "reduce_max", mbc[:, 0:1], sms[:], AX.X,
                       waits=[(vc, vc.n)], incs=[(vc, 1)])
                esrc = sms[:]
                bank_free[f"s{sb}"] = (vc, vc.n)
            else:
                ctx.do("vector", "reduce_max", mbc[:, 0:1], pSb[:, 0:512], AX.X,
                       waits=[(tsn, smv), (vc, vc.n)], incs=[(vc, 1)])
                esrc = pSb[:, 0:512]
            rdv = vc.n
            ctx.do("vector", "tensor_tensor", mnc[:, 0:1], mcol[:, 0:1], mbc[:, 0:1],
                   ALU.max, waits=[(vc, rdv)], incs=[(vc, 1)])
            ctx.do("vector", "tensor_scalar", ngm[:, 0:1], mnc[:, 0:1], -1.0, None,
                   ALU.mult, waits=[(vc, vc.n)], incs=[(vc, 1)])
            ngv = vc.n
            ctx.do("scalar", "activation", alp[:, 0:1], mcol[:, 0:1], AF.Exp,
                   bias=ngm[:, 0:1], waits=[(vc, ngv)], incs=[(ev, 1)])
            alv = ev.n
            ctx.do("scalar", "activation", pb[:], esrc, AF.Exp, bias=ngm[:, 0:1],
                   accum_out=lb[:, 0:1], waits=[(ev, alv), pb_free], incs=[(ev, 1)])
            pev = ev.n
            if not diag:
                bank_free[f"s{sb}"] = (ev, pev)
            else:
                sms_free = (ev, pev)
            ctx.do("vector", "tensor_copy", mcol[:, 0:1], mnc[:, 0:1],
                   waits=[(ev, alv)], incs=[(vc, 1)])
            ctx.do("vector", "scalar_tensor_tensor", lcol[:, 0:1], lcol[:, 0:1],
                   alp[:, 0:1], lb[:, 0:1], ALU.mult, ALU.add,
                   waits=[(ev, pev), (vc, vc.n)], incs=[(vc, 1)])
            for k4 in range(4):
                ctx.do("tensor", "transpose", pT[k4][:], pb[:, k4 * 128:(k4 + 1) * 128],
                       identb[:], waits=[(ev, pev), (ld, ld0), pt_free[k4]],
                       incs=[(tsn, 1)])
                ctx.do("scalar", "activation", pTs[:, k4 * 128:(k4 + 1) * 128],
                       pT[k4][:], AF.Identity, waits=[(tsn, tsn.n), pts_free],
                       incs=[(ev, 1)])
                pt_free[k4] = (ev, ev.n)
            ptv = ev.n
            pb_free = (tsn, tsn.n)
            pvv = []
            for db in range(2):
                for k4 in range(4):
                    ctx.do("tensor", "matmul", pV[db][:, 0:512],
                           pTs[:, k4 * 128:(k4 + 1) * 128],
                           Vt[:, (kb * 4 + k4) * 1024 + db * 512:
                              (kb * 4 + k4) * 1024 + (db + 1) * 512],
                           start=(k4 == 0), stop=(k4 == 3),
                           waits=[(ev, ptv), (ld, ld0), bank_free[f"v{db}"]]
                           if k4 == 0 else (), incs=[(tsn, 1)] if k4 == 3 else ())
                pvv.append(tsn.n)
            pts_free = (tsn, tsn.n)
            for db in range(2):
                ctx.do("vector", "scalar_tensor_tensor",
                       acc[:, db * 512:(db + 1) * 512], acc[:, db * 512:(db + 1) * 512],
                       alp[:, 0:1], pV[db][:, 0:512], ALU.mult, ALU.add,
                       waits=[(tsn, pvv[db]), (ev, alv), (vc, vc.n),
                              (vc, msv)], incs=[(vc, 1)])
                bank_free[f"v{db}"] = (vc, vc.n)
        lastq = tsn.n
        ctx.do("vector", "reciprocal", rec[:, 0:1], lcol[:, 0:1],
               waits=[(vc, vc.n)], incs=[(vc, 1)])
        oi = qt % 2
        ctx.do("vector", "tensor_scalar", oatt[oi][:], acc[:], rec[:, 0:1], None,
               ALU.mult, waits=[(vc, vc.n), (eo[oi], ost[oi])], incs=[(vc, 1)])
        ctx.do("scalar", "dma_start", out=dram["attn"][qt * 128:(qt + 1) * 128, :],
               in_=oatt[oi][:], waits=[(vc, vc.n)], incs=[(eo[oi], 16)])
        ost[oi] = eo[oi].n
    ctx.wait("scalar", eo[0], eo[0].n)
    ctx.wait("scalar", eo[1], eo[1].n)


def emit_p5a(ctx, dram, es):
    nc = ctx.nc
    xnf = es.enter_context(nc.sbuf_tensor("Fxnf", [128, 8 * T], BF16))
    W1T = es.enter_context(nc.sbuf_tensor("FW1T", [128, 8 * DH], BF16))
    W2T = es.enter_context(nc.sbuf_tensor("FW2T", [128, 8 * DH], BF16))
    s1c = es.enter_context(nc.sbuf_tensor("Fs1c", [128, 32], F32))
    s2c = es.enter_context(nc.sbuf_tensor("Fs2c", [128, 32], F32))
    svp = es.enter_context(nc.sbuf_tensor("Fsvp", [32, 128], F32))
    ident = es.enter_context(nc.sbuf_tensor("Fident", [128, 128], F32))
    t1 = [es.enter_context(nc.sbuf_tensor(f"Ft1{i}", [128, 512], F32)) for i in range(2)]
    ab = [es.enter_context(nc.sbuf_tensor(f"Fab{i}", [128, 512], BF16)) for i in range(2)]
    pk = [es.enter_context(nc.psum_tensor(f"pF{i}", [128, 512], F32)) for i in range(4)]

    ld = ctx.sem("Fld"); gl = ctx.sem("Fgl"); tsn = ctx.sem("Fts")
    ev = ctx.sem("Fev"); vc = ctx.sem("Fvc")
    fo_ = [ctx.sem("Fo0"), ctx.sem("Fo1")]
    identd = nc.inline_tensor(np.eye(128, dtype=np.float32), "identF_d").ap()

    ctx.do("sync", "dma_start",
           out=xnf[:].rearrange("p (dc t) -> p dc t", t=T),
           in_=dram["xn1f"].rearrange("(dc p) t -> p dc t", p=128), incs=[(ld, 16)])
    ctx.do("sync", "dma_start", out=ident[:], in_=identd[:], incs=[(ld, 16)])
    identbd = nc.inline_tensor(np.eye(128, dtype=BF), "identFb_d").ap()
    identb_f = es.enter_context(nc.sbuf_tensor("Fidb", [128, 128], BF16))
    ctx.do("sync", "dma_start", out=identb_f[:], in_=identbd[:], incs=[(ld, 16)])
    _g3 = ctx.sem("ptg3")
    fws = [ctx.sem("Wst0"), ctx.sem("Wst1")]
    fie = ctx.sem("Tie")
    wstage = [es.enter_context(nc.sbuf_tensor(f"Fwst{i}", [128, 1024], BF16))
              for i in range(2)]
    pTb = [es.enter_context(nc.psum_tensor(f"FpT{i}", [128, 128], BF16))
           for i in range(2)]
    ptfree = [0, 0]
    wsfree = [0, 0]
    pi = 0
    for Wt, nm in [(W1T, "w1_bf"), (W2T, "w2_bf")]:
        for m in range(32):
            s = fws[pi % 2]
            ctx.do("sync", "dma_start", out=wstage[pi % 2][:],
                   in_=dram[nm][m * 128:(m + 1) * 128, :],
                   waits=[(_g3, _g3.n), (tsn, wsfree[pi % 2])], incs=[(s, 16)])
            for dc in range(8):
                ctx.do("tensor", "transpose", pTb[dc % 2][:],
                       wstage[pi % 2][:, dc * 128:(dc + 1) * 128], identb_f[:],
                       waits=[(s, s.n), (fie, ptfree[dc % 2]), (ld, 48)],
                       incs=[(tsn, 1)])
                ctx.do("scalar", "activation",
                       Wt[:, dc * DH + m * 128: dc * DH + m * 128 + 128],
                       pTb[dc % 2][:], AF.Identity, waits=[(tsn, tsn.n)],
                       incs=[(fie, 1)])
                ptfree[dc % 2] = fie.n
            wsfree[pi % 2] = tsn.n
            pi += 1
    wiv = fie.n
    ld0 = ld.n
    sv = {}
    for i, nm in enumerate(["L1_ffn_w1_s", "L1_ffn_w2_s"]):
        ctx.do("gpsimd", "dma_start", out=svp[:],
               in_=dram[nm].rearrange("(m p) -> m p", p=128),
               waits=[(tsn, sv.get("tr", 0))], incs=[(gl, 16)])
        ctx.do("tensor", "transpose", pk[0][:, 0:32], svp[:], ident[0:32, 0:32],
               waits=[(gl, gl.n), (ld, ld0)] + ([(ev, sv["ev"])] if i else []),
               incs=[(tsn, 1)])
        sv["tr"] = tsn.n
        ctx.do("scalar", "activation", (s1c if i == 0 else s2c)[:], pk[0][:, 0:32],
               AF.Identity, waits=[(tsn, tsn.n)], incs=[(ev, 1)])
        sv["ev"] = ev.n
    bank_free = {i: ((ev, sv["ev"]) if i == 0 else (tsn, 0)) for i in range(4)}
    t1f = [(vc, 0), (vc, 0)]
    abf = [(fo_[0], 0), (fo_[1], 0)]
    for m in range(32):
        for tb in range(8):
            i = (m * 8 + tb) % 2
            bA, bB = 2 * i, 2 * i + 1
            for dc in range(8):
                ctx.do("tensor", "matmul", pk[bA][:, 0:512],
                       W1T[:, dc * DH + m * 128:dc * DH + (m + 1) * 128],
                       xnf[:, dc * T + tb * 512:dc * T + (tb + 1) * 512],
                       start=(dc == 0), stop=(dc == 7),
                       waits=[(ld, ld0), (fie, wiv), bank_free[bA]] if dc == 0 else ())
            for dc in range(8):
                ctx.do("tensor", "matmul", pk[bB][:, 0:512],
                       W2T[:, dc * DH + m * 128:dc * DH + (m + 1) * 128],
                       xnf[:, dc * T + tb * 512:dc * T + (tb + 1) * 512],
                       start=(dc == 0), stop=(dc == 7),
                       waits=[bank_free[bB]] if dc == 0 else (),
                       incs=[(tsn, 1)] if dc == 7 else ())
            mmv = tsn.n
            ctx.do("scalar", "activation", t1[i][:], pk[bA][:, 0:512], AF.Silu,
                   scale=s1c[:, m:m + 1], waits=[(tsn, mmv), (ev, sv["ev"]),
                                                 t1f[i]], incs=[(ev, 1)])
            bank_free[bA] = (ev, ev.n)
            ctx.do("vector", "scalar_tensor_tensor", ab[i][:], t1[i][:],
                   s2c[:, m:m + 1], pk[bB][:, 0:512], ALU.mult, ALU.mult,
                   waits=[(ev, ev.n), abf[i]], incs=[(vc, 1)])
            t1f[i] = (vc, vc.n)
            bank_free[bB] = (vc, vc.n)
            ctx.do("scalar", "dma_start",
                   out=dram["a5"][m * 128:(m + 1) * 128, tb * 512:(tb + 1) * 512],
                   in_=ab[i][:], waits=[(vc, vc.n)], incs=[(fo_[i], 16)])
            abf[i] = (fo_[i], fo_[i].n)
    ctx.wait("scalar", fo_[0], fo_[0].n)
    ctx.wait("scalar", fo_[1], fo_[1].n)


def _emit_hmm(ctx, dram, es, src_name, w_name, s_name, dst_name, tag):
    nc = ctx.nc
    hT = [es.enter_context(nc.sbuf_tensor(f"{tag}hT{i}", [128, 32 * 512], BF16))
          for i in range(2)]
    ablk = [es.enter_context(nc.sbuf_tensor(f"{tag}ab{i}", [128, 32 * 512], BF16))
            for i in range(2)]
    sc_ = es.enter_context(nc.sbuf_tensor(f"{tag}sc", [128, 32], F32))
    svp = es.enter_context(nc.sbuf_tensor(f"{tag}svp", [32, 128], F32))
    ident = es.enter_context(nc.sbuf_tensor(f"{tag}id", [128, 128], F32))
    ob = [es.enter_context(nc.sbuf_tensor(f"{tag}ob{i}", [128, 512], BF16))
          for i in range(2)]
    pk = [es.enter_context(nc.psum_tensor(f"p{tag}{i}", [128, 512], F32))
          for i in range(4)]

    ld = ctx.sem("Ghld"); tsn = ctx.sem("Ghts")
    ev = ctx.sem("Ghev")
    gs = ctx.sem("Ghgs")
    hw_ = [ctx.sem("Ghhw0"), ctx.sem("Ghhw1")]
    ga_ = [ctx.sem("Ghga0"), ctx.sem("Ghga1")]
    go_ = [ctx.sem("Ghgo0"), ctx.sem("Ghgo1")]
    identd = nc.inline_tensor(np.eye(128, dtype=np.float32), f"id{tag}_d").ap()
    ctx.do("sync", "dma_start", out=ident[:], in_=identd[:], incs=[(ld, 16)])
    ctx.do("gpsimd", "dma_start", out=svp[:],
           in_=dram[s_name].rearrange("(m p) -> m p", p=128), incs=[(gs, 16)])
    ctx.do("tensor", "transpose", pk[0][:, 0:32], svp[:], ident[0:32, 0:32],
           waits=[(gs, gs.n), (ld, ld.n)], incs=[(tsn, 1)])
    ctx.do("scalar", "activation", sc_[:], pk[0][:, 0:32], AF.Identity,
           waits=[(tsn, tsn.n)], incs=[(ev, 1)])
    scv = ev.n

    wldv = {}
    aldv = {}
    mmvb = {}

    _g3 = ctx.sem("ptg3")

    def load_stripe(s8, w=()):
        s = hw_[s8 % 2]
        for ic in range(32):
            ctx.do("sync", "dma_start",
                   out=hT[s8 % 2][:, ic * 512:(ic + 1) * 512],
                   in_=dram[w_name][s8 * 512:(s8 + 1) * 512, ic * 128:(ic + 1) * 128],
                   transpose=True,
                   waits=(list(w) + [(_g3, _g3.n)]) if ic == 0 else (),
                   incs=[(s, 16)])
        wldv[s8] = (s, s.n)

    def load_ablk(s8, tb):
        po = (s8, tb - 2) if tb >= 2 else (s8 - 1, 6 + tb)
        w = [(tsn, mmvb[po])] if po in mmvb else ()
        s = ga_[tb % 2]
        ctx.do("gpsimd", "dma_start",
               out=ablk[tb % 2][:].rearrange("p (ic t) -> p ic t", t=512),
               in_=dram[src_name][0:DH, tb * 512:(tb + 1) * 512]
               .rearrange("(ic p) t -> p ic t", p=128),
               waits=w, incs=[(s, 16)])
        aldv[(s8, tb)] = (s, s.n)

    load_stripe(0)
    load_ablk(0, 0)
    bank_free = {i: ((ev, scv) if i == 0 else (tsn, 0)) for i in range(4)}
    obf = [(go_[0], 0), (go_[1], 0)]
    for s8 in range(8):
        if s8 + 1 < 8:
            load_stripe(s8 + 1, w=[(tsn, mmvb[(s8 - 1, 7)])] if s8 >= 1 else ())
        for tb in range(8):
            nxt = (s8, tb + 1) if tb + 1 < 8 else (s8 + 1, 0)
            if nxt[0] < 8:
                load_ablk(*nxt)
            av = aldv[(s8, tb)]
            for m in range(4):
                bk = (tb * 4 + m) % 4
                for ic in range(32):
                    ctx.do("tensor", "matmul", pk[bk][:, 0:512],
                           hT[s8 % 2][:, ic * 512 + m * 128:ic * 512 + (m + 1) * 128],
                           ablk[tb % 2][:, ic * 512:(ic + 1) * 512],
                           start=(ic == 0), stop=(ic == 31),
                           waits=[wldv[s8], av, bank_free[bk]]
                           if ic == 0 else (), incs=[(tsn, 1)] if ic == 31 else ())
                mmv = tsn.n
                oi = (tb * 4 + m) % 2
                ctx.do("scalar", "activation", ob[oi][:], pk[bk][:, 0:512], AF.Gelu,
                       scale=sc_[:, s8 * 4 + m:s8 * 4 + m + 1],
                       waits=[(tsn, mmv), (ev, scv), obf[oi]], incs=[(ev, 1)])
                bank_free[bk] = (ev, ev.n)
                ctx.do("scalar", "dma_start",
                       out=dram[dst_name][(s8 * 4 + m) * 128:(s8 * 4 + m + 1) * 128,
                                          tb * 512:(tb + 1) * 512],
                       in_=ob[oi][:], waits=[(ev, ev.n)], incs=[(go_[oi], 16)])
                obf[oi] = (go_[oi], go_[oi].n)
            mmvb[(s8, tb)] = tsn.n
    ctx.wait("scalar", go_[0], go_[0].n)
    ctx.wait("scalar", go_[1], go_[1].n)


def emit_p5b(ctx, dram, es):
    _emit_hmm(ctx, dram, es, "a5", "h1_bf", "L1_ffn_h1_s", "b5", "G")


def emit_p5c(ctx, dram, es):
    _emit_hmm(ctx, dram, es, "b5", "h2_bf", "L1_ffn_h2_s", "c5", "H")


def emit_p5d(ctx, dram, es):
    nc = ctx.nc
    W3T = es.enter_context(nc.sbuf_tensor("JW3T", [128, 32 * 1024], BF16))
    cblk = [es.enter_context(nc.sbuf_tensor(f"Jcb{i}", [128, 32 * 512], BF16))
            for i in range(2)]
    s3r = es.enter_context(nc.sbuf_tensor("Js3r", [1, 1024], F32))
    s3b = es.enter_context(nc.sbuf_tensor("Js3b", [128, 1024], F32))
    onesr = es.enter_context(nc.sbuf_tensor("Jones", [1, 128], F32))
    x1t = es.enter_context(nc.sbuf_tensor("Jx1", [128, 4 * 1024], F32))
    att = es.enter_context(nc.sbuf_tensor("Jat", [128, 4 * 1024], F32))
    t2 = [es.enter_context(nc.sbuf_tensor(f"Jt2{i}", [128, 512], F32)) for i in range(2)]
    oub = [es.enter_context(nc.sbuf_tensor(f"Jou{i}", [128, 512], F32)) for i in range(2)]
    pk = [es.enter_context(nc.psum_tensor(f"pJ{i}", [128, 512], F32)) for i in range(4)]

    ld = ctx.sem("Jld"); tsn = ctx.sem("Jts")
    ev = ctx.sem("Jev"); vc = ctx.sem("Jvc"); gp = ctx.sem("Jgp")
    jgs = ctx.sem("Jgs")
    jmu = ctx.sem("Jmu")
    jc_ = [ctx.sem("Jc0"), ctx.sem("Jc1")]
    jo_ = [ctx.sem("Jo0"), ctx.sem("Jo1")]

    _g3 = ctx.sem("ptg3")
    for hc in range(32):
        ctx.do("sync", "dma_start", out=W3T[:, hc * 1024:(hc + 1) * 1024],
               in_=dram["w3_bf"][0:D, hc * 128:(hc + 1) * 128],
               transpose=True, waits=[(_g3, _g3.n)], incs=[(ld, 16)])
    ctx.do("gpsimd", "dma_start", out=s3r[:],
           in_=dram["L1_ffn_w3_s"].rearrange("(a b) -> a b", a=1), incs=[(jgs, 16)])
    ctx.do("vector", "memset", onesr[:], 1.0, incs=[(vc, 1)])
    for db in range(2):
        ctx.do("tensor", "matmul", pk[2 + db][:, 0:512], onesr[:],
               s3r[0:1, db * 512:(db + 1) * 512], start=True, stop=True,
               waits=[(jgs, 16), (vc, 1)], incs=[(tsn, 1)])
        ctx.do("scalar", "activation", s3b[:, db * 512:(db + 1) * 512],
               pk[2 + db][:, 0:512], AF.Identity, waits=[(tsn, tsn.n)],
               incs=[(ev, 1)])
    s3v = ev.n
    ld0 = ld.n

    cldv = {}
    lastc = {}
    mload = (vc, 0)

    def load_c(tb, w=()):
        s = jc_[tb % 2]
        ctx.do("gpsimd", "dma_start",
               out=cblk[tb % 2][:].rearrange("p (hc t) -> p hc t", t=512),
               in_=dram["c5"][0:DH, tb * 512:(tb + 1) * 512]
               .rearrange("(hc p) t -> p hc t", p=128),
               waits=w, incs=[(s, 16)])
        cldv[tb] = (s, s.n)

    load_c(0)
    bank_free = {i: ((ev, s3v) if i >= 2 else (tsn, 0)) for i in range(4)}
    t2f = [(vc, 0), (vc, 0)]
    ouf = [(jo_[0], 0), (jo_[1], 0)]
    for tb in range(8):
        if tb + 1 < 8:
            load_c(tb + 1, w=[(tsn, lastc.get(tb - 1, 0))] if tb >= 1 else ())
        ctx.do("sync", "dma_start",
               out=x1t[:].rearrange("p (tt d) -> p tt d", d=1024),
               in_=dram["x1"][tb * 512:(tb + 1) * 512, :]
               .rearrange("(tt p) d -> p tt d", p=128),
               waits=[mload, (gp, gp.n)], incs=[(jmu, 16)])
        ctx.do("sync", "dma_start",
               out=att[:].rearrange("p (tt d) -> p tt d", d=1024),
               in_=dram["attn"][tb * 512:(tb + 1) * 512, :]
               .rearrange("(tt p) d -> p tt d", p=128), incs=[(jmu, 16)])
        mlv = jmu.n
        for tt in range(4):
            for db in range(2):
                bk = (tt * 2 + db) % 4
                for hc in range(32):
                    ctx.do("tensor", "matmul", pk[bk][:, 0:512],
                           cblk[tb % 2][:, hc * 512 + tt * 128:hc * 512 + (tt + 1) * 128],
                           W3T[:, hc * 1024 + db * 512:hc * 1024 + (db + 1) * 512],
                           start=(hc == 0), stop=(hc == 31),
                           waits=[(ld, ld0), cldv[tb], bank_free[bk]]
                           if hc == 0 else (), incs=[(tsn, 1)] if hc == 31 else ())
                mmv = tsn.n
                oi = db
                ctx.do("vector", "tensor_tensor", t2[oi][:], pk[bk][:, 0:512],
                       s3b[:, db * 512:(db + 1) * 512], ALU.mult,
                       waits=[(tsn, mmv), (ev, s3v), t2f[oi]], incs=[(vc, 1)])
                bank_free[bk] = (vc, vc.n)
                ctx.do("gpsimd", "tensor_tensor", t2[oi][:], t2[oi][:],
                       x1t[:, tt * 1024 + db * 512:tt * 1024 + (db + 1) * 512],
                       ALU.add, waits=[(vc, vc.n), (jmu, mlv)], incs=[(gp, 1)])
                ctx.do("vector", "tensor_tensor", oub[oi][:], t2[oi][:],
                       att[:, tt * 1024 + db * 512:tt * 1024 + (db + 1) * 512],
                       ALU.add, waits=[(gp, gp.n), ouf[oi]], incs=[(vc, 1)])
                t2f[oi] = (vc, vc.n)
                ctx.do("scalar", "dma_start",
                       out=dram["out"][tb * 512 + tt * 128:tb * 512 + (tt + 1) * 128,
                                       db * 512:(db + 1) * 512],
                       in_=oub[oi][:], waits=[(vc, vc.n)], incs=[(jo_[oi], 16)])
                ouf[oi] = (jo_[oi], jo_[oi].n)
        lastc[tb] = tsn.n
        mload = (vc, vc.n)
    ctx.wait("scalar", jo_[0], jo_[0].n)
    ctx.wait("scalar", jo_[1], jo_[1].n)



PHASES = []


def _register_phases():
    PHASES.clear()
    PHASES.append(("pt", emit_pt,
                   [src for _, src, *_ in WCAST],
                   [n + "_bf" for n, *_ in WCAST]))
    PHASES.append(("p0", lambda c, d, e: emit_ln(c, d, e, "u", "xn0", "xn0f", "ln0"),
                   ["u"], ["xn0", "xn0f"]))
    PHASES.append(("p1a", emit_p1a,
                   ["wip_bf", "xn0f", "L0_attn_conv_w", "L0_attn_conv_b",
                    "L0_attn_dt_bias"],
                   ["zs", "xsbc", "dtv", "lndt"]))
    PHASES.append(("p1b", emit_p1b,
                   ["wop_bf", "zs", "xsbc", "dtv", "lndt", "L0_attn_A_log",
                    "L0_attn_Dp", "L0_attn_norm_w"],
                   ["macc"]))
    PHASES.append(("p2", emit_p2,
                   ["xn0f", "wr_bf", "wk_bf", "wv_bf", "L0_ffn_mu_r",
                    "L0_ffn_mu_k", "macc", "u"],
                   ["x1"]))
    PHASES.append(("p3", lambda c, d, e: emit_ln(c, d, e, "x1", "xn1", "xn1f", "ln1"),
                   ["x1"], ["xn1", "xn1f"]))
    PHASES.append(("p4a", emit_p4a,
                   ["xn1f", "qw_bf", "kw_bf", "vw_bf"], ["kf", "vt", "qf"]))
    PHASES.append(("p4b", emit_p4b, ["kf", "vt", "qf"], ["attn"]))
    PHASES.append(("p5a", emit_p5a,
                   ["xn1f", "w1_bf", "w2_bf", "L1_ffn_w1_s", "L1_ffn_w2_s"], ["a5"]))
    PHASES.append(("p5b", emit_p5b, ["a5", "h1_bf", "L1_ffn_h1_s"], ["b5"]))
    PHASES.append(("p5c", emit_p5c, ["b5", "h2_bf", "L1_ffn_h2_s"], ["c5"]))
    PHASES.append(("p5d", emit_p5d,
                   ["c5", "w3_bf", "L1_ffn_w3_s", "x1", "attn"], ["out"]))


_register_phases()


def _declare_dram(nc, names, kinds):
    dram = {}
    for n in names:
        if n == "u":
            shape, dt = MODULE_IN["u"][1], MODULE_IN["u"][2]
        elif n == "out":
            shape, dt = [T, D], F32
        elif n in SCRATCH:
            shape, dt = SCRATCH[n]
        elif n in WEIGHT_IN:
            shape, dt = WEIGHT_IN[n]
        elif n in VEC_IN:
            shape, dt = VEC_IN[n]
        else:
            raise KeyError(n)
        kind = "ExternalInput" if kinds[n] == "in" else "ExternalOutput"
        dram[n] = nc.dram_tensor(n, shape, dt, kind=kind).ap()
    return dram


def _build_phase_program(pname):
    for name, emit, ins, outs in PHASES:
        if name == pname:
            break
    else:
        raise KeyError(pname)
    nc = bass.Bass("TRN2", target_bir_lowering=False)
    kinds = {n: "in" for n in ins}
    kinds.update({n: "out" for n in outs})
    es = contextlib.ExitStack()
    with es:
        dram = _declare_dram(nc, list(kinds), kinds)
        if pname == "pt":
            dram["zpad"] = nc.inline_tensor(
                np.zeros((DIP_PAD - D_IN_PROJ, D), dtype=BF), "zpad").ap()
        ctx = Ctx(nc)
        emit(ctx, dram, es)
        ctx.build_block(es)
    return nc, ins, outs


def _build_fused_program():
    nc = bass.Bass("TRN2", target_bir_lowering=False)
    dram = {}
    dram["u"] = nc.dram_tensor("u", MODULE_IN["u"][1], MODULE_IN["u"][2],
                               kind="ExternalInput").ap()
    for srcn, (shape, dt) in WEIGHT_IN.items():
        dram[srcn] = nc.dram_tensor(srcn, shape, dt, kind="ExternalInput").ap()
    for vn, (shape, dt) in VEC_IN.items():
        dram[vn] = nc.dram_tensor(vn, shape, dt, kind="ExternalInput").ap()
    dram["out"] = nc.dram_tensor("out", [T, D], F32, kind="ExternalOutput").ap()
    for sn, (shape, dt) in SCRATCH.items():
        dram[sn] = nc.dram_tensor(sn, shape, dt, kind="Internal").ap()
    ctx = Ctx(nc)
    ins = set()
    for name, emit, pin, pout in PHASES:
        for n in pin:
            if n == "u" or n in WEIGHT_IN or n in VEC_IN:
                ins.add(n)
        ctx.begin_phase(name)
        with contextlib.ExitStack() as pes:
            emit(ctx, dram, pes)
        if name != "pt":
            ctx.barrier()
    es = contextlib.ExitStack()
    with es:
        ctx.build_block(es)
    return nc, sorted(ins), ["out"]


FUSED = True

_PROG_CACHE = {}


def run_phase(pname, stores):
    if pname not in _PROG_CACHE:
        _PROG_CACHE[pname] = _build_phase_program(pname)
    nc, ins, outs = _PROG_CACHE[pname]
    in_maps = [{n: st[n] for n in ins} for st in stores]
    res = run_bass_kernel_spmd(nc, in_maps, core_ids=list(range(len(stores))))
    for st, r in zip(stores, res.results):
        for n in outs:
            st[n] = r[n]


def make_stores(inputs):
    stores = []
    u = np.ascontiguousarray(inputs["L0_attn_u"])
    for c in range(NCORES):
        st = {"u": u[c]}
        for k, v in inputs.items():
            if k != "L0_attn_u":
                st[k] = np.ascontiguousarray(v)
        stores.append(st)
    return stores


def kernel(**inputs):
    stores = make_stores(inputs)
    if FUSED:
        if "fused" not in _PROG_CACHE:
            _PROG_CACHE["fused"] = _build_fused_program()
        nc, ins, outs = _PROG_CACHE["fused"]
        in_maps = [{n: st[n] for n in ins} for st in stores]
        res = run_bass_kernel_spmd(nc, in_maps, core_ids=list(range(len(stores))))
        for st, r in zip(stores, res.results):
            st["out"] = r["out"]
    else:
        for pname, *_ in PHASES:
            run_phase(pname, stores)
    out = np.stack([st["out"] for st in stores], axis=0)
    return out.astype(np.float32)


# version 10
# speedup vs baseline: 1.1582x; 1.1582x over previous
import contextlib

import numpy as np
import ml_dtypes
import concourse.bass as bass
import concourse.mybir as mybir
from concourse.bass_utils import run_bass_kernel_spmd

F32 = mybir.dt.float32
BF16 = mybir.dt.bfloat16
AF = mybir.ActivationFunctionType
ALU = mybir.AluOpType
AX = mybir.AxisListType

NCORES = 8
T = 4096
D = 1024
DI = 2048
DSTATE = 64
NH = 16
PD = 128
DCONV = 4
CHUNK = 256
NCH = T // CHUNK
CONV_DIM = DI + 2 * DSTATE
D_IN_PROJ = 2 * DI + 2 * DSTATE + NH
DIP_PAD = 4352
DH = 4096

BF = ml_dtypes.bfloat16
EPS = 1e-5
NEG = -1.0e30



class Sem:
    __slots__ = ("name", "n", "h")

    def __init__(self, name):
        self.name = name
        self.n = 0
        self.h = None

    def inc(self, k=1):
        self.n += k
        return self.n


class Ctx:
    ENGINES = ("sync", "tensor", "vector", "scalar", "gpsimd")

    def __init__(self, nc):
        self.nc = nc
        self.ops = {e: [] for e in self.ENGINES}
        self.sems = {}
        self._last_wait = {}

    def sem(self, name):
        if name not in self.sems:
            self.sems[name] = Sem(name)
        return self.sems[name]

    def begin_phase(self, tag):
        pass

    def barrier(self):
        bs = {e: self.sem(f"bar_{e}") for e in self.ENGINES}
        for e in self.ENGINES:
            s = bs[e]
            tgt = s.n + 1
            self.ops[e].append(lambda eng, _s=s, _t=tgt: eng.sem_inc(_s.h, 1))
            s.n = tgt
        for e in self.ENGINES:
            for e2 in self.ENGINES:
                self.wait(e, bs[e2], bs[e2].n)

    def do(self, engine, opname, *args, waits=(), incs=(), **kw):
        wlist = []
        for s, v in waits:
            key = (engine, s.name)
            if self._last_wait.get(key, -1) >= v:
                continue
            self._last_wait[key] = v
            wlist.append((s, v))
        ilist = list(incs)
        assert len(ilist) <= 1, "one sem update per instruction"
        for s, k in ilist:
            s.inc(k)

        def run(eng, _wl=wlist, _il=ilist, _op=opname, _a=args, _kw=kw):
            for s, v in _wl:
                eng.wait_ge(s.h, v)
            inst = getattr(eng, _op)(*_a, **_kw)
            for s, k in _il:
                inst.then_inc(s.h, k)

        self.ops[engine].append(run)

    def wait(self, engine, s, v):
        key = (engine, s.name)
        if self._last_wait.get(key, -1) >= v:
            return
        self._last_wait[key] = v
        self.ops[engine].append(lambda eng, _s=s, _v=v: eng.wait_ge(_s.h, _v))

    def build_block(self, es):
        for s in self.sems.values():
            s.h = es.enter_context(self.nc.semaphore(s.name))
        block = es.enter_context(self.nc.Block())
        for e in self.ENGINES:
            fns = self.ops[e]
            if not fns:
                continue
            getattr(block, e)(lambda eng, _fns=fns: [f(eng) for f in _fns])


def ceil(a, b):
    return -(-a // b)



WCAST = [
    ("wip", "L0_attn_in_proj_w", D_IN_PROJ, D, DIP_PAD),
    ("wop", "L0_attn_out_proj_w", D, DI, None),
    ("wr", "L0_ffn_W_r", D, D, None),
    ("wk", "L0_ffn_W_k", D, D, None),
    ("wv", "L0_ffn_W_v", D, D, None),
    ("qw", "L1_attn_Qw", D, D, None),
    ("kw", "L1_attn_Kw", D, D, None),
    ("vw", "L1_attn_Vw", D, D, None),
    ("w1", "L1_ffn_w1_w", DH, D, None),
    ("w2", "L1_ffn_w2_w", DH, D, None),
    ("h1", "L1_ffn_h1_w", DH, DH, None),
    ("h2", "L1_ffn_h2_w", DH, DH, None),
    ("w3", "L1_ffn_w3_w", D, DH, None),
]

SCRATCH = {}
for _n, _src, _r, _c, _p in WCAST:
    SCRATCH[_n + "_bf"] = ([_p if _p else _r, _c], BF16)
SCRATCH.update({
    "xn0": ([T, D], BF16),
    "xn0f": ([D, T + 8], BF16),
    "xn1f": ([D, T + 8], BF16),
    "zs": ([DI, T], F32),
    "xsbc": ([CONV_DIM, T], BF16),
    "dtv": ([16, T], F32),
    "lndt": ([16, T], F32),
    "macc": ([T, D], F32),
    "x1": ([T, D], F32),
    "xn1": ([T, D], BF16),
    "qf": ([D, T], BF16),
    "kf": ([D, T], BF16),
    "vt": ([T, D], BF16),
    "attn": ([T, D], F32),
    "a5": ([DH, T], BF16),
    "b5": ([DH, T], BF16),
    "c5": ([DH, T], BF16),
})

MODULE_IN = {
    "u": ("L0_attn_u", [T, D], F32),
}

VEC_IN = {
    "L0_attn_conv_w": ([CONV_DIM, DCONV], F32),
    "L0_attn_conv_b": ([CONV_DIM], F32),
    "L0_attn_dt_bias": ([16], F32),
    "L0_attn_A_log": ([16], F32),
    "L0_attn_Dp": ([16], F32),
    "L0_attn_norm_w": ([DI], F32),
    "L0_ffn_mu_r": ([D], F32),
    "L0_ffn_mu_k": ([D], F32),
    "L1_ffn_w1_s": ([DH], F32),
    "L1_ffn_w2_s": ([DH], F32),
    "L1_ffn_h1_s": ([DH], F32),
    "L1_ffn_h2_s": ([DH], F32),
    "L1_ffn_w3_s": ([D], F32),
}

WEIGHT_IN = {src: ([r, c], F32) for _, src, r, c, _ in WCAST}



PT_GROUP = {"wip": "ptg1", "wop": "ptg2", "wr": "ptg2", "wk": "ptg2",
            "wv": "ptg2", "qw": "ptg3", "kw": "ptg3", "vw": "ptg3",
            "w1": "ptg3", "w2": "ptg3", "h1": "ptg3", "h2": "ptg3",
            "w3": "ptg3"}
PT_ORDER = ["wip", "wop", "wr", "wk", "wv", "qw", "kw", "vw", "w1", "w2",
            "h1", "h2", "w3"]


def emit_pt(ctx, dram, es):
    if "zpad" not in dram:
        dram["zpad"] = ctx.nc.inline_tensor(
            np.zeros((DIP_PAD - D_IN_PROJ, D), dtype=BF), "zpad").ap()
    wc = {n: e for (n, *e_) in [(w[0], w) for w in WCAST] for e in [e_]}
    wmap = {w[0]: w for w in WCAST}
    for name in PT_ORDER:
        _, src, rows, cols, pad = wmap[name]
        g = ctx.sem(PT_GROUP[name])
        dst = dram[name + "_bf"]
        step = max(512, min(rows, (1 << 25) // (cols * 4)))
        r = 0
        while r < rows:
            rr = min(step, rows - r)
            ctx.do("gpsimd", "dma_start", out=dst[r:r + rr, :],
                   in_=dram[src][r:r + rr, :], incs=[(g, 16)])
            r += rr
        if pad:
            ctx.do("gpsimd", "dma_start", out=dst[rows:pad, :],
                   in_=dram["zpad"][0:pad - rows, :], incs=[(g, 16)])
    for gn in ["ptg1", "ptg2", "ptg3"]:
        g = ctx.sem(gn)
        ctx.wait("gpsimd", g, g.n)


def emit_ln(ctx, dram, es, src_name, dst_name, fm_name, tag, semtag="ln"):
    nc = ctx.nc
    NTT = T // 128
    xt = [es.enter_context(nc.sbuf_tensor(f"{tag}x{i}", [128, D], F32)) for i in range(2)]
    xq = es.enter_context(nc.sbuf_tensor(f"{tag}xq", [128, D], F32))
    on = [es.enter_context(nc.sbuf_tensor(f"{tag}o{i}", [128, D], BF16)) for i in range(2)]
    fmS = [es.enter_context(nc.sbuf_tensor(f"{tag}f{i}", [128, 8 * 512], BF16)) for i in range(2)]
    idb = es.enter_context(nc.sbuf_tensor(f"{tag}ib", [128, 128], BF16))
    cs = es.enter_context(nc.sbuf_tensor(f"{tag}cs", [128, 1], F32))
    cq = es.enter_context(nc.sbuf_tensor(f"{tag}cq", [128, 1], F32))
    nm = es.enter_context(nc.sbuf_tensor(f"{tag}nm", [128, 1], F32))
    cv = es.enter_context(nc.sbuf_tensor(f"{tag}cv", [128, 1], F32))
    cr = es.enter_context(nc.sbuf_tensor(f"{tag}cr", [128, 1], F32))
    cte = es.enter_context(nc.sbuf_tensor(f"{tag}ct", [128, 1], F32))
    pT = [es.enter_context(nc.psum_tensor(f"{tag}pT{i}", [128, 128], BF16)) for i in range(2)]

    ld = [ctx.sem(f"{semtag}ld0"), ctx.sem(f"{semtag}ld1")]
    cn = ctx.sem(f"{semtag}cn")
    sq = ctx.sem(f"{semtag}sq")
    vv = ctx.sem(f"{semtag}vv")
    ts = ctx.sem(f"{semtag}ts")
    fe = ctx.sem(f"{semtag}fe")
    st = [ctx.sem(f"{semtag}st0"), ctx.sem(f"{semtag}st1")]
    fs = [ctx.sem(f"{semtag}fs0"), ctx.sem(f"{semtag}fs1")]

    identd = nc.inline_tensor(np.eye(128, dtype=BF), f"ib{tag}_d").ap()
    zcold = nc.inline_tensor(np.zeros((D, 1), dtype=BF), f"zc{tag}_d").ap()
    ctx.do("sync", "dma_start", out=idb[:], in_=identd[:], incs=[(ld[0], 16)])
    ctx.do("gpsimd", "dma_start", out=dram[fm_name][0:D, 0:1], in_=zcold[:],
           allow_slow_non_contiguous=True, incs=[(fs[0], 16)])
    ctx.do("vector", "memset", cte[:], float(EPS), incs=[(cn, 1)])
    vv0, sq0, ts0 = vv.n, sq.n, ts.n
    ptfree = [fe.n, fe.n]
    src, dst, fmd = dram[src_name], dram[dst_name], dram[fm_name]
    for i in range(NTT):
        b = i % 2
        g = i % 8 // 4
        w = [(vv, vv0 + (i - 2) * 2 + 2)] if i >= 2 else []
        ctx.do("sync", "dma_start", out=xt[b][:], in_=src[i * 128:(i + 1) * 128, :],
               waits=w, incs=[(ld[b], 16)])
        ldv = (ld[b], ld[b].n)
        ctx.do("scalar", "activation", xq[:], xt[b][:], AF.Square,
               accum_out=cq[:, 0:1], waits=[ldv], incs=[(sq, 1)])
        tc = ctx.sem(f"{semtag}tc")
        ctx.do("vector", "reduce_sum", cs[:, 0:1], xt[b][:], AX.X,
               waits=[ldv], incs=[(tc, 1)])
        ctx.do("vector", "tensor_scalar", nm[:, 0:1], cs[:, 0:1], -1.0 / D, None, ALU.mult,
               waits=[(tc, tc.n)], incs=[(tc, 1)])
        ctx.do("vector", "tensor_scalar", cv[:, 0:1], cq[:, 0:1], 1.0 / D, None, ALU.mult,
               waits=[(sq, sq0 + 2 * i + 1)], incs=[(tc, 1)])
        ctx.do("vector", "scalar_tensor_tensor", cv[:, 0:1], nm[:, 0:1], nm[:, 0:1],
               cv[:, 0:1], ALU.mult, ALU.subtract, waits=[(tc, tc.n)], incs=[(vv, 1)])
        ctx.do("scalar", "activation", cr[:, 0:1], cv[:, 0:1], AF.Ln,
               bias=cte[:, 0:1], scale=-1.0, waits=[(vv, vv0 + i * 2 + 1), (cn, cn.n)])
        ctx.do("scalar", "activation", cr[:, 0:1], cr[:, 0:1], AF.Exp,
               scale=-0.5, incs=[(sq, 1)])
        w = [(sq, sq.n), (st[b], st[b].n)]
        if i >= 2:
            w.append((ts, ts0 + (i - 2) * 8 + 8))
        ctx.do("vector", "tensor_scalar", on[b][:], xt[b][:], nm[:, 0:1], cr[:, 0:1],
               ALU.add, op1=ALU.mult, waits=w, incs=[(vv, 1)])
        xnv = vv.n
        ctx.do("scalar", "dma_start", out=dst[i * 128:(i + 1) * 128, :], in_=on[b][:],
               waits=[(vv, xnv)], incs=[(st[b], 16)])
        for dc in range(8):
            pb = pT[dc % 2]
            ctx.do("tensor", "transpose", pb[:], on[b][:, dc * 128:(dc + 1) * 128],
                   idb[:], waits=[(vv, xnv), (ld[0], 16), (fe, ptfree[dc % 2])],
                   incs=[(ts, 1)])
            ctx.do("scalar", "activation",
                   fmS[g][:, dc * 512 + (i % 4) * 128: dc * 512 + (i % 4) * 128 + 128],
                   pb[:], AF.Identity,
                   waits=[(ts, ts.n), (fs[g], fs[g].n)], incs=[(fe, 1)])
            ptfree[dc % 2] = fe.n
        if i % 4 == 3:
            g0 = (i // 4) * 512
            for dc in range(8):
                ctx.do("scalar", "dma_start",
                       out=fmd[dc * 128:(dc + 1) * 128, 1 + g0:1 + g0 + 512],
                       in_=fmS[g][:, dc * 512:(dc + 1) * 512],
                       waits=[(fe, fe.n)], incs=[(fs[g], 16)])
    ctx.wait("scalar", st[0], st[0].n)
    ctx.wait("scalar", st[1], st[1].n)
    ctx.wait("scalar", fs[0], fs[0].n)
    ctx.wait("scalar", fs[1], fs[1].n)


def emit_p1a(ctx, dram, es):
    nc = ctx.nc
    NT = 8
    TB = 512
    CS = 516

    ipT = es.enter_context(nc.sbuf_tensor("ipT", [128, 8 * DIP_PAD], BF16))
    xnb = [es.enter_context(nc.sbuf_tensor(f"xnb{i}", [128, 8 * TB], BF16))
           for i in range(3)]
    xbc = es.enter_context(nc.sbuf_tensor("xbc", [128, 17 * CS], F32))
    acc = [es.enter_context(nc.sbuf_tensor(f"acc{i}", [128, TB], F32)) for i in range(2)]
    zb = [es.enter_context(nc.sbuf_tensor(f"zb{i}", [128, TB], F32)) for i in range(2)]
    ob = [es.enter_context(nc.sbuf_tensor(f"ob{i}", [128, TB], BF16)) for i in range(2)]
    dtraw = es.enter_context(nc.sbuf_tensor("dtraw", [16, T], F32))
    dps = [es.enter_context(nc.sbuf_tensor(f"dps{i}", [16, T], F32)) for i in range(2)]
    cw = es.enter_context(nc.sbuf_tensor("cw", [128, 17 * 4], F32))
    cbp = es.enter_context(nc.sbuf_tensor("cbp", [17, 128], F32))
    cb = es.enter_context(nc.sbuf_tensor("cb", [128, 17], F32))
    dtb = es.enter_context(nc.sbuf_tensor("dtb", [16, 1], F32))
    ident = es.enter_context(nc.sbuf_tensor("identA", [128, 128], F32))
    c1 = es.enter_context(nc.sbuf_tensor("c1A", [16, 1], F32))
    ps = [es.enter_context(nc.psum_tensor(f"pA{i}", [128, TB], F32)) for i in range(4)]
    pst = es.enter_context(nc.psum_tensor("pAt", [128, 17], F32))

    ld = ctx.sem("Ald")
    xs_ = [ctx.sem("Axn0"), ctx.sem("Axn1"), ctx.sem("Axn2")]
    zs_ = [ctx.sem("Az0"), ctx.sem("Az1")]
    xo_ = [ctx.sem("Ax0"), ctx.sem("Ax1")]
    gl = ctx.sem("Agl")
    ts = ctx.sem("Ats")
    ev = ctx.sem("Aev")
    vc = ctx.sem("Avc")
    st = ctx.sem("Ast")

    identd = nc.inline_tensor(np.eye(128, dtype=np.float32), "identA_d").ap()
    identbd = nc.inline_tensor(np.eye(128, dtype=BF), "identAb_d").ap()
    identb_a = es.enter_context(nc.sbuf_tensor("Aidb", [128, 128], BF16))

    _g1 = ctx.sem("ptg1")
    ctx.do("sync", "dma_start", out=identb_a[:], in_=identbd[:], incs=[(ld, 16)])
    ws_ = [ctx.sem("Wst0"), ctx.sem("Wst1")]
    aie = ctx.sem("Tie")
    wstage = [es.enter_context(nc.sbuf_tensor(f"Awst{i}", [128, 1024], BF16))
              for i in range(2)]
    pTb = [es.enter_context(nc.psum_tensor(f"ApT{i}", [128, 128], BF16))
           for i in range(2)]
    ptfree = [0, 0]
    wsfree = [0, 0]
    for m in range(34):
        s = ws_[m % 2]
        ctx.do("sync", "dma_start", out=wstage[m % 2][:],
               in_=dram["wip_bf"][m * 128:(m + 1) * 128, :],
               waits=[(_g1, _g1.n), (ts, wsfree[m % 2])], incs=[(s, 16)])
        for dc in range(8):
            ctx.do("tensor", "transpose", pTb[dc % 2][:],
                   wstage[m % 2][:, dc * 128:(dc + 1) * 128], identb_a[:],
                   waits=[(s, s.n), (aie, ptfree[dc % 2]), (ld, 16)],
                   incs=[(ts, 1)])
            ctx.do("scalar", "activation",
                   ipT[:, dc * DIP_PAD + m * 128: dc * DIP_PAD + m * 128 + 128],
                   pTb[dc % 2][:], AF.Identity, waits=[(ts, ts.n)], incs=[(aie, 1)])
            ptfree[dc % 2] = aie.n
        wsfree[m % 2] = ts.n
    ipv = aie.n

    xnld = {}

    def load_xn_block(tb, waits=()):
        s = xs_[tb % 3]
        ctx.do("sync", "dma_start",
               out=xnb[tb % 3][:].rearrange("p (dc t) -> p dc t", t=TB),
               in_=dram["xn0f"][0:D, 1 + tb * TB:1 + (tb + 1) * TB]
               .rearrange("(dc p) t -> p dc t", p=128),
               waits=waits, incs=[(s, 16)])
        xnld[tb] = (s, s.n)

    load_xn_block(0)
    load_xn_block(1)
    ctx.do("gpsimd", "dma_start",
           out=cw[:].rearrange("p (c w) -> p c w", w=4),
           in_=dram["L0_attn_conv_w"].rearrange("(c p) w -> p c w", p=128),
           incs=[(gl, 16)])
    ctx.do("gpsimd", "dma_start", out=cbp[:],
           in_=dram["L0_attn_conv_b"].rearrange("(c p) -> c p", p=128),
           incs=[(gl, 16)])
    ctx.do("gpsimd", "dma_start", out=dtb[:],
           in_=dram["L0_attn_dt_bias"].rearrange("(a b) -> a b", b=1),
           incs=[(gl, 16)])
    ctx.do("sync", "dma_start", out=ident[:], in_=identd[:], incs=[(ld, 16)])
    ctx.do("tensor", "transpose", pst[:, 0:17], cbp[:], ident[0:17, 0:17],
           waits=[(gl, 32), (ld, ld.n)], incs=[(ts, 1)])
    ctx.do("scalar", "activation", cb[:], pst[:, 0:17], AF.Identity,
           waits=[(ts, ts.n)], incs=[(ev, 1)])
    ctx.do("vector", "memset", xbc[:], 0.0, incs=[(vc, 1)])
    ctx.do("vector", "memset", c1[:], 1.0, incs=[(vc, 1)])

    gl_all = gl.n
    bank_free = [(ev, 0)] * 4
    nb = 0
    tailv = [0] * 17
    evacv = [[0] * 34 for _ in range(NT)]
    zst = [0, 0]
    ost = [0, 0]
    siluv = [0, 0]

    lastmm = [0] * NT
    for tb in range(NT):
        if tb + 2 < NT:
            load_xn_block(tb + 2, waits=[(ts, lastmm[tb - 1])] if tb >= 1 else ())
        for m in range(34):
            bk = ps[nb]
            w = [(aie, ipv), xnld[tb], bank_free[nb]]
            for k in range(8):
                ctx.do("tensor", "matmul",
                       bk[:, :] if m != 33 else bk[0:16, :],
                       ipT[:, k * DIP_PAD + m * 128: k * DIP_PAD + m * 128 + (128 if m != 33 else 16)],
                       xnb[tb % 3][:, k * TB: k * TB + TB],
                       start=(k == 0), stop=(k == 7),
                       waits=w if k == 0 else (),
                       incs=[(ts, 1)] if k == 7 else ())
            lastmm[tb] = ts.n
            mmv = ts.n
            if m < 16:
                zi = (tb * 16 + m) % 2
                ctx.do("scalar", "activation", zb[zi][:], bk[:], AF.Silu,
                       waits=[(ts, mmv), (zs_[zi], zs_[zi].n)], incs=[(ev, 1)])
                evc = ev.n
                ctx.do("scalar", "dma_start",
                       out=dram["zs"][m * 128:(m + 1) * 128, tb * TB:(tb + 1) * TB],
                       in_=zb[zi][:], waits=[(ev, evc)], incs=[(zs_[zi], 16)])
            elif m < 33:
                cc = m - 16
                r0 = cc * CS
                w2 = [(ts, mmv), (vc, 1)]
                if tb > 0:
                    w2.append((vc, tailv[cc]))
                ctx.do("scalar", "activation", xbc[:, r0 + 3:r0 + 3 + TB], bk[:],
                       AF.Identity, waits=w2, incs=[(ev, 1)])
            else:
                ctx.do("scalar", "activation", dtraw[0:16, tb * TB:(tb + 1) * TB],
                       bk[0:16, :], AF.Identity, waits=[(ts, mmv)], incs=[(ev, 1)])
            evacv[tb][m] = ev.n
            bank_free[nb] = (ev, ev.n)
            nb = (nb + 1) % 4
        for cc in range(17):
            r0 = cc * CS
            ai = cc % 2
            aw = [(ev, max(evacv[tb][16 + cc], siluv[ai])), (gl, gl_all)]
            ctx.do("vector", "tensor_scalar", acc[ai][:], xbc[:, r0:r0 + TB],
                   cw[:, cc * 4:cc * 4 + 1], cb[:, cc:cc + 1], ALU.mult, op1=ALU.add,
                   waits=aw + [(vc, vc.n)], incs=[(vc, 1)])
            for wtap in range(1, 4):
                ctx.do("vector", "scalar_tensor_tensor", acc[ai][:],
                       xbc[:, r0 + wtap:r0 + wtap + TB], cw[:, cc * 4 + wtap:cc * 4 + wtap + 1],
                       acc[ai][:], ALU.mult, ALU.add,
                       waits=[(vc, vc.n)], incs=[(vc, 1)])
            convv = vc.n
            oi = cc % 2
            ctx.do("scalar", "activation", ob[oi][:], acc[ai][:], AF.Silu,
                   waits=[(vc, convv), (xo_[oi], xo_[oi].n)], incs=[(ev, 1)])
            siluv[ai] = ev.n
            ctx.do("scalar", "dma_start",
                   out=dram["xsbc"][cc * 128:(cc + 1) * 128, tb * TB:(tb + 1) * TB],
                   in_=ob[oi][:], waits=[(ev, ev.n)], incs=[(xo_[oi], 16)])
            ctx.do("vector", "tensor_copy", xbc[:, r0:r0 + 3],
                   xbc[:, r0 + TB:r0 + TB + 3], waits=[(vc, convv)], incs=[(vc, 1)])
            tailv[cc] = vc.n
    dn = dtraw[0:16, :]
    d1, d2, d3 = dtraw[0:16, :], dps[0][0:16, :], dps[1][0:16, :]
    ctx.do("vector", "tensor_scalar", d1, dn, dtb[0:16, 0:1], None, ALU.add,
           waits=[(ev, evacv[NT - 1][33]), (gl, gl_all), (vc, vc.n)], incs=[(vc, 1)])
    ctx.do("scalar", "activation", d2, d1, AF.Abs,
           waits=[(vc, vc.n)], incs=[(ev, 1)])
    ctx.do("scalar", "activation", d3, d2, AF.Exp, scale=-1.0,
           waits=[(ev, ev.n)], incs=[(ev, 1)])
    expv = ev.n
    ctx.do("scalar", "activation", d3, d3, AF.Ln, bias=c1[0:16, 0:1],
           waits=[(ev, expv), (vc, 2)], incs=[(ev, 1)])
    ctx.do("vector", "tensor_scalar", d2, d1, 0.0, None, ALU.max,
           waits=[(vc, vc.n), (ev, expv)], incs=[(vc, 1)])
    ctx.do("vector", "tensor_tensor", d1, d2, d3, ALU.add,
           waits=[(vc, vc.n), (ev, ev.n)], incs=[(vc, 1)])
    ctx.do("sync", "dma_start", out=dram["dtv"][:], in_=d1,
           waits=[(vc, vc.n)], incs=[(st, 16)])
    ctx.do("scalar", "activation", d2, d1, AF.Ln, waits=[(vc, vc.n)], incs=[(ev, 1)])
    ctx.do("sync", "dma_start", out=dram["lndt"][:], in_=d2,
           waits=[(ev, ev.n)], incs=[(st, 16)])
    ctx.wait("sync", st, st.n)
    for s in (zs_ + xo_):
        ctx.wait("scalar", s, s.n)


def emit_p1b(ctx, dram, es):
    nc = ctx.nc
    CH = CHUNK

    WoT = es.enter_context(nc.sbuf_tensor("WoT", [128, 16 * 1024], BF16))
    dtv = es.enter_context(nc.sbuf_tensor("dtvS", [16, T], F32))
    lnd = es.enter_context(nc.sbuf_tensor("lndS", [16, T], F32))
    af = es.enter_context(nc.sbuf_tensor("afS", [16, T], F32))
    prev = es.enter_context(nc.sbuf_tensor("prevS", [64, NH * 128], F32))
    prevb = es.enter_context(nc.sbuf_tensor("prevbS", [64, NH * 128], BF16))
    ident = es.enter_context(nc.sbuf_tensor("identB", [128, 128], F32))
    sel = es.enter_context(nc.sbuf_tensor("selB", [16, NH * 128], F32))
    mk = es.enter_context(nc.sbuf_tensor("mkB", [128, 2 * CH], F32))
    onesr = es.enter_context(nc.sbuf_tensor("onesrB", [1, 128], F32))
    onesh = es.enter_context(nc.sbuf_tensor("oneshB", [16, 128], F32))
    onesc = es.enter_context(nc.sbuf_tensor("onescB", [128, 1], BF16))
    Acol = es.enter_context(nc.sbuf_tensor("AcolB", [16, 1], F32))
    Dpb = es.enter_context(nc.sbuf_tensor("DpbB", [128, 16], F32))
    nw = es.enter_context(nc.sbuf_tensor("nwB", [128, 16], F32))
    nwp = es.enter_context(nc.sbuf_tensor("nwpB", [16, 128], F32))
    epsc = es.enter_context(nc.sbuf_tensor("epscB", [1, 1], F32))
    vtmp = es.enter_context(nc.sbuf_tensor("vtmpB", [16, 1], F32))
    xsf = es.enter_context(nc.sbuf_tensor("xsfB", [128, NH * CH], BF16))
    xst = [es.enter_context(nc.sbuf_tensor(f"xstB{i}", [128, DI], BF16)) for i in range(2)]
    Bcf = es.enter_context(nc.sbuf_tensor("BcfB", [64, CH], BF16))
    Ccf = es.enter_context(nc.sbuf_tensor("CcfB", [64, CH], BF16))
    Bct = [es.enter_context(nc.sbuf_tensor(f"BctB{i}", [128, 64], BF16)) for i in range(2)]
    Bcw = [es.enter_context(nc.sbuf_tensor(f"BcwB{i}", [128, 64], BF16)) for i in range(2)]
    acum = es.enter_context(nc.sbuf_tensor("acumB", [16, CH], F32))
    eac = es.enter_context(nc.sbuf_tensor("eacB", [16, CH], F32))
    acT = [es.enter_context(nc.sbuf_tensor(f"acTB{i}", [128, 16], F32)) for i in range(2)]
    colk = [es.enter_context(nc.sbuf_tensor(f"colkB{i}", [128, 16], F32)) for i in range(2)]
    dtds = [es.enter_context(nc.sbuf_tensor(f"dtdsB{i}", [128, 16], F32)) for i in range(2)]
    dtT = [es.enter_context(nc.sbuf_tensor(f"dtTB{i}", [128, 16], F32)) for i in range(2)]
    alb = es.enter_context(nc.sbuf_tensor("albB", [128, 16], F32))
    alc = es.enter_context(nc.sbuf_tensor("alcB", [16, 1], F32))
    alr = es.enter_context(nc.sbuf_tensor("alrB", [1, 16], F32))
    cdb = es.enter_context(nc.sbuf_tensor("cdbB", [64, 16], F32))
    scT = [es.enter_context(nc.sbuf_tensor(f"scTB{i}", [128, CH], F32)) for i in range(2)]
    arg = [es.enter_context(nc.sbuf_tensor(f"argB{i}", [128, CH], F32)) for i in range(2)]
    eaa = [es.enter_context(nc.sbuf_tensor(f"eaaB{i}", [128, CH], F32)) for i in range(2)]
    MT = [es.enter_context(nc.sbuf_tensor(f"MTB{i}", [128, CH], BF16)) for i in range(2)]
    Ccs = es.enter_context(nc.sbuf_tensor("CcsB", [64, CH], BF16))
    Yf = es.enter_context(nc.sbuf_tensor("YfB", [128, NH * CH], F32))
    zsc = es.enter_context(nc.sbuf_tensor("zscB", [128, NH * CH], F32))
    ysq = [es.enter_context(nc.sbuf_tensor(f"ysqB{i}", [128, CH], BF16)) for i in range(2)]
    r1 = es.enter_context(nc.sbuf_tensor("r1B", [1, CH], F32))
    ynb = es.enter_context(nc.sbuf_tensor("ynbB", [128, NH * CH], BF16))
    osb = [es.enter_context(nc.sbuf_tensor(f"osbB{i}", [128, 512], F32)) for i in range(2)]
    pk = [es.enter_context(nc.psum_tensor(f"pB{i}", [128, 512], F32)) if i != 1
          else None for i in range(8)]
    pkT = es.enter_context(nc.psum_tensor("pBT", [128, 512], BF16))

    ld = ctx.sem("Bld"); gl = ctx.sem("Bgl"); tsn = ctx.sem("Bts")
    ev = ctx.sem("Bev"); vc = ctx.sem("Bvc"); gp = ctx.sem("Bgp")
    gs = ctx.sem("Bgs")
    so_ = [ctx.sem("Bso0"), ctx.sem("Bso1")]

    identd = nc.inline_tensor(np.eye(128, dtype=np.float32), "identB_d").ap()
    seln = np.zeros((16, NH * 128), dtype=np.float32)
    for h in range(NH):
        seln[h, h * 128:(h + 1) * 128] = 1.0
    seld = nc.inline_tensor(seln, "selB_d").ap()
    mkn = np.zeros((128, 2 * CH), dtype=np.float32)
    for kt in range(2):
        for kk in range(128):
            gk = kt * 128 + kk
            mkn[kk, kt * CH + gk:(kt + 1) * CH] = 1.0
    mkd = nc.inline_tensor(mkn, "mkB_d").ap()

    _g2 = ctx.sem("ptg2")
    bws = [ctx.sem("Wst0"), ctx.sem("Wst1")]
    bie = ctx.sem("Tie")
    wstage = [es.enter_context(nc.sbuf_tensor(f"Bwst{i}", [128, DI], BF16))
              for i in range(2)]
    identbd = nc.inline_tensor(np.eye(128, dtype=BF), "identBb_d").ap()
    identb_b = es.enter_context(nc.sbuf_tensor("Bidb", [128, 128], BF16))
    ctx.do("sync", "dma_start", out=identb_b[:], in_=identbd[:], incs=[(ld, 16)])
    _ptfree = [0, 0]
    _wsfree = [0, 0]
    for m in range(8):
        s = bws[m % 2]
        ctx.do("sync", "dma_start", out=wstage[m % 2][:],
               in_=dram["wop_bf"][m * 128:(m + 1) * 128, :],
               waits=[(_g2, _g2.n), (tsn, _wsfree[m % 2])], incs=[(s, 16)])
        for ec in range(16):
            ctx.do("tensor", "transpose", pkT[0:128, 0:128],
                   wstage[m % 2][:, ec * 128:(ec + 1) * 128], identb_b[:],
                   waits=[(s, s.n), (bie, _ptfree[0]), (ld, 16)],
                   incs=[(tsn, 1)])
            ctx.do("scalar", "activation",
                   WoT[:, ec * 1024 + m * 128: ec * 1024 + m * 128 + 128],
                   pkT[0:128, 0:128], AF.Identity, waits=[(tsn, tsn.n)],
                   incs=[(bie, 1)])
            _ptfree[0] = bie.n
        _wsfree[m % 2] = tsn.n
    wov = bie.n
    ctx.do("sync", "dma_start", out=dtv[:], in_=dram["dtv"][:], incs=[(ld, 16)])
    ctx.do("sync", "dma_start", out=lnd[:], in_=dram["lndt"][:], incs=[(ld, 16)])
    ctx.do("sync", "dma_start", out=ident[:], in_=identd[:], incs=[(ld, 16)])
    ctx.do("sync", "dma_start", out=sel[:], in_=seld[:], incs=[(ld, 16)])
    ctx.do("sync", "dma_start", out=mk[:], in_=mkd[:], incs=[(ld, 16)])
    ctx.do("gpsimd", "dma_start", out=vtmp[:],
           in_=dram["L0_attn_A_log"].rearrange("(a b) -> a b", b=1), incs=[(gs, 16)])
    ctx.do("gpsimd", "dma_start", out=Acol[:],
           in_=dram["L0_attn_Dp"].rearrange("(a b) -> a b", b=1), incs=[(gs, 16)])
    ctx.do("gpsimd", "dma_start", out=nwp[:],
           in_=dram["L0_attn_norm_w"].rearrange("(c p) -> c p", p=128), incs=[(gs, 16)])
    ld0, gl0 = ld.n, gs.n

    for t_, v_ in [(onesr, 1.0), (onesh, 1.0), (onesc, 1.0), (epsc, float(EPS))]:
        ctx.do("vector", "memset", t_[:], v_, incs=[(vc, 1)])
    ctx.do("vector", "memset", prev[:], 0.0, incs=[(vc, 1)])
    msv = vc.n
    ctx.do("tensor", "transpose", pk[2][0:1, 0:16], Acol[:], ident[0:16, 0:16],
           waits=[(gs, gl0), (ld, ld0)], incs=[(tsn, 1)])
    ctx.do("scalar", "activation", alr[:], pk[2][0:1, 0:16], AF.Identity,
           waits=[(tsn, tsn.n)], incs=[(ev, 1)])
    ctx.do("tensor", "matmul", pk[2][:, 0:16], onesr[:], alr[:], start=True,
           stop=True, waits=[(ev, ev.n), (vc, msv)], incs=[(tsn, 1)])
    ctx.do("scalar", "activation", Dpb[:], pk[2][:, 0:16], AF.Identity,
           waits=[(tsn, tsn.n)], incs=[(ev, 1)])
    dpv = ev.n
    ctx.do("tensor", "transpose", pk[2][:, 0:16], nwp[:], ident[0:16, 0:16],
           waits=[(ev, dpv)], incs=[(tsn, 1)])
    ctx.do("scalar", "activation", nw[:], pk[2][:, 0:16], AF.Identity,
           waits=[(tsn, tsn.n)], incs=[(ev, 1)])
    nwv = ev.n
    ctx.do("scalar", "activation", Acol[:], vtmp[:], AF.Exp,
           waits=[(gs, gl0), (ev, dpv)], incs=[(ev, 1)])
    ctx.do("vector", "tensor_scalar", Acol[:], Acol[:], -1.0, None, ALU.mult,
           waits=[(ev, ev.n)], incs=[(vc, 1)])
    ctx.do("vector", "tensor_scalar", af[:], dtv[:], Acol[0:16, 0:1], None, ALU.mult,
           waits=[(vc, vc.n), (ld, ld0)], incs=[(vc, 1)])
    afv = vc.n

    bank_free = {i: (tsn, 0) for i in range(8)}
    G = {k: (tsn, 0) for k in
         ["xsf", "xst", "Bcf", "Ccf", "Bct", "zsc", "scT", "acum", "eac",
          "alb", "alr", "cdb", "acT", "dtT", "dtds", "MT0", "MT1", "Ccs",
          "Bcw", "prevdone", "Yf", "ynb", "arg0", "arg1", "eaa0", "eaa1"]}
    ysqv = [0] * 16
    gavp = [0] * 16
    osbst = [0, 0]
    ldch = {}

    def load_chunk(c):
        t0 = c * CH
        ctx.do("sync", "dma_start",
               out=xsf[:].rearrange("p (ec t) -> p ec t", t=CH),
               in_=dram["xsbc"][0:DI, t0:t0 + CH].rearrange("(ec p) t -> p ec t", p=128),
               waits=[G["xsf"]], incs=[(ld, 16)])
        ctx.do("sync", "dma_start", out=Bcf[:], in_=dram["xsbc"][DI:DI + 64, t0:t0 + CH],
               waits=[G["Bcf"]], incs=[(ld, 16)])
        ctx.do("sync", "dma_start", out=Ccf[:],
               in_=dram["xsbc"][DI + 64:DI + 128, t0:t0 + CH],
               waits=[G["Ccf"]], incs=[(ld, 16)])
        ctx.do("gpsimd", "dma_start",
               out=zsc[:].rearrange("p (ec t) -> p ec t", t=CH),
               in_=dram["zs"][0:DI, t0:t0 + CH].rearrange("(ec p) t -> p ec t", p=128),
               waits=[G["zsc"]], incs=[(gl, 16)])
        ldch[c] = (ld.n, gl.n)

    load_chunk(0)

    xtie = ctx.sem("Bxe")
    _xptf = [0, 0]
    for c in range(NCH):
        t0 = c * CH
        lv, gv = ldch[c]
        acs = af[0:16, t0:t0 + CH]
        for tt in range(2):
            for ec in range(16):
                ctx.do("tensor", "transpose", pkT[0:128, 0:128],
                       xsf[:, ec * CH + tt * 128:ec * CH + (tt + 1) * 128],
                       identb_b[:],
                       waits=[(ld, lv), (xtie, _xptf[0]), G["xst"]],
                       incs=[(tsn, 1)])
                ctx.do("scalar", "activation",
                       xst[tt][:, ec * 128:(ec + 1) * 128], pkT[0:128, 0:128],
                       AF.Identity, waits=[(tsn, tsn.n)], incs=[(xtie, 1)])
                _xptf[0] = xtie.n
            ctx.do("tensor", "transpose", pkT[0:128, 0:64],
                   Bcf[0:64, tt * 128:(tt + 1) * 128], identb_b[0:64, 0:64],
                   waits=[(ld, lv), (xtie, _xptf[0])], incs=[(tsn, 1)])
            ctx.do("scalar", "activation", Bct[tt][:, 0:64],
                   pkT[0:128, 0:64], AF.Identity,
                   waits=[(tsn, tsn.n)], incs=[(xtie, 1)])
            _xptf[0] = xtie.n
        xstv = xtie.n
        ctx.do("vector", "tensor_tensor_scan", acum[0:16, 0:128], onesh[:],
               acs[:, 0:128], 0.0, ALU.mult, ALU.add,
               waits=[(vc, afv), G["acum"]], incs=[(vc, 1)])
        ctx.do("vector", "tensor_copy", vtmp[:], acum[0:16, 127:128],
               waits=[(vc, vc.n)], incs=[(vc, 1)])
        ctx.do("vector", "tensor_tensor_scan", acum[0:16, 128:256], onesh[:],
               acs[:, 128:256], vtmp[0:16, 0:1], ALU.mult, ALU.add,
               waits=[(vc, vc.n)], incs=[(vc, 1)])
        acv = vc.n
        ctx.do("vector", "reduce_sum", alc[0:16, 0:1], acs, AX.X,
               waits=[(vc, afv)], incs=[(vc, 1)])
        alv = vc.n
        ctx.do("scalar", "activation", eac[:], acum[0:16, :], AF.Exp,
               waits=[(vc, acv), G["eac"]], incs=[(ev, 1)])
        eacv = ev.n

        def small_tr(dst, src_ap, k, wt):
            ctx.do("tensor", "transpose", pk[2][:, 0:k], src_ap, ident[0:k, 0:k],
                   waits=wt + [bank_free[2]], incs=[(tsn, 1)])
            ctx.do("scalar", "activation", dst, pk[2][:, 0:dst.shape[1]], AF.Identity,
                   waits=[(tsn, tsn.n)], incs=[(ev, 1)])
            bank_free[2] = (ev, ev.n)
            return ev.n

        for tt in range(2):
            small_tr(acT[tt][:], acum[0:16, tt * 128:(tt + 1) * 128], 16,
                     [(vc, acv), G["acT"]])
            small_tr(dtT[tt][:], dtv[0:16, t0 + tt * 128:t0 + (tt + 1) * 128], 16,
                     [(ld, ld0), G["dtT"]])
            ctx.do("tensor", "transpose", pk[2][:, 0:16],
                   lnd[0:16, t0 + tt * 128:t0 + (tt + 1) * 128], ident[0:16, 0:16],
                   waits=[(ld, ld0), bank_free[2]], incs=[(tsn, 1)])
            ctx.do("vector", "tensor_tensor", colk[tt][:], pk[2][:, 0:16], acT[tt][:],
                   ALU.subtract, waits=[(tsn, tsn.n), (ev, ev.n)], incs=[(vc, 1)])
            bank_free[2] = (vc, vc.n)
        ctx.do("tensor", "transpose", pk[2][0:1, 0:16], alc[:], ident[0:16, 0:16],
               waits=[(vc, alv), bank_free[2], G["alr"]], incs=[(tsn, 1)])
        ctx.do("scalar", "activation", alr[:], pk[2][0:1, 0:16], AF.Identity,
               waits=[(tsn, tsn.n)], incs=[(ev, 1)])
        alrv = ev.n
        bank_free[2] = (ev, alrv)
        ctx.do("tensor", "matmul", pk[2][:, 0:16], onesr[:], alr[:], start=True,
               stop=True, waits=[(ev, alrv), bank_free[2]], incs=[(tsn, 1)])
        ctx.do("scalar", "activation", alb[:], pk[2][:, 0:16], AF.Identity,
               waits=[(tsn, tsn.n), G["alb"]], incs=[(ev, 1)])
        albv = ev.n
        bank_free[2] = (ev, albv)
        ctx.do("scalar", "activation", alr[:], alr[:], AF.Exp,
               waits=[(ev, albv)], incs=[(ev, 1)])
        cdrv = ev.n
        ctx.do("tensor", "matmul", pk[2][0:64, 0:16], onesr[0:1, 0:64], alr[:],
               start=True, stop=True, waits=[(ev, cdrv), bank_free[2]], incs=[(tsn, 1)])
        ctx.do("scalar", "activation", cdb[:], pk[2][0:64, 0:16], AF.Identity,
               waits=[(tsn, tsn.n), G["cdb"]], incs=[(ev, 1)])
        cdbv = ev.n
        bank_free[2] = (ev, cdbv)
        dtdv = []
        for tt in range(2):
            ctx.do("vector", "tensor_tensor", dtds[tt][:], alb[:], acT[tt][:],
                   ALU.subtract, waits=[(ev, albv), (vc, vc.n), G["dtds"]],
                   incs=[(vc, 1)])
            ctx.do("scalar", "activation", dtds[tt][:], dtds[tt][:], AF.Exp,
                   waits=[(vc, vc.n)], incs=[(ev, 1)])
            ctx.do("vector", "tensor_tensor", dtds[tt][:], dtds[tt][:], dtT[tt][:],
                   ALU.mult, waits=[(ev, ev.n)], incs=[(vc, 1)])
            dtdv.append(vc.n)
        scv = []
        for kt in range(2):
            ctx.do("tensor", "matmul", pk[0][:, 0:CH], Bcf[:, kt * 128:(kt + 1) * 128],
                   Ccf[:], start=True, stop=True,
                   waits=[(ld, lv), bank_free[0]], incs=[(tsn, 1)])
            ctx.do("vector", "tensor_tensor", scT[kt][:], pk[0][:, 0:CH],
                   mk[:, kt * CH:(kt + 1) * CH], ALU.mult,
                   waits=[(tsn, tsn.n), (ld, ld0), G["scT"]], incs=[(vc, 1)])
            bank_free[0] = (vc, vc.n)
            scv.append(vc.n)
        ctx.do("scalar", "activation", prevb[:], prev[:], AF.Identity,
               waits=[(vc, msv if c == 0 else G["prevdone"][1])], incs=[(ev, 1)])
        pbv = ev.n
        epv = []
        for h in range(NH):
            yb = 5 + (h % 2)
            ctx.do("tensor", "matmul", pk[3][:, 0:CH], sel[:, h * 128:(h + 1) * 128],
                   acum[0:16, :], start=True, stop=True,
                   waits=[(vc, acv), bank_free[3]], incs=[(tsn, 1)])
            bcv = tsn.n
            ctx.do("tensor", "matmul", pk[4][0:64, 0:CH],
                   sel[:, h * 128:h * 128 + 64], eac[:], start=True, stop=True,
                   waits=[(ev, eacv), bank_free[4]], incs=[(tsn, 1)])
            ebv = tsn.n
            ctx.do("vector", "tensor_tensor", Ccs[:], pk[4][0:64, 0:CH], Ccf[:],
                   ALU.mult, waits=[(tsn, ebv), (ld, lv), G["Ccs"]], incs=[(vc, 1)])
            bank_free[4] = (vc, vc.n)
            ccsv = vc.n
            mtv = []
            for kt in range(2):
                ctx.do("vector", "tensor_scalar", arg[kt][:], pk[3][:, 0:CH],
                       colk[kt][:, h:h + 1], 30.0, ALU.add, op1=ALU.min,
                       waits=[(tsn, bcv), (vc, vc.n), G[f"arg{kt}"]], incs=[(vc, 1)])
                ctx.do("scalar", "activation", eaa[kt][:], arg[kt][:], AF.Exp,
                       waits=[(vc, vc.n), G[f"eaa{kt}"]], incs=[(ev, 1)])
                ctx.do("gpsimd", "tensor_tensor", MT[kt][:], eaa[kt][:], scT[kt][:],
                       ALU.mult, waits=[(ev, ev.n), (vc, scv[kt]), G[f"MT{kt}"]],
                       incs=[(gp, 1)])
                G[f"arg{kt}"] = (ev, ev.n)
                G[f"eaa{kt}"] = (gp, gp.n)
                mtv.append(gp.n)
            bank_free[3] = (vc, vc.n)
            for kt in range(2):
                ctx.do("tensor", "matmul", pk[yb][:, 0:CH],
                       xst[kt][:, h * 128:(h + 1) * 128], MT[kt][:],
                       start=(kt == 0), stop=False,
                       waits=[(gp, mtv[kt]), (xtie, xstv), bank_free[yb]],
                       incs=[(tsn, 1)])
                G[f"MT{kt}"] = (tsn, tsn.n)
            ctx.do("tensor", "matmul", pk[yb][:, 0:CH],
                   prevb[0:64, h * 128:(h + 1) * 128], Ccs[0:64, :],
                   start=False, stop=True, waits=[(ev, pbv), (vc, ccsv)],
                   incs=[(tsn, 1)])
            yv = tsn.n
            G["Ccs"] = (tsn, yv)
            ctx.do("vector", "scalar_tensor_tensor", Yf[:, h * CH:(h + 1) * CH],
                   xsf[:, h * CH:(h + 1) * CH], Dpb[:, h:h + 1], pk[yb][:, 0:CH],
                   ALU.mult, ALU.add,
                   waits=[(tsn, yv), (ev, dpv), (ld, lv),
                          (ev, ysqv[h]), (gp, gavp[h])],
                   incs=[(vc, 1)])
            bank_free[yb] = (vc, vc.n)
            epv.append(vc.n)
            bcwv = []
            for tt in range(2):
                ctx.do("gpsimd", "tensor_scalar", Bcw[tt][:], Bct[tt][:],
                       dtds[tt][:, h:h + 1], None, ALU.mult,
                       waits=[(xtie, xstv), (vc, dtdv[tt]), G["Bcw"]], incs=[(gp, 1)])
                bcwv.append(gp.n)
            for tt in range(2):
                ctx.do("tensor", "matmul", pk[7][0:64, 0:128], Bcw[tt][:],
                       xst[tt][:, h * 128:(h + 1) * 128],
                       start=(tt == 0), stop=(tt == 1),
                       waits=[(gp, bcwv[1])] + ([bank_free[7]] if tt == 0 else []),
                       incs=[(tsn, 1)])
            spv = tsn.n
            G["Bcw"] = (tsn, spv)
            ctx.do("vector", "scalar_tensor_tensor", prev[:, h * 128:(h + 1) * 128],
                   prev[:, h * 128:(h + 1) * 128], cdb[0:64, h:h + 1],
                   pk[7][0:64, 0:128], ALU.mult, ALU.add,
                   waits=[(tsn, spv), (ev, max(pbv, cdbv)), (vc, vc.n)], incs=[(vc, 1)])
            bank_free[7] = (vc, vc.n)
        G["prevdone"] = (vc, vc.n)
        G["acum"] = (tsn, bcv)
        G["eac"] = (tsn, ebv)
        G["scT"] = (gp, max(mtv))
        G["acT"] = (vc, max(dtdv))
        G["dtT"] = (vc, max(dtdv))
        G["alb"] = (vc, max(dtdv))
        G["alr"] = (tsn, tsn.n)
        G["cdb"] = (vc, vc.n)
        G["dtds"] = (gp, gp.n)
        G["Bct"] = (gp, gp.n)
        gav = []
        for ec in range(16):
            ctx.do("gpsimd", "tensor_tensor", Yf[:, ec * CH:(ec + 1) * CH],
                   Yf[:, ec * CH:(ec + 1) * CH], zsc[:, ec * CH:(ec + 1) * CH],
                   ALU.mult, waits=[(vc, epv[ec]), (gl, gv)], incs=[(gp, 1)])
            gav.append(gp.n)
        G["zsc"] = (gp, gp.n)
        gavp = list(gav)
        rmm = []
        for ec in range(16):
            qi = ec % 2
            ctx.do("scalar", "activation", ysq[qi][:], Yf[:, ec * CH:(ec + 1) * CH],
                   AF.Square, waits=[(gp, gav[ec]),
                                     (tsn, rmm[ec - 2]) if ec >= 2 else (tsn, 0)],
                   incs=[(ev, 1)])
            ysqv[ec] = ev.n
            ctx.do("tensor", "matmul", pk[0][0:1, 0:CH], onesc[:], ysq[qi][:],
                   start=(ec == 0), stop=(ec == 15),
                   waits=[(ev, ev.n)] + ([bank_free[0]] if ec == 0 else []),
                   incs=[(tsn, 1)])
            rmm.append(tsn.n)
        ctx.do("scalar", "activation", r1[:], pk[0][0:1, 0:CH], AF.Ln,
               scale=1.0 / DI, bias=epsc[0:1, 0:1], waits=[(tsn, rmm[15])],
               incs=[(ev, 1)])
        bank_free[0] = (ev, ev.n)
        ctx.do("scalar", "activation", r1[:], r1[:], AF.Exp, scale=-0.5,
               waits=[(ev, ev.n)], incs=[(ev, 1)])
        rsv = ev.n
        ctx.do("tensor", "matmul", pk[2][:, 0:CH], onesr[:], r1[:], start=True,
               stop=True, waits=[(ev, rsv), bank_free[2]], incs=[(tsn, 1)])
        rbv = tsn.n
        ynv = []
        for ec in range(16):
            ctx.do("vector", "scalar_tensor_tensor", ynb[:, ec * CH:(ec + 1) * CH],
                   Yf[:, ec * CH:(ec + 1) * CH], nw[:, ec:ec + 1], pk[2][:, 0:CH],
                   ALU.mult, ALU.mult,
                   waits=[(tsn, rbv), (ev, nwv), (gp, gav[ec]), G["ynb"]],
                   incs=[(vc, 1)])
            ynv.append(vc.n)
        bank_free[2] = (vc, vc.n)
        G["Yf"] = (vc, vc.n)
        opv = 0
        for idx, (tt, db) in enumerate([(a, b) for a in range(2) for b in range(2)]):
            ob_ = pk[3] if idx % 2 == 0 else pk[4]
            obi = 3 if idx % 2 == 0 else 4
            for ec in range(16):
                ctx.do("tensor", "matmul", ob_[:, 0:512],
                       ynb[:, ec * CH + tt * 128:ec * CH + (tt + 1) * 128],
                       WoT[:, ec * 1024 + db * 512:ec * 1024 + (db + 1) * 512],
                       start=(ec == 0), stop=(ec == 15),
                       waits=[(vc, ynv[15]), (bie, wov), bank_free[obi]]
                       if ec == 0 else (), incs=[(tsn, 1)] if ec == 15 else ())
            opv = tsn.n
            oi = idx % 2
            ctx.do("scalar", "activation", osb[oi][:], ob_[:, 0:512], AF.Identity,
                   waits=[(tsn, opv), (so_[oi], so_[oi].n)], incs=[(ev, 1)])
            bank_free[obi] = (ev, ev.n)
            ctx.do("scalar", "dma_start",
                   out=dram["macc"][t0 + tt * 128:t0 + (tt + 1) * 128,
                                    db * 512:(db + 1) * 512],
                   in_=osb[oi][:], waits=[(ev, ev.n)], incs=[(so_[oi], 16)])
        G["ynb"] = (tsn, opv)
        G["xsf"] = (vc, epv[15])
        G["xst"] = (tsn, spv)
        G["Bcf"] = (tsn, tsn.n)
        G["Ccf"] = (vc, vc.n)
        if c + 1 < NCH:
            load_chunk(c + 1)
    ctx.wait("scalar", so_[0], so_[0].n)
    ctx.wait("scalar", so_[1], so_[1].n)


def emit_p2(ctx, dram, es):
    nc = ctx.nc
    NT = 8
    TB = 512

    W_ = {k: es.enter_context(nc.sbuf_tensor(f"C{k}", [128, 8 * 1024], BF16))
          for k in ["r1", "r2", "k1", "k2", "v"]}
    mur = es.enter_context(nc.sbuf_tensor("Cmur", [128, 8], F32))
    muk = es.enter_context(nc.sbuf_tensor("Cmuk", [128, 8], F32))
    omr = es.enter_context(nc.sbuf_tensor("Comr", [128, 8], F32))
    omk = es.enter_context(nc.sbuf_tensor("Comk", [128, 8], F32))
    mup = es.enter_context(nc.sbuf_tensor("Cmup", [8, 128], F32))
    ident = es.enter_context(nc.sbuf_tensor("Cident", [128, 128], F32))
    identb_c = es.enter_context(nc.sbuf_tensor("Cidb", [128, 128], BF16))
    xnblk = [es.enter_context(nc.sbuf_tensor(f"Cxn{i}", [128, 8 * TB], BF16))
             for i in range(2)]
    xpblk = [es.enter_context(nc.sbuf_tensor(f"Cxp{i}", [128, 8 * TB], BF16))
             for i in range(2)]
    kab = es.enter_context(nc.sbuf_tensor("Ckab", [128, 8 * TB], BF16))
    ktmp = [es.enter_context(nc.sbuf_tensor(f"Ckt{i}", [128, TB], F32)) for i in range(2)]
    rbuf = es.enter_context(nc.sbuf_tensor("Crbuf", [128, 8 * 512], BF16))
    mtile = es.enter_context(nc.sbuf_tensor("Cmt", [128, 4 * 1024], F32))
    utile = es.enter_context(nc.sbuf_tensor("Cut", [128, 4 * 1024], F32))
    t2 = [es.enter_context(nc.sbuf_tensor(f"Ct2{i}", [128, 512], F32)) for i in range(2)]
    x1b = [es.enter_context(nc.sbuf_tensor(f"Cx1{i}", [128, 512], F32)) for i in range(2)]
    wstage = [es.enter_context(nc.sbuf_tensor(f"Cwst{i}", [128, 1024], BF16))
              for i in range(2)]
    pk = [es.enter_context(nc.psum_tensor(f"pC{i}", [128, 512], F32)) for i in range(7)]
    pTb = es.enter_context(nc.psum_tensor("CpT", [128, 128], BF16))

    ld = ctx.sem("Cld"); gl = ctx.sem("Cgl"); tsn = ctx.sem("Cts")
    ev = ctx.sem("Cev"); vc = ctx.sem("Cvc"); gp = ctx.sem("Cgp")
    mu_ld = ctx.sem("Cmu")
    xst_ = [ctx.sem("Cxo0"), ctx.sem("Cxo1")]
    xbl = [ctx.sem("Cxb0"), ctx.sem("Cxb1")]
    cws = [ctx.sem("Wst0"), ctx.sem("Wst1")]
    cie = ctx.sem("Tie")

    identd = nc.inline_tensor(np.eye(128, dtype=np.float32), "identC_d").ap()
    identbd = nc.inline_tensor(np.eye(128, dtype=BF), "identCb_d").ap()
    ctx.do("sync", "dma_start", out=ident[:], in_=identd[:], incs=[(ld, 16)])
    ctx.do("sync", "dma_start", out=identb_c[:], in_=identbd[:], incs=[(ld, 16)])
    muv = {}
    for i, nm in enumerate(["L0_ffn_mu_r", "L0_ffn_mu_k"]):
        ctx.do("gpsimd", "dma_start", out=mup[:],
               in_=dram[nm].rearrange("(c p) -> c p", p=128),
               waits=[(tsn, muv.get("tr", 0))], incs=[(gl, 16)])
        ctx.do("tensor", "transpose", pk[0][:, 0:8], mup[:], ident[0:8, 0:8],
               waits=[(gl, gl.n), (ld, 32)] + ([(ev, muv["ev"])] if i else []),
               incs=[(tsn, 1)])
        muv["tr"] = tsn.n
        ctx.do("scalar", "activation", (mur if i == 0 else muk)[:], pk[0][:, 0:8],
               AF.Identity, waits=[(tsn, tsn.n)], incs=[(ev, 1)])
        muv["ev"] = ev.n
    ctx.do("vector", "tensor_scalar", omr[:], mur[:], -1.0, 1.0, ALU.mult,
           op1=ALU.add, waits=[(ev, muv["ev"])], incs=[(vc, 1)])
    ctx.do("vector", "tensor_scalar", omk[:], muk[:], -1.0, 1.0, ALU.mult,
           op1=ALU.add, incs=[(vc, 1)])
    omv = vc.n
    _g2 = ctx.sem("ptg2")
    ptfree = [0, 0]
    wsfree = [0, 0]
    pi = 0
    plan = [("r1", "wr_bf", mur), ("r2", "wr_bf", omr), ("k1", "wk_bf", muk),
            ("k2", "wk_bf", omk), ("v", "wv_bf", None)]
    for key, nm, sc in plan:
        Wt = W_[key]
        for m in range(8):
            s = cws[pi % 2]
            ctx.do("sync", "dma_start", out=wstage[pi % 2][:],
                   in_=dram[nm][m * 128:(m + 1) * 128, :],
                   waits=[(_g2, _g2.n), (tsn, wsfree[pi % 2])], incs=[(s, 16)])
            for dc in range(8):
                ctx.do("tensor", "transpose", pTb[:],
                       wstage[pi % 2][:, dc * 128:(dc + 1) * 128], identb_c[:],
                       waits=[(s, s.n), (cie, ptfree[0]), (ld, 32),
                              (vc, omv)], incs=[(tsn, 1)])
                kw = {}
                if sc is not None:
                    kw["scale"] = sc[:, dc:dc + 1]
                ctx.do("scalar", "activation",
                       Wt[:, dc * 1024 + m * 128: dc * 1024 + m * 128 + 128],
                       pTb[:], AF.Identity, waits=[(tsn, tsn.n)],
                       incs=[(cie, 1)], **kw)
                ptfree[0] = cie.n
            wsfree[pi % 2] = tsn.n
            pi += 1
    wiv = cie.n

    def load_xn(tb, w=()):
        s = xbl[tb % 2]
        ctx.do("sync", "dma_start",
               out=xnblk[tb % 2][:].rearrange("p (dc t) -> p dc t", t=TB),
               in_=dram["xn0f"][0:D, 1 + tb * TB:1 + (tb + 1) * TB]
               .rearrange("(dc p) t -> p dc t", p=128), waits=w, incs=[(s, 16)])
        ctx.do("sync", "dma_start",
               out=xpblk[tb % 2][:].rearrange("p (dc t) -> p dc t", t=TB),
               in_=dram["xn0f"][0:D, tb * TB:(tb + 1) * TB]
               .rearrange("(dc p) t -> p dc t", p=128), incs=[(s, 16)])
        return (s, s.n)

    xnv = [None] * NT
    xnv[0] = load_xn(0)
    xnv[1] = load_xn(1)
    bank_free = {i: (tsn, 0) for i in range(7)}
    bank_free[0] = (ev, muv["ev"])
    t2f = [(vc, 0), (vc, 0)]
    x1f = [(xst_[0], 0), (xst_[1], 0)]
    ktf = [(ev, 0), (ev, 0)]
    rb_free = (vc, 0)
    ka_free = (tsn, 0)
    mload = (vc, 0)
    lastmm = [0] * NT
    for tb in range(NT):
        ctx.do("sync", "dma_start",
               out=mtile[:].rearrange("p (tt d) -> p tt d", d=1024),
               in_=dram["macc"][tb * TB:(tb + 1) * TB, :]
               .rearrange("(tt p) d -> p tt d", p=128),
               waits=[mload], incs=[(mu_ld, 16)])
        ctx.do("sync", "dma_start",
               out=utile[:].rearrange("p (tt d) -> p tt d", d=1024),
               in_=dram["u"][tb * TB:(tb + 1) * TB, :]
               .rearrange("(tt p) d -> p tt d", p=128), incs=[(mu_ld, 16)])
        mlv = mu_ld.n
        rev = []
        for tt in range(4):
            for db in range(2):
                bi = (tt * 2 + db) % 3
                for half, (Wt, xb) in enumerate(
                        [(W_["r1"], xnblk[tb % 2]), (W_["r2"], xpblk[tb % 2])]):
                    for dc in range(8):
                        ctx.do("tensor", "matmul", pk[bi][:, 0:512],
                               xb[:, dc * TB + tt * 128:dc * TB + (tt + 1) * 128],
                               Wt[:, dc * 1024 + db * 512:dc * 1024 + (db + 1) * 512],
                               start=(half == 0 and dc == 0),
                               stop=(half == 1 and dc == 7),
                               waits=[xnv[tb], (cie, wiv), bank_free[bi]]
                               if half == 0 and dc == 0 else (),
                               incs=[(tsn, 1)] if half == 1 and dc == 7 else ())
                ctx.do("scalar", "activation", rbuf[:, (tt * 2 + db) * 512:
                                                    (tt * 2 + db + 1) * 512],
                       pk[bi][:, 0:512], AF.Sigmoid,
                       waits=[(tsn, tsn.n), rb_free], incs=[(ev, 1)])
                bank_free[bi] = (ev, ev.n)
                rev.append(ev.n)
        kav = []
        for m in range(8):
            bi = 3 + m % 2
            for half, (Wt, xb) in enumerate(
                    [(W_["k1"], xnblk[tb % 2]), (W_["k2"], xpblk[tb % 2])]):
                for dc in range(8):
                    ctx.do("tensor", "matmul", pk[bi][:, 0:512],
                           Wt[:, dc * 1024 + m * 128:dc * 1024 + m * 128 + 128],
                           xb[:, dc * TB:(dc + 1) * TB],
                           start=(half == 0 and dc == 0),
                           stop=(half == 1 and dc == 7),
                           waits=[xnv[tb], (cie, wiv), bank_free[bi], ka_free]
                           if half == 0 and dc == 0 else (),
                           incs=[(tsn, 1)] if half == 1 and dc == 7 else ())
            mmv = tsn.n
            qi = m % 2
            ctx.do("vector", "tensor_scalar", ktmp[qi][:], pk[bi][:, 0:512], 0.0,
                   None, ALU.max, waits=[(tsn, mmv), ktf[qi]], incs=[(vc, 1)])
            bank_free[bi] = (vc, vc.n)
            ctx.do("scalar", "activation", kab[:, m * TB:(m + 1) * TB], ktmp[qi][:],
                   AF.Square, waits=[(vc, vc.n)], incs=[(ev, 1)])
            ktf[qi] = (ev, ev.n)
            kav.append(ev.n)
        if tb + 2 < NT:
            xnv[tb + 2] = load_xn(tb + 2, w=[(tsn, tsn.n)])
        for tt in range(4):
            for db in range(2):
                bi = 5 + db
                for m in range(8):
                    ctx.do("tensor", "matmul", pk[bi][:, 0:512],
                           kab[:, m * TB + tt * 128:m * TB + (tt + 1) * 128],
                           W_["v"][:, m * 1024 + db * 512:m * 1024 + (db + 1) * 512],
                           start=(m == 0), stop=(m == 7),
                           waits=[(ev, kav[7]), (cie, wiv), bank_free[bi]]
                           if m == 0 else (), incs=[(tsn, 1)] if m == 7 else ())
                mmv = tsn.n
                oi = db
                ctx.do("vector", "tensor_tensor", t2[oi][:], pk[bi][:, 0:512],
                       rbuf[:, (tt * 2 + db) * 512:(tt * 2 + db + 1) * 512],
                       ALU.mult, waits=[(tsn, mmv), (ev, rev[tt * 2 + db]),
                                        t2f[oi]], incs=[(vc, 1)])
                bank_free[bi] = (vc, vc.n)
                ctx.do("gpsimd", "tensor_tensor", t2[oi][:], t2[oi][:],
                       mtile[:, tt * 1024 + db * 512:tt * 1024 + (db + 1) * 512],
                       ALU.add, waits=[(vc, vc.n), (mu_ld, mlv)], incs=[(gp, 1)])
                ctx.do("vector", "tensor_tensor", x1b[oi][:], t2[oi][:],
                       utile[:, tt * 1024 + db * 512:tt * 1024 + (db + 1) * 512],
                       ALU.add, waits=[(gp, gp.n), x1f[oi]], incs=[(vc, 1)])
                t2f[oi] = (vc, vc.n)
                ctx.do("scalar", "dma_start",
                       out=dram["x1"][tb * TB + tt * 128:tb * TB + (tt + 1) * 128,
                                      db * 512:(db + 1) * 512],
                       in_=x1b[oi][:], waits=[(vc, vc.n)], incs=[(xst_[oi], 16)])
                x1f[oi] = (xst_[oi], xst_[oi].n)
        lastmm[tb] = tsn.n
        ka_free = (tsn, tsn.n)
        rb_free = (vc, vc.n)
        mload = (vc, vc.n)
    ctx.wait("scalar", xst_[0], xst_[0].n)
    ctx.wait("scalar", xst_[1], xst_[1].n)


def emit_p4a(ctx, dram, es):
    nc = ctx.nc
    xnf = es.enter_context(nc.sbuf_tensor("Dxnf", [128, 8 * T], BF16))
    qwS = es.enter_context(nc.sbuf_tensor("DqwS", [128, 8 * 1024], BF16))
    kwS = es.enter_context(nc.sbuf_tensor("DkwS", [128, 8 * 1024], BF16))
    vwS = es.enter_context(nc.sbuf_tensor("DvwS", [128, 8 * 1024], BF16))
    ob = [es.enter_context(nc.sbuf_tensor(f"Dob{i}", [128, 512], BF16)) for i in range(4)]
    pk = [es.enter_context(nc.psum_tensor(f"pD{i}", [128, 512], F32)) for i in range(4)]

    ld = ctx.sem("Dld"); tsn = ctx.sem("Dts"); ev = ctx.sem("Dev")
    do_ = [ctx.sem(f"Do{i}") for i in range(4)]

    ctx.do("sync", "dma_start",
           out=xnf[:].rearrange("p (dc t) -> p dc t", t=T),
           in_=dram["xn1f"][0:D, 1:1 + T].rearrange("(dc p) t -> p dc t", p=128),
           incs=[(ld, 16)])
    _g3 = ctx.sem("ptg3")
    for Wt, nm in [(qwS, "qw_bf"), (kwS, "kw_bf"), (vwS, "vw_bf")]:
        ctx.do("sync", "dma_start",
               out=Wt[:].rearrange("p (dc j) -> p dc j", j=1024),
               in_=dram[nm].rearrange("(dc p) j -> p dc j", p=128),
               waits=[(_g3, _g3.n)], incs=[(ld, 16)])
    ld0 = ld.n
    bank_free = {i: (ev, 0) for i in range(4)}
    obst = [0] * 4
    nb = 0
    for kind in ["k", "q", "v"]:
        if kind in ("k", "q"):
            wS = kwS if kind == "k" else qwS
            for tb in range(8):
                for m in range(8):
                    for dc in range(8):
                        ctx.do("tensor", "matmul", pk[nb][:, 0:512],
                               wS[:, dc * 1024 + m * 128:dc * 1024 + (m + 1) * 128],
                               xnf[:, dc * T + tb * 512:dc * T + (tb + 1) * 512],
                               start=(dc == 0), stop=(dc == 7),
                               waits=[(ld, ld0), bank_free[nb]] if dc == 0 else (),
                               incs=[(tsn, 1)] if dc == 7 else ())
                    ctx.do("scalar", "activation", ob[nb][:], pk[nb][:, 0:512],
                           AF.Identity, scale=(1.0 / 32.0 if kind == "q" else 1.0),
                           waits=[(tsn, tsn.n), (do_[nb], do_[nb].n)], incs=[(ev, 1)])
                    bank_free[nb] = (ev, ev.n)
                    ctx.do("scalar", "dma_start",
                           out=dram["kf" if kind == "k" else "qf"]
                           [m * 128:(m + 1) * 128, tb * 512:(tb + 1) * 512],
                           in_=ob[nb][:], waits=[(ev, ev.n)], incs=[(do_[nb], 16)])
                    nb = (nb + 1) % 4
        else:
            for tt in range(32):
                for db in range(2):
                    for dc in range(8):
                        ctx.do("tensor", "matmul", pk[nb][:, 0:512],
                               xnf[:, dc * T + tt * 128:dc * T + (tt + 1) * 128],
                               vwS[:, dc * 1024 + db * 512:dc * 1024 + (db + 1) * 512],
                               start=(dc == 0), stop=(dc == 7),
                               waits=[(ld, ld0), bank_free[nb]] if dc == 0 else (),
                               incs=[(tsn, 1)] if dc == 7 else ())
                    ctx.do("scalar", "activation", ob[nb][:], pk[nb][:, 0:512],
                           AF.Identity, waits=[(tsn, tsn.n), (do_[nb], do_[nb].n)],
                           incs=[(ev, 1)])
                    bank_free[nb] = (ev, ev.n)
                    ctx.do("scalar", "dma_start",
                           out=dram["vt"][tt * 128:(tt + 1) * 128,
                                          db * 512:(db + 1) * 512],
                           in_=ob[nb][:], waits=[(ev, ev.n)], incs=[(do_[nb], 16)])
                    nb = (nb + 1) % 4
    for s in do_:
        ctx.wait("scalar", s, s.n)


def emit_p4b(ctx, dram, es):
    nc = ctx.nc
    Kf = es.enter_context(nc.sbuf_tensor("EKf", [128, 8 * T], BF16))
    Vt = es.enter_context(nc.sbuf_tensor("EVt", [128, 32 * 1024], BF16))
    qt_ = [es.enter_context(nc.sbuf_tensor(f"Eqt{i}", [128, 8 * 128], BF16)) for i in range(2)]
    mkt = es.enter_context(nc.sbuf_tensor("Emkt", [128, 4 * 512], F32))
    identb = es.enter_context(nc.sbuf_tensor("Eib", [128, 128], BF16))
    sms = es.enter_context(nc.sbuf_tensor("Esms", [128, 512], F32))
    pb = es.enter_context(nc.sbuf_tensor("Epb", [128, 512], BF16))
    pTs = es.enter_context(nc.sbuf_tensor("EpTs", [128, 512], BF16))
    acc = es.enter_context(nc.sbuf_tensor("Eacc", [128, 1024], F32))
    mcol = es.enter_context(nc.sbuf_tensor("Emc", [128, 1], F32))
    mbc = es.enter_context(nc.sbuf_tensor("Emb", [128, 1], F32))
    mnc = es.enter_context(nc.sbuf_tensor("Emn", [128, 1], F32))
    ngm = es.enter_context(nc.sbuf_tensor("Engm", [128, 1], F32))
    alp = es.enter_context(nc.sbuf_tensor("Ealp", [128, 1], F32))
    lcol = es.enter_context(nc.sbuf_tensor("Elc", [128, 1], F32))
    lb = es.enter_context(nc.sbuf_tensor("Elb", [128, 1], F32))
    rec = es.enter_context(nc.sbuf_tensor("Erec", [128, 1], F32))
    oatt = [es.enter_context(nc.sbuf_tensor(f"Eo{i}", [128, 1024], F32)) for i in range(2)]
    pS = [es.enter_context(nc.psum_tensor(f"pE{i}", [128, 512], F32)) for i in range(2)]
    pV = [es.enter_context(nc.psum_tensor(f"pEv{i}", [128, 512], F32)) for i in range(2)]
    pT = [es.enter_context(nc.psum_tensor(f"pEt{i}", [128, 128], BF16)) for i in range(4)]

    ld = ctx.sem("Eld"); tsn = ctx.sem("Ets"); ev = ctx.sem("Eev")
    vc = ctx.sem("Evc")
    eq = [ctx.sem("Eq0"), ctx.sem("Eq1")]
    eo = [ctx.sem("Eo0"), ctx.sem("Eo1")]

    identd = nc.inline_tensor(np.eye(128, dtype=BF), "identE_d").ap()
    mkn = np.zeros((128, 4 * 512), dtype=np.float32)
    for v in range(4):
        for r_ in range(128):
            for cb in range(4):
                pass
    for v in range(4):
        cols = np.arange(512)
        rows = np.arange(128)[:, None]
        mkn[:, v * 512:(v + 1) * 512] = np.where(v * 128 + rows >= cols, 0.0, NEG)
    mkd = nc.inline_tensor(mkn, "mkE_d").ap()

    ctx.do("sync", "dma_start", out=Kf[:].rearrange("p (m k) -> p m k", k=T),
           in_=dram["kf"].rearrange("(m p) k -> p m k", p=128), incs=[(ld, 16)])
    ctx.do("sync", "dma_start", out=Vt[:].rearrange("p (tt j) -> p tt j", j=1024),
           in_=dram["vt"].rearrange("(tt p) j -> p tt j", p=128), incs=[(ld, 16)])
    ctx.do("sync", "dma_start", out=mkt[:], in_=mkd[:], incs=[(ld, 16)])
    ctx.do("sync", "dma_start", out=identb[:], in_=identd[:], incs=[(ld, 16)])
    ld0 = ld.n
    ctx.do("vector", "memset", acc[:], 0.0, incs=[(vc, 1)])
    ctx.do("vector", "memset", lcol[:], 1.0, incs=[(vc, 1)])
    msv = vc.n

    qld = {}

    def load_q(qt, w=()):
        s = eq[qt % 2]
        ctx.do("sync", "dma_start",
               out=qt_[qt % 2][:].rearrange("p (jc q) -> p jc q", q=128),
               in_=dram["qf"][0:D, qt * 128:(qt + 1) * 128]
               .rearrange("(jc p) q -> p jc q", p=128),
               waits=w, incs=[(s, 16)])
        qld[qt] = (s, s.n)

    load_q(0)
    bank_free = {k: (vc, 0) for k in ["s0", "s1", "v0", "v1"]}
    pt_free = [(ev, 0)] * 4
    sms_free = (ev, 0)
    pb_free = (tsn, 0)
    pts_free = (tsn, 0)
    ost = [0, 0]
    lastq = 0

    for qt in range(32):
        if qt + 1 < 32:
            load_q(qt + 1, w=[(tsn, lastq)])
        nkb = qt // 4 + 1
        ctx.do("vector", "memset", mcol[:], -3.0e38, waits=[(vc, vc.n)], incs=[(vc, 1)])
        for kb in range(nkb):
            sb = kb % 2
            pSb = pS[sb]
            diag = (kb == qt // 4)
            for jc in range(8):
                ctx.do("tensor", "matmul", pSb[:, 0:512],
                       qt_[qt % 2][:, jc * 128:(jc + 1) * 128],
                       Kf[:, jc * T + kb * 512:jc * T + (kb + 1) * 512],
                       start=(jc == 0), stop=(jc == 7),
                       waits=[qld[qt], (ld, ld0), bank_free[f"s{sb}"]] if jc == 0 else (),
                       incs=[(tsn, 1)] if jc == 7 else ())
            smv = tsn.n
            if diag:
                ctx.do("vector", "tensor_tensor", sms[:], pSb[:, 0:512],
                       mkt[:, (qt % 4) * 512:(qt % 4 + 1) * 512], ALU.add,
                       waits=[(tsn, smv), (ld, ld0), sms_free], incs=[(vc, 1)])
                ctx.do("vector", "reduce_max", mbc[:, 0:1], sms[:], AX.X,
                       waits=[(vc, vc.n)], incs=[(vc, 1)])
                esrc = sms[:]
                bank_free[f"s{sb}"] = (vc, vc.n)
            else:
                ctx.do("vector", "reduce_max", mbc[:, 0:1], pSb[:, 0:512], AX.X,
                       waits=[(tsn, smv), (vc, vc.n)], incs=[(vc, 1)])
                esrc = pSb[:, 0:512]
            rdv = vc.n
            ctx.do("vector", "tensor_tensor", mnc[:, 0:1], mcol[:, 0:1], mbc[:, 0:1],
                   ALU.max, waits=[(vc, rdv)], incs=[(vc, 1)])
            ctx.do("vector", "tensor_scalar", ngm[:, 0:1], mnc[:, 0:1], -1.0, None,
                   ALU.mult, waits=[(vc, vc.n)], incs=[(vc, 1)])
            ngv = vc.n
            ctx.do("scalar", "activation", alp[:, 0:1], mcol[:, 0:1], AF.Exp,
                   bias=ngm[:, 0:1], waits=[(vc, ngv)], incs=[(ev, 1)])
            alv = ev.n
            ctx.do("scalar", "activation", pb[:], esrc, AF.Exp, bias=ngm[:, 0:1],
                   accum_out=lb[:, 0:1], waits=[(ev, alv), pb_free], incs=[(ev, 1)])
            pev = ev.n
            if not diag:
                bank_free[f"s{sb}"] = (ev, pev)
            else:
                sms_free = (ev, pev)
            ctx.do("vector", "tensor_copy", mcol[:, 0:1], mnc[:, 0:1],
                   waits=[(ev, alv)], incs=[(vc, 1)])
            ctx.do("vector", "scalar_tensor_tensor", lcol[:, 0:1], lcol[:, 0:1],
                   alp[:, 0:1], lb[:, 0:1], ALU.mult, ALU.add,
                   waits=[(ev, pev), (vc, vc.n)], incs=[(vc, 1)])
            for k4 in range(4):
                ctx.do("tensor", "transpose", pT[k4][:], pb[:, k4 * 128:(k4 + 1) * 128],
                       identb[:], waits=[(ev, pev), (ld, ld0), pt_free[k4]],
                       incs=[(tsn, 1)])
                ctx.do("scalar", "activation", pTs[:, k4 * 128:(k4 + 1) * 128],
                       pT[k4][:], AF.Identity, waits=[(tsn, tsn.n), pts_free],
                       incs=[(ev, 1)])
                pt_free[k4] = (ev, ev.n)
            ptv = ev.n
            pb_free = (tsn, tsn.n)
            pvv = []
            for db in range(2):
                for k4 in range(4):
                    ctx.do("tensor", "matmul", pV[db][:, 0:512],
                           pTs[:, k4 * 128:(k4 + 1) * 128],
                           Vt[:, (kb * 4 + k4) * 1024 + db * 512:
                              (kb * 4 + k4) * 1024 + (db + 1) * 512],
                           start=(k4 == 0), stop=(k4 == 3),
                           waits=[(ev, ptv), (ld, ld0), bank_free[f"v{db}"]]
                           if k4 == 0 else (), incs=[(tsn, 1)] if k4 == 3 else ())
                pvv.append(tsn.n)
            pts_free = (tsn, tsn.n)
            for db in range(2):
                ctx.do("vector", "scalar_tensor_tensor",
                       acc[:, db * 512:(db + 1) * 512], acc[:, db * 512:(db + 1) * 512],
                       alp[:, 0:1], pV[db][:, 0:512], ALU.mult, ALU.add,
                       waits=[(tsn, pvv[db]), (ev, alv), (vc, vc.n),
                              (vc, msv)], incs=[(vc, 1)])
                bank_free[f"v{db}"] = (vc, vc.n)
        lastq = tsn.n
        ctx.do("vector", "reciprocal", rec[:, 0:1], lcol[:, 0:1],
               waits=[(vc, vc.n)], incs=[(vc, 1)])
        oi = qt % 2
        ctx.do("vector", "tensor_scalar", oatt[oi][:], acc[:], rec[:, 0:1], None,
               ALU.mult, waits=[(vc, vc.n), (eo[oi], ost[oi])], incs=[(vc, 1)])
        ctx.do("scalar", "dma_start", out=dram["attn"][qt * 128:(qt + 1) * 128, :],
               in_=oatt[oi][:], waits=[(vc, vc.n)], incs=[(eo[oi], 16)])
        ost[oi] = eo[oi].n
    ctx.wait("scalar", eo[0], eo[0].n)
    ctx.wait("scalar", eo[1], eo[1].n)


def emit_p5a(ctx, dram, es):
    nc = ctx.nc
    xnf = es.enter_context(nc.sbuf_tensor("Fxnf", [128, 8 * T], BF16))
    W1T = es.enter_context(nc.sbuf_tensor("FW1T", [128, 8 * DH], BF16))
    W2T = es.enter_context(nc.sbuf_tensor("FW2T", [128, 8 * DH], BF16))
    s1c = es.enter_context(nc.sbuf_tensor("Fs1c", [128, 32], F32))
    s2c = es.enter_context(nc.sbuf_tensor("Fs2c", [128, 32], F32))
    svp = es.enter_context(nc.sbuf_tensor("Fsvp", [32, 128], F32))
    ident = es.enter_context(nc.sbuf_tensor("Fident", [128, 128], F32))
    t1 = [es.enter_context(nc.sbuf_tensor(f"Ft1{i}", [128, 512], F32)) for i in range(2)]
    ab = [es.enter_context(nc.sbuf_tensor(f"Fab{i}", [128, 512], BF16)) for i in range(2)]
    pk = [es.enter_context(nc.psum_tensor(f"pF{i}", [128, 512], F32)) for i in range(4)]

    ld = ctx.sem("Fld"); gl = ctx.sem("Fgl"); tsn = ctx.sem("Fts")
    ev = ctx.sem("Fev"); vc = ctx.sem("Fvc")
    fo_ = [ctx.sem("Fo0"), ctx.sem("Fo1")]
    identd = nc.inline_tensor(np.eye(128, dtype=np.float32), "identF_d").ap()

    ctx.do("sync", "dma_start",
           out=xnf[:].rearrange("p (dc t) -> p dc t", t=T),
           in_=dram["xn1f"][0:D, 1:1 + T].rearrange("(dc p) t -> p dc t", p=128),
           incs=[(ld, 16)])
    ctx.do("sync", "dma_start", out=ident[:], in_=identd[:], incs=[(ld, 16)])
    identbd = nc.inline_tensor(np.eye(128, dtype=BF), "identFb_d").ap()
    identb_f = es.enter_context(nc.sbuf_tensor("Fidb", [128, 128], BF16))
    ctx.do("sync", "dma_start", out=identb_f[:], in_=identbd[:], incs=[(ld, 16)])
    _g3 = ctx.sem("ptg3")
    fws = [ctx.sem("Wst0"), ctx.sem("Wst1")]
    fie = ctx.sem("Tie")
    wstage = [es.enter_context(nc.sbuf_tensor(f"Fwst{i}", [128, 1024], BF16))
              for i in range(2)]
    pTb = [es.enter_context(nc.psum_tensor(f"FpT{i}", [128, 128], BF16))
           for i in range(2)]
    ptfree = [0, 0]
    wsfree = [0, 0]
    pi = 0
    for Wt, nm in [(W1T, "w1_bf"), (W2T, "w2_bf")]:
        for m in range(32):
            s = fws[pi % 2]
            ctx.do("sync", "dma_start", out=wstage[pi % 2][:],
                   in_=dram[nm][m * 128:(m + 1) * 128, :],
                   waits=[(_g3, _g3.n), (tsn, wsfree[pi % 2])], incs=[(s, 16)])
            for dc in range(8):
                ctx.do("tensor", "transpose", pTb[dc % 2][:],
                       wstage[pi % 2][:, dc * 128:(dc + 1) * 128], identb_f[:],
                       waits=[(s, s.n), (fie, ptfree[dc % 2]), (ld, 48)],
                       incs=[(tsn, 1)])
                ctx.do("scalar", "activation",
                       Wt[:, dc * DH + m * 128: dc * DH + m * 128 + 128],
                       pTb[dc % 2][:], AF.Identity, waits=[(tsn, tsn.n)],
                       incs=[(fie, 1)])
                ptfree[dc % 2] = fie.n
            wsfree[pi % 2] = tsn.n
            pi += 1
    wiv = fie.n
    ld0 = ld.n
    sv = {}
    for i, nm in enumerate(["L1_ffn_w1_s", "L1_ffn_w2_s"]):
        ctx.do("gpsimd", "dma_start", out=svp[:],
               in_=dram[nm].rearrange("(m p) -> m p", p=128),
               waits=[(tsn, sv.get("tr", 0))], incs=[(gl, 16)])
        ctx.do("tensor", "transpose", pk[0][:, 0:32], svp[:], ident[0:32, 0:32],
               waits=[(gl, gl.n), (ld, ld0)] + ([(ev, sv["ev"])] if i else []),
               incs=[(tsn, 1)])
        sv["tr"] = tsn.n
        ctx.do("scalar", "activation", (s1c if i == 0 else s2c)[:], pk[0][:, 0:32],
               AF.Identity, waits=[(tsn, tsn.n)], incs=[(ev, 1)])
        sv["ev"] = ev.n
    bank_free = {i: ((ev, sv["ev"]) if i == 0 else (tsn, 0)) for i in range(4)}
    t1f = [(vc, 0), (vc, 0)]
    abf = [(fo_[0], 0), (fo_[1], 0)]
    for m in range(32):
        for tb in range(8):
            i = (m * 8 + tb) % 2
            bA, bB = 2 * i, 2 * i + 1
            for dc in range(8):
                ctx.do("tensor", "matmul", pk[bA][:, 0:512],
                       W1T[:, dc * DH + m * 128:dc * DH + (m + 1) * 128],
                       xnf[:, dc * T + tb * 512:dc * T + (tb + 1) * 512],
                       start=(dc == 0), stop=(dc == 7),
                       waits=[(ld, ld0), (fie, wiv), bank_free[bA]] if dc == 0 else ())
            for dc in range(8):
                ctx.do("tensor", "matmul", pk[bB][:, 0:512],
                       W2T[:, dc * DH + m * 128:dc * DH + (m + 1) * 128],
                       xnf[:, dc * T + tb * 512:dc * T + (tb + 1) * 512],
                       start=(dc == 0), stop=(dc == 7),
                       waits=[bank_free[bB]] if dc == 0 else (),
                       incs=[(tsn, 1)] if dc == 7 else ())
            mmv = tsn.n
            ctx.do("scalar", "activation", t1[i][:], pk[bA][:, 0:512], AF.Silu,
                   scale=s1c[:, m:m + 1], waits=[(tsn, mmv), (ev, sv["ev"]),
                                                 t1f[i]], incs=[(ev, 1)])
            bank_free[bA] = (ev, ev.n)
            ctx.do("vector", "scalar_tensor_tensor", ab[i][:], t1[i][:],
                   s2c[:, m:m + 1], pk[bB][:, 0:512], ALU.mult, ALU.mult,
                   waits=[(ev, ev.n), abf[i]], incs=[(vc, 1)])
            t1f[i] = (vc, vc.n)
            bank_free[bB] = (vc, vc.n)
            ctx.do("scalar", "dma_start",
                   out=dram["a5"][m * 128:(m + 1) * 128, tb * 512:(tb + 1) * 512],
                   in_=ab[i][:], waits=[(vc, vc.n)], incs=[(fo_[i], 16)])
            abf[i] = (fo_[i], fo_[i].n)
    ctx.wait("scalar", fo_[0], fo_[0].n)
    ctx.wait("scalar", fo_[1], fo_[1].n)


def _emit_hmm(ctx, dram, es, src_name, w_name, s_name, dst_name, tag):
    nc = ctx.nc
    hT = [es.enter_context(nc.sbuf_tensor(f"{tag}hT{i}", [128, 32 * 512], BF16))
          for i in range(2)]
    ablk = [es.enter_context(nc.sbuf_tensor(f"{tag}ab{i}", [128, 32 * 512], BF16))
            for i in range(2)]
    sc_ = es.enter_context(nc.sbuf_tensor(f"{tag}sc", [128, 32], F32))
    svp = es.enter_context(nc.sbuf_tensor(f"{tag}svp", [32, 128], F32))
    ident = es.enter_context(nc.sbuf_tensor(f"{tag}id", [128, 128], F32))
    ob = [es.enter_context(nc.sbuf_tensor(f"{tag}ob{i}", [128, 512], BF16))
          for i in range(2)]
    pk = [es.enter_context(nc.psum_tensor(f"p{tag}{i}", [128, 512], F32))
          for i in range(4)]

    ld = ctx.sem("Ghld"); tsn = ctx.sem("Ghts")
    ev = ctx.sem("Ghev")
    gs = ctx.sem("Ghgs")
    hw_ = [ctx.sem("Ghhw0"), ctx.sem("Ghhw1")]
    ga_ = [ctx.sem("Ghga0"), ctx.sem("Ghga1")]
    go_ = [ctx.sem("Ghgo0"), ctx.sem("Ghgo1")]
    identd = nc.inline_tensor(np.eye(128, dtype=np.float32), f"id{tag}_d").ap()
    ctx.do("sync", "dma_start", out=ident[:], in_=identd[:], incs=[(ld, 16)])
    ctx.do("gpsimd", "dma_start", out=svp[:],
           in_=dram[s_name].rearrange("(m p) -> m p", p=128), incs=[(gs, 16)])
    ctx.do("tensor", "transpose", pk[0][:, 0:32], svp[:], ident[0:32, 0:32],
           waits=[(gs, gs.n), (ld, ld.n)], incs=[(tsn, 1)])
    ctx.do("scalar", "activation", sc_[:], pk[0][:, 0:32], AF.Identity,
           waits=[(tsn, tsn.n)], incs=[(ev, 1)])
    scv = ev.n

    wldv = {}
    aldv = {}
    mmvb = {}

    _g3 = ctx.sem("ptg3")

    def load_stripe(s8, w=()):
        s = hw_[s8 % 2]
        for ic in range(32):
            ctx.do("sync", "dma_start",
                   out=hT[s8 % 2][:, ic * 512:(ic + 1) * 512],
                   in_=dram[w_name][s8 * 512:(s8 + 1) * 512, ic * 128:(ic + 1) * 128],
                   transpose=True,
                   waits=(list(w) + [(_g3, _g3.n)]) if ic == 0 else (),
                   incs=[(s, 16)])
        wldv[s8] = (s, s.n)

    def load_ablk(s8, tb):
        po = (s8, tb - 2) if tb >= 2 else (s8 - 1, 6 + tb)
        w = [(tsn, mmvb[po])] if po in mmvb else ()
        s = ga_[tb % 2]
        ctx.do("gpsimd", "dma_start",
               out=ablk[tb % 2][:].rearrange("p (ic t) -> p ic t", t=512),
               in_=dram[src_name][0:DH, tb * 512:(tb + 1) * 512]
               .rearrange("(ic p) t -> p ic t", p=128),
               waits=w, incs=[(s, 16)])
        aldv[(s8, tb)] = (s, s.n)

    load_stripe(0)
    load_ablk(0, 0)
    bank_free = {i: ((ev, scv) if i == 0 else (tsn, 0)) for i in range(4)}
    obf = [(go_[0], 0), (go_[1], 0)]
    for s8 in range(8):
        if s8 + 1 < 8:
            load_stripe(s8 + 1, w=[(tsn, mmvb[(s8 - 1, 7)])] if s8 >= 1 else ())
        for tb in range(8):
            nxt = (s8, tb + 1) if tb + 1 < 8 else (s8 + 1, 0)
            if nxt[0] < 8:
                load_ablk(*nxt)
            av = aldv[(s8, tb)]
            for m in range(4):
                bk = (tb * 4 + m) % 4
                for ic in range(32):
                    ctx.do("tensor", "matmul", pk[bk][:, 0:512],
                           hT[s8 % 2][:, ic * 512 + m * 128:ic * 512 + (m + 1) * 128],
                           ablk[tb % 2][:, ic * 512:(ic + 1) * 512],
                           start=(ic == 0), stop=(ic == 31),
                           waits=[wldv[s8], av, bank_free[bk]]
                           if ic == 0 else (), incs=[(tsn, 1)] if ic == 31 else ())
                mmv = tsn.n
                oi = (tb * 4 + m) % 2
                ctx.do("scalar", "activation", ob[oi][:], pk[bk][:, 0:512], AF.Gelu,
                       scale=sc_[:, s8 * 4 + m:s8 * 4 + m + 1],
                       waits=[(tsn, mmv), (ev, scv), obf[oi]], incs=[(ev, 1)])
                bank_free[bk] = (ev, ev.n)
                ctx.do("scalar", "dma_start",
                       out=dram[dst_name][(s8 * 4 + m) * 128:(s8 * 4 + m + 1) * 128,
                                          tb * 512:(tb + 1) * 512],
                       in_=ob[oi][:], waits=[(ev, ev.n)], incs=[(go_[oi], 16)])
                obf[oi] = (go_[oi], go_[oi].n)
            mmvb[(s8, tb)] = tsn.n
    ctx.wait("scalar", go_[0], go_[0].n)
    ctx.wait("scalar", go_[1], go_[1].n)


def emit_p5b(ctx, dram, es):
    _emit_hmm(ctx, dram, es, "a5", "h1_bf", "L1_ffn_h1_s", "b5", "G")


def emit_p5c(ctx, dram, es):
    _emit_hmm(ctx, dram, es, "b5", "h2_bf", "L1_ffn_h2_s", "c5", "H")


def emit_p5d(ctx, dram, es):
    nc = ctx.nc
    W3T = es.enter_context(nc.sbuf_tensor("JW3T", [128, 32 * 1024], BF16))
    cblk = [es.enter_context(nc.sbuf_tensor(f"Jcb{i}", [128, 32 * 512], BF16))
            for i in range(2)]
    s3r = es.enter_context(nc.sbuf_tensor("Js3r", [1, 1024], F32))
    s3b = es.enter_context(nc.sbuf_tensor("Js3b", [128, 1024], F32))
    onesr = es.enter_context(nc.sbuf_tensor("Jones", [1, 128], F32))
    x1t = es.enter_context(nc.sbuf_tensor("Jx1", [128, 4 * 1024], F32))
    att = es.enter_context(nc.sbuf_tensor("Jat", [128, 4 * 1024], F32))
    t2 = [es.enter_context(nc.sbuf_tensor(f"Jt2{i}", [128, 512], F32)) for i in range(2)]
    oub = [es.enter_context(nc.sbuf_tensor(f"Jou{i}", [128, 512], F32)) for i in range(2)]
    pk = [es.enter_context(nc.psum_tensor(f"pJ{i}", [128, 512], F32)) for i in range(4)]

    ld = ctx.sem("Jld"); tsn = ctx.sem("Jts")
    ev = ctx.sem("Jev"); vc = ctx.sem("Jvc"); gp = ctx.sem("Jgp")
    jgs = ctx.sem("Jgs")
    jmu = ctx.sem("Jmu")
    jc_ = [ctx.sem("Jc0"), ctx.sem("Jc1")]
    jo_ = [ctx.sem("Jo0"), ctx.sem("Jo1")]

    _g3 = ctx.sem("ptg3")
    for hc in range(32):
        ctx.do("sync", "dma_start", out=W3T[:, hc * 1024:(hc + 1) * 1024],
               in_=dram["w3_bf"][0:D, hc * 128:(hc + 1) * 128],
               transpose=True, waits=[(_g3, _g3.n)], incs=[(ld, 16)])
    ctx.do("gpsimd", "dma_start", out=s3r[:],
           in_=dram["L1_ffn_w3_s"].rearrange("(a b) -> a b", a=1), incs=[(jgs, 16)])
    ctx.do("vector", "memset", onesr[:], 1.0, incs=[(vc, 1)])
    for db in range(2):
        ctx.do("tensor", "matmul", pk[2 + db][:, 0:512], onesr[:],
               s3r[0:1, db * 512:(db + 1) * 512], start=True, stop=True,
               waits=[(jgs, 16), (vc, 1)], incs=[(tsn, 1)])
        ctx.do("scalar", "activation", s3b[:, db * 512:(db + 1) * 512],
               pk[2 + db][:, 0:512], AF.Identity, waits=[(tsn, tsn.n)],
               incs=[(ev, 1)])
    s3v = ev.n
    ld0 = ld.n

    cldv = {}
    lastc = {}
    mload = (vc, 0)

    def load_c(tb, w=()):
        s = jc_[tb % 2]
        ctx.do("gpsimd", "dma_start",
               out=cblk[tb % 2][:].rearrange("p (hc t) -> p hc t", t=512),
               in_=dram["c5"][0:DH, tb * 512:(tb + 1) * 512]
               .rearrange("(hc p) t -> p hc t", p=128),
               waits=w, incs=[(s, 16)])
        cldv[tb] = (s, s.n)

    load_c(0)
    bank_free = {i: ((ev, s3v) if i >= 2 else (tsn, 0)) for i in range(4)}
    t2f = [(vc, 0), (vc, 0)]
    ouf = [(jo_[0], 0), (jo_[1], 0)]
    for tb in range(8):
        if tb + 1 < 8:
            load_c(tb + 1, w=[(tsn, lastc.get(tb - 1, 0))] if tb >= 1 else ())
        ctx.do("sync", "dma_start",
               out=x1t[:].rearrange("p (tt d) -> p tt d", d=1024),
               in_=dram["x1"][tb * 512:(tb + 1) * 512, :]
               .rearrange("(tt p) d -> p tt d", p=128),
               waits=[mload, (gp, gp.n)], incs=[(jmu, 16)])
        ctx.do("sync", "dma_start",
               out=att[:].rearrange("p (tt d) -> p tt d", d=1024),
               in_=dram["attn"][tb * 512:(tb + 1) * 512, :]
               .rearrange("(tt p) d -> p tt d", p=128), incs=[(jmu, 16)])
        mlv = jmu.n
        for tt in range(4):
            for db in range(2):
                bk = (tt * 2 + db) % 4
                for hc in range(32):
                    ctx.do("tensor", "matmul", pk[bk][:, 0:512],
                           cblk[tb % 2][:, hc * 512 + tt * 128:hc * 512 + (tt + 1) * 128],
                           W3T[:, hc * 1024 + db * 512:hc * 1024 + (db + 1) * 512],
                           start=(hc == 0), stop=(hc == 31),
                           waits=[(ld, ld0), cldv[tb], bank_free[bk]]
                           if hc == 0 else (), incs=[(tsn, 1)] if hc == 31 else ())
                mmv = tsn.n
                oi = db
                ctx.do("vector", "tensor_tensor", t2[oi][:], pk[bk][:, 0:512],
                       s3b[:, db * 512:(db + 1) * 512], ALU.mult,
                       waits=[(tsn, mmv), (ev, s3v), t2f[oi]], incs=[(vc, 1)])
                bank_free[bk] = (vc, vc.n)
                ctx.do("gpsimd", "tensor_tensor", t2[oi][:], t2[oi][:],
                       x1t[:, tt * 1024 + db * 512:tt * 1024 + (db + 1) * 512],
                       ALU.add, waits=[(vc, vc.n), (jmu, mlv)], incs=[(gp, 1)])
                ctx.do("vector", "tensor_tensor", oub[oi][:], t2[oi][:],
                       att[:, tt * 1024 + db * 512:tt * 1024 + (db + 1) * 512],
                       ALU.add, waits=[(gp, gp.n), ouf[oi]], incs=[(vc, 1)])
                t2f[oi] = (vc, vc.n)
                ctx.do("scalar", "dma_start",
                       out=dram["out"][tb * 512 + tt * 128:tb * 512 + (tt + 1) * 128,
                                       db * 512:(db + 1) * 512],
                       in_=oub[oi][:], waits=[(vc, vc.n)], incs=[(jo_[oi], 16)])
                ouf[oi] = (jo_[oi], jo_[oi].n)
        lastc[tb] = tsn.n
        mload = (vc, vc.n)
    ctx.wait("scalar", jo_[0], jo_[0].n)
    ctx.wait("scalar", jo_[1], jo_[1].n)



PHASES = []


def _register_phases():
    PHASES.clear()
    PHASES.append(("pt", emit_pt,
                   [src for _, src, *_ in WCAST],
                   [n + "_bf" for n, *_ in WCAST]))
    PHASES.append(("p0", lambda c, d, e: emit_ln(c, d, e, "u", "xn0", "xn0f", "ln0"),
                   ["u"], ["xn0", "xn0f"]))
    PHASES.append(("p1a", emit_p1a,
                   ["wip_bf", "xn0f", "L0_attn_conv_w", "L0_attn_conv_b",
                    "L0_attn_dt_bias"],
                   ["zs", "xsbc", "dtv", "lndt"]))
    PHASES.append(("p1b", emit_p1b,
                   ["wop_bf", "zs", "xsbc", "dtv", "lndt", "L0_attn_A_log",
                    "L0_attn_Dp", "L0_attn_norm_w"],
                   ["macc"]))
    PHASES.append(("p2", emit_p2,
                   ["xn0f", "wr_bf", "wk_bf", "wv_bf", "L0_ffn_mu_r",
                    "L0_ffn_mu_k", "macc", "u"],
                   ["x1"]))
    PHASES.append(("p3", lambda c, d, e: emit_ln(c, d, e, "x1", "xn1", "xn1f", "ln1"),
                   ["x1"], ["xn1", "xn1f"]))
    PHASES.append(("p4a", emit_p4a,
                   ["xn1f", "qw_bf", "kw_bf", "vw_bf"], ["kf", "vt", "qf"]))
    PHASES.append(("p4b", emit_p4b, ["kf", "vt", "qf"], ["attn"]))
    PHASES.append(("p5a", emit_p5a,
                   ["xn1f", "w1_bf", "w2_bf", "L1_ffn_w1_s", "L1_ffn_w2_s"], ["a5"]))
    PHASES.append(("p5b", emit_p5b, ["a5", "h1_bf", "L1_ffn_h1_s"], ["b5"]))
    PHASES.append(("p5c", emit_p5c, ["b5", "h2_bf", "L1_ffn_h2_s"], ["c5"]))
    PHASES.append(("p5d", emit_p5d,
                   ["c5", "w3_bf", "L1_ffn_w3_s", "x1", "attn"], ["out"]))


_register_phases()


def _declare_dram(nc, names, kinds):
    dram = {}
    for n in names:
        if n == "u":
            shape, dt = MODULE_IN["u"][1], MODULE_IN["u"][2]
        elif n == "out":
            shape, dt = [T, D], F32
        elif n in SCRATCH:
            shape, dt = SCRATCH[n]
        elif n in WEIGHT_IN:
            shape, dt = WEIGHT_IN[n]
        elif n in VEC_IN:
            shape, dt = VEC_IN[n]
        else:
            raise KeyError(n)
        kind = "ExternalInput" if kinds[n] == "in" else "ExternalOutput"
        dram[n] = nc.dram_tensor(n, shape, dt, kind=kind).ap()
    return dram


def _build_phase_program(pname):
    for name, emit, ins, outs in PHASES:
        if name == pname:
            break
    else:
        raise KeyError(pname)
    nc = bass.Bass("TRN2", target_bir_lowering=False)
    kinds = {n: "in" for n in ins}
    kinds.update({n: "out" for n in outs})
    es = contextlib.ExitStack()
    with es:
        dram = _declare_dram(nc, list(kinds), kinds)
        if pname == "pt":
            dram["zpad"] = nc.inline_tensor(
                np.zeros((DIP_PAD - D_IN_PROJ, D), dtype=BF), "zpad").ap()
        ctx = Ctx(nc)
        emit(ctx, dram, es)
        ctx.build_block(es)
    return nc, ins, outs


def _build_fused_program():
    nc = bass.Bass("TRN2", target_bir_lowering=False)
    dram = {}
    dram["u"] = nc.dram_tensor("u", MODULE_IN["u"][1], MODULE_IN["u"][2],
                               kind="ExternalInput").ap()
    for srcn, (shape, dt) in WEIGHT_IN.items():
        dram[srcn] = nc.dram_tensor(srcn, shape, dt, kind="ExternalInput").ap()
    for vn, (shape, dt) in VEC_IN.items():
        dram[vn] = nc.dram_tensor(vn, shape, dt, kind="ExternalInput").ap()
    dram["out"] = nc.dram_tensor("out", [T, D], F32, kind="ExternalOutput").ap()
    for sn, (shape, dt) in SCRATCH.items():
        dram[sn] = nc.dram_tensor(sn, shape, dt, kind="Internal").ap()
    ctx = Ctx(nc)
    ins = set()
    for name, emit, pin, pout in PHASES:
        for n in pin:
            if n == "u" or n in WEIGHT_IN or n in VEC_IN:
                ins.add(n)
        ctx.begin_phase(name)
        with contextlib.ExitStack() as pes:
            emit(ctx, dram, pes)
        if name != "pt":
            ctx.barrier()
    es = contextlib.ExitStack()
    with es:
        ctx.build_block(es)
    return nc, sorted(ins), ["out"]


FUSED = True

_PROG_CACHE = {}


def run_phase(pname, stores):
    if pname not in _PROG_CACHE:
        _PROG_CACHE[pname] = _build_phase_program(pname)
    nc, ins, outs = _PROG_CACHE[pname]
    in_maps = [{n: st[n] for n in ins} for st in stores]
    res = run_bass_kernel_spmd(nc, in_maps, core_ids=list(range(len(stores))))
    for st, r in zip(stores, res.results):
        for n in outs:
            st[n] = r[n]


def make_stores(inputs):
    stores = []
    u = np.ascontiguousarray(inputs["L0_attn_u"])
    for c in range(NCORES):
        st = {"u": u[c]}
        for k, v in inputs.items():
            if k != "L0_attn_u":
                st[k] = np.ascontiguousarray(v)
        stores.append(st)
    return stores


def kernel(**inputs):
    stores = make_stores(inputs)
    if FUSED:
        if "fused" not in _PROG_CACHE:
            _PROG_CACHE["fused"] = _build_fused_program()
        nc, ins, outs = _PROG_CACHE["fused"]
        in_maps = [{n: st[n] for n in ins} for st in stores]
        res = run_bass_kernel_spmd(nc, in_maps, core_ids=list(range(len(stores))))
        for st, r in zip(stores, res.results):
            st["out"] = r["out"]
    else:
        for pname, *_ in PHASES:
            run_phase(pname, stores)
    out = np.stack([st["out"] for st in stores], axis=0)
    return out.astype(np.float32)


# version 11
# speedup vs baseline: 1.2145x; 1.0486x over previous
import contextlib

import numpy as np
import ml_dtypes
import concourse.bass as bass
import concourse.mybir as mybir
from concourse.bass_utils import run_bass_kernel_spmd

F32 = mybir.dt.float32
BF16 = mybir.dt.bfloat16
AF = mybir.ActivationFunctionType
ALU = mybir.AluOpType
AX = mybir.AxisListType

NCORES = 8
T = 4096
D = 1024
DI = 2048
DSTATE = 64
NH = 16
PD = 128
DCONV = 4
CHUNK = 256
NCH = T // CHUNK
CONV_DIM = DI + 2 * DSTATE
D_IN_PROJ = 2 * DI + 2 * DSTATE + NH
DIP_PAD = 4352
DH = 4096

BF = ml_dtypes.bfloat16
EPS = 1e-5
NEG = -1.0e30



class Sem:
    __slots__ = ("name", "n", "h")

    def __init__(self, name):
        self.name = name
        self.n = 0
        self.h = None

    def inc(self, k=1):
        self.n += k
        return self.n


class Ctx:
    ENGINES = ("sync", "tensor", "vector", "scalar", "gpsimd")

    def __init__(self, nc):
        self.nc = nc
        self.ops = {e: [] for e in self.ENGINES}
        self.sems = {}
        self._last_wait = {}

    def sem(self, name):
        if name not in self.sems:
            self.sems[name] = Sem(name)
        return self.sems[name]

    def begin_phase(self, tag):
        pass

    def barrier(self):
        bs = {e: self.sem(f"bar_{e}") for e in self.ENGINES}
        for e in self.ENGINES:
            s = bs[e]
            tgt = s.n + 1
            self.ops[e].append(lambda eng, _s=s, _t=tgt: eng.sem_inc(_s.h, 1))
            s.n = tgt
        for e in self.ENGINES:
            for e2 in self.ENGINES:
                self.wait(e, bs[e2], bs[e2].n)

    def do(self, engine, opname, *args, waits=(), incs=(), **kw):
        wlist = []
        for s, v in waits:
            key = (engine, s.name)
            if self._last_wait.get(key, -1) >= v:
                continue
            self._last_wait[key] = v
            wlist.append((s, v))
        ilist = list(incs)
        assert len(ilist) <= 1, "one sem update per instruction"
        for s, k in ilist:
            s.inc(k)

        def run(eng, _wl=wlist, _il=ilist, _op=opname, _a=args, _kw=kw):
            for s, v in _wl:
                eng.wait_ge(s.h, v)
            inst = getattr(eng, _op)(*_a, **_kw)
            for s, k in _il:
                inst.then_inc(s.h, k)

        self.ops[engine].append(run)

    def wait(self, engine, s, v):
        key = (engine, s.name)
        if self._last_wait.get(key, -1) >= v:
            return
        self._last_wait[key] = v
        self.ops[engine].append(lambda eng, _s=s, _v=v: eng.wait_ge(_s.h, _v))

    def build_block(self, es):
        for s in self.sems.values():
            s.h = es.enter_context(self.nc.semaphore(s.name))
        block = es.enter_context(self.nc.Block())
        for e in self.ENGINES:
            fns = self.ops[e]
            if not fns:
                continue
            getattr(block, e)(lambda eng, _fns=fns: [f(eng) for f in _fns])


def ceil(a, b):
    return -(-a // b)



WCAST = [
    ("wip", "L0_attn_in_proj_w", D_IN_PROJ, D, DIP_PAD),
    ("wop", "L0_attn_out_proj_w", D, DI, None),
    ("wr", "L0_ffn_W_r", D, D, None),
    ("wk", "L0_ffn_W_k", D, D, None),
    ("wv", "L0_ffn_W_v", D, D, None),
    ("qw", "L1_attn_Qw", D, D, None),
    ("kw", "L1_attn_Kw", D, D, None),
    ("vw", "L1_attn_Vw", D, D, None),
    ("w1", "L1_ffn_w1_w", DH, D, None),
    ("w2", "L1_ffn_w2_w", DH, D, None),
    ("h1", "L1_ffn_h1_w", DH, DH, None),
    ("h2", "L1_ffn_h2_w", DH, DH, None),
    ("w3", "L1_ffn_w3_w", D, DH, None),
]

SCRATCH = {}
for _n, _src, _r, _c, _p in WCAST:
    SCRATCH[_n + "_bf"] = ([_p if _p else _r, _c], BF16)
SCRATCH.update({
    "xn0": ([T, D], BF16),
    "xn0f": ([D, T + 8], BF16),
    "xn1f": ([D, T + 8], BF16),
    "zs": ([DI, T], F32),
    "xsbc": ([CONV_DIM, T], BF16),
    "dtv": ([16, T], F32),
    "lndt": ([16, T], F32),
    "macc": ([T, D], F32),
    "x1": ([T, D], F32),
    "xn1": ([T, D], BF16),
    "qf": ([D, T], BF16),
    "kf": ([D, T], BF16),
    "vt": ([T, D], BF16),
    "attn": ([T, D], F32),
    "a5": ([DH, T], BF16),
    "b5": ([DH, T], BF16),
    "c5": ([DH, T], BF16),
})

MODULE_IN = {
    "u": ("L0_attn_u", [T, D], F32),
}

VEC_IN = {
    "L0_attn_conv_w": ([CONV_DIM, DCONV], F32),
    "L0_attn_conv_b": ([CONV_DIM], F32),
    "L0_attn_dt_bias": ([16], F32),
    "L0_attn_A_log": ([16], F32),
    "L0_attn_Dp": ([16], F32),
    "L0_attn_norm_w": ([DI], F32),
    "L0_ffn_mu_r": ([D], F32),
    "L0_ffn_mu_k": ([D], F32),
    "L1_ffn_w1_s": ([DH], F32),
    "L1_ffn_w2_s": ([DH], F32),
    "L1_ffn_h1_s": ([DH], F32),
    "L1_ffn_h2_s": ([DH], F32),
    "L1_ffn_w3_s": ([D], F32),
}

WEIGHT_IN = {src: ([r, c], F32) for _, src, r, c, _ in WCAST}



PT_GROUP = {"wip": "ptg1", "wop": "ptg2", "wr": "ptg2", "wk": "ptg2",
            "wv": "ptg2", "qw": "ptg3", "kw": "ptg3", "vw": "ptg3",
            "w1": "ptg3", "w2": "ptg3", "h1": "ptg3", "h2": "ptg3",
            "w3": "ptg3"}
PT_ORDER = ["wip", "wop", "wr", "wk", "wv", "qw", "kw", "vw", "w1", "w2",
            "h1", "h2", "w3"]


def emit_pt(ctx, dram, es):
    if "zpad" not in dram:
        dram["zpad"] = ctx.nc.inline_tensor(
            np.zeros((DIP_PAD - D_IN_PROJ, D), dtype=BF), "zpad").ap()
    wc = {n: e for (n, *e_) in [(w[0], w) for w in WCAST] for e in [e_]}
    wmap = {w[0]: w for w in WCAST}
    for name in PT_ORDER:
        _, src, rows, cols, pad = wmap[name]
        g = ctx.sem(PT_GROUP[name])
        dst = dram[name + "_bf"]
        step = max(512, min(rows, (1 << 25) // (cols * 4)))
        r = 0
        while r < rows:
            rr = min(step, rows - r)
            ctx.do("gpsimd", "dma_start", out=dst[r:r + rr, :],
                   in_=dram[src][r:r + rr, :], incs=[(g, 16)])
            r += rr
        if pad:
            ctx.do("gpsimd", "dma_start", out=dst[rows:pad, :],
                   in_=dram["zpad"][0:pad - rows, :], incs=[(g, 16)])
    for gn in ["ptg1", "ptg2", "ptg3"]:
        g = ctx.sem(gn)
        ctx.wait("gpsimd", g, g.n)


def emit_ln(ctx, dram, es, src_name, dst_name, fm_name, tag, semtag="ln"):
    nc = ctx.nc
    NTT = T // 128
    xt = [es.enter_context(nc.sbuf_tensor(f"{tag}x{i}", [128, D], F32)) for i in range(2)]
    xq = es.enter_context(nc.sbuf_tensor(f"{tag}xq", [128, D], F32))
    on = [es.enter_context(nc.sbuf_tensor(f"{tag}o{i}", [128, D], BF16)) for i in range(2)]
    fmS = [es.enter_context(nc.sbuf_tensor(f"{tag}f{i}", [128, 8 * 512], BF16)) for i in range(2)]
    idb = es.enter_context(nc.sbuf_tensor(f"{tag}ib", [128, 128], BF16))
    CS = es.enter_context(nc.sbuf_tensor(f"{tag}CS", [128, 32], F32))
    CQ = es.enter_context(nc.sbuf_tensor(f"{tag}CQ", [128, 32], F32))
    NM = es.enter_context(nc.sbuf_tensor(f"{tag}NM", [128, 32], F32))
    CV = es.enter_context(nc.sbuf_tensor(f"{tag}CV", [128, 32], F32))
    CR = es.enter_context(nc.sbuf_tensor(f"{tag}CR", [128, 32], F32))
    cte = es.enter_context(nc.sbuf_tensor(f"{tag}ct", [128, 1], F32))
    pT = [es.enter_context(nc.psum_tensor(f"{tag}pT{i}", [128, 128], BF16)) for i in range(2)]

    ld = [ctx.sem(f"{semtag}ld0"), ctx.sem(f"{semtag}ld1")]
    cn = ctx.sem(f"{semtag}cn")
    sq = ctx.sem(f"{semtag}sq")
    vv = ctx.sem(f"{semtag}vv")
    ts = ctx.sem(f"{semtag}ts")
    fe = ctx.sem(f"{semtag}fe")
    st = [ctx.sem(f"{semtag}st0"), ctx.sem(f"{semtag}st1")]
    fs = [ctx.sem(f"{semtag}fs0"), ctx.sem(f"{semtag}fs1")]
    tc = ctx.sem(f"{semtag}tc")

    identd = nc.inline_tensor(np.eye(128, dtype=BF), f"ib{tag}_d").ap()
    zcold = nc.inline_tensor(np.zeros((D, 1), dtype=BF), f"zc{tag}_d").ap()
    ctx.do("sync", "dma_start", out=idb[:], in_=identd[:], incs=[(ld[0], 16)])
    ctx.do("gpsimd", "dma_start", out=dram[fm_name][0:D, 0:1], in_=zcold[:],
           allow_slow_non_contiguous=True, incs=[(fs[0], 16)])
    ctx.do("vector", "memset", cte[:], float(EPS), incs=[(cn, 1)])
    src, dst, fmd = dram[src_name], dram[dst_name], dram[fm_name]
    ldv = {}
    rdv = {}
    for i in range(NTT):
        b = i % 2
        w = [(vv, rdv[i - 2])] if i >= 2 else []
        ctx.do("sync", "dma_start", out=xt[b][:], in_=src[i * 128:(i + 1) * 128, :],
               waits=w, incs=[(ld[b], 16)])
        lv = (ld[b], ld[b].n)
        ctx.do("scalar", "activation", xq[:], xt[b][:], AF.Square,
               accum_out=CQ[:, i:i + 1], waits=[lv], incs=[(sq, 1)])
        ctx.do("vector", "reduce_sum", CS[:, i:i + 1], xt[b][:], AX.X,
               waits=[lv], incs=[(vv, 1)])
        rdv[i] = vv.n
    sqall, vvall = sq.n, vv.n
    ctx.do("vector", "tensor_scalar", NM[:], CS[:], -1.0 / D, None, ALU.mult,
           waits=[(vv, vvall)], incs=[(tc, 1)])
    ctx.do("vector", "tensor_scalar", CV[:], CQ[:], 1.0 / D, None, ALU.mult,
           waits=[(sq, sqall), (tc, tc.n)], incs=[(tc, 1)])
    ctx.do("vector", "tensor_tensor", CR[:], NM[:], NM[:], ALU.mult,
           waits=[(tc, tc.n)], incs=[(tc, 1)])
    ctx.do("vector", "tensor_tensor", CV[:], CR[:], CV[:], ALU.subtract,
           waits=[(tc, tc.n)], incs=[(vv, 1)])
    ctx.do("scalar", "activation", CR[:], CV[:], AF.Ln, bias=cte[:, 0:1],
           scale=-1.0, waits=[(vv, vv.n), (cn, cn.n), (sq, sqall)])
    ctx.do("scalar", "activation", CR[:], CR[:], AF.Exp, scale=-0.5, incs=[(sq, 1)])
    rstv = sq.n
    ptfree = [fe.n, fe.n]
    for i in range(NTT):
        b = i % 2
        g = i % 8 // 4
        w = [(vv, rdv[i]) if i < 2 else (vv, vv.n)]
        ctx.do("sync", "dma_start", out=xt[b][:], in_=src[i * 128:(i + 1) * 128, :],
               waits=[(ts, ts.n)] if i >= 2 else [(vv, vvall)], incs=[(ld[b], 16)])
        lv = (ld[b], ld[b].n)
        ctx.do("vector", "tensor_scalar", on[b][:], xt[b][:], NM[:, i:i + 1],
               CR[:, i:i + 1], ALU.add, op1=ALU.mult,
               waits=[lv, (sq, rstv), (st[b], st[b].n)], incs=[(vv, 1)])
        xnv = vv.n
        ctx.do("scalar", "dma_start", out=dst[i * 128:(i + 1) * 128, :], in_=on[b][:],
               waits=[(vv, xnv)], incs=[(st[b], 16)])
        for dc in range(8):
            pb = pT[dc % 2]
            ctx.do("tensor", "transpose", pb[:], on[b][:, dc * 128:(dc + 1) * 128],
                   idb[:], waits=[(vv, xnv), (ld[0], 16), (fe, ptfree[dc % 2])],
                   incs=[(ts, 1)])
            ctx.do("scalar", "activation",
                   fmS[g][:, dc * 512 + (i % 4) * 128: dc * 512 + (i % 4) * 128 + 128],
                   pb[:], AF.Identity,
                   waits=[(ts, ts.n), (fs[g], fs[g].n)], incs=[(fe, 1)])
            ptfree[dc % 2] = fe.n
        if i % 4 == 3:
            g0 = (i // 4) * 512
            for dc in range(8):
                ctx.do("scalar", "dma_start",
                       out=fmd[dc * 128:(dc + 1) * 128, 1 + g0:1 + g0 + 512],
                       in_=fmS[g][:, dc * 512:(dc + 1) * 512],
                       waits=[(fe, fe.n)], incs=[(fs[g], 16)])
    ctx.wait("scalar", st[0], st[0].n)
    ctx.wait("scalar", st[1], st[1].n)
    ctx.wait("scalar", fs[0], fs[0].n)
    ctx.wait("scalar", fs[1], fs[1].n)


def emit_p1a(ctx, dram, es):
    nc = ctx.nc
    NT = 8
    TB = 512
    CS = 516

    ipT = es.enter_context(nc.sbuf_tensor("ipT", [128, 8 * DIP_PAD], BF16))
    xnb = [es.enter_context(nc.sbuf_tensor(f"xnb{i}", [128, 8 * TB], BF16))
           for i in range(3)]
    xbc = es.enter_context(nc.sbuf_tensor("xbc", [128, 17 * CS], F32))
    acc = [es.enter_context(nc.sbuf_tensor(f"acc{i}", [128, TB], F32)) for i in range(2)]
    zb = [es.enter_context(nc.sbuf_tensor(f"zb{i}", [128, TB], F32)) for i in range(2)]
    ob = [es.enter_context(nc.sbuf_tensor(f"ob{i}", [128, TB], BF16)) for i in range(2)]
    dtraw = es.enter_context(nc.sbuf_tensor("dtraw", [16, T], F32))
    dps = [es.enter_context(nc.sbuf_tensor(f"dps{i}", [16, T], F32)) for i in range(2)]
    cw = es.enter_context(nc.sbuf_tensor("cw", [128, 17 * 4], F32))
    cbp = es.enter_context(nc.sbuf_tensor("cbp", [17, 128], F32))
    cb = es.enter_context(nc.sbuf_tensor("cb", [128, 17], F32))
    dtb = es.enter_context(nc.sbuf_tensor("dtb", [16, 1], F32))
    ident = es.enter_context(nc.sbuf_tensor("identA", [128, 128], F32))
    c1 = es.enter_context(nc.sbuf_tensor("c1A", [16, 1], F32))
    ps = [es.enter_context(nc.psum_tensor(f"pA{i}", [128, TB], F32)) for i in range(4)]
    pst = es.enter_context(nc.psum_tensor("pAt", [128, 17], F32))

    ld = ctx.sem("Ald")
    xs_ = [ctx.sem("Axn0"), ctx.sem("Axn1"), ctx.sem("Axn2")]
    zs_ = [ctx.sem("Az0"), ctx.sem("Az1")]
    xo_ = [ctx.sem("Ax0"), ctx.sem("Ax1")]
    gl = ctx.sem("Agl")
    ts = ctx.sem("Ats")
    ev = ctx.sem("Aev")
    vc = ctx.sem("Avc")
    st = ctx.sem("Ast")

    identd = nc.inline_tensor(np.eye(128, dtype=np.float32), "identA_d").ap()
    identbd = nc.inline_tensor(np.eye(128, dtype=BF), "identAb_d").ap()
    identb_a = es.enter_context(nc.sbuf_tensor("Aidb", [128, 128], BF16))

    _g1 = ctx.sem("ptg1")
    ctx.do("sync", "dma_start", out=identb_a[:], in_=identbd[:], incs=[(ld, 16)])
    ws_ = [ctx.sem("Wst0"), ctx.sem("Wst1")]
    aie = ctx.sem("Tie")
    wstage = [es.enter_context(nc.sbuf_tensor(f"Awst{i}", [128, 1024], BF16))
              for i in range(2)]
    pTb = [es.enter_context(nc.psum_tensor(f"ApT{i}", [128, 128], BF16))
           for i in range(2)]
    ptfree = [0, 0]
    wsfree = [0, 0]
    for m in range(34):
        s = ws_[m % 2]
        ctx.do("sync", "dma_start", out=wstage[m % 2][:],
               in_=dram["wip_bf"][m * 128:(m + 1) * 128, :],
               waits=[(_g1, _g1.n), (ts, wsfree[m % 2])], incs=[(s, 16)])
        for dc in range(8):
            ctx.do("tensor", "transpose", pTb[dc % 2][:],
                   wstage[m % 2][:, dc * 128:(dc + 1) * 128], identb_a[:],
                   waits=[(s, s.n), (aie, ptfree[dc % 2]), (ld, 16)],
                   incs=[(ts, 1)])
            ctx.do("scalar", "activation",
                   ipT[:, dc * DIP_PAD + m * 128: dc * DIP_PAD + m * 128 + 128],
                   pTb[dc % 2][:], AF.Identity, waits=[(ts, ts.n)], incs=[(aie, 1)])
            ptfree[dc % 2] = aie.n
        wsfree[m % 2] = ts.n
    ipv = aie.n

    xnld = {}

    def load_xn_block(tb, waits=()):
        s = xs_[tb % 3]
        ctx.do("sync", "dma_start",
               out=xnb[tb % 3][:].rearrange("p (dc t) -> p dc t", t=TB),
               in_=dram["xn0f"][0:D, 1 + tb * TB:1 + (tb + 1) * TB]
               .rearrange("(dc p) t -> p dc t", p=128),
               waits=waits, incs=[(s, 16)])
        xnld[tb] = (s, s.n)

    load_xn_block(0)
    load_xn_block(1)
    ctx.do("gpsimd", "dma_start",
           out=cw[:].rearrange("p (c w) -> p c w", w=4),
           in_=dram["L0_attn_conv_w"].rearrange("(c p) w -> p c w", p=128),
           incs=[(gl, 16)])
    ctx.do("gpsimd", "dma_start", out=cbp[:],
           in_=dram["L0_attn_conv_b"].rearrange("(c p) -> c p", p=128),
           incs=[(gl, 16)])
    ctx.do("gpsimd", "dma_start", out=dtb[:],
           in_=dram["L0_attn_dt_bias"].rearrange("(a b) -> a b", b=1),
           incs=[(gl, 16)])
    ctx.do("sync", "dma_start", out=ident[:], in_=identd[:], incs=[(ld, 16)])
    ctx.do("tensor", "transpose", pst[:, 0:17], cbp[:], ident[0:17, 0:17],
           waits=[(gl, 32), (ld, ld.n)], incs=[(ts, 1)])
    ctx.do("scalar", "activation", cb[:], pst[:, 0:17], AF.Identity,
           waits=[(ts, ts.n)], incs=[(ev, 1)])
    ctx.do("vector", "memset", xbc[:], 0.0, incs=[(vc, 1)])
    ctx.do("vector", "memset", c1[:], 1.0, incs=[(vc, 1)])

    gl_all = gl.n
    bank_free = [(ev, 0)] * 4
    nb = 0
    tailv = [0] * 17
    evacv = [[0] * 34 for _ in range(NT)]
    zst = [0, 0]
    ost = [0, 0]
    siluv = [0, 0]

    lastmm = [0] * NT
    for tb in range(NT):
        if tb + 2 < NT:
            load_xn_block(tb + 2, waits=[(ts, lastmm[tb - 1])] if tb >= 1 else ())
        for m in range(34):
            bk = ps[nb]
            w = [(aie, ipv), xnld[tb], bank_free[nb]]
            for k in range(8):
                ctx.do("tensor", "matmul",
                       bk[:, :] if m != 33 else bk[0:16, :],
                       ipT[:, k * DIP_PAD + m * 128: k * DIP_PAD + m * 128 + (128 if m != 33 else 16)],
                       xnb[tb % 3][:, k * TB: k * TB + TB],
                       start=(k == 0), stop=(k == 7),
                       waits=w if k == 0 else (),
                       incs=[(ts, 1)] if k == 7 else ())
            lastmm[tb] = ts.n
            mmv = ts.n
            if m < 16:
                zi = (tb * 16 + m) % 2
                ctx.do("scalar", "activation", zb[zi][:], bk[:], AF.Silu,
                       waits=[(ts, mmv), (zs_[zi], zs_[zi].n)], incs=[(ev, 1)])
                evc = ev.n
                ctx.do("scalar", "dma_start",
                       out=dram["zs"][m * 128:(m + 1) * 128, tb * TB:(tb + 1) * TB],
                       in_=zb[zi][:], waits=[(ev, evc)], incs=[(zs_[zi], 16)])
            elif m < 33:
                cc = m - 16
                r0 = cc * CS
                w2 = [(ts, mmv), (vc, 1)]
                if tb > 0:
                    w2.append((vc, tailv[cc]))
                ctx.do("scalar", "activation", xbc[:, r0 + 3:r0 + 3 + TB], bk[:],
                       AF.Identity, waits=w2, incs=[(ev, 1)])
            else:
                ctx.do("scalar", "activation", dtraw[0:16, tb * TB:(tb + 1) * TB],
                       bk[0:16, :], AF.Identity, waits=[(ts, mmv)], incs=[(ev, 1)])
            evacv[tb][m] = ev.n
            bank_free[nb] = (ev, ev.n)
            nb = (nb + 1) % 4
        for cc in range(17):
            r0 = cc * CS
            ai = cc % 2
            aw = [(ev, max(evacv[tb][16 + cc], siluv[ai])), (gl, gl_all)]
            ctx.do("vector", "tensor_scalar", acc[ai][:], xbc[:, r0:r0 + TB],
                   cw[:, cc * 4:cc * 4 + 1], cb[:, cc:cc + 1], ALU.mult, op1=ALU.add,
                   waits=aw + [(vc, vc.n)], incs=[(vc, 1)])
            for wtap in range(1, 4):
                ctx.do("vector", "scalar_tensor_tensor", acc[ai][:],
                       xbc[:, r0 + wtap:r0 + wtap + TB], cw[:, cc * 4 + wtap:cc * 4 + wtap + 1],
                       acc[ai][:], ALU.mult, ALU.add,
                       waits=[(vc, vc.n)], incs=[(vc, 1)])
            convv = vc.n
            oi = cc % 2
            ctx.do("scalar", "activation", ob[oi][:], acc[ai][:], AF.Silu,
                   waits=[(vc, convv), (xo_[oi], xo_[oi].n)], incs=[(ev, 1)])
            siluv[ai] = ev.n
            ctx.do("scalar", "dma_start",
                   out=dram["xsbc"][cc * 128:(cc + 1) * 128, tb * TB:(tb + 1) * TB],
                   in_=ob[oi][:], waits=[(ev, ev.n)], incs=[(xo_[oi], 16)])
            ctx.do("vector", "tensor_copy", xbc[:, r0:r0 + 3],
                   xbc[:, r0 + TB:r0 + TB + 3], waits=[(vc, convv)], incs=[(vc, 1)])
            tailv[cc] = vc.n
    dn = dtraw[0:16, :]
    d1, d2, d3 = dtraw[0:16, :], dps[0][0:16, :], dps[1][0:16, :]
    ctx.do("vector", "tensor_scalar", d1, dn, dtb[0:16, 0:1], None, ALU.add,
           waits=[(ev, evacv[NT - 1][33]), (gl, gl_all), (vc, vc.n)], incs=[(vc, 1)])
    ctx.do("scalar", "activation", d2, d1, AF.Abs,
           waits=[(vc, vc.n)], incs=[(ev, 1)])
    ctx.do("scalar", "activation", d3, d2, AF.Exp, scale=-1.0,
           waits=[(ev, ev.n)], incs=[(ev, 1)])
    expv = ev.n
    ctx.do("scalar", "activation", d3, d3, AF.Ln, bias=c1[0:16, 0:1],
           waits=[(ev, expv), (vc, 2)], incs=[(ev, 1)])
    ctx.do("vector", "tensor_scalar", d2, d1, 0.0, None, ALU.max,
           waits=[(vc, vc.n), (ev, expv)], incs=[(vc, 1)])
    ctx.do("vector", "tensor_tensor", d1, d2, d3, ALU.add,
           waits=[(vc, vc.n), (ev, ev.n)], incs=[(vc, 1)])
    ctx.do("sync", "dma_start", out=dram["dtv"][:], in_=d1,
           waits=[(vc, vc.n)], incs=[(st, 16)])
    ctx.do("scalar", "activation", d2, d1, AF.Ln, waits=[(vc, vc.n)], incs=[(ev, 1)])
    ctx.do("sync", "dma_start", out=dram["lndt"][:], in_=d2,
           waits=[(ev, ev.n)], incs=[(st, 16)])
    ctx.wait("sync", st, st.n)
    for s in (zs_ + xo_):
        ctx.wait("scalar", s, s.n)


def emit_p1b(ctx, dram, es):
    nc = ctx.nc
    CH = CHUNK

    WoT = es.enter_context(nc.sbuf_tensor("WoT", [128, 16 * 1024], BF16))
    dtv = es.enter_context(nc.sbuf_tensor("dtvS", [16, T], F32))
    lnd = es.enter_context(nc.sbuf_tensor("lndS", [16, T], F32))
    af = es.enter_context(nc.sbuf_tensor("afS", [16, T], F32))
    prev = es.enter_context(nc.sbuf_tensor("prevS", [64, NH * 128], F32))
    prevb = es.enter_context(nc.sbuf_tensor("prevbS", [64, NH * 128], BF16))
    ident = es.enter_context(nc.sbuf_tensor("identB", [128, 128], F32))
    sel = es.enter_context(nc.sbuf_tensor("selB", [16, NH * 128], F32))
    mk = es.enter_context(nc.sbuf_tensor("mkB", [128, 2 * CH], F32))
    onesr = es.enter_context(nc.sbuf_tensor("onesrB", [1, 128], F32))
    onesh = es.enter_context(nc.sbuf_tensor("oneshB", [16, 128], F32))
    onesc = es.enter_context(nc.sbuf_tensor("onescB", [128, 1], BF16))
    Acol = es.enter_context(nc.sbuf_tensor("AcolB", [16, 1], F32))
    Dpb = es.enter_context(nc.sbuf_tensor("DpbB", [128, 16], F32))
    nw = es.enter_context(nc.sbuf_tensor("nwB", [128, 16], F32))
    nwp = es.enter_context(nc.sbuf_tensor("nwpB", [16, 128], F32))
    epsc = es.enter_context(nc.sbuf_tensor("epscB", [1, 1], F32))
    vtmp = es.enter_context(nc.sbuf_tensor("vtmpB", [16, 1], F32))
    xsf = es.enter_context(nc.sbuf_tensor("xsfB", [128, NH * CH], BF16))
    xst = [es.enter_context(nc.sbuf_tensor(f"xstB{i}", [128, DI], BF16)) for i in range(2)]
    Bcf = es.enter_context(nc.sbuf_tensor("BcfB", [64, CH], BF16))
    Ccf = es.enter_context(nc.sbuf_tensor("CcfB", [64, CH], BF16))
    Bct = [es.enter_context(nc.sbuf_tensor(f"BctB{i}", [128, 64], BF16)) for i in range(2)]
    Bcw = [[es.enter_context(nc.sbuf_tensor(f"BcwB{p}{i}", [128, 64], BF16))
            for i in range(2)] for p in range(2)]
    acum = es.enter_context(nc.sbuf_tensor("acumB", [16, CH], F32))
    eac = es.enter_context(nc.sbuf_tensor("eacB", [16, CH], F32))
    acT = [es.enter_context(nc.sbuf_tensor(f"acTB{i}", [128, 16], F32)) for i in range(2)]
    colk = [es.enter_context(nc.sbuf_tensor(f"colkB{i}", [128, 16], F32)) for i in range(2)]
    dtds = [es.enter_context(nc.sbuf_tensor(f"dtdsB{i}", [128, 16], F32)) for i in range(2)]
    dtT = [es.enter_context(nc.sbuf_tensor(f"dtTB{i}", [128, 16], F32)) for i in range(2)]
    alb = es.enter_context(nc.sbuf_tensor("albB", [128, 16], F32))
    alc = es.enter_context(nc.sbuf_tensor("alcB", [16, 1], F32))
    alr = es.enter_context(nc.sbuf_tensor("alrB", [1, 16], F32))
    cdb = es.enter_context(nc.sbuf_tensor("cdbB", [64, 16], F32))
    scT = [es.enter_context(nc.sbuf_tensor(f"scTB{i}", [128, CH], F32)) for i in range(2)]
    arg = [es.enter_context(nc.sbuf_tensor(f"argB{i}", [128, CH], F32)) for i in range(2)]
    eaa = [es.enter_context(nc.sbuf_tensor(f"eaaB{i}", [128, CH], F32)) for i in range(2)]
    MT = [es.enter_context(nc.sbuf_tensor(f"MTB{i}", [128, CH], BF16)) for i in range(4)]
    Ccs = [es.enter_context(nc.sbuf_tensor(f"CcsB{i}", [64, CH], BF16)) for i in range(2)]
    Yf = es.enter_context(nc.sbuf_tensor("YfB", [128, NH * CH], F32))
    zsc = es.enter_context(nc.sbuf_tensor("zscB", [128, NH * CH], F32))
    ysq = [es.enter_context(nc.sbuf_tensor(f"ysqB{i}", [128, CH], BF16)) for i in range(2)]
    r1 = es.enter_context(nc.sbuf_tensor("r1B", [1, CH], F32))
    ynb = es.enter_context(nc.sbuf_tensor("ynbB", [128, NH * CH], BF16))
    osb = [es.enter_context(nc.sbuf_tensor(f"osbB{i}", [128, 512], F32)) for i in range(2)]
    pk = [es.enter_context(nc.psum_tensor(f"pB{i}", [128, 512], F32)) if i != 1
          else None for i in range(8)]
    pkT = es.enter_context(nc.psum_tensor("pBT", [128, 512], BF16))

    ld = ctx.sem("Bld"); gl = ctx.sem("Bgl"); tsn = ctx.sem("Bts")
    ev = ctx.sem("Bev"); vc = ctx.sem("Bvc"); gp = ctx.sem("Bgp")
    gs = ctx.sem("Bgs")
    so_ = [ctx.sem("Bso0"), ctx.sem("Bso1")]

    identd = nc.inline_tensor(np.eye(128, dtype=np.float32), "identB_d").ap()
    seln = np.zeros((16, NH * 128), dtype=np.float32)
    for h in range(NH):
        seln[h, h * 128:(h + 1) * 128] = 1.0
    seld = nc.inline_tensor(seln, "selB_d").ap()
    mkn = np.zeros((128, 2 * CH), dtype=np.float32)
    for kt in range(2):
        for kk in range(128):
            gk = kt * 128 + kk
            mkn[kk, kt * CH + gk:(kt + 1) * CH] = 1.0
    mkd = nc.inline_tensor(mkn, "mkB_d").ap()

    _g2 = ctx.sem("ptg2")
    bws = [ctx.sem("Wst0"), ctx.sem("Wst1")]
    bie = ctx.sem("Tie")
    wstage = [es.enter_context(nc.sbuf_tensor(f"Bwst{i}", [128, DI], BF16))
              for i in range(2)]
    identbd = nc.inline_tensor(np.eye(128, dtype=BF), "identBb_d").ap()
    identb_b = es.enter_context(nc.sbuf_tensor("Bidb", [128, 128], BF16))
    ctx.do("sync", "dma_start", out=identb_b[:], in_=identbd[:], incs=[(ld, 16)])
    _ptfree = [0, 0]
    _wsfree = [0, 0]
    for m in range(8):
        s = bws[m % 2]
        ctx.do("sync", "dma_start", out=wstage[m % 2][:],
               in_=dram["wop_bf"][m * 128:(m + 1) * 128, :],
               waits=[(_g2, _g2.n), (tsn, _wsfree[m % 2])], incs=[(s, 16)])
        for ec in range(16):
            ctx.do("tensor", "transpose", pkT[0:128, 0:128],
                   wstage[m % 2][:, ec * 128:(ec + 1) * 128], identb_b[:],
                   waits=[(s, s.n), (bie, _ptfree[0]), (ld, 16)],
                   incs=[(tsn, 1)])
            ctx.do("scalar", "activation",
                   WoT[:, ec * 1024 + m * 128: ec * 1024 + m * 128 + 128],
                   pkT[0:128, 0:128], AF.Identity, waits=[(tsn, tsn.n)],
                   incs=[(bie, 1)])
            _ptfree[0] = bie.n
        _wsfree[m % 2] = tsn.n
    wov = bie.n
    ctx.do("sync", "dma_start", out=dtv[:], in_=dram["dtv"][:], incs=[(ld, 16)])
    ctx.do("sync", "dma_start", out=lnd[:], in_=dram["lndt"][:], incs=[(ld, 16)])
    ctx.do("sync", "dma_start", out=ident[:], in_=identd[:], incs=[(ld, 16)])
    ctx.do("sync", "dma_start", out=sel[:], in_=seld[:], incs=[(ld, 16)])
    ctx.do("sync", "dma_start", out=mk[:], in_=mkd[:], incs=[(ld, 16)])
    ctx.do("gpsimd", "dma_start", out=vtmp[:],
           in_=dram["L0_attn_A_log"].rearrange("(a b) -> a b", b=1), incs=[(gs, 16)])
    ctx.do("gpsimd", "dma_start", out=Acol[:],
           in_=dram["L0_attn_Dp"].rearrange("(a b) -> a b", b=1), incs=[(gs, 16)])
    ctx.do("gpsimd", "dma_start", out=nwp[:],
           in_=dram["L0_attn_norm_w"].rearrange("(c p) -> c p", p=128), incs=[(gs, 16)])
    ld0, gl0 = ld.n, gs.n

    for t_, v_ in [(onesr, 1.0), (onesh, 1.0), (onesc, 1.0), (epsc, float(EPS))]:
        ctx.do("vector", "memset", t_[:], v_, incs=[(vc, 1)])
    ctx.do("vector", "memset", prev[:], 0.0, incs=[(vc, 1)])
    msv = vc.n
    ctx.do("tensor", "transpose", pk[2][0:1, 0:16], Acol[:], ident[0:16, 0:16],
           waits=[(gs, gl0), (ld, ld0)], incs=[(tsn, 1)])
    ctx.do("scalar", "activation", alr[:], pk[2][0:1, 0:16], AF.Identity,
           waits=[(tsn, tsn.n)], incs=[(ev, 1)])
    ctx.do("tensor", "matmul", pk[2][:, 0:16], onesr[:], alr[:], start=True,
           stop=True, waits=[(ev, ev.n), (vc, msv)], incs=[(tsn, 1)])
    ctx.do("scalar", "activation", Dpb[:], pk[2][:, 0:16], AF.Identity,
           waits=[(tsn, tsn.n)], incs=[(ev, 1)])
    dpv = ev.n
    ctx.do("tensor", "transpose", pk[2][:, 0:16], nwp[:], ident[0:16, 0:16],
           waits=[(ev, dpv)], incs=[(tsn, 1)])
    ctx.do("scalar", "activation", nw[:], pk[2][:, 0:16], AF.Identity,
           waits=[(tsn, tsn.n)], incs=[(ev, 1)])
    nwv = ev.n
    ctx.do("scalar", "activation", Acol[:], vtmp[:], AF.Exp,
           waits=[(gs, gl0), (ev, dpv)], incs=[(ev, 1)])
    ctx.do("vector", "tensor_scalar", Acol[:], Acol[:], -1.0, None, ALU.mult,
           waits=[(ev, ev.n)], incs=[(vc, 1)])
    ctx.do("vector", "tensor_scalar", af[:], dtv[:], Acol[0:16, 0:1], None, ALU.mult,
           waits=[(vc, vc.n), (ld, ld0)], incs=[(vc, 1)])
    afv = vc.n

    bank_free = {i: (tsn, 0) for i in range(8)}
    G = {k: (tsn, 0) for k in
         ["xsf", "xst", "Bcf", "Ccf", "Bct", "zsc", "scT", "acum", "eac",
          "alb", "alr", "cdb", "acT", "dtT", "dtds", "MT0", "MT1", "Ccs",
          "Bcw", "prevdone", "Yf", "ynb", "arg0", "arg1", "eaa0", "eaa1"]}
    ysqv = [0] * 16
    gavp = [0] * 16
    osbst = [0, 0]
    ldch = {}

    def load_chunk(c):
        t0 = c * CH
        ctx.do("sync", "dma_start",
               out=xsf[:].rearrange("p (ec t) -> p ec t", t=CH),
               in_=dram["xsbc"][0:DI, t0:t0 + CH].rearrange("(ec p) t -> p ec t", p=128),
               waits=[G["xsf"]], incs=[(ld, 16)])
        ctx.do("sync", "dma_start", out=Bcf[:], in_=dram["xsbc"][DI:DI + 64, t0:t0 + CH],
               waits=[G["Bcf"]], incs=[(ld, 16)])
        ctx.do("sync", "dma_start", out=Ccf[:],
               in_=dram["xsbc"][DI + 64:DI + 128, t0:t0 + CH],
               waits=[G["Ccf"]], incs=[(ld, 16)])
        ctx.do("gpsimd", "dma_start",
               out=zsc[:].rearrange("p (ec t) -> p ec t", t=CH),
               in_=dram["zs"][0:DI, t0:t0 + CH].rearrange("(ec p) t -> p ec t", p=128),
               waits=[G["zsc"]], incs=[(gl, 16)])
        ldch[c] = (ld.n, gl.n)

    load_chunk(0)

    xtie = ctx.sem("Bxe")
    _xptf = [0, 0]
    for c in range(NCH):
        t0 = c * CH
        lv, gv = ldch[c]
        acs = af[0:16, t0:t0 + CH]
        for tt in range(2):
            for ec in range(16):
                ctx.do("tensor", "transpose", pkT[0:128, 0:128],
                       xsf[:, ec * CH + tt * 128:ec * CH + (tt + 1) * 128],
                       identb_b[:],
                       waits=[(ld, lv), (xtie, _xptf[0]), G["xst"]],
                       incs=[(tsn, 1)])
                ctx.do("scalar", "activation",
                       xst[tt][:, ec * 128:(ec + 1) * 128], pkT[0:128, 0:128],
                       AF.Identity, waits=[(tsn, tsn.n)], incs=[(xtie, 1)])
                _xptf[0] = xtie.n
            ctx.do("tensor", "transpose", pkT[0:128, 0:64],
                   Bcf[0:64, tt * 128:(tt + 1) * 128], identb_b[0:64, 0:64],
                   waits=[(ld, lv), (xtie, _xptf[0])], incs=[(tsn, 1)])
            ctx.do("scalar", "activation", Bct[tt][:, 0:64],
                   pkT[0:128, 0:64], AF.Identity,
                   waits=[(tsn, tsn.n)], incs=[(xtie, 1)])
            _xptf[0] = xtie.n
        xstv = xtie.n
        ctx.do("vector", "tensor_tensor_scan", acum[0:16, 0:128], onesh[:],
               acs[:, 0:128], 0.0, ALU.mult, ALU.add,
               waits=[(vc, afv), G["acum"]], incs=[(vc, 1)])
        ctx.do("vector", "tensor_copy", vtmp[:], acum[0:16, 127:128],
               waits=[(vc, vc.n)], incs=[(vc, 1)])
        ctx.do("vector", "tensor_tensor_scan", acum[0:16, 128:256], onesh[:],
               acs[:, 128:256], vtmp[0:16, 0:1], ALU.mult, ALU.add,
               waits=[(vc, vc.n)], incs=[(vc, 1)])
        acv = vc.n
        ctx.do("vector", "reduce_sum", alc[0:16, 0:1], acs, AX.X,
               waits=[(vc, afv)], incs=[(vc, 1)])
        alv = vc.n
        ctx.do("scalar", "activation", eac[:], acum[0:16, :], AF.Exp,
               waits=[(vc, acv), G["eac"]], incs=[(ev, 1)])
        eacv = ev.n

        def small_tr(dst, src_ap, k, wt):
            ctx.do("tensor", "transpose", pk[2][:, 0:k], src_ap, ident[0:k, 0:k],
                   waits=wt + [bank_free[2]], incs=[(tsn, 1)])
            ctx.do("scalar", "activation", dst, pk[2][:, 0:dst.shape[1]], AF.Identity,
                   waits=[(tsn, tsn.n)], incs=[(ev, 1)])
            bank_free[2] = (ev, ev.n)
            return ev.n

        for tt in range(2):
            small_tr(acT[tt][:], acum[0:16, tt * 128:(tt + 1) * 128], 16,
                     [(vc, acv), G["acT"]])
            small_tr(dtT[tt][:], dtv[0:16, t0 + tt * 128:t0 + (tt + 1) * 128], 16,
                     [(ld, ld0), G["dtT"]])
            ctx.do("tensor", "transpose", pk[2][:, 0:16],
                   lnd[0:16, t0 + tt * 128:t0 + (tt + 1) * 128], ident[0:16, 0:16],
                   waits=[(ld, ld0), bank_free[2]], incs=[(tsn, 1)])
            ctx.do("vector", "tensor_tensor", colk[tt][:], pk[2][:, 0:16], acT[tt][:],
                   ALU.subtract, waits=[(tsn, tsn.n), (ev, ev.n)], incs=[(vc, 1)])
            bank_free[2] = (vc, vc.n)
        ctx.do("tensor", "transpose", pk[2][0:1, 0:16], alc[:], ident[0:16, 0:16],
               waits=[(vc, alv), bank_free[2], G["alr"]], incs=[(tsn, 1)])
        ctx.do("scalar", "activation", alr[:], pk[2][0:1, 0:16], AF.Identity,
               waits=[(tsn, tsn.n)], incs=[(ev, 1)])
        alrv = ev.n
        bank_free[2] = (ev, alrv)
        ctx.do("tensor", "matmul", pk[2][:, 0:16], onesr[:], alr[:], start=True,
               stop=True, waits=[(ev, alrv), bank_free[2]], incs=[(tsn, 1)])
        ctx.do("scalar", "activation", alb[:], pk[2][:, 0:16], AF.Identity,
               waits=[(tsn, tsn.n), G["alb"]], incs=[(ev, 1)])
        albv = ev.n
        bank_free[2] = (ev, albv)
        ctx.do("scalar", "activation", alr[:], alr[:], AF.Exp,
               waits=[(ev, albv)], incs=[(ev, 1)])
        cdrv = ev.n
        ctx.do("tensor", "matmul", pk[2][0:64, 0:16], onesr[0:1, 0:64], alr[:],
               start=True, stop=True, waits=[(ev, cdrv), bank_free[2]], incs=[(tsn, 1)])
        ctx.do("scalar", "activation", cdb[:], pk[2][0:64, 0:16], AF.Identity,
               waits=[(tsn, tsn.n), G["cdb"]], incs=[(ev, 1)])
        cdbv = ev.n
        bank_free[2] = (ev, cdbv)
        dtdv = []
        for tt in range(2):
            ctx.do("vector", "tensor_tensor", dtds[tt][:], alb[:], acT[tt][:],
                   ALU.subtract, waits=[(ev, albv), (vc, vc.n), G["dtds"]],
                   incs=[(vc, 1)])
            ctx.do("scalar", "activation", dtds[tt][:], dtds[tt][:], AF.Exp,
                   waits=[(vc, vc.n)], incs=[(ev, 1)])
            ctx.do("vector", "tensor_tensor", dtds[tt][:], dtds[tt][:], dtT[tt][:],
                   ALU.mult, waits=[(ev, ev.n)], incs=[(vc, 1)])
            dtdv.append(vc.n)
        scv = []
        for kt in range(2):
            ctx.do("tensor", "matmul", pk[0][:, 0:CH], Bcf[:, kt * 128:(kt + 1) * 128],
                   Ccf[:], start=True, stop=True,
                   waits=[(ld, lv), bank_free[0]], incs=[(tsn, 1)])
            ctx.do("vector", "tensor_tensor", scT[kt][:], pk[0][:, 0:CH],
                   mk[:, kt * CH:(kt + 1) * CH], ALU.mult,
                   waits=[(tsn, tsn.n), (ld, ld0), G["scT"]], incs=[(vc, 1)])
            bank_free[0] = (vc, vc.n)
            scv.append(vc.n)
        ctx.do("scalar", "activation", prevb[:], prev[:], AF.Identity,
               waits=[(vc, msv if c == 0 else G["prevdone"][1])], incs=[(ev, 1)])
        pbv = ev.n
        epv = []
        st_bc = {}
        st_eb = {}
        st_arg = {}
        st_exp = {}
        st_mt = {}
        st_ccs = {}
        st_y = {}
        st_sp = {}
        st_bcw = {}

        def s_bc(h):
            bb = 3 + (h % 2)
            ctx.do("tensor", "matmul", pk[bb][:, 0:CH], sel[:, h * 128:(h + 1) * 128],
                   acum[0:16, :], start=True, stop=True,
                   waits=[(vc, acv), bank_free[bb]], incs=[(tsn, 1)])
            st_bc[h] = tsn.n

        def s_eb(h):
            ctx.do("tensor", "matmul", pk[6][0:64, 0:CH],
                   sel[:, h * 128:h * 128 + 64], eac[:], start=True, stop=True,
                   waits=[(ev, eacv), bank_free[6]], incs=[(tsn, 1)])
            st_eb[h] = tsn.n

        def s_mid(h):
            bb = 3 + (h % 2)
            par = h % 2
            for kt in range(2):
                ctx.do("vector", "tensor_scalar", arg[kt][:], pk[bb][:, 0:CH],
                       colk[kt][:, h:h + 1], 30.0, ALU.add, op1=ALU.min,
                       waits=[(tsn, st_bc[h]), (vc, vc.n),
                              (ev, st_exp.get(h - 1, 0))], incs=[(vc, 1)])
            st_arg[h] = vc.n
            bank_free[bb] = (vc, vc.n)
            for kt in range(2):
                ctx.do("vector", "tensor_scalar", Bcw[par][kt][:], Bct[kt][:],
                       dtds[kt][:, h:h + 1], None, ALU.mult,
                       waits=[(xtie, xstv), (vc, dtdv[kt]),
                              (tsn, st_sp.get(h - 2, 0))], incs=[(vc, 1)])
            st_bcw[h] = vc.n
            for kt in range(2):
                ctx.do("scalar", "activation", eaa[kt][:], arg[kt][:], AF.Exp,
                       waits=[(vc, st_arg[h]), (vc, st_mt.get(h - 1, 0)),
                              (ev, ev.n)], incs=[(ev, 1)])
            st_exp[h] = ev.n

        def s_ccs(h):
            par = h % 2
            ctx.do("vector", "tensor_tensor", Ccs[par][:], pk[6][0:64, 0:CH], Ccf[:],
                   ALU.mult, waits=[(tsn, st_eb[h]), (ld, lv),
                                    (tsn, st_y.get(h - 1, 0))], incs=[(vc, 1)])
            st_ccs[h] = vc.n
            bank_free[6] = (vc, vc.n)

        def s_mt(h):
            par = h % 2
            for kt in range(2):
                ctx.do("vector", "tensor_tensor", MT[par * 2 + kt][:], eaa[kt][:],
                       scT[kt][:], ALU.mult,
                       waits=[(ev, st_exp[h]), (vc, scv[kt]),
                              (tsn, st_y.get(h - 1, 0))], incs=[(vc, 1)])
            st_mt[h] = vc.n

        def s_pe(h):
            par = h % 2
            for kt in range(2):
                ctx.do("tensor", "matmul", pk[5][:, 0:CH],
                       xst[kt][:, h * 128:(h + 1) * 128], MT[par * 2 + kt][:],
                       start=(kt == 0), stop=False,
                       waits=[(vc, st_mt[h]), (xtie, xstv), bank_free[5]],
                       incs=[(tsn, 1)])
            ctx.do("tensor", "matmul", pk[5][:, 0:CH],
                   prevb[0:64, h * 128:(h + 1) * 128], Ccs[par][0:64, :],
                   start=False, stop=True,
                   waits=[(ev, pbv), (vc, st_ccs[h])], incs=[(tsn, 1)])
            st_y[h] = tsn.n
            for tt in range(2):
                ctx.do("tensor", "matmul", pk[7][0:64, 0:128], Bcw[par][tt][:],
                       xst[tt][:, h * 128:(h + 1) * 128],
                       start=(tt == 0), stop=(tt == 1),
                       waits=[(vc, st_bcw[h])] + ([bank_free[7]] if tt == 0 else []),
                       incs=[(tsn, 1)])
            st_sp[h] = tsn.n

        def s_epi(h):
            ctx.do("vector", "scalar_tensor_tensor", Yf[:, h * CH:(h + 1) * CH],
                   xsf[:, h * CH:(h + 1) * CH], Dpb[:, h:h + 1], pk[5][:, 0:CH],
                   ALU.mult, ALU.add,
                   waits=[(tsn, st_y[h]), (ev, dpv), (ld, lv),
                          (ev, ysqv[h]), (vc, gavp[h])], incs=[(vc, 1)])
            bank_free[5] = (vc, vc.n)
            epv.append(vc.n)
            ctx.do("vector", "scalar_tensor_tensor", prev[:, h * 128:(h + 1) * 128],
                   prev[:, h * 128:(h + 1) * 128], cdb[0:64, h:h + 1],
                   pk[7][0:64, 0:128], ALU.mult, ALU.add,
                   waits=[(tsn, st_sp[h]), (ev, max(pbv, cdbv)), (vc, vc.n)],
                   incs=[(vc, 1)])
            bank_free[7] = (vc, vc.n)

        s_bc(0)
        s_eb(0)
        s_bc(1)
        s_mid(0)
        s_ccs(0)
        s_mt(0)
        for h in range(NH):
            s_pe(h)
            if h + 2 < NH:
                s_bc(h + 2)
            if h + 1 < NH:
                s_eb(h + 1)
                s_mid(h + 1)
                s_ccs(h + 1)
            s_epi(h)
            if h + 1 < NH:
                s_mt(h + 1)
        mtv = [st_mt[NH - 1]] * 2
        spv = st_sp[NH - 1]
        yv = st_y[NH - 1]
        ebv = st_eb[NH - 1]
        bcv = st_bc[NH - 1]
        G["prevdone"] = (vc, vc.n)
        G["acum"] = (tsn, bcv)
        G["eac"] = (tsn, ebv)
        G["scT"] = (vc, max(mtv))
        G["acT"] = (vc, max(dtdv))
        G["dtT"] = (vc, max(dtdv))
        G["alb"] = (vc, max(dtdv))
        G["alr"] = (tsn, tsn.n)
        G["cdb"] = (vc, vc.n)
        G["dtds"] = (vc, st_bcw[NH - 1])
        G["Bct"] = (vc, st_bcw[NH - 1])
        gav = []
        for ec in range(16):
            ctx.do("vector", "tensor_tensor", Yf[:, ec * CH:(ec + 1) * CH],
                   Yf[:, ec * CH:(ec + 1) * CH], zsc[:, ec * CH:(ec + 1) * CH],
                   ALU.mult, waits=[(vc, epv[ec]), (gl, gv)], incs=[(vc, 1)])
            gav.append(vc.n)
        G["zsc"] = (vc, vc.n)
        gavp = list(gav)
        rmm = []
        for ec in range(16):
            qi = ec % 2
            ctx.do("scalar", "activation", ysq[qi][:], Yf[:, ec * CH:(ec + 1) * CH],
                   AF.Square, waits=[(vc, gav[ec]),
                                     (tsn, rmm[ec - 2]) if ec >= 2 else (tsn, 0)],
                   incs=[(ev, 1)])
            ysqv[ec] = ev.n
            ctx.do("tensor", "matmul", pk[0][0:1, 0:CH], onesc[:], ysq[qi][:],
                   start=(ec == 0), stop=(ec == 15),
                   waits=[(ev, ev.n)] + ([bank_free[0]] if ec == 0 else []),
                   incs=[(tsn, 1)])
            rmm.append(tsn.n)
        ctx.do("scalar", "activation", r1[:], pk[0][0:1, 0:CH], AF.Ln,
               scale=1.0 / DI, bias=epsc[0:1, 0:1], waits=[(tsn, rmm[15])],
               incs=[(ev, 1)])
        bank_free[0] = (ev, ev.n)
        ctx.do("scalar", "activation", r1[:], r1[:], AF.Exp, scale=-0.5,
               waits=[(ev, ev.n)], incs=[(ev, 1)])
        rsv = ev.n
        ctx.do("tensor", "matmul", pk[2][:, 0:CH], onesr[:], r1[:], start=True,
               stop=True, waits=[(ev, rsv), bank_free[2]], incs=[(tsn, 1)])
        rbv = tsn.n
        ynv = []
        for ec in range(16):
            ctx.do("vector", "scalar_tensor_tensor", ynb[:, ec * CH:(ec + 1) * CH],
                   Yf[:, ec * CH:(ec + 1) * CH], nw[:, ec:ec + 1], pk[2][:, 0:CH],
                   ALU.mult, ALU.mult,
                   waits=[(tsn, rbv), (ev, nwv), (vc, gav[ec]), G["ynb"]],
                   incs=[(vc, 1)])
            ynv.append(vc.n)
        bank_free[2] = (vc, vc.n)
        G["Yf"] = (vc, vc.n)
        opv = 0
        for idx, (tt, db) in enumerate([(a, b) for a in range(2) for b in range(2)]):
            ob_ = pk[3] if idx % 2 == 0 else pk[4]
            obi = 3 if idx % 2 == 0 else 4
            for ec in range(16):
                ctx.do("tensor", "matmul", ob_[:, 0:512],
                       ynb[:, ec * CH + tt * 128:ec * CH + (tt + 1) * 128],
                       WoT[:, ec * 1024 + db * 512:ec * 1024 + (db + 1) * 512],
                       start=(ec == 0), stop=(ec == 15),
                       waits=[(vc, ynv[15]), (bie, wov), bank_free[obi]]
                       if ec == 0 else (), incs=[(tsn, 1)] if ec == 15 else ())
            opv = tsn.n
            oi = idx % 2
            ctx.do("scalar", "activation", osb[oi][:], ob_[:, 0:512], AF.Identity,
                   waits=[(tsn, opv), (so_[oi], so_[oi].n)], incs=[(ev, 1)])
            bank_free[obi] = (ev, ev.n)
            ctx.do("scalar", "dma_start",
                   out=dram["macc"][t0 + tt * 128:t0 + (tt + 1) * 128,
                                    db * 512:(db + 1) * 512],
                   in_=osb[oi][:], waits=[(ev, ev.n)], incs=[(so_[oi], 16)])
        G["ynb"] = (tsn, opv)
        G["xsf"] = (vc, epv[15])
        G["xst"] = (tsn, spv)
        G["Bcf"] = (tsn, tsn.n)
        G["Ccf"] = (vc, vc.n)
        if c + 1 < NCH:
            load_chunk(c + 1)
    ctx.wait("scalar", so_[0], so_[0].n)
    ctx.wait("scalar", so_[1], so_[1].n)


def emit_p2(ctx, dram, es):
    nc = ctx.nc
    NT = 8
    TB = 512

    W_ = {k: es.enter_context(nc.sbuf_tensor(f"C{k}", [128, 8 * 1024], BF16))
          for k in ["r1", "r2", "k1", "k2", "v"]}
    mur = es.enter_context(nc.sbuf_tensor("Cmur", [128, 8], F32))
    muk = es.enter_context(nc.sbuf_tensor("Cmuk", [128, 8], F32))
    omr = es.enter_context(nc.sbuf_tensor("Comr", [128, 8], F32))
    omk = es.enter_context(nc.sbuf_tensor("Comk", [128, 8], F32))
    mup = es.enter_context(nc.sbuf_tensor("Cmup", [8, 128], F32))
    ident = es.enter_context(nc.sbuf_tensor("Cident", [128, 128], F32))
    identb_c = es.enter_context(nc.sbuf_tensor("Cidb", [128, 128], BF16))
    xnblk = [es.enter_context(nc.sbuf_tensor(f"Cxn{i}", [128, 8 * TB], BF16))
             for i in range(2)]
    xpblk = [es.enter_context(nc.sbuf_tensor(f"Cxp{i}", [128, 8 * TB], BF16))
             for i in range(2)]
    kab = es.enter_context(nc.sbuf_tensor("Ckab", [128, 8 * TB], BF16))
    ktmp = [es.enter_context(nc.sbuf_tensor(f"Ckt{i}", [128, TB], F32)) for i in range(2)]
    rbuf = es.enter_context(nc.sbuf_tensor("Crbuf", [128, 8 * 512], BF16))
    mtile = es.enter_context(nc.sbuf_tensor("Cmt", [128, 4 * 1024], F32))
    utile = es.enter_context(nc.sbuf_tensor("Cut", [128, 4 * 1024], F32))
    t2 = [es.enter_context(nc.sbuf_tensor(f"Ct2{i}", [128, 512], F32)) for i in range(2)]
    x1b = [es.enter_context(nc.sbuf_tensor(f"Cx1{i}", [128, 512], F32)) for i in range(2)]
    wstage = [es.enter_context(nc.sbuf_tensor(f"Cwst{i}", [128, 1024], BF16))
              for i in range(2)]
    pk = [es.enter_context(nc.psum_tensor(f"pC{i}", [128, 512], F32)) for i in range(7)]
    pTb = es.enter_context(nc.psum_tensor("CpT", [128, 128], BF16))

    ld = ctx.sem("Cld"); gl = ctx.sem("Cgl"); tsn = ctx.sem("Cts")
    ev = ctx.sem("Cev"); vc = ctx.sem("Cvc"); gp = ctx.sem("Cgp")
    mu_ld = ctx.sem("Cmu")
    xst_ = [ctx.sem("Cxo0"), ctx.sem("Cxo1")]
    xbl = [ctx.sem("Cxb0"), ctx.sem("Cxb1")]
    cws = [ctx.sem("Wst0"), ctx.sem("Wst1")]
    cie = ctx.sem("Tie")

    identd = nc.inline_tensor(np.eye(128, dtype=np.float32), "identC_d").ap()
    identbd = nc.inline_tensor(np.eye(128, dtype=BF), "identCb_d").ap()
    ctx.do("sync", "dma_start", out=ident[:], in_=identd[:], incs=[(ld, 16)])
    ctx.do("sync", "dma_start", out=identb_c[:], in_=identbd[:], incs=[(ld, 16)])
    muv = {}
    for i, nm in enumerate(["L0_ffn_mu_r", "L0_ffn_mu_k"]):
        ctx.do("gpsimd", "dma_start", out=mup[:],
               in_=dram[nm].rearrange("(c p) -> c p", p=128),
               waits=[(tsn, muv.get("tr", 0))], incs=[(gl, 16)])
        ctx.do("tensor", "transpose", pk[0][:, 0:8], mup[:], ident[0:8, 0:8],
               waits=[(gl, gl.n), (ld, 32)] + ([(ev, muv["ev"])] if i else []),
               incs=[(tsn, 1)])
        muv["tr"] = tsn.n
        ctx.do("scalar", "activation", (mur if i == 0 else muk)[:], pk[0][:, 0:8],
               AF.Identity, waits=[(tsn, tsn.n)], incs=[(ev, 1)])
        muv["ev"] = ev.n
    ctx.do("vector", "tensor_scalar", omr[:], mur[:], -1.0, 1.0, ALU.mult,
           op1=ALU.add, waits=[(ev, muv["ev"])], incs=[(vc, 1)])
    ctx.do("vector", "tensor_scalar", omk[:], muk[:], -1.0, 1.0, ALU.mult,
           op1=ALU.add, incs=[(vc, 1)])
    omv = vc.n
    _g2 = ctx.sem("ptg2")
    ptfree = [0, 0]
    wsfree = [0, 0]
    pi = 0
    plan = [("r1", "wr_bf", mur), ("r2", "wr_bf", omr), ("k1", "wk_bf", muk),
            ("k2", "wk_bf", omk), ("v", "wv_bf", None)]
    for key, nm, sc in plan:
        Wt = W_[key]
        for m in range(8):
            s = cws[pi % 2]
            ctx.do("sync", "dma_start", out=wstage[pi % 2][:],
                   in_=dram[nm][m * 128:(m + 1) * 128, :],
                   waits=[(_g2, _g2.n), (tsn, wsfree[pi % 2])], incs=[(s, 16)])
            for dc in range(8):
                ctx.do("tensor", "transpose", pTb[:],
                       wstage[pi % 2][:, dc * 128:(dc + 1) * 128], identb_c[:],
                       waits=[(s, s.n), (cie, ptfree[0]), (ld, 32),
                              (vc, omv)], incs=[(tsn, 1)])
                kw = {}
                if sc is not None:
                    kw["scale"] = sc[:, dc:dc + 1]
                ctx.do("scalar", "activation",
                       Wt[:, dc * 1024 + m * 128: dc * 1024 + m * 128 + 128],
                       pTb[:], AF.Identity, waits=[(tsn, tsn.n)],
                       incs=[(cie, 1)], **kw)
                ptfree[0] = cie.n
            wsfree[pi % 2] = tsn.n
            pi += 1
    wiv = cie.n

    def load_xn(tb, w=()):
        s = xbl[tb % 2]
        ctx.do("sync", "dma_start",
               out=xnblk[tb % 2][:].rearrange("p (dc t) -> p dc t", t=TB),
               in_=dram["xn0f"][0:D, 1 + tb * TB:1 + (tb + 1) * TB]
               .rearrange("(dc p) t -> p dc t", p=128), waits=w, incs=[(s, 16)])
        ctx.do("sync", "dma_start",
               out=xpblk[tb % 2][:].rearrange("p (dc t) -> p dc t", t=TB),
               in_=dram["xn0f"][0:D, tb * TB:(tb + 1) * TB]
               .rearrange("(dc p) t -> p dc t", p=128), incs=[(s, 16)])
        return (s, s.n)

    xnv = [None] * NT
    xnv[0] = load_xn(0)
    xnv[1] = load_xn(1)
    bank_free = {i: (tsn, 0) for i in range(7)}
    bank_free[0] = (ev, muv["ev"])
    t2f = [(vc, 0), (vc, 0)]
    x1f = [(xst_[0], 0), (xst_[1], 0)]
    ktf = [(ev, 0), (ev, 0)]
    rb_free = (vc, 0)
    ka_free = (tsn, 0)
    mload = (vc, 0)
    lastmm = [0] * NT
    for tb in range(NT):
        ctx.do("sync", "dma_start",
               out=mtile[:].rearrange("p (tt d) -> p tt d", d=1024),
               in_=dram["macc"][tb * TB:(tb + 1) * TB, :]
               .rearrange("(tt p) d -> p tt d", p=128),
               waits=[mload], incs=[(mu_ld, 16)])
        ctx.do("sync", "dma_start",
               out=utile[:].rearrange("p (tt d) -> p tt d", d=1024),
               in_=dram["u"][tb * TB:(tb + 1) * TB, :]
               .rearrange("(tt p) d -> p tt d", p=128), incs=[(mu_ld, 16)])
        mlv = mu_ld.n
        rev = []
        for tt in range(4):
            for db in range(2):
                bi = (tt * 2 + db) % 3
                for half, (Wt, xb) in enumerate(
                        [(W_["r1"], xnblk[tb % 2]), (W_["r2"], xpblk[tb % 2])]):
                    for dc in range(8):
                        ctx.do("tensor", "matmul", pk[bi][:, 0:512],
                               xb[:, dc * TB + tt * 128:dc * TB + (tt + 1) * 128],
                               Wt[:, dc * 1024 + db * 512:dc * 1024 + (db + 1) * 512],
                               start=(half == 0 and dc == 0),
                               stop=(half == 1 and dc == 7),
                               waits=[xnv[tb], (cie, wiv), bank_free[bi]]
                               if half == 0 and dc == 0 else (),
                               incs=[(tsn, 1)] if half == 1 and dc == 7 else ())
                ctx.do("scalar", "activation", rbuf[:, (tt * 2 + db) * 512:
                                                    (tt * 2 + db + 1) * 512],
                       pk[bi][:, 0:512], AF.Sigmoid,
                       waits=[(tsn, tsn.n), rb_free], incs=[(ev, 1)])
                bank_free[bi] = (ev, ev.n)
                rev.append(ev.n)
        kav = []
        for m in range(8):
            bi = 3 + m % 2
            for half, (Wt, xb) in enumerate(
                    [(W_["k1"], xnblk[tb % 2]), (W_["k2"], xpblk[tb % 2])]):
                for dc in range(8):
                    ctx.do("tensor", "matmul", pk[bi][:, 0:512],
                           Wt[:, dc * 1024 + m * 128:dc * 1024 + m * 128 + 128],
                           xb[:, dc * TB:(dc + 1) * TB],
                           start=(half == 0 and dc == 0),
                           stop=(half == 1 and dc == 7),
                           waits=[xnv[tb], (cie, wiv), bank_free[bi], ka_free]
                           if half == 0 and dc == 0 else (),
                           incs=[(tsn, 1)] if half == 1 and dc == 7 else ())
            mmv = tsn.n
            qi = m % 2
            ctx.do("vector", "tensor_scalar", ktmp[qi][:], pk[bi][:, 0:512], 0.0,
                   None, ALU.max, waits=[(tsn, mmv), ktf[qi]], incs=[(vc, 1)])
            bank_free[bi] = (vc, vc.n)
            ctx.do("scalar", "activation", kab[:, m * TB:(m + 1) * TB], ktmp[qi][:],
                   AF.Square, waits=[(vc, vc.n)], incs=[(ev, 1)])
            ktf[qi] = (ev, ev.n)
            kav.append(ev.n)
        if tb + 2 < NT:
            xnv[tb + 2] = load_xn(tb + 2, w=[(tsn, tsn.n)])
        for tt in range(4):
            for db in range(2):
                bi = 5 + db
                for m in range(8):
                    ctx.do("tensor", "matmul", pk[bi][:, 0:512],
                           kab[:, m * TB + tt * 128:m * TB + (tt + 1) * 128],
                           W_["v"][:, m * 1024 + db * 512:m * 1024 + (db + 1) * 512],
                           start=(m == 0), stop=(m == 7),
                           waits=[(ev, kav[7]), (cie, wiv), bank_free[bi]]
                           if m == 0 else (), incs=[(tsn, 1)] if m == 7 else ())
                mmv = tsn.n
                oi = db
                ctx.do("vector", "tensor_tensor", t2[oi][:], pk[bi][:, 0:512],
                       rbuf[:, (tt * 2 + db) * 512:(tt * 2 + db + 1) * 512],
                       ALU.mult, waits=[(tsn, mmv), (ev, rev[tt * 2 + db]),
                                        t2f[oi]], incs=[(vc, 1)])
                bank_free[bi] = (vc, vc.n)
                ctx.do("gpsimd", "tensor_tensor", t2[oi][:], t2[oi][:],
                       mtile[:, tt * 1024 + db * 512:tt * 1024 + (db + 1) * 512],
                       ALU.add, waits=[(vc, vc.n), (mu_ld, mlv)], incs=[(gp, 1)])
                ctx.do("vector", "tensor_tensor", x1b[oi][:], t2[oi][:],
                       utile[:, tt * 1024 + db * 512:tt * 1024 + (db + 1) * 512],
                       ALU.add, waits=[(gp, gp.n), x1f[oi]], incs=[(vc, 1)])
                t2f[oi] = (vc, vc.n)
                ctx.do("scalar", "dma_start",
                       out=dram["x1"][tb * TB + tt * 128:tb * TB + (tt + 1) * 128,
                                      db * 512:(db + 1) * 512],
                       in_=x1b[oi][:], waits=[(vc, vc.n)], incs=[(xst_[oi], 16)])
                x1f[oi] = (xst_[oi], xst_[oi].n)
        lastmm[tb] = tsn.n
        ka_free = (tsn, tsn.n)
        rb_free = (vc, vc.n)
        mload = (vc, vc.n)
    ctx.wait("scalar", xst_[0], xst_[0].n)
    ctx.wait("scalar", xst_[1], xst_[1].n)


def emit_p4a(ctx, dram, es):
    nc = ctx.nc
    xnf = es.enter_context(nc.sbuf_tensor("Dxnf", [128, 8 * T], BF16))
    qwS = es.enter_context(nc.sbuf_tensor("DqwS", [128, 8 * 1024], BF16))
    kwS = es.enter_context(nc.sbuf_tensor("DkwS", [128, 8 * 1024], BF16))
    vwS = es.enter_context(nc.sbuf_tensor("DvwS", [128, 8 * 1024], BF16))
    ob = [es.enter_context(nc.sbuf_tensor(f"Dob{i}", [128, 512], BF16)) for i in range(4)]
    pk = [es.enter_context(nc.psum_tensor(f"pD{i}", [128, 512], F32)) for i in range(4)]

    ld = ctx.sem("Dld"); tsn = ctx.sem("Dts"); ev = ctx.sem("Dev")
    do_ = [ctx.sem(f"Do{i}") for i in range(4)]

    ctx.do("sync", "dma_start",
           out=xnf[:].rearrange("p (dc t) -> p dc t", t=T),
           in_=dram["xn1f"][0:D, 1:1 + T].rearrange("(dc p) t -> p dc t", p=128),
           incs=[(ld, 16)])
    _g3 = ctx.sem("ptg3")
    for Wt, nm in [(qwS, "qw_bf"), (kwS, "kw_bf"), (vwS, "vw_bf")]:
        ctx.do("sync", "dma_start",
               out=Wt[:].rearrange("p (dc j) -> p dc j", j=1024),
               in_=dram[nm].rearrange("(dc p) j -> p dc j", p=128),
               waits=[(_g3, _g3.n)], incs=[(ld, 16)])
    ld0 = ld.n
    bank_free = {i: (ev, 0) for i in range(4)}
    obst = [0] * 4
    nb = 0
    for kind in ["k", "q", "v"]:
        if kind in ("k", "q"):
            wS = kwS if kind == "k" else qwS
            for tb in range(8):
                for m in range(8):
                    for dc in range(8):
                        ctx.do("tensor", "matmul", pk[nb][:, 0:512],
                               wS[:, dc * 1024 + m * 128:dc * 1024 + (m + 1) * 128],
                               xnf[:, dc * T + tb * 512:dc * T + (tb + 1) * 512],
                               start=(dc == 0), stop=(dc == 7),
                               waits=[(ld, ld0), bank_free[nb]] if dc == 0 else (),
                               incs=[(tsn, 1)] if dc == 7 else ())
                    ctx.do("scalar", "activation", ob[nb][:], pk[nb][:, 0:512],
                           AF.Identity, scale=(1.0 / 32.0 if kind == "q" else 1.0),
                           waits=[(tsn, tsn.n), (do_[nb], do_[nb].n)], incs=[(ev, 1)])
                    bank_free[nb] = (ev, ev.n)
                    ctx.do("scalar", "dma_start",
                           out=dram["kf" if kind == "k" else "qf"]
                           [m * 128:(m + 1) * 128, tb * 512:(tb + 1) * 512],
                           in_=ob[nb][:], waits=[(ev, ev.n)], incs=[(do_[nb], 16)])
                    nb = (nb + 1) % 4
        else:
            for tt in range(32):
                for db in range(2):
                    for dc in range(8):
                        ctx.do("tensor", "matmul", pk[nb][:, 0:512],
                               xnf[:, dc * T + tt * 128:dc * T + (tt + 1) * 128],
                               vwS[:, dc * 1024 + db * 512:dc * 1024 + (db + 1) * 512],
                               start=(dc == 0), stop=(dc == 7),
                               waits=[(ld, ld0), bank_free[nb]] if dc == 0 else (),
                               incs=[(tsn, 1)] if dc == 7 else ())
                    ctx.do("scalar", "activation", ob[nb][:], pk[nb][:, 0:512],
                           AF.Identity, waits=[(tsn, tsn.n), (do_[nb], do_[nb].n)],
                           incs=[(ev, 1)])
                    bank_free[nb] = (ev, ev.n)
                    ctx.do("scalar", "dma_start",
                           out=dram["vt"][tt * 128:(tt + 1) * 128,
                                          db * 512:(db + 1) * 512],
                           in_=ob[nb][:], waits=[(ev, ev.n)], incs=[(do_[nb], 16)])
                    nb = (nb + 1) % 4
    for s in do_:
        ctx.wait("scalar", s, s.n)


def emit_p4b(ctx, dram, es):
    nc = ctx.nc
    Kf = es.enter_context(nc.sbuf_tensor("EKf", [128, 8 * T], BF16))
    Vt = es.enter_context(nc.sbuf_tensor("EVt", [128, 32 * 1024], BF16))
    qt_ = [es.enter_context(nc.sbuf_tensor(f"Eqt{i}", [128, 8 * 128], BF16)) for i in range(2)]
    mkt = es.enter_context(nc.sbuf_tensor("Emkt", [128, 4 * 512], F32))
    identb = es.enter_context(nc.sbuf_tensor("Eib", [128, 128], BF16))
    sms = es.enter_context(nc.sbuf_tensor("Esms", [128, 512], F32))
    pb = es.enter_context(nc.sbuf_tensor("Epb", [128, 512], BF16))
    pTs = es.enter_context(nc.sbuf_tensor("EpTs", [128, 512], BF16))
    acc = es.enter_context(nc.sbuf_tensor("Eacc", [128, 1024], F32))
    mcol = es.enter_context(nc.sbuf_tensor("Emc", [128, 1], F32))
    mbc = es.enter_context(nc.sbuf_tensor("Emb", [128, 1], F32))
    mnc = es.enter_context(nc.sbuf_tensor("Emn", [128, 1], F32))
    ngm = es.enter_context(nc.sbuf_tensor("Engm", [128, 1], F32))
    alp = es.enter_context(nc.sbuf_tensor("Ealp", [128, 1], F32))
    lcol = es.enter_context(nc.sbuf_tensor("Elc", [128, 1], F32))
    lb = es.enter_context(nc.sbuf_tensor("Elb", [128, 1], F32))
    rec = es.enter_context(nc.sbuf_tensor("Erec", [128, 1], F32))
    oatt = [es.enter_context(nc.sbuf_tensor(f"Eo{i}", [128, 1024], F32)) for i in range(2)]
    pS = [es.enter_context(nc.psum_tensor(f"pE{i}", [128, 512], F32)) for i in range(2)]
    pV = [es.enter_context(nc.psum_tensor(f"pEv{i}", [128, 512], F32)) for i in range(2)]
    pT = [es.enter_context(nc.psum_tensor(f"pEt{i}", [128, 128], BF16)) for i in range(4)]

    ld = ctx.sem("Eld"); tsn = ctx.sem("Ets"); ev = ctx.sem("Eev")
    vc = ctx.sem("Evc")
    eq = [ctx.sem("Eq0"), ctx.sem("Eq1")]
    eo = [ctx.sem("Eo0"), ctx.sem("Eo1")]

    identd = nc.inline_tensor(np.eye(128, dtype=BF), "identE_d").ap()
    mkn = np.zeros((128, 4 * 512), dtype=np.float32)
    for v in range(4):
        for r_ in range(128):
            for cb in range(4):
                pass
    for v in range(4):
        cols = np.arange(512)
        rows = np.arange(128)[:, None]
        mkn[:, v * 512:(v + 1) * 512] = np.where(v * 128 + rows >= cols, 0.0, NEG)
    mkd = nc.inline_tensor(mkn, "mkE_d").ap()

    ctx.do("sync", "dma_start", out=Kf[:].rearrange("p (m k) -> p m k", k=T),
           in_=dram["kf"].rearrange("(m p) k -> p m k", p=128), incs=[(ld, 16)])
    ctx.do("sync", "dma_start", out=Vt[:].rearrange("p (tt j) -> p tt j", j=1024),
           in_=dram["vt"].rearrange("(tt p) j -> p tt j", p=128), incs=[(ld, 16)])
    ctx.do("sync", "dma_start", out=mkt[:], in_=mkd[:], incs=[(ld, 16)])
    ctx.do("sync", "dma_start", out=identb[:], in_=identd[:], incs=[(ld, 16)])
    ld0 = ld.n
    ctx.do("vector", "memset", acc[:], 0.0, incs=[(vc, 1)])
    ctx.do("vector", "memset", lcol[:], 1.0, incs=[(vc, 1)])
    msv = vc.n

    qld = {}

    def load_q(qt, w=()):
        s = eq[qt % 2]
        ctx.do("sync", "dma_start",
               out=qt_[qt % 2][:].rearrange("p (jc q) -> p jc q", q=128),
               in_=dram["qf"][0:D, qt * 128:(qt + 1) * 128]
               .rearrange("(jc p) q -> p jc q", p=128),
               waits=w, incs=[(s, 16)])
        qld[qt] = (s, s.n)

    load_q(0)
    bank_free = {k: (vc, 0) for k in ["s0", "s1", "v0", "v1"]}
    pt_free = [(ev, 0)] * 4
    sms_free = (ev, 0)
    pb_free = (tsn, 0)
    pts_free = (tsn, 0)
    ost = [0, 0]
    lastq = 0

    for qt in range(32):
        if qt + 1 < 32:
            load_q(qt + 1, w=[(tsn, lastq)])
        nkb = qt // 4 + 1
        ctx.do("vector", "memset", mcol[:], -3.0e38, waits=[(vc, vc.n)], incs=[(vc, 1)])
        for kb in range(nkb):
            sb = kb % 2
            pSb = pS[sb]
            diag = (kb == qt // 4)
            for jc in range(8):
                ctx.do("tensor", "matmul", pSb[:, 0:512],
                       qt_[qt % 2][:, jc * 128:(jc + 1) * 128],
                       Kf[:, jc * T + kb * 512:jc * T + (kb + 1) * 512],
                       start=(jc == 0), stop=(jc == 7),
                       waits=[qld[qt], (ld, ld0), bank_free[f"s{sb}"]] if jc == 0 else (),
                       incs=[(tsn, 1)] if jc == 7 else ())
            smv = tsn.n
            if diag:
                ctx.do("vector", "tensor_tensor", sms[:], pSb[:, 0:512],
                       mkt[:, (qt % 4) * 512:(qt % 4 + 1) * 512], ALU.add,
                       waits=[(tsn, smv), (ld, ld0), sms_free], incs=[(vc, 1)])
                ctx.do("vector", "reduce_max", mbc[:, 0:1], sms[:], AX.X,
                       waits=[(vc, vc.n)], incs=[(vc, 1)])
                esrc = sms[:]
                bank_free[f"s{sb}"] = (vc, vc.n)
            else:
                ctx.do("vector", "reduce_max", mbc[:, 0:1], pSb[:, 0:512], AX.X,
                       waits=[(tsn, smv), (vc, vc.n)], incs=[(vc, 1)])
                esrc = pSb[:, 0:512]
            rdv = vc.n
            ctx.do("vector", "tensor_tensor", mnc[:, 0:1], mcol[:, 0:1], mbc[:, 0:1],
                   ALU.max, waits=[(vc, rdv)], incs=[(vc, 1)])
            ctx.do("vector", "tensor_scalar", ngm[:, 0:1], mnc[:, 0:1], -1.0, None,
                   ALU.mult, waits=[(vc, vc.n)], incs=[(vc, 1)])
            ngv = vc.n
            ctx.do("scalar", "activation", alp[:, 0:1], mcol[:, 0:1], AF.Exp,
                   bias=ngm[:, 0:1], waits=[(vc, ngv)], incs=[(ev, 1)])
            alv = ev.n
            ctx.do("scalar", "activation", pb[:], esrc, AF.Exp, bias=ngm[:, 0:1],
                   accum_out=lb[:, 0:1], waits=[(ev, alv), pb_free], incs=[(ev, 1)])
            pev = ev.n
            if not diag:
                bank_free[f"s{sb}"] = (ev, pev)
            else:
                sms_free = (ev, pev)
            ctx.do("vector", "tensor_copy", mcol[:, 0:1], mnc[:, 0:1],
                   waits=[(ev, alv)], incs=[(vc, 1)])
            ctx.do("vector", "scalar_tensor_tensor", lcol[:, 0:1], lcol[:, 0:1],
                   alp[:, 0:1], lb[:, 0:1], ALU.mult, ALU.add,
                   waits=[(ev, pev), (vc, vc.n)], incs=[(vc, 1)])
            for k4 in range(4):
                ctx.do("tensor", "transpose", pT[k4][:], pb[:, k4 * 128:(k4 + 1) * 128],
                       identb[:], waits=[(ev, pev), (ld, ld0), pt_free[k4]],
                       incs=[(tsn, 1)])
                ctx.do("scalar", "activation", pTs[:, k4 * 128:(k4 + 1) * 128],
                       pT[k4][:], AF.Identity, waits=[(tsn, tsn.n), pts_free],
                       incs=[(ev, 1)])
                pt_free[k4] = (ev, ev.n)
            ptv = ev.n
            pb_free = (tsn, tsn.n)
            pvv = []
            for db in range(2):
                for k4 in range(4):
                    ctx.do("tensor", "matmul", pV[db][:, 0:512],
                           pTs[:, k4 * 128:(k4 + 1) * 128],
                           Vt[:, (kb * 4 + k4) * 1024 + db * 512:
                              (kb * 4 + k4) * 1024 + (db + 1) * 512],
                           start=(k4 == 0), stop=(k4 == 3),
                           waits=[(ev, ptv), (ld, ld0), bank_free[f"v{db}"]]
                           if k4 == 0 else (), incs=[(tsn, 1)] if k4 == 3 else ())
                pvv.append(tsn.n)
            pts_free = (tsn, tsn.n)
            for db in range(2):
                ctx.do("vector", "scalar_tensor_tensor",
                       acc[:, db * 512:(db + 1) * 512], acc[:, db * 512:(db + 1) * 512],
                       alp[:, 0:1], pV[db][:, 0:512], ALU.mult, ALU.add,
                       waits=[(tsn, pvv[db]), (ev, alv), (vc, vc.n),
                              (vc, msv)], incs=[(vc, 1)])
                bank_free[f"v{db}"] = (vc, vc.n)
        lastq = tsn.n
        ctx.do("vector", "reciprocal", rec[:, 0:1], lcol[:, 0:1],
               waits=[(vc, vc.n)], incs=[(vc, 1)])
        oi = qt % 2
        ctx.do("vector", "tensor_scalar", oatt[oi][:], acc[:], rec[:, 0:1], None,
               ALU.mult, waits=[(vc, vc.n), (eo[oi], ost[oi])], incs=[(vc, 1)])
        ctx.do("scalar", "dma_start", out=dram["attn"][qt * 128:(qt + 1) * 128, :],
               in_=oatt[oi][:], waits=[(vc, vc.n)], incs=[(eo[oi], 16)])
        ost[oi] = eo[oi].n
    ctx.wait("scalar", eo[0], eo[0].n)
    ctx.wait("scalar", eo[1], eo[1].n)


def emit_p5a(ctx, dram, es):
    nc = ctx.nc
    xnf = es.enter_context(nc.sbuf_tensor("Fxnf", [128, 8 * T], BF16))
    W1T = es.enter_context(nc.sbuf_tensor("FW1T", [128, 8 * DH], BF16))
    W2T = es.enter_context(nc.sbuf_tensor("FW2T", [128, 8 * DH], BF16))
    s1c = es.enter_context(nc.sbuf_tensor("Fs1c", [128, 32], F32))
    s2c = es.enter_context(nc.sbuf_tensor("Fs2c", [128, 32], F32))
    svp = es.enter_context(nc.sbuf_tensor("Fsvp", [32, 128], F32))
    ident = es.enter_context(nc.sbuf_tensor("Fident", [128, 128], F32))
    t1 = [es.enter_context(nc.sbuf_tensor(f"Ft1{i}", [128, 512], F32)) for i in range(2)]
    ab = [es.enter_context(nc.sbuf_tensor(f"Fab{i}", [128, 512], BF16)) for i in range(2)]
    pk = [es.enter_context(nc.psum_tensor(f"pF{i}", [128, 512], F32)) for i in range(4)]

    ld = ctx.sem("Fld"); gl = ctx.sem("Fgl"); tsn = ctx.sem("Fts")
    ev = ctx.sem("Fev"); vc = ctx.sem("Fvc")
    fo_ = [ctx.sem("Fo0"), ctx.sem("Fo1")]
    identd = nc.inline_tensor(np.eye(128, dtype=np.float32), "identF_d").ap()

    ctx.do("sync", "dma_start",
           out=xnf[:].rearrange("p (dc t) -> p dc t", t=T),
           in_=dram["xn1f"][0:D, 1:1 + T].rearrange("(dc p) t -> p dc t", p=128),
           incs=[(ld, 16)])
    ctx.do("sync", "dma_start", out=ident[:], in_=identd[:], incs=[(ld, 16)])
    identbd = nc.inline_tensor(np.eye(128, dtype=BF), "identFb_d").ap()
    identb_f = es.enter_context(nc.sbuf_tensor("Fidb", [128, 128], BF16))
    ctx.do("sync", "dma_start", out=identb_f[:], in_=identbd[:], incs=[(ld, 16)])
    _g3 = ctx.sem("ptg3")
    fws = [ctx.sem("Wst0"), ctx.sem("Wst1")]
    fie = ctx.sem("Tie")
    wstage = [es.enter_context(nc.sbuf_tensor(f"Fwst{i}", [128, 1024], BF16))
              for i in range(2)]
    pTb = [es.enter_context(nc.psum_tensor(f"FpT{i}", [128, 128], BF16))
           for i in range(2)]
    ptfree = [0, 0]
    wsfree = [0, 0]
    pi = 0
    for Wt, nm in [(W1T, "w1_bf"), (W2T, "w2_bf")]:
        for m in range(32):
            s = fws[pi % 2]
            ctx.do("sync", "dma_start", out=wstage[pi % 2][:],
                   in_=dram[nm][m * 128:(m + 1) * 128, :],
                   waits=[(_g3, _g3.n), (tsn, wsfree[pi % 2])], incs=[(s, 16)])
            for dc in range(8):
                ctx.do("tensor", "transpose", pTb[dc % 2][:],
                       wstage[pi % 2][:, dc * 128:(dc + 1) * 128], identb_f[:],
                       waits=[(s, s.n), (fie, ptfree[dc % 2]), (ld, 48)],
                       incs=[(tsn, 1)])
                ctx.do("scalar", "activation",
                       Wt[:, dc * DH + m * 128: dc * DH + m * 128 + 128],
                       pTb[dc % 2][:], AF.Identity, waits=[(tsn, tsn.n)],
                       incs=[(fie, 1)])
                ptfree[dc % 2] = fie.n
            wsfree[pi % 2] = tsn.n
            pi += 1
    wiv = fie.n
    ld0 = ld.n
    sv = {}
    for i, nm in enumerate(["L1_ffn_w1_s", "L1_ffn_w2_s"]):
        ctx.do("gpsimd", "dma_start", out=svp[:],
               in_=dram[nm].rearrange("(m p) -> m p", p=128),
               waits=[(tsn, sv.get("tr", 0))], incs=[(gl, 16)])
        ctx.do("tensor", "transpose", pk[0][:, 0:32], svp[:], ident[0:32, 0:32],
               waits=[(gl, gl.n), (ld, ld0)] + ([(ev, sv["ev"])] if i else []),
               incs=[(tsn, 1)])
        sv["tr"] = tsn.n
        ctx.do("scalar", "activation", (s1c if i == 0 else s2c)[:], pk[0][:, 0:32],
               AF.Identity, waits=[(tsn, tsn.n)], incs=[(ev, 1)])
        sv["ev"] = ev.n
    bank_free = {i: ((ev, sv["ev"]) if i == 0 else (tsn, 0)) for i in range(4)}
    t1f = [(vc, 0), (vc, 0)]
    abf = [(fo_[0], 0), (fo_[1], 0)]
    for m in range(32):
        for tb in range(8):
            i = (m * 8 + tb) % 2
            bA, bB = 2 * i, 2 * i + 1
            for dc in range(8):
                ctx.do("tensor", "matmul", pk[bA][:, 0:512],
                       W1T[:, dc * DH + m * 128:dc * DH + (m + 1) * 128],
                       xnf[:, dc * T + tb * 512:dc * T + (tb + 1) * 512],
                       start=(dc == 0), stop=(dc == 7),
                       waits=[(ld, ld0), (fie, wiv), bank_free[bA]] if dc == 0 else ())
            for dc in range(8):
                ctx.do("tensor", "matmul", pk[bB][:, 0:512],
                       W2T[:, dc * DH + m * 128:dc * DH + (m + 1) * 128],
                       xnf[:, dc * T + tb * 512:dc * T + (tb + 1) * 512],
                       start=(dc == 0), stop=(dc == 7),
                       waits=[bank_free[bB]] if dc == 0 else (),
                       incs=[(tsn, 1)] if dc == 7 else ())
            mmv = tsn.n
            ctx.do("scalar", "activation", t1[i][:], pk[bA][:, 0:512], AF.Silu,
                   scale=s1c[:, m:m + 1], waits=[(tsn, mmv), (ev, sv["ev"]),
                                                 t1f[i]], incs=[(ev, 1)])
            bank_free[bA] = (ev, ev.n)
            ctx.do("vector", "scalar_tensor_tensor", ab[i][:], t1[i][:],
                   s2c[:, m:m + 1], pk[bB][:, 0:512], ALU.mult, ALU.mult,
                   waits=[(ev, ev.n), abf[i]], incs=[(vc, 1)])
            t1f[i] = (vc, vc.n)
            bank_free[bB] = (vc, vc.n)
            ctx.do("scalar", "dma_start",
                   out=dram["a5"][m * 128:(m + 1) * 128, tb * 512:(tb + 1) * 512],
                   in_=ab[i][:], waits=[(vc, vc.n)], incs=[(fo_[i], 16)])
            abf[i] = (fo_[i], fo_[i].n)
    ctx.wait("scalar", fo_[0], fo_[0].n)
    ctx.wait("scalar", fo_[1], fo_[1].n)


def _emit_hmm(ctx, dram, es, src_name, w_name, s_name, dst_name, tag):
    nc = ctx.nc
    hT = [es.enter_context(nc.sbuf_tensor(f"{tag}hT{i}", [128, 32 * 512], BF16))
          for i in range(2)]
    ablk = [es.enter_context(nc.sbuf_tensor(f"{tag}ab{i}", [128, 32 * 512], BF16))
            for i in range(2)]
    sc_ = es.enter_context(nc.sbuf_tensor(f"{tag}sc", [128, 32], F32))
    svp = es.enter_context(nc.sbuf_tensor(f"{tag}svp", [32, 128], F32))
    ident = es.enter_context(nc.sbuf_tensor(f"{tag}id", [128, 128], F32))
    ob = [es.enter_context(nc.sbuf_tensor(f"{tag}ob{i}", [128, 512], BF16))
          for i in range(2)]
    pk = [es.enter_context(nc.psum_tensor(f"p{tag}{i}", [128, 512], F32))
          for i in range(4)]

    ld = ctx.sem("Ghld"); tsn = ctx.sem("Ghts")
    ev = ctx.sem("Ghev")
    gs = ctx.sem("Ghgs")
    hw_ = [ctx.sem("Ghhw0"), ctx.sem("Ghhw1")]
    ga_ = [ctx.sem("Ghga0"), ctx.sem("Ghga1")]
    go_ = [ctx.sem("Ghgo0"), ctx.sem("Ghgo1")]
    identd = nc.inline_tensor(np.eye(128, dtype=np.float32), f"id{tag}_d").ap()
    ctx.do("sync", "dma_start", out=ident[:], in_=identd[:], incs=[(ld, 16)])
    ctx.do("gpsimd", "dma_start", out=svp[:],
           in_=dram[s_name].rearrange("(m p) -> m p", p=128), incs=[(gs, 16)])
    ctx.do("tensor", "transpose", pk[0][:, 0:32], svp[:], ident[0:32, 0:32],
           waits=[(gs, gs.n), (ld, ld.n)], incs=[(tsn, 1)])
    ctx.do("scalar", "activation", sc_[:], pk[0][:, 0:32], AF.Identity,
           waits=[(tsn, tsn.n)], incs=[(ev, 1)])
    scv = ev.n

    wldv = {}
    aldv = {}
    mmvb = {}

    _g3 = ctx.sem("ptg3")

    def load_stripe(s8, w=()):
        s = hw_[s8 % 2]
        for ic in range(32):
            ctx.do("sync", "dma_start",
                   out=hT[s8 % 2][:, ic * 512:(ic + 1) * 512],
                   in_=dram[w_name][s8 * 512:(s8 + 1) * 512, ic * 128:(ic + 1) * 128],
                   transpose=True,
                   waits=(list(w) + [(_g3, _g3.n)]) if ic == 0 else (),
                   incs=[(s, 16)])
        wldv[s8] = (s, s.n)

    def load_ablk(s8, tb):
        po = (s8, tb - 2) if tb >= 2 else (s8 - 1, 6 + tb)
        w = [(tsn, mmvb[po])] if po in mmvb else ()
        s = ga_[tb % 2]
        ctx.do("gpsimd", "dma_start",
               out=ablk[tb % 2][:].rearrange("p (ic t) -> p ic t", t=512),
               in_=dram[src_name][0:DH, tb * 512:(tb + 1) * 512]
               .rearrange("(ic p) t -> p ic t", p=128),
               waits=w, incs=[(s, 16)])
        aldv[(s8, tb)] = (s, s.n)

    load_stripe(0)
    load_ablk(0, 0)
    bank_free = {i: ((ev, scv) if i == 0 else (tsn, 0)) for i in range(4)}
    obf = [(go_[0], 0), (go_[1], 0)]
    for s8 in range(8):
        if s8 + 1 < 8:
            load_stripe(s8 + 1, w=[(tsn, mmvb[(s8 - 1, 7)])] if s8 >= 1 else ())
        for tb in range(8):
            nxt = (s8, tb + 1) if tb + 1 < 8 else (s8 + 1, 0)
            if nxt[0] < 8:
                load_ablk(*nxt)
            av = aldv[(s8, tb)]
            for m in range(4):
                bk = (tb * 4 + m) % 4
                for ic in range(32):
                    ctx.do("tensor", "matmul", pk[bk][:, 0:512],
                           hT[s8 % 2][:, ic * 512 + m * 128:ic * 512 + (m + 1) * 128],
                           ablk[tb % 2][:, ic * 512:(ic + 1) * 512],
                           start=(ic == 0), stop=(ic == 31),
                           waits=[wldv[s8], av, bank_free[bk]]
                           if ic == 0 else (), incs=[(tsn, 1)] if ic == 31 else ())
                mmv = tsn.n
                oi = (tb * 4 + m) % 2
                ctx.do("scalar", "activation", ob[oi][:], pk[bk][:, 0:512], AF.Gelu,
                       scale=sc_[:, s8 * 4 + m:s8 * 4 + m + 1],
                       waits=[(tsn, mmv), (ev, scv), obf[oi]], incs=[(ev, 1)])
                bank_free[bk] = (ev, ev.n)
                ctx.do("scalar", "dma_start",
                       out=dram[dst_name][(s8 * 4 + m) * 128:(s8 * 4 + m + 1) * 128,
                                          tb * 512:(tb + 1) * 512],
                       in_=ob[oi][:], waits=[(ev, ev.n)], incs=[(go_[oi], 16)])
                obf[oi] = (go_[oi], go_[oi].n)
            mmvb[(s8, tb)] = tsn.n
    ctx.wait("scalar", go_[0], go_[0].n)
    ctx.wait("scalar", go_[1], go_[1].n)


def emit_p5b(ctx, dram, es):
    _emit_hmm(ctx, dram, es, "a5", "h1_bf", "L1_ffn_h1_s", "b5", "G")


def emit_p5c(ctx, dram, es):
    _emit_hmm(ctx, dram, es, "b5", "h2_bf", "L1_ffn_h2_s", "c5", "H")


def emit_p5d(ctx, dram, es):
    nc = ctx.nc
    W3T = es.enter_context(nc.sbuf_tensor("JW3T", [128, 32 * 1024], BF16))
    cblk = [es.enter_context(nc.sbuf_tensor(f"Jcb{i}", [128, 32 * 512], BF16))
            for i in range(2)]
    s3r = es.enter_context(nc.sbuf_tensor("Js3r", [1, 1024], F32))
    s3b = es.enter_context(nc.sbuf_tensor("Js3b", [128, 1024], F32))
    onesr = es.enter_context(nc.sbuf_tensor("Jones", [1, 128], F32))
    x1t = es.enter_context(nc.sbuf_tensor("Jx1", [128, 4 * 1024], F32))
    att = es.enter_context(nc.sbuf_tensor("Jat", [128, 4 * 1024], F32))
    t2 = [es.enter_context(nc.sbuf_tensor(f"Jt2{i}", [128, 512], F32)) for i in range(2)]
    oub = [es.enter_context(nc.sbuf_tensor(f"Jou{i}", [128, 512], F32)) for i in range(2)]
    pk = [es.enter_context(nc.psum_tensor(f"pJ{i}", [128, 512], F32)) for i in range(4)]

    ld = ctx.sem("Jld"); tsn = ctx.sem("Jts")
    ev = ctx.sem("Jev"); vc = ctx.sem("Jvc"); gp = ctx.sem("Jgp")
    jgs = ctx.sem("Jgs")
    jmu = ctx.sem("Jmu")
    jc_ = [ctx.sem("Jc0"), ctx.sem("Jc1")]
    jo_ = [ctx.sem("Jo0"), ctx.sem("Jo1")]

    _g3 = ctx.sem("ptg3")
    for hc in range(32):
        ctx.do("sync", "dma_start", out=W3T[:, hc * 1024:(hc + 1) * 1024],
               in_=dram["w3_bf"][0:D, hc * 128:(hc + 1) * 128],
               transpose=True, waits=[(_g3, _g3.n)], incs=[(ld, 16)])
    ctx.do("gpsimd", "dma_start", out=s3r[:],
           in_=dram["L1_ffn_w3_s"].rearrange("(a b) -> a b", a=1), incs=[(jgs, 16)])
    ctx.do("vector", "memset", onesr[:], 1.0, incs=[(vc, 1)])
    for db in range(2):
        ctx.do("tensor", "matmul", pk[2 + db][:, 0:512], onesr[:],
               s3r[0:1, db * 512:(db + 1) * 512], start=True, stop=True,
               waits=[(jgs, 16), (vc, 1)], incs=[(tsn, 1)])
        ctx.do("scalar", "activation", s3b[:, db * 512:(db + 1) * 512],
               pk[2 + db][:, 0:512], AF.Identity, waits=[(tsn, tsn.n)],
               incs=[(ev, 1)])
    s3v = ev.n
    ld0 = ld.n

    cldv = {}
    lastc = {}
    mload = (vc, 0)

    def load_c(tb, w=()):
        s = jc_[tb % 2]
        ctx.do("gpsimd", "dma_start",
               out=cblk[tb % 2][:].rearrange("p (hc t) -> p hc t", t=512),
               in_=dram["c5"][0:DH, tb * 512:(tb + 1) * 512]
               .rearrange("(hc p) t -> p hc t", p=128),
               waits=w, incs=[(s, 16)])
        cldv[tb] = (s, s.n)

    load_c(0)
    bank_free = {i: ((ev, s3v) if i >= 2 else (tsn, 0)) for i in range(4)}
    t2f = [(vc, 0), (vc, 0)]
    ouf = [(jo_[0], 0), (jo_[1], 0)]
    for tb in range(8):
        if tb + 1 < 8:
            load_c(tb + 1, w=[(tsn, lastc.get(tb - 1, 0))] if tb >= 1 else ())
        ctx.do("sync", "dma_start",
               out=x1t[:].rearrange("p (tt d) -> p tt d", d=1024),
               in_=dram["x1"][tb * 512:(tb + 1) * 512, :]
               .rearrange("(tt p) d -> p tt d", p=128),
               waits=[mload, (gp, gp.n)], incs=[(jmu, 16)])
        ctx.do("sync", "dma_start",
               out=att[:].rearrange("p (tt d) -> p tt d", d=1024),
               in_=dram["attn"][tb * 512:(tb + 1) * 512, :]
               .rearrange("(tt p) d -> p tt d", p=128), incs=[(jmu, 16)])
        mlv = jmu.n
        for tt in range(4):
            for db in range(2):
                bk = (tt * 2 + db) % 4
                for hc in range(32):
                    ctx.do("tensor", "matmul", pk[bk][:, 0:512],
                           cblk[tb % 2][:, hc * 512 + tt * 128:hc * 512 + (tt + 1) * 128],
                           W3T[:, hc * 1024 + db * 512:hc * 1024 + (db + 1) * 512],
                           start=(hc == 0), stop=(hc == 31),
                           waits=[(ld, ld0), cldv[tb], bank_free[bk]]
                           if hc == 0 else (), incs=[(tsn, 1)] if hc == 31 else ())
                mmv = tsn.n
                oi = db
                ctx.do("vector", "tensor_tensor", t2[oi][:], pk[bk][:, 0:512],
                       s3b[:, db * 512:(db + 1) * 512], ALU.mult,
                       waits=[(tsn, mmv), (ev, s3v), t2f[oi]], incs=[(vc, 1)])
                bank_free[bk] = (vc, vc.n)
                ctx.do("gpsimd", "tensor_tensor", t2[oi][:], t2[oi][:],
                       x1t[:, tt * 1024 + db * 512:tt * 1024 + (db + 1) * 512],
                       ALU.add, waits=[(vc, vc.n), (jmu, mlv)], incs=[(gp, 1)])
                ctx.do("vector", "tensor_tensor", oub[oi][:], t2[oi][:],
                       att[:, tt * 1024 + db * 512:tt * 1024 + (db + 1) * 512],
                       ALU.add, waits=[(gp, gp.n), ouf[oi]], incs=[(vc, 1)])
                t2f[oi] = (vc, vc.n)
                ctx.do("scalar", "dma_start",
                       out=dram["out"][tb * 512 + tt * 128:tb * 512 + (tt + 1) * 128,
                                       db * 512:(db + 1) * 512],
                       in_=oub[oi][:], waits=[(vc, vc.n)], incs=[(jo_[oi], 16)])
                ouf[oi] = (jo_[oi], jo_[oi].n)
        lastc[tb] = tsn.n
        mload = (vc, vc.n)
    ctx.wait("scalar", jo_[0], jo_[0].n)
    ctx.wait("scalar", jo_[1], jo_[1].n)



PHASES = []


def _register_phases():
    PHASES.clear()
    PHASES.append(("pt", emit_pt,
                   [src for _, src, *_ in WCAST],
                   [n + "_bf" for n, *_ in WCAST]))
    PHASES.append(("p0", lambda c, d, e: emit_ln(c, d, e, "u", "xn0", "xn0f", "ln0"),
                   ["u"], ["xn0", "xn0f"]))
    PHASES.append(("p1a", emit_p1a,
                   ["wip_bf", "xn0f", "L0_attn_conv_w", "L0_attn_conv_b",
                    "L0_attn_dt_bias"],
                   ["zs", "xsbc", "dtv", "lndt"]))
    PHASES.append(("p1b", emit_p1b,
                   ["wop_bf", "zs", "xsbc", "dtv", "lndt", "L0_attn_A_log",
                    "L0_attn_Dp", "L0_attn_norm_w"],
                   ["macc"]))
    PHASES.append(("p2", emit_p2,
                   ["xn0f", "wr_bf", "wk_bf", "wv_bf", "L0_ffn_mu_r",
                    "L0_ffn_mu_k", "macc", "u"],
                   ["x1"]))
    PHASES.append(("p3", lambda c, d, e: emit_ln(c, d, e, "x1", "xn1", "xn1f", "ln1"),
                   ["x1"], ["xn1", "xn1f"]))
    PHASES.append(("p4a", emit_p4a,
                   ["xn1f", "qw_bf", "kw_bf", "vw_bf"], ["kf", "vt", "qf"]))
    PHASES.append(("p4b", emit_p4b, ["kf", "vt", "qf"], ["attn"]))
    PHASES.append(("p5a", emit_p5a,
                   ["xn1f", "w1_bf", "w2_bf", "L1_ffn_w1_s", "L1_ffn_w2_s"], ["a5"]))
    PHASES.append(("p5b", emit_p5b, ["a5", "h1_bf", "L1_ffn_h1_s"], ["b5"]))
    PHASES.append(("p5c", emit_p5c, ["b5", "h2_bf", "L1_ffn_h2_s"], ["c5"]))
    PHASES.append(("p5d", emit_p5d,
                   ["c5", "w3_bf", "L1_ffn_w3_s", "x1", "attn"], ["out"]))


_register_phases()


def _declare_dram(nc, names, kinds):
    dram = {}
    for n in names:
        if n == "u":
            shape, dt = MODULE_IN["u"][1], MODULE_IN["u"][2]
        elif n == "out":
            shape, dt = [T, D], F32
        elif n in SCRATCH:
            shape, dt = SCRATCH[n]
        elif n in WEIGHT_IN:
            shape, dt = WEIGHT_IN[n]
        elif n in VEC_IN:
            shape, dt = VEC_IN[n]
        else:
            raise KeyError(n)
        kind = "ExternalInput" if kinds[n] == "in" else "ExternalOutput"
        dram[n] = nc.dram_tensor(n, shape, dt, kind=kind).ap()
    return dram


def _build_phase_program(pname):
    for name, emit, ins, outs in PHASES:
        if name == pname:
            break
    else:
        raise KeyError(pname)
    nc = bass.Bass("TRN2", target_bir_lowering=False)
    kinds = {n: "in" for n in ins}
    kinds.update({n: "out" for n in outs})
    es = contextlib.ExitStack()
    with es:
        dram = _declare_dram(nc, list(kinds), kinds)
        if pname == "pt":
            dram["zpad"] = nc.inline_tensor(
                np.zeros((DIP_PAD - D_IN_PROJ, D), dtype=BF), "zpad").ap()
        ctx = Ctx(nc)
        emit(ctx, dram, es)
        ctx.build_block(es)
    return nc, ins, outs


def _build_fused_program():
    nc = bass.Bass("TRN2", target_bir_lowering=False)
    dram = {}
    dram["u"] = nc.dram_tensor("u", MODULE_IN["u"][1], MODULE_IN["u"][2],
                               kind="ExternalInput").ap()
    for srcn, (shape, dt) in WEIGHT_IN.items():
        dram[srcn] = nc.dram_tensor(srcn, shape, dt, kind="ExternalInput").ap()
    for vn, (shape, dt) in VEC_IN.items():
        dram[vn] = nc.dram_tensor(vn, shape, dt, kind="ExternalInput").ap()
    dram["out"] = nc.dram_tensor("out", [T, D], F32, kind="ExternalOutput").ap()
    for sn, (shape, dt) in SCRATCH.items():
        dram[sn] = nc.dram_tensor(sn, shape, dt, kind="Internal").ap()
    ctx = Ctx(nc)
    ins = set()
    for name, emit, pin, pout in PHASES:
        for n in pin:
            if n == "u" or n in WEIGHT_IN or n in VEC_IN:
                ins.add(n)
        ctx.begin_phase(name)
        with contextlib.ExitStack() as pes:
            emit(ctx, dram, pes)
        if name != "pt":
            ctx.barrier()
    es = contextlib.ExitStack()
    with es:
        ctx.build_block(es)
    return nc, sorted(ins), ["out"]


FUSED = True

_PROG_CACHE = {}


def run_phase(pname, stores):
    if pname not in _PROG_CACHE:
        _PROG_CACHE[pname] = _build_phase_program(pname)
    nc, ins, outs = _PROG_CACHE[pname]
    in_maps = [{n: st[n] for n in ins} for st in stores]
    res = run_bass_kernel_spmd(nc, in_maps, core_ids=list(range(len(stores))))
    for st, r in zip(stores, res.results):
        for n in outs:
            st[n] = r[n]


def make_stores(inputs):
    stores = []
    u = np.ascontiguousarray(inputs["L0_attn_u"])
    for c in range(NCORES):
        st = {"u": u[c]}
        for k, v in inputs.items():
            if k != "L0_attn_u":
                st[k] = np.ascontiguousarray(v)
        stores.append(st)
    return stores


def kernel(**inputs):
    stores = make_stores(inputs)
    if FUSED:
        if "fused" not in _PROG_CACHE:
            _PROG_CACHE["fused"] = _build_fused_program()
        nc, ins, outs = _PROG_CACHE["fused"]
        in_maps = [{n: st[n] for n in ins} for st in stores]
        res = run_bass_kernel_spmd(nc, in_maps, core_ids=list(range(len(stores))))
        for st, r in zip(stores, res.results):
            st["out"] = r["out"]
    else:
        for pname, *_ in PHASES:
            run_phase(pname, stores)
    out = np.stack([st["out"] for st in stores], axis=0)
    return out.astype(np.float32)


# version 14
# speedup vs baseline: 1.2203x; 1.0048x over previous
import contextlib

import numpy as np
import ml_dtypes
import concourse.bass as bass
import concourse.mybir as mybir
from concourse.bass_utils import run_bass_kernel_spmd

F32 = mybir.dt.float32
BF16 = mybir.dt.bfloat16
AF = mybir.ActivationFunctionType
ALU = mybir.AluOpType
AX = mybir.AxisListType

NCORES = 8
T = 4096
D = 1024
DI = 2048
DSTATE = 64
NH = 16
PD = 128
DCONV = 4
CHUNK = 256
NCH = T // CHUNK
CONV_DIM = DI + 2 * DSTATE
D_IN_PROJ = 2 * DI + 2 * DSTATE + NH
DIP_PAD = 4352
DH = 4096

BF = ml_dtypes.bfloat16
EPS = 1e-5
NEG = -1.0e30



class Sem:
    __slots__ = ("name", "n", "h")

    def __init__(self, name):
        self.name = name
        self.n = 0
        self.h = None

    def inc(self, k=1):
        self.n += k
        return self.n


class Ctx:
    ENGINES = ("sync", "tensor", "vector", "scalar", "gpsimd")

    def __init__(self, nc):
        self.nc = nc
        self.ops = {e: [] for e in self.ENGINES}
        self.sems = {}
        self._last_wait = {}

    def sem(self, name):
        if name not in self.sems:
            self.sems[name] = Sem(name)
        return self.sems[name]

    def begin_phase(self, tag):
        pass

    def barrier(self):
        bs = {e: self.sem(f"bar_{e}") for e in self.ENGINES}
        for e in self.ENGINES:
            s = bs[e]
            tgt = s.n + 1
            self.ops[e].append(lambda eng, _s=s, _t=tgt: eng.sem_inc(_s.h, 1))
            s.n = tgt
        for e in self.ENGINES:
            for e2 in self.ENGINES:
                self.wait(e, bs[e2], bs[e2].n)

    def do(self, engine, opname, *args, waits=(), incs=(), **kw):
        wlist = []
        for s, v in waits:
            key = (engine, s.name)
            if self._last_wait.get(key, -1) >= v:
                continue
            self._last_wait[key] = v
            wlist.append((s, v))
        ilist = list(incs)
        assert len(ilist) <= 1, "one sem update per instruction"
        for s, k in ilist:
            s.inc(k)

        def run(eng, _wl=wlist, _il=ilist, _op=opname, _a=args, _kw=kw):
            for s, v in _wl:
                eng.wait_ge(s.h, v)
            inst = getattr(eng, _op)(*_a, **_kw)
            for s, k in _il:
                inst.then_inc(s.h, k)

        self.ops[engine].append(run)

    def wait(self, engine, s, v):
        key = (engine, s.name)
        if self._last_wait.get(key, -1) >= v:
            return
        self._last_wait[key] = v
        self.ops[engine].append(lambda eng, _s=s, _v=v: eng.wait_ge(_s.h, _v))

    def build_block(self, es):
        for s in self.sems.values():
            s.h = es.enter_context(self.nc.semaphore(s.name))
        block = es.enter_context(self.nc.Block())
        for e in self.ENGINES:
            fns = self.ops[e]
            if not fns:
                continue
            getattr(block, e)(lambda eng, _fns=fns: [f(eng) for f in _fns])


def ceil(a, b):
    return -(-a // b)



WCAST = [
    ("wip", "L0_attn_in_proj_w", D_IN_PROJ, D, DIP_PAD),
    ("wop", "L0_attn_out_proj_w", D, DI, None),
    ("wr", "L0_ffn_W_r", D, D, None),
    ("wk", "L0_ffn_W_k", D, D, None),
    ("wv", "L0_ffn_W_v", D, D, None),
    ("qw", "L1_attn_Qw", D, D, None),
    ("kw", "L1_attn_Kw", D, D, None),
    ("vw", "L1_attn_Vw", D, D, None),
    ("w1", "L1_ffn_w1_w", DH, D, None),
    ("w2", "L1_ffn_w2_w", DH, D, None),
    ("h1", "L1_ffn_h1_w", DH, DH, None),
    ("h2", "L1_ffn_h2_w", DH, DH, None),
    ("w3", "L1_ffn_w3_w", D, DH, None),
]

SCRATCH = {}
for _n, _src, _r, _c, _p in WCAST:
    SCRATCH[_n + "_bf"] = ([_p if _p else _r, _c], BF16)
SCRATCH.update({
    "xn0": ([T, D], BF16),
    "xn0f": ([D, T + 8], BF16),
    "xn1f": ([D, T + 8], BF16),
    "zs": ([DI, T], F32),
    "xsbc": ([CONV_DIM, T], BF16),
    "dtv": ([16, T], F32),
    "lndt": ([16, T], F32),
    "macc": ([T, D], F32),
    "x1": ([T, D], F32),
    "xn1": ([T, D], BF16),
    "qf": ([D, T], BF16),
    "kf": ([D, T], BF16),
    "vt": ([T, D], BF16),
    "attn": ([T, D], F32),
    "a5": ([DH, T], BF16),
    "b5": ([DH, T], BF16),
    "c5": ([DH, T], BF16),
})

MODULE_IN = {
    "u": ("L0_attn_u", [T, D], F32),
}

VEC_IN = {
    "L0_attn_conv_w": ([CONV_DIM, DCONV], F32),
    "L0_attn_conv_b": ([CONV_DIM], F32),
    "L0_attn_dt_bias": ([16], F32),
    "L0_attn_A_log": ([16], F32),
    "L0_attn_Dp": ([16], F32),
    "L0_attn_norm_w": ([DI], F32),
    "L0_ffn_mu_r": ([D], F32),
    "L0_ffn_mu_k": ([D], F32),
    "L1_ffn_w1_s": ([DH], F32),
    "L1_ffn_w2_s": ([DH], F32),
    "L1_ffn_h1_s": ([DH], F32),
    "L1_ffn_h2_s": ([DH], F32),
    "L1_ffn_w3_s": ([D], F32),
}

WEIGHT_IN = {src: ([r, c], F32) for _, src, r, c, _ in WCAST}



PT_GROUP = {"wip": "ptg1", "wop": "ptg2", "wr": "ptg2", "wk": "ptg2",
            "wv": "ptg2", "qw": "ptg3", "kw": "ptg3", "vw": "ptg3",
            "w1": "ptg3", "w2": "ptg3", "h1": "ptg4", "h2": "ptg4",
            "w3": "ptg3"}
PT_ORDER = ["wip", "wop", "wr", "wk", "wv", "qw", "kw", "vw", "w1", "w2",
            "w3"]
PT_ORDER_LATE = ["h1", "h2"]


def _cast_weights(ctx, dram, order):
    wmap = {w[0]: w for w in WCAST}
    for name in order:
        _, src, rows, cols, pad = wmap[name]
        g = ctx.sem(PT_GROUP[name])
        dst = dram[name + "_bf"]
        step = max(512, min(rows, (1 << 25) // (cols * 4)))
        r = 0
        while r < rows:
            rr = min(step, rows - r)
            ctx.do("gpsimd", "dma_start", out=dst[r:r + rr, :],
                   in_=dram[src][r:r + rr, :], incs=[(g, 16)])
            r += rr
        if pad:
            ctx.do("gpsimd", "dma_start", out=dst[rows:pad, :],
                   in_=dram["zpad"][0:pad - rows, :], incs=[(g, 16)])


def emit_pt(ctx, dram, es):
    if "zpad" not in dram:
        dram["zpad"] = ctx.nc.inline_tensor(
            np.zeros((DIP_PAD - D_IN_PROJ, D), dtype=BF), "zpad").ap()
    _cast_weights(ctx, dram, PT_ORDER)
    for gn in ["ptg1", "ptg2", "ptg3"]:
        g = ctx.sem(gn)
        ctx.wait("gpsimd", g, g.n)


def emit_pth(ctx, dram, es):
    _cast_weights(ctx, dram, PT_ORDER_LATE)
    g = ctx.sem("ptg4")
    ctx.wait("gpsimd", g, g.n)


def emit_ln(ctx, dram, es, src_name, dst_name, fm_name, tag, semtag="ln"):
    nc = ctx.nc
    NTT = T // 128
    xt = [es.enter_context(nc.sbuf_tensor(f"{tag}x{i}", [128, D], F32)) for i in range(2)]
    xq = es.enter_context(nc.sbuf_tensor(f"{tag}xq", [128, D], F32))
    on = [es.enter_context(nc.sbuf_tensor(f"{tag}o{i}", [128, D], BF16)) for i in range(2)]
    fmS = [es.enter_context(nc.sbuf_tensor(f"{tag}f{i}", [128, 8 * 512], BF16)) for i in range(2)]
    idb = es.enter_context(nc.sbuf_tensor(f"{tag}ib", [128, 128], BF16))
    CS = es.enter_context(nc.sbuf_tensor(f"{tag}CS", [128, 32], F32))
    CQ = es.enter_context(nc.sbuf_tensor(f"{tag}CQ", [128, 32], F32))
    NM = es.enter_context(nc.sbuf_tensor(f"{tag}NM", [128, 32], F32))
    CV = es.enter_context(nc.sbuf_tensor(f"{tag}CV", [128, 32], F32))
    CR = es.enter_context(nc.sbuf_tensor(f"{tag}CR", [128, 32], F32))
    cte = es.enter_context(nc.sbuf_tensor(f"{tag}ct", [128, 1], F32))
    pT = [es.enter_context(nc.psum_tensor(f"{tag}pT{i}", [128, 128], BF16)) for i in range(2)]

    ld = [ctx.sem(f"{semtag}ld0"), ctx.sem(f"{semtag}ld1")]
    cn = ctx.sem(f"{semtag}cn")
    sq = ctx.sem(f"{semtag}sq")
    vv = ctx.sem(f"{semtag}vv")
    ts = ctx.sem(f"{semtag}ts")
    fe = ctx.sem(f"{semtag}fe")
    st = [ctx.sem(f"{semtag}st0"), ctx.sem(f"{semtag}st1")]
    fs = [ctx.sem(f"{semtag}fs0"), ctx.sem(f"{semtag}fs1")]
    tc = ctx.sem(f"{semtag}tc")

    identd = nc.inline_tensor(np.eye(128, dtype=BF), f"ib{tag}_d").ap()
    zcold = nc.inline_tensor(np.zeros((D, 1), dtype=BF), f"zc{tag}_d").ap()
    ctx.do("sync", "dma_start", out=idb[:], in_=identd[:], incs=[(ld[0], 16)])
    ctx.do("gpsimd", "dma_start", out=dram[fm_name][0:D, 0:1], in_=zcold[:],
           allow_slow_non_contiguous=True, incs=[(fs[0], 16)])
    ctx.do("vector", "memset", cte[:], float(EPS), incs=[(cn, 1)])
    src, dst, fmd = dram[src_name], dram[dst_name], dram[fm_name]
    ldv = {}
    rdv = {}
    for i in range(NTT):
        b = i % 2
        w = [(vv, rdv[i - 2])] if i >= 2 else []
        ctx.do("sync", "dma_start", out=xt[b][:], in_=src[i * 128:(i + 1) * 128, :],
               waits=w, incs=[(ld[b], 16)])
        lv = (ld[b], ld[b].n)
        ctx.do("scalar", "activation", xq[:], xt[b][:], AF.Square,
               accum_out=CQ[:, i:i + 1], waits=[lv], incs=[(sq, 1)])
        ctx.do("vector", "reduce_sum", CS[:, i:i + 1], xt[b][:], AX.X,
               waits=[lv], incs=[(vv, 1)])
        rdv[i] = vv.n
    sqall, vvall = sq.n, vv.n
    ctx.do("vector", "tensor_scalar", NM[:], CS[:], -1.0 / D, None, ALU.mult,
           waits=[(vv, vvall)], incs=[(tc, 1)])
    ctx.do("vector", "tensor_scalar", CV[:], CQ[:], 1.0 / D, None, ALU.mult,
           waits=[(sq, sqall), (tc, tc.n)], incs=[(tc, 1)])
    ctx.do("vector", "tensor_tensor", CR[:], NM[:], NM[:], ALU.mult,
           waits=[(tc, tc.n)], incs=[(tc, 1)])
    ctx.do("vector", "tensor_tensor", CV[:], CR[:], CV[:], ALU.subtract,
           waits=[(tc, tc.n)], incs=[(vv, 1)])
    ctx.do("scalar", "activation", CR[:], CV[:], AF.Ln, bias=cte[:, 0:1],
           scale=-1.0, waits=[(vv, vv.n), (cn, cn.n), (sq, sqall)])
    ctx.do("scalar", "activation", CR[:], CR[:], AF.Exp, scale=-0.5, incs=[(sq, 1)])
    rstv = sq.n
    ptfree = [fe.n, fe.n]
    xnprev = {}
    trdone = {}
    for i in range(NTT):
        b = i % 2
        g = i % 8 // 4
        ctx.do("sync", "dma_start", out=xt[b][:], in_=src[i * 128:(i + 1) * 128, :],
               waits=[(vv, xnprev[i - 2])] if i >= 2 else [(vv, vvall)],
               incs=[(ld[b], 16)])
        lv = (ld[b], ld[b].n)
        ctx.do("vector", "tensor_scalar", on[b][:], xt[b][:], NM[:, i:i + 1],
               CR[:, i:i + 1], ALU.add, op1=ALU.mult,
               waits=[lv, (sq, rstv), (st[b], st[b].n),
                      (ts, trdone.get(i - 2, 0))], incs=[(vv, 1)])
        xnv = vv.n
        xnprev[i] = xnv
        ctx.do("scalar", "dma_start", out=dst[i * 128:(i + 1) * 128, :], in_=on[b][:],
               waits=[(vv, xnv)], incs=[(st[b], 16)])
        for dc in range(8):
            pb = pT[dc % 2]
            ctx.do("tensor", "transpose", pb[:], on[b][:, dc * 128:(dc + 1) * 128],
                   idb[:], waits=[(vv, xnv), (ld[0], 16), (fe, ptfree[dc % 2])],
                   incs=[(ts, 1)])
            ctx.do("scalar", "activation",
                   fmS[g][:, dc * 512 + (i % 4) * 128: dc * 512 + (i % 4) * 128 + 128],
                   pb[:], AF.Identity,
                   waits=[(ts, ts.n), (fs[g], fs[g].n)], incs=[(fe, 1)])
            ptfree[dc % 2] = fe.n
        trdone[i] = ts.n
        if i % 4 == 3:
            g0 = (i // 4) * 512
            for dc in range(8):
                ctx.do("scalar", "dma_start",
                       out=fmd[dc * 128:(dc + 1) * 128, 1 + g0:1 + g0 + 512],
                       in_=fmS[g][:, dc * 512:(dc + 1) * 512],
                       waits=[(fe, fe.n)], incs=[(fs[g], 16)])
    ctx.wait("scalar", st[0], st[0].n)
    ctx.wait("scalar", st[1], st[1].n)
    ctx.wait("scalar", fs[0], fs[0].n)
    ctx.wait("scalar", fs[1], fs[1].n)


def emit_p1a(ctx, dram, es):
    nc = ctx.nc
    NT = 8
    TB = 512
    CS = 516

    ipT = es.enter_context(nc.sbuf_tensor("ipT", [128, 8 * DIP_PAD], BF16))
    xnb = [es.enter_context(nc.sbuf_tensor(f"xnb{i}", [128, 8 * TB], BF16))
           for i in range(3)]
    xbc = es.enter_context(nc.sbuf_tensor("xbc", [128, 17 * CS], BF16))
    dgw = es.enter_context(nc.sbuf_tensor("dgw", [128, 68 * 128], BF16))
    cbrow = es.enter_context(nc.sbuf_tensor("cbrow", [1, CONV_DIM], BF16))
    onesr5 = es.enter_context(nc.sbuf_tensor("onesr5", [1, TB], BF16))
    zb = [es.enter_context(nc.sbuf_tensor(f"zb{i}", [128, TB], F32)) for i in range(2)]
    ob = [es.enter_context(nc.sbuf_tensor(f"ob{i}", [128, TB], BF16)) for i in range(2)]
    dtraw = es.enter_context(nc.sbuf_tensor("dtraw", [16, T], F32))
    dps = [es.enter_context(nc.sbuf_tensor(f"dps{i}", [16, T], F32)) for i in range(2)]
    cw = es.enter_context(nc.sbuf_tensor("cw", [128, 17 * 4], F32))
    cbp = es.enter_context(nc.sbuf_tensor("cbp", [17, 128], F32))
    cb = es.enter_context(nc.sbuf_tensor("cb", [128, 17], F32))
    dtb = es.enter_context(nc.sbuf_tensor("dtb", [16, 1], F32))
    ident = es.enter_context(nc.sbuf_tensor("identA", [128, 128], F32))
    c1 = es.enter_context(nc.sbuf_tensor("c1A", [16, 1], F32))
    ps = [es.enter_context(nc.psum_tensor(f"pA{i}", [128, TB], F32)) for i in range(4)]
    pst = es.enter_context(nc.psum_tensor("pAt", [128, 17], F32))

    ld = ctx.sem("Ald")
    xs_ = [ctx.sem("Axn0"), ctx.sem("Axn1"), ctx.sem("Axn2")]
    zs_ = [ctx.sem("Az0"), ctx.sem("Az1")]
    xo_ = [ctx.sem("Ax0"), ctx.sem("Ax1")]
    gl = ctx.sem("Agl")
    ts = ctx.sem("Ats")
    ev = ctx.sem("Aev")
    vc = ctx.sem("Avc")
    st = ctx.sem("Ast")

    identd = nc.inline_tensor(np.eye(128, dtype=np.float32), "identA_d").ap()
    identbd = nc.inline_tensor(np.eye(128, dtype=BF), "identAb_d").ap()
    identb_a = es.enter_context(nc.sbuf_tensor("Aidb", [128, 128], BF16))

    _g1 = ctx.sem("ptg1")
    ctx.do("sync", "dma_start", out=identb_a[:], in_=identbd[:], incs=[(ld, 16)])
    ws_ = [ctx.sem("Wst0"), ctx.sem("Wst1")]
    aie = ctx.sem("Tie")
    wstage = [es.enter_context(nc.sbuf_tensor(f"Awst{i}", [128, 1024], BF16))
              for i in range(2)]
    pTb = [es.enter_context(nc.psum_tensor(f"ApT{i}", [128, 128], BF16))
           for i in range(2)]
    ptfree = [0, 0]
    wsfree = [0, 0]
    for m in range(34):
        s = ws_[m % 2]
        ctx.do("sync", "dma_start", out=wstage[m % 2][:],
               in_=dram["wip_bf"][m * 128:(m + 1) * 128, :],
               waits=[(_g1, _g1.n), (ts, wsfree[m % 2])], incs=[(s, 16)])
        for dc in range(8):
            ctx.do("tensor", "transpose", pTb[dc % 2][:],
                   wstage[m % 2][:, dc * 128:(dc + 1) * 128], identb_a[:],
                   waits=[(s, s.n), (aie, ptfree[dc % 2]), (ld, 16)],
                   incs=[(ts, 1)])
            ctx.do("scalar", "activation",
                   ipT[:, dc * DIP_PAD + m * 128: dc * DIP_PAD + m * 128 + 128],
                   pTb[dc % 2][:], AF.Identity, waits=[(ts, ts.n)], incs=[(aie, 1)])
            ptfree[dc % 2] = aie.n
        wsfree[m % 2] = ts.n
    ipv = aie.n

    xnld = {}

    def load_xn_block(tb, waits=()):
        s = xs_[tb % 3]
        ctx.do("sync", "dma_start",
               out=xnb[tb % 3][:].rearrange("p (dc t) -> p dc t", t=TB),
               in_=dram["xn0f"][0:D, 1 + tb * TB:1 + (tb + 1) * TB]
               .rearrange("(dc p) t -> p dc t", p=128),
               waits=waits, incs=[(s, 16)])
        xnld[tb] = (s, s.n)

    load_xn_block(0)
    load_xn_block(1)
    ctx.do("gpsimd", "dma_start",
           out=cw[:].rearrange("p (c w) -> p c w", w=4),
           in_=dram["L0_attn_conv_w"].rearrange("(c p) w -> p c w", p=128),
           incs=[(gl, 16)])
    ctx.do("gpsimd", "dma_start", out=cbp[:],
           in_=dram["L0_attn_conv_b"].rearrange("(c p) -> c p", p=128),
           incs=[(gl, 16)])
    ctx.do("gpsimd", "dma_start", out=dtb[:],
           in_=dram["L0_attn_dt_bias"].rearrange("(a b) -> a b", b=1),
           incs=[(gl, 16)])
    ctx.do("sync", "dma_start", out=ident[:], in_=identd[:], incs=[(ld, 16)])
    ctx.do("tensor", "transpose", pst[:, 0:17], cbp[:], ident[0:17, 0:17],
           waits=[(gl, 32), (ld, ld.n)], incs=[(ts, 1)])
    ctx.do("scalar", "activation", cb[:], pst[:, 0:17], AF.Identity,
           waits=[(ts, ts.n)], incs=[(ev, 1)])
    ctx.do("vector", "memset", xbc[:], 0.0, incs=[(vc, 1)])
    ctx.do("vector", "memset", c1[:], 1.0, incs=[(vc, 1)])
    ctx.do("vector", "memset", onesr5[:], 1.0, incs=[(vc, 1)])
    ctx.do("gpsimd", "dma_start", out=cbrow[:],
           in_=dram["L0_attn_conv_b"].rearrange("(a b) -> a b", a=1),
           incs=[(gl, 16)])
    for cc in range(17):
        for wtap in range(4):
            k = cc * 4 + wtap
            ctx.do("vector", "tensor_scalar", dgw[:, k * 128:(k + 1) * 128],
                   identb_a[:], cw[:, k:k + 1], None, ALU.mult,
                   waits=[(gl, gl.n), (ld, ld.n)], incs=[(vc, 1)])
    dgv = vc.n

    gl_all = gl.n
    bank_free = [(ev, 0)] * 4
    nb = 0
    tailv = [0] * 17
    evacv = [[0] * 34 for _ in range(NT)]
    zst = [0, 0]
    ost = [0, 0]
    siluv = [0, 0]

    lastmm = [0] * NT
    for tb in range(NT):
        if tb + 2 < NT:
            load_xn_block(tb + 2, waits=[(ts, lastmm[tb - 1])] if tb >= 1 else ())
        for m in range(34):
            bk = ps[nb]
            w = [(aie, ipv), xnld[tb], bank_free[nb]]
            for k in range(8):
                ctx.do("tensor", "matmul",
                       bk[:, :] if m != 33 else bk[0:16, :],
                       ipT[:, k * DIP_PAD + m * 128: k * DIP_PAD + m * 128 + (128 if m != 33 else 16)],
                       xnb[tb % 3][:, k * TB: k * TB + TB],
                       start=(k == 0), stop=(k == 7),
                       waits=w if k == 0 else (),
                       incs=[(ts, 1)] if k == 7 else ())
            lastmm[tb] = ts.n
            mmv = ts.n
            if m < 16:
                zi = (tb * 16 + m) % 2
                ctx.do("scalar", "activation", zb[zi][:], bk[:], AF.Silu,
                       waits=[(ts, mmv), (zs_[zi], zs_[zi].n)], incs=[(ev, 1)])
                evc = ev.n
                ctx.do("scalar", "dma_start",
                       out=dram["zs"][m * 128:(m + 1) * 128, tb * TB:(tb + 1) * TB],
                       in_=zb[zi][:], waits=[(ev, evc)], incs=[(zs_[zi], 16)])
            elif m < 33:
                cc = m - 16
                r0 = cc * CS
                w2 = [(ts, mmv), (vc, 1)]
                if tb > 0:
                    w2.append((vc, tailv[cc]))
                ctx.do("scalar", "activation", xbc[:, r0 + 3:r0 + 3 + TB], bk[:],
                       AF.Identity, waits=w2, incs=[(ev, 1)])
            else:
                ctx.do("scalar", "activation", dtraw[0:16, tb * TB:(tb + 1) * TB],
                       bk[0:16, :], AF.Identity, waits=[(ts, mmv)], incs=[(ev, 1)])
            evacv[tb][m] = ev.n
            bank_free[nb] = (ev, ev.n)
            nb = (nb + 1) % 4
        for cc in range(17):
            r0 = cc * CS
            cbk = ps[cc % 2]
            aw = [(ev, evacv[tb][16 + cc]), (vc, dgv)]
            for wtap in range(4):
                ctx.do("tensor", "matmul", cbk[:, 0:TB],
                       dgw[:, (cc * 4 + wtap) * 128:(cc * 4 + wtap + 1) * 128],
                       xbc[:, r0 + wtap:r0 + wtap + TB],
                       start=(wtap == 0), stop=False,
                       waits=aw + [bank_free[cc % 2]] if wtap == 0 else (),
                       incs=())
            ctx.do("tensor", "matmul", cbk[:, 0:TB],
                   cbrow[0:1, cc * 128:(cc + 1) * 128], onesr5[:],
                   start=False, stop=True, incs=[(ts, 1)])
            convv = ts.n
            oi = cc % 2
            ctx.do("scalar", "activation", ob[oi][:], cbk[:, 0:TB], AF.Silu,
                   waits=[(ts, convv), (xo_[oi], xo_[oi].n)], incs=[(ev, 1)])
            bank_free[cc % 2] = (ev, ev.n)
            ctx.do("scalar", "dma_start",
                   out=dram["xsbc"][cc * 128:(cc + 1) * 128, tb * TB:(tb + 1) * TB],
                   in_=ob[oi][:], waits=[(ev, ev.n)], incs=[(xo_[oi], 16)])
            ctx.do("vector", "tensor_copy", xbc[:, r0:r0 + 3],
                   xbc[:, r0 + TB:r0 + TB + 3], waits=[(ts, convv)], incs=[(vc, 1)])
            tailv[cc] = vc.n
    dn = dtraw[0:16, :]
    d1, d2, d3 = dtraw[0:16, :], dps[0][0:16, :], dps[1][0:16, :]
    ctx.do("vector", "tensor_scalar", d1, dn, dtb[0:16, 0:1], None, ALU.add,
           waits=[(ev, evacv[NT - 1][33]), (gl, gl_all), (vc, vc.n)], incs=[(vc, 1)])
    ctx.do("scalar", "activation", d2, d1, AF.Abs,
           waits=[(vc, vc.n)], incs=[(ev, 1)])
    ctx.do("scalar", "activation", d3, d2, AF.Exp, scale=-1.0,
           waits=[(ev, ev.n)], incs=[(ev, 1)])
    expv = ev.n
    ctx.do("scalar", "activation", d3, d3, AF.Ln, bias=c1[0:16, 0:1],
           waits=[(ev, expv), (vc, 2)], incs=[(ev, 1)])
    ctx.do("vector", "tensor_scalar", d2, d1, 0.0, None, ALU.max,
           waits=[(vc, vc.n), (ev, expv)], incs=[(vc, 1)])
    ctx.do("vector", "tensor_tensor", d1, d2, d3, ALU.add,
           waits=[(vc, vc.n), (ev, ev.n)], incs=[(vc, 1)])
    ctx.do("sync", "dma_start", out=dram["dtv"][:], in_=d1,
           waits=[(vc, vc.n)], incs=[(st, 16)])
    ctx.do("scalar", "activation", d2, d1, AF.Ln, waits=[(vc, vc.n)], incs=[(ev, 1)])
    ctx.do("sync", "dma_start", out=dram["lndt"][:], in_=d2,
           waits=[(ev, ev.n)], incs=[(st, 16)])
    ctx.wait("sync", st, st.n)
    for s in (zs_ + xo_):
        ctx.wait("scalar", s, s.n)


def emit_p1b(ctx, dram, es):
    nc = ctx.nc
    CH = CHUNK

    WoT = es.enter_context(nc.sbuf_tensor("WoT", [128, 16 * 1024], BF16))
    dtv = es.enter_context(nc.sbuf_tensor("dtvS", [16, T], F32))
    lnd = es.enter_context(nc.sbuf_tensor("lndS", [16, T], F32))
    af = es.enter_context(nc.sbuf_tensor("afS", [16, T], F32))
    prev = es.enter_context(nc.sbuf_tensor("prevS", [64, NH * 128], F32))
    prevb = es.enter_context(nc.sbuf_tensor("prevbS", [64, NH * 128], BF16))
    ident = es.enter_context(nc.sbuf_tensor("identB", [128, 128], F32))
    sel = es.enter_context(nc.sbuf_tensor("selB", [16, NH * 128], F32))
    mk = es.enter_context(nc.sbuf_tensor("mkB", [128, 2 * CH], F32))
    onesr = es.enter_context(nc.sbuf_tensor("onesrB", [1, 128], F32))
    onesh = es.enter_context(nc.sbuf_tensor("oneshB", [16, 128], F32))
    onesc = es.enter_context(nc.sbuf_tensor("onescB", [128, 1], BF16))
    Acol = es.enter_context(nc.sbuf_tensor("AcolB", [16, 1], F32))
    Dpb = es.enter_context(nc.sbuf_tensor("DpbB", [128, 16], F32))
    nw = es.enter_context(nc.sbuf_tensor("nwB", [128, 16], F32))
    nwp = es.enter_context(nc.sbuf_tensor("nwpB", [16, 128], F32))
    epsc = es.enter_context(nc.sbuf_tensor("epscB", [1, 1], F32))
    vtmp = es.enter_context(nc.sbuf_tensor("vtmpB", [16, 1], F32))
    xsf = es.enter_context(nc.sbuf_tensor("xsfB", [128, NH * CH], BF16))
    xst = [es.enter_context(nc.sbuf_tensor(f"xstB{i}", [128, DI], BF16)) for i in range(2)]
    Bcf = es.enter_context(nc.sbuf_tensor("BcfB", [64, CH], BF16))
    Ccf = es.enter_context(nc.sbuf_tensor("CcfB", [64, CH], BF16))
    Bct = [es.enter_context(nc.sbuf_tensor(f"BctB{i}", [128, 64], BF16)) for i in range(2)]
    Bcw = [[es.enter_context(nc.sbuf_tensor(f"BcwB{p}{i}", [128, 64], BF16))
            for i in range(2)] for p in range(2)]
    acum = es.enter_context(nc.sbuf_tensor("acumB", [16, CH], F32))
    eac = es.enter_context(nc.sbuf_tensor("eacB", [16, CH], F32))
    acT = [es.enter_context(nc.sbuf_tensor(f"acTB{i}", [128, 16], F32)) for i in range(2)]
    colk = [es.enter_context(nc.sbuf_tensor(f"colkB{i}", [128, 16], F32)) for i in range(2)]
    dtds = [es.enter_context(nc.sbuf_tensor(f"dtdsB{i}", [128, 16], F32)) for i in range(2)]
    dtT = [es.enter_context(nc.sbuf_tensor(f"dtTB{i}", [128, 16], F32)) for i in range(2)]
    alb = es.enter_context(nc.sbuf_tensor("albB", [128, 16], F32))
    alc = es.enter_context(nc.sbuf_tensor("alcB", [16, 1], F32))
    alr = es.enter_context(nc.sbuf_tensor("alrB", [1, 16], F32))
    cdb = es.enter_context(nc.sbuf_tensor("cdbB", [64, 16], F32))
    scT = [es.enter_context(nc.sbuf_tensor(f"scTB{i}", [128, CH], F32)) for i in range(2)]
    arg = [es.enter_context(nc.sbuf_tensor(f"argB{i}", [128, CH], F32)) for i in range(2)]
    eaa = [es.enter_context(nc.sbuf_tensor(f"eaaB{i}", [128, CH], F32)) for i in range(2)]
    MT = [es.enter_context(nc.sbuf_tensor(f"MTB{i}", [128, CH], BF16)) for i in range(4)]
    Ccs = [es.enter_context(nc.sbuf_tensor(f"CcsB{i}", [64, CH], BF16)) for i in range(2)]
    Yf = es.enter_context(nc.sbuf_tensor("YfB", [128, NH * CH], F32))
    zsc = es.enter_context(nc.sbuf_tensor("zscB", [128, NH * CH], F32))
    ysq = [es.enter_context(nc.sbuf_tensor(f"ysqB{i}", [128, CH], BF16)) for i in range(2)]
    r1 = es.enter_context(nc.sbuf_tensor("r1B", [1, CH], F32))
    ynb = es.enter_context(nc.sbuf_tensor("ynbB", [128, NH * CH], BF16))
    osb = [es.enter_context(nc.sbuf_tensor(f"osbB{i}", [128, 512], F32)) for i in range(2)]
    pk = [es.enter_context(nc.psum_tensor(f"pB{i}", [128, 512], F32)) if i != 1
          else None for i in range(8)]
    pkT = es.enter_context(nc.psum_tensor("pBT", [128, 512], BF16))

    ld = ctx.sem("Bld"); gl = ctx.sem("Bgl"); tsn = ctx.sem("Bts")
    ev = ctx.sem("Bev"); vc = ctx.sem("Bvc"); gp = ctx.sem("Bgp")
    gs = ctx.sem("Bgs")
    so_ = [ctx.sem("Bso0"), ctx.sem("Bso1")]

    identd = nc.inline_tensor(np.eye(128, dtype=np.float32), "identB_d").ap()
    seln = np.zeros((16, NH * 128), dtype=np.float32)
    for h in range(NH):
        seln[h, h * 128:(h + 1) * 128] = 1.0
    seld = nc.inline_tensor(seln, "selB_d").ap()
    mkn = np.zeros((128, 2 * CH), dtype=np.float32)
    for kt in range(2):
        for kk in range(128):
            gk = kt * 128 + kk
            mkn[kk, kt * CH + gk:(kt + 1) * CH] = 1.0
    mkd = nc.inline_tensor(mkn, "mkB_d").ap()

    _g2 = ctx.sem("ptg2")
    bws = [ctx.sem("Wst0"), ctx.sem("Wst1")]
    bie = ctx.sem("Tie")
    wstage = [es.enter_context(nc.sbuf_tensor(f"Bwst{i}", [128, DI], BF16))
              for i in range(2)]
    identbd = nc.inline_tensor(np.eye(128, dtype=BF), "identBb_d").ap()
    identb_b = es.enter_context(nc.sbuf_tensor("Bidb", [128, 128], BF16))
    ctx.do("sync", "dma_start", out=identb_b[:], in_=identbd[:], incs=[(ld, 16)])
    _ptfree = [0, 0]
    _wsfree = [0, 0]
    for m in range(8):
        s = bws[m % 2]
        ctx.do("sync", "dma_start", out=wstage[m % 2][:],
               in_=dram["wop_bf"][m * 128:(m + 1) * 128, :],
               waits=[(_g2, _g2.n), (tsn, _wsfree[m % 2])], incs=[(s, 16)])
        for ec in range(16):
            ctx.do("tensor", "transpose", pkT[0:128, 0:128],
                   wstage[m % 2][:, ec * 128:(ec + 1) * 128], identb_b[:],
                   waits=[(s, s.n), (bie, _ptfree[0]), (ld, 16)],
                   incs=[(tsn, 1)])
            ctx.do("scalar", "activation",
                   WoT[:, ec * 1024 + m * 128: ec * 1024 + m * 128 + 128],
                   pkT[0:128, 0:128], AF.Identity, waits=[(tsn, tsn.n)],
                   incs=[(bie, 1)])
            _ptfree[0] = bie.n
        _wsfree[m % 2] = tsn.n
    wov = bie.n
    ctx.do("sync", "dma_start", out=dtv[:], in_=dram["dtv"][:], incs=[(ld, 16)])
    ctx.do("sync", "dma_start", out=lnd[:], in_=dram["lndt"][:], incs=[(ld, 16)])
    ctx.do("sync", "dma_start", out=ident[:], in_=identd[:], incs=[(ld, 16)])
    ctx.do("sync", "dma_start", out=sel[:], in_=seld[:], incs=[(ld, 16)])
    ctx.do("sync", "dma_start", out=mk[:], in_=mkd[:], incs=[(ld, 16)])
    ctx.do("gpsimd", "dma_start", out=vtmp[:],
           in_=dram["L0_attn_A_log"].rearrange("(a b) -> a b", b=1), incs=[(gs, 16)])
    ctx.do("gpsimd", "dma_start", out=Acol[:],
           in_=dram["L0_attn_Dp"].rearrange("(a b) -> a b", b=1), incs=[(gs, 16)])
    ctx.do("gpsimd", "dma_start", out=nwp[:],
           in_=dram["L0_attn_norm_w"].rearrange("(c p) -> c p", p=128), incs=[(gs, 16)])
    ld0, gl0 = ld.n, gs.n

    for t_, v_ in [(onesr, 1.0), (onesh, 1.0), (onesc, 1.0), (epsc, float(EPS))]:
        ctx.do("vector", "memset", t_[:], v_, incs=[(vc, 1)])
    ctx.do("vector", "memset", prev[:], 0.0, incs=[(vc, 1)])
    msv = vc.n
    ctx.do("tensor", "transpose", pk[2][0:1, 0:16], Acol[:], ident[0:16, 0:16],
           waits=[(gs, gl0), (ld, ld0)], incs=[(tsn, 1)])
    ctx.do("scalar", "activation", alr[:], pk[2][0:1, 0:16], AF.Identity,
           waits=[(tsn, tsn.n)], incs=[(ev, 1)])
    ctx.do("tensor", "matmul", pk[2][:, 0:16], onesr[:], alr[:], start=True,
           stop=True, waits=[(ev, ev.n), (vc, msv)], incs=[(tsn, 1)])
    ctx.do("scalar", "activation", Dpb[:], pk[2][:, 0:16], AF.Identity,
           waits=[(tsn, tsn.n)], incs=[(ev, 1)])
    dpv = ev.n
    ctx.do("tensor", "transpose", pk[2][:, 0:16], nwp[:], ident[0:16, 0:16],
           waits=[(ev, dpv)], incs=[(tsn, 1)])
    ctx.do("scalar", "activation", nw[:], pk[2][:, 0:16], AF.Identity,
           waits=[(tsn, tsn.n)], incs=[(ev, 1)])
    nwv = ev.n
    ctx.do("scalar", "activation", Acol[:], vtmp[:], AF.Exp,
           waits=[(gs, gl0), (ev, dpv)], incs=[(ev, 1)])
    ctx.do("vector", "tensor_scalar", Acol[:], Acol[:], -1.0, None, ALU.mult,
           waits=[(ev, ev.n)], incs=[(vc, 1)])
    ctx.do("vector", "tensor_scalar", af[:], dtv[:], Acol[0:16, 0:1], None, ALU.mult,
           waits=[(vc, vc.n), (ld, ld0)], incs=[(vc, 1)])
    afv = vc.n

    bank_free = {i: (tsn, 0) for i in range(8)}
    G = {k: (tsn, 0) for k in
         ["xsf", "xst", "Bcf", "Ccf", "Bct", "zsc", "scT", "acum", "eac",
          "alb", "alr", "cdb", "acT", "dtT", "dtds", "MT0", "MT1", "Ccs",
          "Bcw", "prevdone", "Yf", "ynb", "arg0", "arg1", "eaa0", "eaa1"]}
    ysqv = [0] * 16
    gavp = [0] * 16
    osbst = [0, 0]
    ldch = {}

    def load_chunk(c):
        t0 = c * CH
        ctx.do("sync", "dma_start",
               out=xsf[:].rearrange("p (ec t) -> p ec t", t=CH),
               in_=dram["xsbc"][0:DI, t0:t0 + CH].rearrange("(ec p) t -> p ec t", p=128),
               waits=[G["xsf"]], incs=[(ld, 16)])
        ctx.do("sync", "dma_start", out=Bcf[:], in_=dram["xsbc"][DI:DI + 64, t0:t0 + CH],
               waits=[G["Bcf"]], incs=[(ld, 16)])
        ctx.do("sync", "dma_start", out=Ccf[:],
               in_=dram["xsbc"][DI + 64:DI + 128, t0:t0 + CH],
               waits=[G["Ccf"]], incs=[(ld, 16)])
        ctx.do("gpsimd", "dma_start",
               out=zsc[:].rearrange("p (ec t) -> p ec t", t=CH),
               in_=dram["zs"][0:DI, t0:t0 + CH].rearrange("(ec p) t -> p ec t", p=128),
               waits=[G["zsc"]], incs=[(gl, 16)])
        ldch[c] = (ld.n, gl.n)

    load_chunk(0)

    xtie = ctx.sem("Bxe")
    _xptf = [0, 0]
    for c in range(NCH):
        t0 = c * CH
        lv, gv = ldch[c]
        acs = af[0:16, t0:t0 + CH]
        for tt in range(2):
            for ec in range(16):
                ctx.do("tensor", "transpose", pkT[0:128, 0:128],
                       xsf[:, ec * CH + tt * 128:ec * CH + (tt + 1) * 128],
                       identb_b[:],
                       waits=[(ld, lv), (xtie, _xptf[0]), G["xst"]],
                       incs=[(tsn, 1)])
                ctx.do("scalar", "activation",
                       xst[tt][:, ec * 128:(ec + 1) * 128], pkT[0:128, 0:128],
                       AF.Identity, waits=[(tsn, tsn.n)], incs=[(xtie, 1)])
                _xptf[0] = xtie.n
            ctx.do("tensor", "transpose", pkT[0:128, 0:64],
                   Bcf[0:64, tt * 128:(tt + 1) * 128], identb_b[0:64, 0:64],
                   waits=[(ld, lv), (xtie, _xptf[0])], incs=[(tsn, 1)])
            ctx.do("scalar", "activation", Bct[tt][:, 0:64],
                   pkT[0:128, 0:64], AF.Identity,
                   waits=[(tsn, tsn.n)], incs=[(xtie, 1)])
            _xptf[0] = xtie.n
        xstv = xtie.n
        ctx.do("vector", "tensor_tensor_scan", acum[0:16, 0:128], onesh[:],
               acs[:, 0:128], 0.0, ALU.mult, ALU.add,
               waits=[(vc, afv), G["acum"]], incs=[(vc, 1)])
        ctx.do("vector", "tensor_copy", vtmp[:], acum[0:16, 127:128],
               waits=[(vc, vc.n)], incs=[(vc, 1)])
        ctx.do("vector", "tensor_tensor_scan", acum[0:16, 128:256], onesh[:],
               acs[:, 128:256], vtmp[0:16, 0:1], ALU.mult, ALU.add,
               waits=[(vc, vc.n)], incs=[(vc, 1)])
        acv = vc.n
        ctx.do("vector", "reduce_sum", alc[0:16, 0:1], acs, AX.X,
               waits=[(vc, afv)], incs=[(vc, 1)])
        alv = vc.n
        ctx.do("scalar", "activation", eac[:], acum[0:16, :], AF.Exp,
               waits=[(vc, acv), G["eac"]], incs=[(ev, 1)])
        eacv = ev.n

        def small_tr(dst, src_ap, k, wt):
            ctx.do("tensor", "transpose", pk[2][:, 0:k], src_ap, ident[0:k, 0:k],
                   waits=wt + [bank_free[2]], incs=[(tsn, 1)])
            ctx.do("scalar", "activation", dst, pk[2][:, 0:dst.shape[1]], AF.Identity,
                   waits=[(tsn, tsn.n)], incs=[(ev, 1)])
            bank_free[2] = (ev, ev.n)
            return ev.n

        for tt in range(2):
            small_tr(acT[tt][:], acum[0:16, tt * 128:(tt + 1) * 128], 16,
                     [(vc, acv), G["acT"]])
            small_tr(dtT[tt][:], dtv[0:16, t0 + tt * 128:t0 + (tt + 1) * 128], 16,
                     [(ld, ld0), G["dtT"]])
            ctx.do("tensor", "transpose", pk[2][:, 0:16],
                   lnd[0:16, t0 + tt * 128:t0 + (tt + 1) * 128], ident[0:16, 0:16],
                   waits=[(ld, ld0), bank_free[2]], incs=[(tsn, 1)])
            ctx.do("vector", "tensor_tensor", colk[tt][:], pk[2][:, 0:16], acT[tt][:],
                   ALU.subtract, waits=[(tsn, tsn.n), (ev, ev.n)], incs=[(vc, 1)])
            bank_free[2] = (vc, vc.n)
        ctx.do("tensor", "transpose", pk[2][0:1, 0:16], alc[:], ident[0:16, 0:16],
               waits=[(vc, alv), bank_free[2], G["alr"]], incs=[(tsn, 1)])
        ctx.do("scalar", "activation", alr[:], pk[2][0:1, 0:16], AF.Identity,
               waits=[(tsn, tsn.n)], incs=[(ev, 1)])
        alrv = ev.n
        bank_free[2] = (ev, alrv)
        ctx.do("tensor", "matmul", pk[2][:, 0:16], onesr[:], alr[:], start=True,
               stop=True, waits=[(ev, alrv), bank_free[2]], incs=[(tsn, 1)])
        ctx.do("scalar", "activation", alb[:], pk[2][:, 0:16], AF.Identity,
               waits=[(tsn, tsn.n), G["alb"]], incs=[(ev, 1)])
        albv = ev.n
        bank_free[2] = (ev, albv)
        ctx.do("scalar", "activation", alr[:], alr[:], AF.Exp,
               waits=[(ev, albv)], incs=[(ev, 1)])
        cdrv = ev.n
        ctx.do("tensor", "matmul", pk[2][0:64, 0:16], onesr[0:1, 0:64], alr[:],
               start=True, stop=True, waits=[(ev, cdrv), bank_free[2]], incs=[(tsn, 1)])
        ctx.do("scalar", "activation", cdb[:], pk[2][0:64, 0:16], AF.Identity,
               waits=[(tsn, tsn.n), G["cdb"]], incs=[(ev, 1)])
        cdbv = ev.n
        bank_free[2] = (ev, cdbv)
        dtdv = []
        for tt in range(2):
            ctx.do("vector", "tensor_tensor", dtds[tt][:], alb[:], acT[tt][:],
                   ALU.subtract, waits=[(ev, albv), (vc, vc.n), G["dtds"]],
                   incs=[(vc, 1)])
            ctx.do("scalar", "activation", dtds[tt][:], dtds[tt][:], AF.Exp,
                   waits=[(vc, vc.n)], incs=[(ev, 1)])
            ctx.do("vector", "tensor_tensor", dtds[tt][:], dtds[tt][:], dtT[tt][:],
                   ALU.mult, waits=[(ev, ev.n)], incs=[(vc, 1)])
            dtdv.append(vc.n)
        scv = []
        for kt in range(2):
            ctx.do("tensor", "matmul", pk[0][:, 0:CH], Bcf[:, kt * 128:(kt + 1) * 128],
                   Ccf[:], start=True, stop=True,
                   waits=[(ld, lv), bank_free[0]], incs=[(tsn, 1)])
            ctx.do("vector", "tensor_tensor", scT[kt][:], pk[0][:, 0:CH],
                   mk[:, kt * CH:(kt + 1) * CH], ALU.mult,
                   waits=[(tsn, tsn.n), (ld, ld0), G["scT"]], incs=[(vc, 1)])
            bank_free[0] = (vc, vc.n)
            scv.append(vc.n)
        ctx.do("scalar", "activation", prevb[:], prev[:], AF.Identity,
               waits=[(vc, msv if c == 0 else G["prevdone"][1])], incs=[(ev, 1)])
        pbv = ev.n
        epv = []
        st_bc = {}
        st_eb = {}
        st_arg = {}
        st_exp = {}
        st_mt = {}
        st_ccs = {}
        st_y = {}
        st_sp = {}
        st_bcw = {}

        def s_bc(h):
            bb = 3 + (h % 2)
            ctx.do("tensor", "matmul", pk[bb][:, 0:CH], sel[:, h * 128:(h + 1) * 128],
                   acum[0:16, :], start=True, stop=True,
                   waits=[(vc, acv), bank_free[bb]], incs=[(tsn, 1)])
            st_bc[h] = tsn.n

        def s_eb(h):
            ctx.do("tensor", "matmul", pk[6][0:64, 0:CH],
                   sel[:, h * 128:h * 128 + 64], eac[:], start=True, stop=True,
                   waits=[(ev, eacv), bank_free[6]], incs=[(tsn, 1)])
            st_eb[h] = tsn.n

        def s_mid(h):
            bb = 3 + (h % 2)
            par = h % 2
            for kt in range(2):
                ctx.do("vector", "tensor_scalar", arg[kt][:], pk[bb][:, 0:CH],
                       colk[kt][:, h:h + 1], 30.0, ALU.add, op1=ALU.min,
                       waits=[(tsn, st_bc[h]), (vc, vc.n),
                              (ev, st_exp.get(h - 1, 0))], incs=[(vc, 1)])
            st_arg[h] = vc.n
            bank_free[bb] = (vc, vc.n)
            for kt in range(2):
                ctx.do("gpsimd", "tensor_scalar", Bcw[par][kt][:], Bct[kt][:],
                       dtds[kt][:, h:h + 1], None, ALU.mult,
                       waits=[(xtie, xstv), (vc, dtdv[kt]),
                              (tsn, st_sp.get(h - 2, 0))], incs=[(gp, 1)])
            st_bcw[h] = gp.n
            for kt in range(2):
                ctx.do("scalar", "activation", eaa[kt][:], arg[kt][:], AF.Exp,
                       waits=[(vc, st_arg[h]), (gp, st_mt.get(h - 1, 0)),
                              (ev, ev.n)], incs=[(ev, 1)])
            st_exp[h] = ev.n

        def s_ccs(h):
            par = h % 2
            ctx.do("vector", "tensor_tensor", Ccs[par][:], pk[6][0:64, 0:CH], Ccf[:],
                   ALU.mult, waits=[(tsn, st_eb[h]), (ld, lv),
                                    (tsn, st_y.get(h - 1, 0))], incs=[(vc, 1)])
            st_ccs[h] = vc.n
            bank_free[6] = (vc, vc.n)

        def s_mt(h):
            par = h % 2
            for kt in range(2):
                ctx.do("gpsimd", "tensor_tensor", MT[par * 2 + kt][:], eaa[kt][:],
                       scT[kt][:], ALU.mult,
                       waits=[(ev, st_exp[h]), (vc, scv[kt]),
                              (tsn, st_y.get(h - 1, 0))], incs=[(gp, 1)])
            st_mt[h] = gp.n

        def s_pe(h):
            par = h % 2
            for kt in range(2):
                ctx.do("tensor", "matmul", pk[5][:, 0:CH],
                       xst[kt][:, h * 128:(h + 1) * 128], MT[par * 2 + kt][:],
                       start=(kt == 0), stop=False,
                       waits=[(gp, st_mt[h]), (xtie, xstv), bank_free[5]],
                       incs=[(tsn, 1)])
            ctx.do("tensor", "matmul", pk[5][:, 0:CH],
                   prevb[0:64, h * 128:(h + 1) * 128], Ccs[par][0:64, :],
                   start=False, stop=True,
                   waits=[(ev, pbv), (vc, st_ccs[h])], incs=[(tsn, 1)])
            st_y[h] = tsn.n
            for tt in range(2):
                ctx.do("tensor", "matmul", pk[7][0:64, 0:128], Bcw[par][tt][:],
                       xst[tt][:, h * 128:(h + 1) * 128],
                       start=(tt == 0), stop=(tt == 1),
                       waits=[(gp, st_bcw[h])] + ([bank_free[7]] if tt == 0 else []),
                       incs=[(tsn, 1)])
            st_sp[h] = tsn.n

        def s_epi(h):
            ctx.do("vector", "scalar_tensor_tensor", Yf[:, h * CH:(h + 1) * CH],
                   xsf[:, h * CH:(h + 1) * CH], Dpb[:, h:h + 1], pk[5][:, 0:CH],
                   ALU.mult, ALU.add,
                   waits=[(tsn, st_y[h]), (ev, dpv), (ld, lv),
                          (ev, ysqv[h]), (vc, gavp[h])], incs=[(vc, 1)])
            bank_free[5] = (vc, vc.n)
            epv.append(vc.n)
            ctx.do("vector", "scalar_tensor_tensor", prev[:, h * 128:(h + 1) * 128],
                   prev[:, h * 128:(h + 1) * 128], cdb[0:64, h:h + 1],
                   pk[7][0:64, 0:128], ALU.mult, ALU.add,
                   waits=[(tsn, st_sp[h]), (ev, max(pbv, cdbv)), (vc, vc.n)],
                   incs=[(vc, 1)])
            bank_free[7] = (vc, vc.n)

        s_bc(0)
        s_eb(0)
        s_bc(1)
        s_mid(0)
        s_ccs(0)
        s_mt(0)
        for h in range(NH):
            s_pe(h)
            if h + 2 < NH:
                s_bc(h + 2)
            if h + 1 < NH:
                s_eb(h + 1)
                s_mid(h + 1)
                s_ccs(h + 1)
            s_epi(h)
            if h + 1 < NH:
                s_mt(h + 1)
        mtv = [st_mt[NH - 1]] * 2
        spv = st_sp[NH - 1]
        yv = st_y[NH - 1]
        ebv = st_eb[NH - 1]
        bcv = st_bc[NH - 1]
        G["prevdone"] = (vc, vc.n)
        G["acum"] = (tsn, bcv)
        G["eac"] = (tsn, ebv)
        G["scT"] = (gp, max(mtv))
        G["acT"] = (vc, max(dtdv))
        G["dtT"] = (vc, max(dtdv))
        G["alb"] = (vc, max(dtdv))
        G["alr"] = (tsn, tsn.n)
        G["cdb"] = (vc, vc.n)
        G["dtds"] = (gp, st_bcw[NH - 1])
        G["Bct"] = (gp, st_bcw[NH - 1])
        gav = []
        for ec in range(16):
            ctx.do("vector", "tensor_tensor", Yf[:, ec * CH:(ec + 1) * CH],
                   Yf[:, ec * CH:(ec + 1) * CH], zsc[:, ec * CH:(ec + 1) * CH],
                   ALU.mult, waits=[(vc, epv[ec]), (gl, gv)], incs=[(vc, 1)])
            gav.append(vc.n)
        G["zsc"] = (vc, vc.n)
        gavp = list(gav)
        rmm = []
        for ec in range(16):
            qi = ec % 2
            ctx.do("scalar", "activation", ysq[qi][:], Yf[:, ec * CH:(ec + 1) * CH],
                   AF.Square, waits=[(vc, gav[ec]),
                                     (tsn, rmm[ec - 2]) if ec >= 2 else (tsn, 0)],
                   incs=[(ev, 1)])
            ysqv[ec] = ev.n
            ctx.do("tensor", "matmul", pk[0][0:1, 0:CH], onesc[:], ysq[qi][:],
                   start=(ec == 0), stop=(ec == 15),
                   waits=[(ev, ev.n)] + ([bank_free[0]] if ec == 0 else []),
                   incs=[(tsn, 1)])
            rmm.append(tsn.n)
        ctx.do("scalar", "activation", r1[:], pk[0][0:1, 0:CH], AF.Ln,
               scale=1.0 / DI, bias=epsc[0:1, 0:1], waits=[(tsn, rmm[15])],
               incs=[(ev, 1)])
        bank_free[0] = (ev, ev.n)
        ctx.do("scalar", "activation", r1[:], r1[:], AF.Exp, scale=-0.5,
               waits=[(ev, ev.n)], incs=[(ev, 1)])
        rsv = ev.n
        ctx.do("tensor", "matmul", pk[2][:, 0:CH], onesr[:], r1[:], start=True,
               stop=True, waits=[(ev, rsv), bank_free[2]], incs=[(tsn, 1)])
        rbv = tsn.n
        ynv = []
        for ec in range(16):
            ctx.do("vector", "scalar_tensor_tensor", ynb[:, ec * CH:(ec + 1) * CH],
                   Yf[:, ec * CH:(ec + 1) * CH], nw[:, ec:ec + 1], pk[2][:, 0:CH],
                   ALU.mult, ALU.mult,
                   waits=[(tsn, rbv), (ev, nwv), (vc, gav[ec]), G["ynb"]],
                   incs=[(vc, 1)])
            ynv.append(vc.n)
        bank_free[2] = (vc, vc.n)
        G["Yf"] = (vc, vc.n)
        opv = 0
        for idx, (tt, db) in enumerate([(a, b) for a in range(2) for b in range(2)]):
            ob_ = pk[3] if idx % 2 == 0 else pk[4]
            obi = 3 if idx % 2 == 0 else 4
            for ec in range(16):
                ctx.do("tensor", "matmul", ob_[:, 0:512],
                       ynb[:, ec * CH + tt * 128:ec * CH + (tt + 1) * 128],
                       WoT[:, ec * 1024 + db * 512:ec * 1024 + (db + 1) * 512],
                       start=(ec == 0), stop=(ec == 15),
                       waits=[(vc, ynv[15]), (bie, wov), bank_free[obi]]
                       if ec == 0 else (), incs=[(tsn, 1)] if ec == 15 else ())
            opv = tsn.n
            oi = idx % 2
            ctx.do("scalar", "activation", osb[oi][:], ob_[:, 0:512], AF.Identity,
                   waits=[(tsn, opv), (so_[oi], so_[oi].n)], incs=[(ev, 1)])
            bank_free[obi] = (ev, ev.n)
            ctx.do("scalar", "dma_start",
                   out=dram["macc"][t0 + tt * 128:t0 + (tt + 1) * 128,
                                    db * 512:(db + 1) * 512],
                   in_=osb[oi][:], waits=[(ev, ev.n)], incs=[(so_[oi], 16)])
        G["ynb"] = (tsn, opv)
        G["xsf"] = (vc, epv[15])
        G["xst"] = (tsn, spv)
        G["Bcf"] = (tsn, tsn.n)
        G["Ccf"] = (vc, vc.n)
        if c + 1 < NCH:
            load_chunk(c + 1)
    ctx.wait("scalar", so_[0], so_[0].n)
    ctx.wait("scalar", so_[1], so_[1].n)


def emit_p2(ctx, dram, es):
    nc = ctx.nc
    NT = 8
    TB = 512

    W_ = {k: es.enter_context(nc.sbuf_tensor(f"C{k}", [128, 8 * 1024], BF16))
          for k in ["r1", "r2", "k1", "k2", "v"]}
    mur = es.enter_context(nc.sbuf_tensor("Cmur", [128, 8], F32))
    muk = es.enter_context(nc.sbuf_tensor("Cmuk", [128, 8], F32))
    omr = es.enter_context(nc.sbuf_tensor("Comr", [128, 8], F32))
    omk = es.enter_context(nc.sbuf_tensor("Comk", [128, 8], F32))
    mup = es.enter_context(nc.sbuf_tensor("Cmup", [8, 128], F32))
    ident = es.enter_context(nc.sbuf_tensor("Cident", [128, 128], F32))
    identb_c = es.enter_context(nc.sbuf_tensor("Cidb", [128, 128], BF16))
    xnblk = [es.enter_context(nc.sbuf_tensor(f"Cxn{i}", [128, 8 * TB], BF16))
             for i in range(2)]
    xpblk = [es.enter_context(nc.sbuf_tensor(f"Cxp{i}", [128, 8 * TB], BF16))
             for i in range(2)]
    kab = es.enter_context(nc.sbuf_tensor("Ckab", [128, 8 * TB], BF16))
    ktmp = [es.enter_context(nc.sbuf_tensor(f"Ckt{i}", [128, TB], F32)) for i in range(2)]
    rbuf = es.enter_context(nc.sbuf_tensor("Crbuf", [128, 8 * 512], BF16))
    mtile = es.enter_context(nc.sbuf_tensor("Cmt", [128, 4 * 1024], F32))
    utile = es.enter_context(nc.sbuf_tensor("Cut", [128, 4 * 1024], F32))
    t2 = [es.enter_context(nc.sbuf_tensor(f"Ct2{i}", [128, 512], F32)) for i in range(2)]
    x1b = [es.enter_context(nc.sbuf_tensor(f"Cx1{i}", [128, 512], F32)) for i in range(2)]
    wstage = [es.enter_context(nc.sbuf_tensor(f"Cwst{i}", [128, 1024], BF16))
              for i in range(2)]
    pk = [es.enter_context(nc.psum_tensor(f"pC{i}", [128, 512], F32)) for i in range(7)]
    pTb = es.enter_context(nc.psum_tensor("CpT", [128, 128], BF16))

    ld = ctx.sem("Cld"); gl = ctx.sem("Cgl"); tsn = ctx.sem("Cts")
    ev = ctx.sem("Cev"); vc = ctx.sem("Cvc"); gp = ctx.sem("Cgp")
    mu_ld = ctx.sem("Cmu")
    xst_ = [ctx.sem("Cxo0"), ctx.sem("Cxo1")]
    xbl = [ctx.sem("Cxb0"), ctx.sem("Cxb1")]
    cws = [ctx.sem("Wst0"), ctx.sem("Wst1")]
    cie = ctx.sem("Tie")

    identd = nc.inline_tensor(np.eye(128, dtype=np.float32), "identC_d").ap()
    identbd = nc.inline_tensor(np.eye(128, dtype=BF), "identCb_d").ap()
    ctx.do("sync", "dma_start", out=ident[:], in_=identd[:], incs=[(ld, 16)])
    ctx.do("sync", "dma_start", out=identb_c[:], in_=identbd[:], incs=[(ld, 16)])
    muv = {}
    for i, nm in enumerate(["L0_ffn_mu_r", "L0_ffn_mu_k"]):
        ctx.do("gpsimd", "dma_start", out=mup[:],
               in_=dram[nm].rearrange("(c p) -> c p", p=128),
               waits=[(tsn, muv.get("tr", 0))], incs=[(gl, 16)])
        ctx.do("tensor", "transpose", pk[0][:, 0:8], mup[:], ident[0:8, 0:8],
               waits=[(gl, gl.n), (ld, 32)] + ([(ev, muv["ev"])] if i else []),
               incs=[(tsn, 1)])
        muv["tr"] = tsn.n
        ctx.do("scalar", "activation", (mur if i == 0 else muk)[:], pk[0][:, 0:8],
               AF.Identity, waits=[(tsn, tsn.n)], incs=[(ev, 1)])
        muv["ev"] = ev.n
    ctx.do("vector", "tensor_scalar", omr[:], mur[:], -1.0, 1.0, ALU.mult,
           op1=ALU.add, waits=[(ev, muv["ev"])], incs=[(vc, 1)])
    ctx.do("vector", "tensor_scalar", omk[:], muk[:], -1.0, 1.0, ALU.mult,
           op1=ALU.add, incs=[(vc, 1)])
    omv = vc.n
    _g2 = ctx.sem("ptg2")
    ptfree = [0, 0]
    wsfree = [0, 0]
    pi = 0
    plan = [("r1", "wr_bf", mur), ("r2", "wr_bf", omr), ("k1", "wk_bf", muk),
            ("k2", "wk_bf", omk), ("v", "wv_bf", None)]
    for key, nm, sc in plan:
        Wt = W_[key]
        for m in range(8):
            s = cws[pi % 2]
            ctx.do("sync", "dma_start", out=wstage[pi % 2][:],
                   in_=dram[nm][m * 128:(m + 1) * 128, :],
                   waits=[(_g2, _g2.n), (tsn, wsfree[pi % 2])], incs=[(s, 16)])
            for dc in range(8):
                ctx.do("tensor", "transpose", pTb[:],
                       wstage[pi % 2][:, dc * 128:(dc + 1) * 128], identb_c[:],
                       waits=[(s, s.n), (cie, ptfree[0]), (ld, 32),
                              (vc, omv)], incs=[(tsn, 1)])
                kw = {}
                if sc is not None:
                    kw["scale"] = sc[:, dc:dc + 1]
                ctx.do("scalar", "activation",
                       Wt[:, dc * 1024 + m * 128: dc * 1024 + m * 128 + 128],
                       pTb[:], AF.Identity, waits=[(tsn, tsn.n)],
                       incs=[(cie, 1)], **kw)
                ptfree[0] = cie.n
            wsfree[pi % 2] = tsn.n
            pi += 1
    wiv = cie.n

    def load_xn(tb, w=()):
        s = xbl[tb % 2]
        ctx.do("sync", "dma_start",
               out=xnblk[tb % 2][:].rearrange("p (dc t) -> p dc t", t=TB),
               in_=dram["xn0f"][0:D, 1 + tb * TB:1 + (tb + 1) * TB]
               .rearrange("(dc p) t -> p dc t", p=128), waits=w, incs=[(s, 16)])
        ctx.do("sync", "dma_start",
               out=xpblk[tb % 2][:].rearrange("p (dc t) -> p dc t", t=TB),
               in_=dram["xn0f"][0:D, tb * TB:(tb + 1) * TB]
               .rearrange("(dc p) t -> p dc t", p=128), incs=[(s, 16)])
        return (s, s.n)

    xnv = [None] * NT
    xnv[0] = load_xn(0)
    xnv[1] = load_xn(1)
    bank_free = {i: (tsn, 0) for i in range(7)}
    bank_free[0] = (ev, muv["ev"])
    t2f = [(vc, 0), (vc, 0)]
    x1f = [(xst_[0], 0), (xst_[1], 0)]
    ktf = [(ev, 0), (ev, 0)]
    rb_free = (vc, 0)
    ka_free = (tsn, 0)
    mload = (vc, 0)
    lastmm = [0] * NT
    for tb in range(NT):
        ctx.do("sync", "dma_start",
               out=mtile[:].rearrange("p (tt d) -> p tt d", d=1024),
               in_=dram["macc"][tb * TB:(tb + 1) * TB, :]
               .rearrange("(tt p) d -> p tt d", p=128),
               waits=[mload], incs=[(mu_ld, 16)])
        ctx.do("sync", "dma_start",
               out=utile[:].rearrange("p (tt d) -> p tt d", d=1024),
               in_=dram["u"][tb * TB:(tb + 1) * TB, :]
               .rearrange("(tt p) d -> p tt d", p=128), incs=[(mu_ld, 16)])
        mlv = mu_ld.n
        rev = []
        for tt in range(4):
            for db in range(2):
                bi = (tt * 2 + db) % 3
                for half, (Wt, xb) in enumerate(
                        [(W_["r1"], xnblk[tb % 2]), (W_["r2"], xpblk[tb % 2])]):
                    for dc in range(8):
                        ctx.do("tensor", "matmul", pk[bi][:, 0:512],
                               xb[:, dc * TB + tt * 128:dc * TB + (tt + 1) * 128],
                               Wt[:, dc * 1024 + db * 512:dc * 1024 + (db + 1) * 512],
                               start=(half == 0 and dc == 0),
                               stop=(half == 1 and dc == 7),
                               waits=[xnv[tb], (cie, wiv), bank_free[bi]]
                               if half == 0 and dc == 0 else (),
                               incs=[(tsn, 1)] if half == 1 and dc == 7 else ())
                ctx.do("scalar", "activation", rbuf[:, (tt * 2 + db) * 512:
                                                    (tt * 2 + db + 1) * 512],
                       pk[bi][:, 0:512], AF.Sigmoid,
                       waits=[(tsn, tsn.n), rb_free], incs=[(ev, 1)])
                bank_free[bi] = (ev, ev.n)
                rev.append(ev.n)
        kav = []
        for m in range(8):
            bi = 3 + m % 2
            for half, (Wt, xb) in enumerate(
                    [(W_["k1"], xnblk[tb % 2]), (W_["k2"], xpblk[tb % 2])]):
                for dc in range(8):
                    ctx.do("tensor", "matmul", pk[bi][:, 0:512],
                           Wt[:, dc * 1024 + m * 128:dc * 1024 + m * 128 + 128],
                           xb[:, dc * TB:(dc + 1) * TB],
                           start=(half == 0 and dc == 0),
                           stop=(half == 1 and dc == 7),
                           waits=[xnv[tb], (cie, wiv), bank_free[bi], ka_free]
                           if half == 0 and dc == 0 else (),
                           incs=[(tsn, 1)] if half == 1 and dc == 7 else ())
            mmv = tsn.n
            qi = m % 2
            ctx.do("vector", "tensor_scalar", ktmp[qi][:], pk[bi][:, 0:512], 0.0,
                   None, ALU.max, waits=[(tsn, mmv), ktf[qi]], incs=[(vc, 1)])
            bank_free[bi] = (vc, vc.n)
            ctx.do("scalar", "activation", kab[:, m * TB:(m + 1) * TB], ktmp[qi][:],
                   AF.Square, waits=[(vc, vc.n)], incs=[(ev, 1)])
            ktf[qi] = (ev, ev.n)
            kav.append(ev.n)
        if tb + 2 < NT:
            xnv[tb + 2] = load_xn(tb + 2, w=[(tsn, tsn.n)])
        for tt in range(4):
            for db in range(2):
                bi = 5 + db
                for m in range(8):
                    ctx.do("tensor", "matmul", pk[bi][:, 0:512],
                           kab[:, m * TB + tt * 128:m * TB + (tt + 1) * 128],
                           W_["v"][:, m * 1024 + db * 512:m * 1024 + (db + 1) * 512],
                           start=(m == 0), stop=(m == 7),
                           waits=[(ev, kav[7]), (cie, wiv), bank_free[bi]]
                           if m == 0 else (), incs=[(tsn, 1)] if m == 7 else ())
                mmv = tsn.n
                oi = db
                ctx.do("vector", "tensor_tensor", t2[oi][:], pk[bi][:, 0:512],
                       rbuf[:, (tt * 2 + db) * 512:(tt * 2 + db + 1) * 512],
                       ALU.mult, waits=[(tsn, mmv), (ev, rev[tt * 2 + db]),
                                        t2f[oi]], incs=[(vc, 1)])
                bank_free[bi] = (vc, vc.n)
                ctx.do("gpsimd", "tensor_tensor", t2[oi][:], t2[oi][:],
                       mtile[:, tt * 1024 + db * 512:tt * 1024 + (db + 1) * 512],
                       ALU.add, waits=[(vc, vc.n), (mu_ld, mlv)], incs=[(gp, 1)])
                ctx.do("vector", "tensor_tensor", x1b[oi][:], t2[oi][:],
                       utile[:, tt * 1024 + db * 512:tt * 1024 + (db + 1) * 512],
                       ALU.add, waits=[(gp, gp.n), x1f[oi]], incs=[(vc, 1)])
                t2f[oi] = (vc, vc.n)
                ctx.do("scalar", "dma_start",
                       out=dram["x1"][tb * TB + tt * 128:tb * TB + (tt + 1) * 128,
                                      db * 512:(db + 1) * 512],
                       in_=x1b[oi][:], waits=[(vc, vc.n)], incs=[(xst_[oi], 16)])
                x1f[oi] = (xst_[oi], xst_[oi].n)
        lastmm[tb] = tsn.n
        ka_free = (tsn, tsn.n)
        rb_free = (vc, vc.n)
        mload = (vc, vc.n)
    ctx.wait("scalar", xst_[0], xst_[0].n)
    ctx.wait("scalar", xst_[1], xst_[1].n)


def emit_p4a(ctx, dram, es):
    nc = ctx.nc
    xnf = es.enter_context(nc.sbuf_tensor("Dxnf", [128, 8 * T], BF16))
    qwS = es.enter_context(nc.sbuf_tensor("DqwS", [128, 8 * 1024], BF16))
    kwS = es.enter_context(nc.sbuf_tensor("DkwS", [128, 8 * 1024], BF16))
    vwS = es.enter_context(nc.sbuf_tensor("DvwS", [128, 8 * 1024], BF16))
    ob = [es.enter_context(nc.sbuf_tensor(f"Dob{i}", [128, 512], BF16)) for i in range(4)]
    pk = [es.enter_context(nc.psum_tensor(f"pD{i}", [128, 512], F32)) for i in range(4)]

    ld = ctx.sem("Dld"); tsn = ctx.sem("Dts"); ev = ctx.sem("Dev")
    do_ = [ctx.sem(f"Do{i}") for i in range(4)]

    ctx.do("sync", "dma_start",
           out=xnf[:].rearrange("p (dc t) -> p dc t", t=T),
           in_=dram["xn1f"][0:D, 1:1 + T].rearrange("(dc p) t -> p dc t", p=128),
           incs=[(ld, 16)])
    _g3 = ctx.sem("ptg3")
    for Wt, nm in [(qwS, "qw_bf"), (kwS, "kw_bf"), (vwS, "vw_bf")]:
        ctx.do("sync", "dma_start",
               out=Wt[:].rearrange("p (dc j) -> p dc j", j=1024),
               in_=dram[nm].rearrange("(dc p) j -> p dc j", p=128),
               waits=[(_g3, _g3.n)], incs=[(ld, 16)])
    ld0 = ld.n
    bank_free = {i: (ev, 0) for i in range(4)}
    obst = [0] * 4
    nb = 0
    for kind in ["k", "q", "v"]:
        if kind in ("k", "q"):
            wS = kwS if kind == "k" else qwS
            for tb in range(8):
                for m in range(8):
                    for dc in range(8):
                        ctx.do("tensor", "matmul", pk[nb][:, 0:512],
                               wS[:, dc * 1024 + m * 128:dc * 1024 + (m + 1) * 128],
                               xnf[:, dc * T + tb * 512:dc * T + (tb + 1) * 512],
                               start=(dc == 0), stop=(dc == 7),
                               waits=[(ld, ld0), bank_free[nb]] if dc == 0 else (),
                               incs=[(tsn, 1)] if dc == 7 else ())
                    ctx.do("scalar", "activation", ob[nb][:], pk[nb][:, 0:512],
                           AF.Identity, scale=(1.0 / 32.0 if kind == "q" else 1.0),
                           waits=[(tsn, tsn.n), (do_[nb], do_[nb].n)], incs=[(ev, 1)])
                    bank_free[nb] = (ev, ev.n)
                    ctx.do("scalar", "dma_start",
                           out=dram["kf" if kind == "k" else "qf"]
                           [m * 128:(m + 1) * 128, tb * 512:(tb + 1) * 512],
                           in_=ob[nb][:], waits=[(ev, ev.n)], incs=[(do_[nb], 16)])
                    nb = (nb + 1) % 4
        else:
            for tt in range(32):
                for db in range(2):
                    for dc in range(8):
                        ctx.do("tensor", "matmul", pk[nb][:, 0:512],
                               xnf[:, dc * T + tt * 128:dc * T + (tt + 1) * 128],
                               vwS[:, dc * 1024 + db * 512:dc * 1024 + (db + 1) * 512],
                               start=(dc == 0), stop=(dc == 7),
                               waits=[(ld, ld0), bank_free[nb]] if dc == 0 else (),
                               incs=[(tsn, 1)] if dc == 7 else ())
                    ctx.do("scalar", "activation", ob[nb][:], pk[nb][:, 0:512],
                           AF.Identity, waits=[(tsn, tsn.n), (do_[nb], do_[nb].n)],
                           incs=[(ev, 1)])
                    bank_free[nb] = (ev, ev.n)
                    ctx.do("scalar", "dma_start",
                           out=dram["vt"][tt * 128:(tt + 1) * 128,
                                          db * 512:(db + 1) * 512],
                           in_=ob[nb][:], waits=[(ev, ev.n)], incs=[(do_[nb], 16)])
                    nb = (nb + 1) % 4
    for s in do_:
        ctx.wait("scalar", s, s.n)


def emit_p4b(ctx, dram, es):
    nc = ctx.nc
    Kf = es.enter_context(nc.sbuf_tensor("EKf", [128, 8 * T], BF16))
    Vt = es.enter_context(nc.sbuf_tensor("EVt", [128, 32 * 1024], BF16))
    qt_ = [es.enter_context(nc.sbuf_tensor(f"Eqt{i}", [128, 8 * 128], BF16)) for i in range(2)]
    mkt = es.enter_context(nc.sbuf_tensor("Emkt", [128, 4 * 512], F32))
    identb = es.enter_context(nc.sbuf_tensor("Eib", [128, 128], BF16))
    sms = es.enter_context(nc.sbuf_tensor("Esms", [128, 512], F32))
    pb = es.enter_context(nc.sbuf_tensor("Epb", [128, 512], BF16))
    pTs = es.enter_context(nc.sbuf_tensor("EpTs", [128, 512], BF16))
    acc = es.enter_context(nc.sbuf_tensor("Eacc", [128, 1024], F32))
    mcol = es.enter_context(nc.sbuf_tensor("Emc", [128, 1], F32))
    mbc = es.enter_context(nc.sbuf_tensor("Emb", [128, 1], F32))
    mnc = es.enter_context(nc.sbuf_tensor("Emn", [128, 1], F32))
    ngm = es.enter_context(nc.sbuf_tensor("Engm", [128, 1], F32))
    alp = es.enter_context(nc.sbuf_tensor("Ealp", [128, 1], F32))
    lcol = es.enter_context(nc.sbuf_tensor("Elc", [128, 1], F32))
    lb = es.enter_context(nc.sbuf_tensor("Elb", [128, 1], F32))
    rec = es.enter_context(nc.sbuf_tensor("Erec", [128, 1], F32))
    oatt = [es.enter_context(nc.sbuf_tensor(f"Eo{i}", [128, 1024], F32)) for i in range(2)]
    pS = [es.enter_context(nc.psum_tensor(f"pE{i}", [128, 512], F32)) for i in range(2)]
    pV = [es.enter_context(nc.psum_tensor(f"pEv{i}", [128, 512], F32)) for i in range(2)]
    pT = [es.enter_context(nc.psum_tensor(f"pEt{i}", [128, 128], BF16)) for i in range(4)]

    ld = ctx.sem("Eld"); tsn = ctx.sem("Ets"); ev = ctx.sem("Eev")
    vc = ctx.sem("Evc")
    eq = [ctx.sem("Eq0"), ctx.sem("Eq1")]
    eo = [ctx.sem("Eo0"), ctx.sem("Eo1")]

    identd = nc.inline_tensor(np.eye(128, dtype=BF), "identE_d").ap()
    mkn = np.zeros((128, 4 * 512), dtype=np.float32)
    for v in range(4):
        for r_ in range(128):
            for cb in range(4):
                pass
    for v in range(4):
        cols = np.arange(512)
        rows = np.arange(128)[:, None]
        mkn[:, v * 512:(v + 1) * 512] = np.where(v * 128 + rows >= cols, 0.0, NEG)
    mkd = nc.inline_tensor(mkn, "mkE_d").ap()

    ctx.do("sync", "dma_start", out=Kf[:].rearrange("p (m k) -> p m k", k=T),
           in_=dram["kf"].rearrange("(m p) k -> p m k", p=128), incs=[(ld, 16)])
    ctx.do("sync", "dma_start", out=Vt[:].rearrange("p (tt j) -> p tt j", j=1024),
           in_=dram["vt"].rearrange("(tt p) j -> p tt j", p=128), incs=[(ld, 16)])
    ctx.do("sync", "dma_start", out=mkt[:], in_=mkd[:], incs=[(ld, 16)])
    ctx.do("sync", "dma_start", out=identb[:], in_=identd[:], incs=[(ld, 16)])
    ld0 = ld.n
    ctx.do("vector", "memset", acc[:], 0.0, incs=[(vc, 1)])
    ctx.do("vector", "memset", lcol[:], 1.0, incs=[(vc, 1)])
    msv = vc.n

    qld = {}

    def load_q(qt, w=()):
        s = eq[qt % 2]
        ctx.do("sync", "dma_start",
               out=qt_[qt % 2][:].rearrange("p (jc q) -> p jc q", q=128),
               in_=dram["qf"][0:D, qt * 128:(qt + 1) * 128]
               .rearrange("(jc p) q -> p jc q", p=128),
               waits=w, incs=[(s, 16)])
        qld[qt] = (s, s.n)

    load_q(0)
    bank_free = {k: (vc, 0) for k in ["s0", "s1", "v0", "v1"]}
    pt_free = [(ev, 0)] * 4
    sms_free = (ev, 0)
    pb_free = (tsn, 0)
    pts_free = (tsn, 0)
    ost = [0, 0]
    lastq = 0

    for qt in range(32):
        if qt + 1 < 32:
            load_q(qt + 1, w=[(tsn, lastq)])
        nkb = qt // 4 + 1
        ctx.do("vector", "memset", mcol[:], -3.0e38, waits=[(vc, vc.n)], incs=[(vc, 1)])
        for kb in range(nkb):
            sb = kb % 2
            pSb = pS[sb]
            diag = (kb == qt // 4)
            for jc in range(8):
                ctx.do("tensor", "matmul", pSb[:, 0:512],
                       qt_[qt % 2][:, jc * 128:(jc + 1) * 128],
                       Kf[:, jc * T + kb * 512:jc * T + (kb + 1) * 512],
                       start=(jc == 0), stop=(jc == 7),
                       waits=[qld[qt], (ld, ld0), bank_free[f"s{sb}"]] if jc == 0 else (),
                       incs=[(tsn, 1)] if jc == 7 else ())
            smv = tsn.n
            if diag:
                ctx.do("vector", "tensor_tensor", sms[:], pSb[:, 0:512],
                       mkt[:, (qt % 4) * 512:(qt % 4 + 1) * 512], ALU.add,
                       waits=[(tsn, smv), (ld, ld0), sms_free], incs=[(vc, 1)])
                ctx.do("vector", "reduce_max", mbc[:, 0:1], sms[:], AX.X,
                       waits=[(vc, vc.n)], incs=[(vc, 1)])
                esrc = sms[:]
                bank_free[f"s{sb}"] = (vc, vc.n)
            else:
                ctx.do("vector", "reduce_max", mbc[:, 0:1], pSb[:, 0:512], AX.X,
                       waits=[(tsn, smv), (vc, vc.n)], incs=[(vc, 1)])
                esrc = pSb[:, 0:512]
            rdv = vc.n
            ctx.do("vector", "tensor_tensor", mnc[:, 0:1], mcol[:, 0:1], mbc[:, 0:1],
                   ALU.max, waits=[(vc, rdv)], incs=[(vc, 1)])
            ctx.do("vector", "tensor_scalar", ngm[:, 0:1], mnc[:, 0:1], -1.0, None,
                   ALU.mult, waits=[(vc, vc.n)], incs=[(vc, 1)])
            ngv = vc.n
            ctx.do("scalar", "activation", alp[:, 0:1], mcol[:, 0:1], AF.Exp,
                   bias=ngm[:, 0:1], waits=[(vc, ngv)], incs=[(ev, 1)])
            alv = ev.n
            ctx.do("scalar", "activation", pb[:], esrc, AF.Exp, bias=ngm[:, 0:1],
                   accum_out=lb[:, 0:1], waits=[(ev, alv), pb_free], incs=[(ev, 1)])
            pev = ev.n
            if not diag:
                bank_free[f"s{sb}"] = (ev, pev)
            else:
                sms_free = (ev, pev)
            ctx.do("vector", "tensor_copy", mcol[:, 0:1], mnc[:, 0:1],
                   waits=[(ev, alv)], incs=[(vc, 1)])
            ctx.do("vector", "scalar_tensor_tensor", lcol[:, 0:1], lcol[:, 0:1],
                   alp[:, 0:1], lb[:, 0:1], ALU.mult, ALU.add,
                   waits=[(ev, pev), (vc, vc.n)], incs=[(vc, 1)])
            for k4 in range(4):
                ctx.do("tensor", "transpose", pT[k4][:], pb[:, k4 * 128:(k4 + 1) * 128],
                       identb[:], waits=[(ev, pev), (ld, ld0), pt_free[k4]],
                       incs=[(tsn, 1)])
                ctx.do("scalar", "activation", pTs[:, k4 * 128:(k4 + 1) * 128],
                       pT[k4][:], AF.Identity, waits=[(tsn, tsn.n), pts_free],
                       incs=[(ev, 1)])
                pt_free[k4] = (ev, ev.n)
            ptv = ev.n
            pb_free = (tsn, tsn.n)
            pvv = []
            for db in range(2):
                for k4 in range(4):
                    ctx.do("tensor", "matmul", pV[db][:, 0:512],
                           pTs[:, k4 * 128:(k4 + 1) * 128],
                           Vt[:, (kb * 4 + k4) * 1024 + db * 512:
                              (kb * 4 + k4) * 1024 + (db + 1) * 512],
                           start=(k4 == 0), stop=(k4 == 3),
                           waits=[(ev, ptv), (ld, ld0), bank_free[f"v{db}"]]
                           if k4 == 0 else (), incs=[(tsn, 1)] if k4 == 3 else ())
                pvv.append(tsn.n)
            pts_free = (tsn, tsn.n)
            for db in range(2):
                ctx.do("vector", "scalar_tensor_tensor",
                       acc[:, db * 512:(db + 1) * 512], acc[:, db * 512:(db + 1) * 512],
                       alp[:, 0:1], pV[db][:, 0:512], ALU.mult, ALU.add,
                       waits=[(tsn, pvv[db]), (ev, alv), (vc, vc.n),
                              (vc, msv)], incs=[(vc, 1)])
                bank_free[f"v{db}"] = (vc, vc.n)
        lastq = tsn.n
        ctx.do("vector", "reciprocal", rec[:, 0:1], lcol[:, 0:1],
               waits=[(vc, vc.n)], incs=[(vc, 1)])
        oi = qt % 2
        ctx.do("vector", "tensor_scalar", oatt[oi][:], acc[:], rec[:, 0:1], None,
               ALU.mult, waits=[(vc, vc.n), (eo[oi], ost[oi])], incs=[(vc, 1)])
        ctx.do("scalar", "dma_start", out=dram["attn"][qt * 128:(qt + 1) * 128, :],
               in_=oatt[oi][:], waits=[(vc, vc.n)], incs=[(eo[oi], 16)])
        ost[oi] = eo[oi].n
    ctx.wait("scalar", eo[0], eo[0].n)
    ctx.wait("scalar", eo[1], eo[1].n)


def emit_p5a(ctx, dram, es):
    nc = ctx.nc
    xnf = es.enter_context(nc.sbuf_tensor("Fxnf", [128, 8 * T], BF16))
    W1T = es.enter_context(nc.sbuf_tensor("FW1T", [128, 8 * DH], BF16))
    W2T = es.enter_context(nc.sbuf_tensor("FW2T", [128, 8 * DH], BF16))
    s1c = es.enter_context(nc.sbuf_tensor("Fs1c", [128, 32], F32))
    s2c = es.enter_context(nc.sbuf_tensor("Fs2c", [128, 32], F32))
    svp = es.enter_context(nc.sbuf_tensor("Fsvp", [32, 128], F32))
    ident = es.enter_context(nc.sbuf_tensor("Fident", [128, 128], F32))
    t1 = [es.enter_context(nc.sbuf_tensor(f"Ft1{i}", [128, 512], F32)) for i in range(2)]
    ab = [es.enter_context(nc.sbuf_tensor(f"Fab{i}", [128, 512], BF16)) for i in range(2)]
    pk = [es.enter_context(nc.psum_tensor(f"pF{i}", [128, 512], F32)) for i in range(4)]

    ld = ctx.sem("Fld"); gl = ctx.sem("Fgl"); tsn = ctx.sem("Fts")
    ev = ctx.sem("Fev"); vc = ctx.sem("Fvc")
    fo_ = [ctx.sem("Fo0"), ctx.sem("Fo1")]
    identd = nc.inline_tensor(np.eye(128, dtype=np.float32), "identF_d").ap()

    ctx.do("sync", "dma_start",
           out=xnf[:].rearrange("p (dc t) -> p dc t", t=T),
           in_=dram["xn1f"][0:D, 1:1 + T].rearrange("(dc p) t -> p dc t", p=128),
           incs=[(ld, 16)])
    ctx.do("sync", "dma_start", out=ident[:], in_=identd[:], incs=[(ld, 16)])
    identbd = nc.inline_tensor(np.eye(128, dtype=BF), "identFb_d").ap()
    identb_f = es.enter_context(nc.sbuf_tensor("Fidb", [128, 128], BF16))
    ctx.do("sync", "dma_start", out=identb_f[:], in_=identbd[:], incs=[(ld, 16)])
    _g3 = ctx.sem("ptg3")
    fws = [ctx.sem("Wst0"), ctx.sem("Wst1")]
    fie = ctx.sem("Tie")
    wstage = [es.enter_context(nc.sbuf_tensor(f"Fwst{i}", [128, 1024], BF16))
              for i in range(2)]
    pTb = [es.enter_context(nc.psum_tensor(f"FpT{i}", [128, 128], BF16))
           for i in range(2)]
    ptfree = [0, 0]
    wsfree = [0, 0]
    pi = 0
    for Wt, nm in [(W1T, "w1_bf"), (W2T, "w2_bf")]:
        for m in range(32):
            s = fws[pi % 2]
            ctx.do("sync", "dma_start", out=wstage[pi % 2][:],
                   in_=dram[nm][m * 128:(m + 1) * 128, :],
                   waits=[(_g3, _g3.n), (tsn, wsfree[pi % 2])], incs=[(s, 16)])
            for dc in range(8):
                ctx.do("tensor", "transpose", pTb[dc % 2][:],
                       wstage[pi % 2][:, dc * 128:(dc + 1) * 128], identb_f[:],
                       waits=[(s, s.n), (fie, ptfree[dc % 2]), (ld, 48)],
                       incs=[(tsn, 1)])
                ctx.do("scalar", "activation",
                       Wt[:, dc * DH + m * 128: dc * DH + m * 128 + 128],
                       pTb[dc % 2][:], AF.Identity, waits=[(tsn, tsn.n)],
                       incs=[(fie, 1)])
                ptfree[dc % 2] = fie.n
            wsfree[pi % 2] = tsn.n
            pi += 1
    wiv = fie.n
    ld0 = ld.n
    sv = {}
    for i, nm in enumerate(["L1_ffn_w1_s", "L1_ffn_w2_s"]):
        ctx.do("gpsimd", "dma_start", out=svp[:],
               in_=dram[nm].rearrange("(m p) -> m p", p=128),
               waits=[(tsn, sv.get("tr", 0))], incs=[(gl, 16)])
        ctx.do("tensor", "transpose", pk[0][:, 0:32], svp[:], ident[0:32, 0:32],
               waits=[(gl, gl.n), (ld, ld0)] + ([(ev, sv["ev"])] if i else []),
               incs=[(tsn, 1)])
        sv["tr"] = tsn.n
        ctx.do("scalar", "activation", (s1c if i == 0 else s2c)[:], pk[0][:, 0:32],
               AF.Identity, waits=[(tsn, tsn.n)], incs=[(ev, 1)])
        sv["ev"] = ev.n
    bank_free = {i: ((ev, sv["ev"]) if i == 0 else (tsn, 0)) for i in range(4)}
    t1f = [(vc, 0), (vc, 0)]
    abf = [(fo_[0], 0), (fo_[1], 0)]
    for m in range(32):
        for tb in range(8):
            i = (m * 8 + tb) % 2
            bA, bB = 2 * i, 2 * i + 1
            for dc in range(8):
                ctx.do("tensor", "matmul", pk[bA][:, 0:512],
                       W1T[:, dc * DH + m * 128:dc * DH + (m + 1) * 128],
                       xnf[:, dc * T + tb * 512:dc * T + (tb + 1) * 512],
                       start=(dc == 0), stop=(dc == 7),
                       waits=[(ld, ld0), (fie, wiv), bank_free[bA]] if dc == 0 else ())
            for dc in range(8):
                ctx.do("tensor", "matmul", pk[bB][:, 0:512],
                       W2T[:, dc * DH + m * 128:dc * DH + (m + 1) * 128],
                       xnf[:, dc * T + tb * 512:dc * T + (tb + 1) * 512],
                       start=(dc == 0), stop=(dc == 7),
                       waits=[bank_free[bB]] if dc == 0 else (),
                       incs=[(tsn, 1)] if dc == 7 else ())
            mmv = tsn.n
            ctx.do("scalar", "activation", t1[i][:], pk[bA][:, 0:512], AF.Silu,
                   scale=s1c[:, m:m + 1], waits=[(tsn, mmv), (ev, sv["ev"]),
                                                 t1f[i]], incs=[(ev, 1)])
            bank_free[bA] = (ev, ev.n)
            ctx.do("vector", "scalar_tensor_tensor", ab[i][:], t1[i][:],
                   s2c[:, m:m + 1], pk[bB][:, 0:512], ALU.mult, ALU.mult,
                   waits=[(ev, ev.n), abf[i]], incs=[(vc, 1)])
            t1f[i] = (vc, vc.n)
            bank_free[bB] = (vc, vc.n)
            ctx.do("scalar", "dma_start",
                   out=dram["a5"][m * 128:(m + 1) * 128, tb * 512:(tb + 1) * 512],
                   in_=ab[i][:], waits=[(vc, vc.n)], incs=[(fo_[i], 16)])
            abf[i] = (fo_[i], fo_[i].n)
    ctx.wait("scalar", fo_[0], fo_[0].n)
    ctx.wait("scalar", fo_[1], fo_[1].n)


def _emit_hmm(ctx, dram, es, src_name, w_name, s_name, dst_name, tag):
    nc = ctx.nc
    hT = [es.enter_context(nc.sbuf_tensor(f"{tag}hT{i}", [128, 32 * 512], BF16))
          for i in range(2)]
    ablk = [es.enter_context(nc.sbuf_tensor(f"{tag}ab{i}", [128, 32 * 512], BF16))
            for i in range(2)]
    sc_ = es.enter_context(nc.sbuf_tensor(f"{tag}sc", [128, 32], F32))
    svp = es.enter_context(nc.sbuf_tensor(f"{tag}svp", [32, 128], F32))
    ident = es.enter_context(nc.sbuf_tensor(f"{tag}id", [128, 128], F32))
    ob = [es.enter_context(nc.sbuf_tensor(f"{tag}ob{i}", [128, 512], BF16))
          for i in range(2)]
    pk = [es.enter_context(nc.psum_tensor(f"p{tag}{i}", [128, 512], F32))
          for i in range(4)]

    ld = ctx.sem("Ghld"); tsn = ctx.sem("Ghts")
    ev = ctx.sem("Ghev")
    gs = ctx.sem("Ghgs")
    hw_ = [ctx.sem("Ghhw0"), ctx.sem("Ghhw1")]
    ga_ = [ctx.sem("Ghga0"), ctx.sem("Ghga1")]
    go_ = [ctx.sem("Ghgo0"), ctx.sem("Ghgo1")]
    identd = nc.inline_tensor(np.eye(128, dtype=np.float32), f"id{tag}_d").ap()
    ctx.do("sync", "dma_start", out=ident[:], in_=identd[:], incs=[(ld, 16)])
    ctx.do("gpsimd", "dma_start", out=svp[:],
           in_=dram[s_name].rearrange("(m p) -> m p", p=128), incs=[(gs, 16)])
    ctx.do("tensor", "transpose", pk[0][:, 0:32], svp[:], ident[0:32, 0:32],
           waits=[(gs, gs.n), (ld, ld.n)], incs=[(tsn, 1)])
    ctx.do("scalar", "activation", sc_[:], pk[0][:, 0:32], AF.Identity,
           waits=[(tsn, tsn.n)], incs=[(ev, 1)])
    scv = ev.n

    wldv = {}
    aldv = {}
    mmvb = {}

    _g3 = ctx.sem("ptg4")

    def load_stripe(s8, w=()):
        s = hw_[s8 % 2]
        for ic in range(32):
            ctx.do("sync", "dma_start",
                   out=hT[s8 % 2][:, ic * 512:(ic + 1) * 512],
                   in_=dram[w_name][s8 * 512:(s8 + 1) * 512, ic * 128:(ic + 1) * 128],
                   transpose=True,
                   waits=(list(w) + [(_g3, _g3.n)]) if ic == 0 else (),
                   incs=[(s, 16)])
        wldv[s8] = (s, s.n)

    def load_ablk(s8, tb):
        po = (s8, tb - 2) if tb >= 2 else (s8 - 1, 6 + tb)
        w = [(tsn, mmvb[po])] if po in mmvb else ()
        s = ga_[tb % 2]
        ctx.do("gpsimd", "dma_start",
               out=ablk[tb % 2][:].rearrange("p (ic t) -> p ic t", t=512),
               in_=dram[src_name][0:DH, tb * 512:(tb + 1) * 512]
               .rearrange("(ic p) t -> p ic t", p=128),
               waits=w, incs=[(s, 16)])
        aldv[(s8, tb)] = (s, s.n)

    load_stripe(0)
    load_ablk(0, 0)
    bank_free = {i: ((ev, scv) if i == 0 else (tsn, 0)) for i in range(4)}
    obf = [(go_[0], 0), (go_[1], 0)]
    for s8 in range(8):
        if s8 + 1 < 8:
            load_stripe(s8 + 1, w=[(tsn, mmvb[(s8 - 1, 7)])] if s8 >= 1 else ())
        for tb in range(8):
            nxt = (s8, tb + 1) if tb + 1 < 8 else (s8 + 1, 0)
            if nxt[0] < 8:
                load_ablk(*nxt)
            av = aldv[(s8, tb)]
            for m in range(4):
                bk = (tb * 4 + m) % 4
                for ic in range(32):
                    ctx.do("tensor", "matmul", pk[bk][:, 0:512],
                           hT[s8 % 2][:, ic * 512 + m * 128:ic * 512 + (m + 1) * 128],
                           ablk[tb % 2][:, ic * 512:(ic + 1) * 512],
                           start=(ic == 0), stop=(ic == 31),
                           waits=[wldv[s8], av, bank_free[bk]]
                           if ic == 0 else (), incs=[(tsn, 1)] if ic == 31 else ())
                mmv = tsn.n
                oi = (tb * 4 + m) % 2
                ctx.do("scalar", "activation", ob[oi][:], pk[bk][:, 0:512], AF.Gelu,
                       scale=sc_[:, s8 * 4 + m:s8 * 4 + m + 1],
                       waits=[(tsn, mmv), (ev, scv), obf[oi]], incs=[(ev, 1)])
                bank_free[bk] = (ev, ev.n)
                ctx.do("scalar", "dma_start",
                       out=dram[dst_name][(s8 * 4 + m) * 128:(s8 * 4 + m + 1) * 128,
                                          tb * 512:(tb + 1) * 512],
                       in_=ob[oi][:], waits=[(ev, ev.n)], incs=[(go_[oi], 16)])
                obf[oi] = (go_[oi], go_[oi].n)
            mmvb[(s8, tb)] = tsn.n
    ctx.wait("scalar", go_[0], go_[0].n)
    ctx.wait("scalar", go_[1], go_[1].n)


def emit_p5b(ctx, dram, es):
    _emit_hmm(ctx, dram, es, "a5", "h1_bf", "L1_ffn_h1_s", "b5", "G")


def emit_p5c(ctx, dram, es):
    _emit_hmm(ctx, dram, es, "b5", "h2_bf", "L1_ffn_h2_s", "c5", "H")


def emit_p5d(ctx, dram, es):
    nc = ctx.nc
    W3T = es.enter_context(nc.sbuf_tensor("JW3T", [128, 32 * 1024], BF16))
    cblk = [es.enter_context(nc.sbuf_tensor(f"Jcb{i}", [128, 32 * 512], BF16))
            for i in range(2)]
    s3r = es.enter_context(nc.sbuf_tensor("Js3r", [1, 1024], F32))
    s3b = es.enter_context(nc.sbuf_tensor("Js3b", [128, 1024], F32))
    onesr = es.enter_context(nc.sbuf_tensor("Jones", [1, 128], F32))
    x1t = es.enter_context(nc.sbuf_tensor("Jx1", [128, 4 * 1024], F32))
    att = es.enter_context(nc.sbuf_tensor("Jat", [128, 4 * 1024], F32))
    t2 = [es.enter_context(nc.sbuf_tensor(f"Jt2{i}", [128, 512], F32)) for i in range(2)]
    oub = [es.enter_context(nc.sbuf_tensor(f"Jou{i}", [128, 512], F32)) for i in range(2)]
    pk = [es.enter_context(nc.psum_tensor(f"pJ{i}", [128, 512], F32)) for i in range(4)]

    ld = ctx.sem("Jld"); tsn = ctx.sem("Jts")
    ev = ctx.sem("Jev"); vc = ctx.sem("Jvc"); gp = ctx.sem("Jgp")
    jgs = ctx.sem("Jgs")
    jmu = ctx.sem("Jmu")
    jc_ = [ctx.sem("Jc0"), ctx.sem("Jc1")]
    jo_ = [ctx.sem("Jo0"), ctx.sem("Jo1")]

    _g3 = ctx.sem("ptg3")
    for hc in range(32):
        ctx.do("sync", "dma_start", out=W3T[:, hc * 1024:(hc + 1) * 1024],
               in_=dram["w3_bf"][0:D, hc * 128:(hc + 1) * 128],
               transpose=True, waits=[(_g3, _g3.n)], incs=[(ld, 16)])
    ctx.do("gpsimd", "dma_start", out=s3r[:],
           in_=dram["L1_ffn_w3_s"].rearrange("(a b) -> a b", a=1), incs=[(jgs, 16)])
    ctx.do("vector", "memset", onesr[:], 1.0, incs=[(vc, 1)])
    for db in range(2):
        ctx.do("tensor", "matmul", pk[2 + db][:, 0:512], onesr[:],
               s3r[0:1, db * 512:(db + 1) * 512], start=True, stop=True,
               waits=[(jgs, 16), (vc, 1)], incs=[(tsn, 1)])
        ctx.do("scalar", "activation", s3b[:, db * 512:(db + 1) * 512],
               pk[2 + db][:, 0:512], AF.Identity, waits=[(tsn, tsn.n)],
               incs=[(ev, 1)])
    s3v = ev.n
    ld0 = ld.n

    cldv = {}
    lastc = {}
    mload = (vc, 0)

    def load_c(tb, w=()):
        s = jc_[tb % 2]
        ctx.do("gpsimd", "dma_start",
               out=cblk[tb % 2][:].rearrange("p (hc t) -> p hc t", t=512),
               in_=dram["c5"][0:DH, tb * 512:(tb + 1) * 512]
               .rearrange("(hc p) t -> p hc t", p=128),
               waits=w, incs=[(s, 16)])
        cldv[tb] = (s, s.n)

    load_c(0)
    bank_free = {i: ((ev, s3v) if i >= 2 else (tsn, 0)) for i in range(4)}
    t2f = [(vc, 0), (vc, 0)]
    ouf = [(jo_[0], 0), (jo_[1], 0)]
    for tb in range(8):
        if tb + 1 < 8:
            load_c(tb + 1, w=[(tsn, lastc.get(tb - 1, 0))] if tb >= 1 else ())
        ctx.do("sync", "dma_start",
               out=x1t[:].rearrange("p (tt d) -> p tt d", d=1024),
               in_=dram["x1"][tb * 512:(tb + 1) * 512, :]
               .rearrange("(tt p) d -> p tt d", p=128),
               waits=[mload, (gp, gp.n)], incs=[(jmu, 16)])
        ctx.do("sync", "dma_start",
               out=att[:].rearrange("p (tt d) -> p tt d", d=1024),
               in_=dram["attn"][tb * 512:(tb + 1) * 512, :]
               .rearrange("(tt p) d -> p tt d", p=128), incs=[(jmu, 16)])
        mlv = jmu.n
        for tt in range(4):
            for db in range(2):
                bk = (tt * 2 + db) % 4
                for hc in range(32):
                    ctx.do("tensor", "matmul", pk[bk][:, 0:512],
                           cblk[tb % 2][:, hc * 512 + tt * 128:hc * 512 + (tt + 1) * 128],
                           W3T[:, hc * 1024 + db * 512:hc * 1024 + (db + 1) * 512],
                           start=(hc == 0), stop=(hc == 31),
                           waits=[(ld, ld0), cldv[tb], bank_free[bk]]
                           if hc == 0 else (), incs=[(tsn, 1)] if hc == 31 else ())
                mmv = tsn.n
                oi = db
                ctx.do("vector", "tensor_tensor", t2[oi][:], pk[bk][:, 0:512],
                       s3b[:, db * 512:(db + 1) * 512], ALU.mult,
                       waits=[(tsn, mmv), (ev, s3v), t2f[oi]], incs=[(vc, 1)])
                bank_free[bk] = (vc, vc.n)
                ctx.do("gpsimd", "tensor_tensor", t2[oi][:], t2[oi][:],
                       x1t[:, tt * 1024 + db * 512:tt * 1024 + (db + 1) * 512],
                       ALU.add, waits=[(vc, vc.n), (jmu, mlv)], incs=[(gp, 1)])
                ctx.do("vector", "tensor_tensor", oub[oi][:], t2[oi][:],
                       att[:, tt * 1024 + db * 512:tt * 1024 + (db + 1) * 512],
                       ALU.add, waits=[(gp, gp.n), ouf[oi]], incs=[(vc, 1)])
                t2f[oi] = (vc, vc.n)
                ctx.do("scalar", "dma_start",
                       out=dram["out"][tb * 512 + tt * 128:tb * 512 + (tt + 1) * 128,
                                       db * 512:(db + 1) * 512],
                       in_=oub[oi][:], waits=[(vc, vc.n)], incs=[(jo_[oi], 16)])
                ouf[oi] = (jo_[oi], jo_[oi].n)
        lastc[tb] = tsn.n
        mload = (vc, vc.n)
    ctx.wait("scalar", jo_[0], jo_[0].n)
    ctx.wait("scalar", jo_[1], jo_[1].n)



PHASES = []


def _register_phases():
    PHASES.clear()
    PHASES.append(("pt", emit_pt,
                   [src for _, src, *_ in WCAST],
                   [n + "_bf" for n, *_ in WCAST]))
    PHASES.append(("p0", lambda c, d, e: emit_ln(c, d, e, "u", "xn0", "xn0f", "ln0"),
                   ["u"], ["xn0", "xn0f"]))
    PHASES.append(("p1a", emit_p1a,
                   ["wip_bf", "xn0f", "L0_attn_conv_w", "L0_attn_conv_b",
                    "L0_attn_dt_bias"],
                   ["zs", "xsbc", "dtv", "lndt"]))
    PHASES.append(("p1b", emit_p1b,
                   ["wop_bf", "zs", "xsbc", "dtv", "lndt", "L0_attn_A_log",
                    "L0_attn_Dp", "L0_attn_norm_w"],
                   ["macc"]))
    PHASES.append(("p2", emit_p2,
                   ["xn0f", "wr_bf", "wk_bf", "wv_bf", "L0_ffn_mu_r",
                    "L0_ffn_mu_k", "macc", "u"],
                   ["x1"]))
    PHASES.append(("p3", lambda c, d, e: emit_ln(c, d, e, "x1", "xn1", "xn1f", "ln1"),
                   ["x1"], ["xn1", "xn1f"]))
    PHASES.append(("p4a", emit_p4a,
                   ["xn1f", "qw_bf", "kw_bf", "vw_bf"], ["kf", "vt", "qf"]))
    PHASES.append(("pth", emit_pth,
                   ["L1_ffn_h1_w", "L1_ffn_h2_w"], ["h1_bf", "h2_bf"]))
    PHASES.append(("p4b", emit_p4b, ["kf", "vt", "qf"], ["attn"]))
    PHASES.append(("p5a", emit_p5a,
                   ["xn1f", "w1_bf", "w2_bf", "L1_ffn_w1_s", "L1_ffn_w2_s"], ["a5"]))
    PHASES.append(("p5b", emit_p5b, ["a5", "h1_bf", "L1_ffn_h1_s"], ["b5"]))
    PHASES.append(("p5c", emit_p5c, ["b5", "h2_bf", "L1_ffn_h2_s"], ["c5"]))
    PHASES.append(("p5d", emit_p5d,
                   ["c5", "w3_bf", "L1_ffn_w3_s", "x1", "attn"], ["out"]))


_register_phases()


def _declare_dram(nc, names, kinds):
    dram = {}
    for n in names:
        if n == "u":
            shape, dt = MODULE_IN["u"][1], MODULE_IN["u"][2]
        elif n == "out":
            shape, dt = [T, D], F32
        elif n in SCRATCH:
            shape, dt = SCRATCH[n]
        elif n in WEIGHT_IN:
            shape, dt = WEIGHT_IN[n]
        elif n in VEC_IN:
            shape, dt = VEC_IN[n]
        else:
            raise KeyError(n)
        kind = "ExternalInput" if kinds[n] == "in" else "ExternalOutput"
        dram[n] = nc.dram_tensor(n, shape, dt, kind=kind).ap()
    return dram


def _build_phase_program(pname):
    for name, emit, ins, outs in PHASES:
        if name == pname:
            break
    else:
        raise KeyError(pname)
    nc = bass.Bass("TRN2", target_bir_lowering=False)
    kinds = {n: "in" for n in ins}
    kinds.update({n: "out" for n in outs})
    es = contextlib.ExitStack()
    with es:
        dram = _declare_dram(nc, list(kinds), kinds)
        if pname == "pt":
            dram["zpad"] = nc.inline_tensor(
                np.zeros((DIP_PAD - D_IN_PROJ, D), dtype=BF), "zpad").ap()
        ctx = Ctx(nc)
        emit(ctx, dram, es)
        ctx.build_block(es)
    return nc, ins, outs


def _build_fused_program():
    nc = bass.Bass("TRN2", target_bir_lowering=False)
    dram = {}
    dram["u"] = nc.dram_tensor("u", MODULE_IN["u"][1], MODULE_IN["u"][2],
                               kind="ExternalInput").ap()
    for srcn, (shape, dt) in WEIGHT_IN.items():
        dram[srcn] = nc.dram_tensor(srcn, shape, dt, kind="ExternalInput").ap()
    for vn, (shape, dt) in VEC_IN.items():
        dram[vn] = nc.dram_tensor(vn, shape, dt, kind="ExternalInput").ap()
    dram["out"] = nc.dram_tensor("out", [T, D], F32, kind="ExternalOutput").ap()
    for sn, (shape, dt) in SCRATCH.items():
        dram[sn] = nc.dram_tensor(sn, shape, dt, kind="Internal").ap()
    ctx = Ctx(nc)
    ins = set()
    for name, emit, pin, pout in PHASES:
        for n in pin:
            if n == "u" or n in WEIGHT_IN or n in VEC_IN:
                ins.add(n)
        ctx.begin_phase(name)
        with contextlib.ExitStack() as pes:
            emit(ctx, dram, pes)
        if name != "pt":
            ctx.barrier()
    es = contextlib.ExitStack()
    with es:
        ctx.build_block(es)
    return nc, sorted(ins), ["out"]


FUSED = True

_PROG_CACHE = {}


def run_phase(pname, stores):
    if pname not in _PROG_CACHE:
        _PROG_CACHE[pname] = _build_phase_program(pname)
    nc, ins, outs = _PROG_CACHE[pname]
    in_maps = [{n: st[n] for n in ins} for st in stores]
    res = run_bass_kernel_spmd(nc, in_maps, core_ids=list(range(len(stores))))
    for st, r in zip(stores, res.results):
        for n in outs:
            st[n] = r[n]


def make_stores(inputs):
    stores = []
    u = np.ascontiguousarray(inputs["L0_attn_u"])
    for c in range(NCORES):
        st = {"u": u[c]}
        for k, v in inputs.items():
            if k != "L0_attn_u":
                st[k] = np.ascontiguousarray(v)
        stores.append(st)
    return stores


def kernel(**inputs):
    stores = make_stores(inputs)
    if FUSED:
        if "fused" not in _PROG_CACHE:
            _PROG_CACHE["fused"] = _build_fused_program()
        nc, ins, outs = _PROG_CACHE["fused"]
        in_maps = [{n: st[n] for n in ins} for st in stores]
        res = run_bass_kernel_spmd(nc, in_maps, core_ids=list(range(len(stores))))
        for st, r in zip(stores, res.results):
            st["out"] = r["out"]
    else:
        for pname, *_ in PHASES:
            run_phase(pname, stores)
    out = np.stack([st["out"] for st in stores], axis=0)
    return out.astype(np.float32)
